# version 05
# speedup vs baseline: 1.1413x; 1.0955x over previous
import math
import numpy as np
import ml_dtypes
import concourse.bass as bass
import concourse.mybir as mybir
from concourse.bass_utils import run_bass_kernel_spmd

F32 = mybir.dt.float32
BF16 = mybir.dt.bfloat16
AF = mybir.ActivationFunctionType
ALU = mybir.AluOpType
AX = mybir.AxisListType
BF = ml_dtypes.bfloat16

T = 4096
D = 1024
NCH = D // 128
CHUNK = 512
NCK = T // CHUNK
LN_EPS = 1e-5


def build_constants():
    sel8 = np.zeros((8, 8 * 128), dtype=BF)
    for e in range(8):
        sel8[e, e * 128:(e + 1) * 128] = 1
    triu = np.triu(np.ones((128, 128), dtype=np.float32))
    return {
        "c_ident": np.eye(128).astype(BF),
        "c_ones128": np.ones((128, 128), dtype=BF),
        "c_onesrow": np.ones((1, T), dtype=BF),
        "c_sel8": sel8,
        "c_triu": triu.astype(BF),
        "c_triuf": triu,
        "c_identf": np.eye(128, dtype=np.float32),
        "c_alibi": _alibi_table(),
        "c_maskq": np.concatenate([_mask(o * 128) for o in range(4)], axis=1),
        "c_nullrow": np.array([[-2.0, -1.0], [1.0, 1.0]], dtype=BF),
        "c_kaug2": np.stack([(np.arange(T) % 128), np.ones(T)]).astype(BF),
        "c_q512": (np.arange(T) % 512).astype(BF).reshape(1, T),
        "c_blk64": _blk64().astype(BF),
        "c_blk64f": _blk64(),
    }


def _alibi_slopes(heads=16):
    start = 2.0 ** (-2.0 ** (-(math.log2(heads) - 3.0)))
    return np.array([start * start ** i for i in range(heads)], dtype=np.float64)


def _alibi_table():
    sl = _alibi_slopes()
    t = np.zeros((16, 33), dtype=np.float32)
    for h in range(16):
        for dlt in range(-3, 30):
            t[h, dlt + 3] = -sl[h] * 128.0 * dlt
    return np.tile(t.reshape(1, 16 * 33), (128, 1)).astype(np.float32)


def _mask(off):
    p = np.arange(128)[:, None]
    f = np.arange(512)[None, :]
    return np.where((p + off) <= f, 0.0, -30000.0).astype(BF)


def _blk64():
    m = np.zeros((128, 64), dtype=np.float32)
    m[0:64, 0] = 1.0
    m[64:128, 32] = 1.0
    return m

def _l1_consts():
    bb = np.arange(64)[:, None]
    mm = np.arange(65)[None, :]
    aa = np.arange(64)[:, None]
    cc = np.arange(64)[None, :]
    C1 = np.concatenate([np.cos(2 * np.pi * bb * mm / 128.0),
                         -np.sin(2 * np.pi * bb * mm / 128.0)], 1)
    C2c = np.cos(2 * np.pi * aa * cc / 64.0)
    C2s = np.sin(2 * np.pi * aa * cc / 64.0)
    z64 = np.zeros((64, 64))
    bd = lambda M: np.block([[M, z64], [z64, M]])
    BD = np.concatenate([bd(C2c), bd(C2s), bd(-C2s)], 1)
    TW = np.concatenate([np.tile(np.cos(2 * np.pi * aa * mm / 8192.0), (2, 1)),
                         np.tile(np.sin(2 * np.pi * aa * mm / 8192.0), (2, 1))], 1)
    wm = np.ones(65)
    wm[0] = wm[64] = 0.5
    mm2 = np.arange(65)[:, None]
    bb2 = np.arange(64)[None, :]
    ICb = np.concatenate([2 * wm[:, None] * np.cos(2 * np.pi * bb2 * mm2 / 128.0) / 8192.0,
                          -2 * wm[:, None] * np.sin(2 * np.pi * bb2 * mm2 / 128.0) / 8192.0], 1)
    return {"c_fC1": C1.astype(np.float32), "c_fBD": BD.astype(np.float32),
            "c_fTW": TW.astype(np.float32), "c_fICb": ICb.astype(np.float32)}


def _pos_enc():
    pos = np.arange(T, dtype=np.float64)[:, None]
    i = np.arange(D)[None, :]
    angle = pos / np.power(10000.0, (2 * (i // 2)) / float(D))
    p = np.where(i % 2 == 0, np.sin(angle), np.cos(angle))
    return np.ascontiguousarray(p.T).astype(np.float32)


CONSTS = build_constants()
CONSTS.update(_l1_consts())
BIGCONSTS = {"c_posT": _pos_enc()}
CONST_DTYPES = {k: v.dtype for k, v in CONSTS.items()}



class Sem:
    __slots__ = ("h", "n")

    def __init__(self, nc, name):
        self.h = nc.alloc_semaphore(name)
        self.n = 0

    def inc(self, ins, k=1):
        ins.then_inc(self.h, k)
        self.n += k
        return ins

    def wait(self, eng, v=None):
        eng.wait_ge(self.h, self.n if v is None else v)


class Ctx:
    def __init__(self, nc):
        self.nc = nc
        self.dram = {}
        self.sb = {}
        self._names = set()
        self._frame = ""
        self.banks = [nc.alloc_psum_tensor("bank%d" % j, [128, 512], F32) for j in range(8)]

    def _uniq(self, name):
        name = self._frame + name
        assert name not in self._names, name
        self._names.add(name)
        return name

    def frame(self, tag, extras=()):
        import contextlib

        @contextlib.contextmanager
        def _cm():
            self._frame = tag + "_"
            self.sb = {}
            if hasattr(self, "_ln_bufs"):
                del self._ln_bufs
            with self.nc.cleanup_on_exit():
                self.load_consts(extra=extras)
                yield
                self.nc.all_engine_barrier()
        return _cm()

    def din(self, name, shape, dtype=F32):
        ap = self.nc.dram_tensor(name, list(shape), dtype, kind="ExternalInput").ap()
        self.dram[name] = ap
        return ap

    def dout(self, name, shape, dtype=F32):
        ap = self.nc.dram_tensor(name, list(shape), dtype, kind="ExternalOutput").ap()
        self.dram[name] = ap
        return ap

    def dint(self, name, shape, dtype=F32):
        ap = self.nc.dram_tensor(name, list(shape), dtype, kind="Internal").ap()
        self.dram[name] = ap
        return ap

    def sbuf(self, name, shape, dtype=F32):
        t = self.nc.alloc_sbuf_tensor(self._uniq(name), list(shape), dtype)
        self.sb[name] = t
        return t

    def sem(self, name):
        return Sem(self.nc, self._uniq(name))

    def bank(self, j, rows=128, cols=512, dtype=F32):
        ap = self.banks[j][:]
        if dtype != F32:
            ap = ap.bitcast(dtype)
        return ap[0:rows, 0:cols]

    def ln_bufs(self):
        if not hasattr(self, "_ln_bufs"):
            self._ln_bufs = {
                "xb": self.sbuf("lnb_xb", [128, NCH * CHUNK]),
                "xbf": self.sbuf("lnb_xbf", [128, NCH * CHUNK], BF16),
                "meanS": [self.sbuf("lnb_m0", [128, CHUNK]), self.sbuf("lnb_m1", [128, CHUNK])],
                "rstd": [self.sbuf("lnb_r0", [128, CHUNK]), self.sbuf("lnb_r1", [128, CHUNK])],
                "m2": self.sbuf("lnb_m2", [128, CHUNK]),
                "varS": self.sbuf("lnb_varS", [128, CHUNK]),
            }
        return self._ln_bufs

    def load_consts(self, extra=()):
        for k, v in CONSTS.items():
            if k not in self.dram:
                self.din(k, list(v.shape), BF16 if v.dtype == BF else F32)
        specs = {
            "ident": ("c_ident", [128, 128], BF16),
            "ones128": ("c_ones128", [128, 128], BF16),
            "onesrow": ("c_onesrow", [1, T], BF16),
            "identf": ("c_identf", [128, 128], F32),
            "sel8": ("c_sel8", [8, 8 * 128], BF16),
            "triu": ("c_triu", [128, 128], BF16),
            "triuf": ("c_triuf", [128, 128], F32),
            "alibit": ("c_alibi", [128, 16 * 33], F32),
            "maskq": ("c_maskq", [128, 4 * 512], BF16),
            "q512": ("c_q512", [1, T], BF16),
            "blk64": ("c_blk64", [128, 64], BF16),
        }
        names = ["ident", "ones128", "onesrow", "identf"] + list(extra)
        tensors = {n: self.sbuf(n, specs[n][1], specs[n][2]) for n in names}
        self.identf = tensors["identf"]
        epscol = self.sbuf("epscol", [128, 1])
        epscol2 = self.sbuf("epscol2", [128, 1])
        s = self.sem("s_const")
        with self.nc.Block() as b:
            @b.sync
            def _(q):
                for n in names:
                    q.dma_start(tensors[n][:], self.dram[specs[n][0]]).then_inc(s.h, 16)
            s.n = len(names) * 16

            @b.vector
            def _(v):
                v.memset(epscol[:], LN_EPS)
                v.memset(epscol2[:], LN_EPS * LN_EPS).then_inc(s.h, 1)
            s.n += 1
        self.s_const = s
        return s


def tile(t2d, c, w=T):
    return t2d[:, c * w:(c + 1) * w]


def mm_chain(t_eng, psum_ap, pairs):
    n = len(pairs)
    ins = None
    for j, (lh, rh) in enumerate(pairs):
        ins = t_eng.matmul(psum_ap, lh, rh, start=(j == 0), stop=(j == n - 1))
    return ins



def emit_ln(ctx, src_name, lnx_at, tag, double_ln=False, bank0=0, src_sem=None,
            ring=None, cons=None, to_hbm=None, lnx_view=None):
    nc = ctx.nc
    ones128 = ctx.sb["ones128"]
    src = ctx.dram[src_name].rearrange("(c p) t -> p c t", p=128)

    s_in = [ctx.sem(tag + "_in0"), ctx.sem(tag + "_in1")]
    s_mm1 = ctx.sem(tag + "_mm1")
    s_mm = ctx.sem(tag + "_mm")
    s_cast = ctx.sem(tag + "_cast")
    s_csq = ctx.sem(tag + "_csq")
    s_stat = ctx.sem(tag + "_stat")
    s_done = ctx.sem(tag + "_done")
    s_psf = ctx.sem(tag + "_psf")

    lb = ctx.ln_bufs()
    xb_ = lb["xb"]
    xb = [xb_, xb_]
    xbf_ = lb["xbf"]
    xbf = [xbf_, xbf_]
    meanS = lb["meanS"]
    rstd = lb["rstd"]
    m2 = lb["m2"]
    varS = lb["varS"]
    p_sum = [ctx.bank(bank0), ctx.bank(bank0 + 1)]
    p_sq = [ctx.bank(bank0 + 2), ctx.bank(bank0 + 3)]

    if to_hbm is not None:
        lnx_dst = ctx.dram[to_hbm].rearrange("(c p) t -> p c t", p=128)
        s_lnst = [ctx.sem(tag + "_st0"), ctx.sem(tag + "_st1")]

    with nc.Block() as b:
        if to_hbm is not None:
            @b.gpsimd
            def _(g):
                for k in range(NCK):
                    g.wait_ge(s_done.h, (k + 1) * 2)
                    g.dma_start(lnx_dst[:, :, k * CHUNK:(k + 1) * CHUNK],
                                lnx_view(k)).then_inc(s_lnst[k % 2].h, 16)
            s_lnst[0].n = (NCK // 2) * 16
            s_lnst[1].n = (NCK // 2) * 16

        @b.sync
        def _(q):
            q.wait_ge(ctx.s_const.h, ctx.s_const.n)
            for k in range(NCK):
                if src_sem is not None:
                    for _sm, _fn in (src_sem if isinstance(src_sem, list) else [src_sem]):
                        q.wait_ge(_sm.h, _fn(k))
                if k >= 1:
                    q.wait_ge(s_done.h, k * 2)
                q.dma_start(xb[k % 2][:].rearrange("p (c f) -> p c f", c=NCH),
                            src[:, :, k * CHUNK:(k + 1) * CHUNK]).then_inc(s_in[k % 2].h, 16)
        s_in[0].n = (NCK // 2) * 16
        s_in[1].n = (NCK - NCK // 2) * 16

        @b.scalar
        def _(a):
            for k in range(NCK):
                i = k % 2
                a.wait_ge(s_in[k % 2].h, (k // 2 + 1) * 16)
                if k >= 1:
                    a.wait_ge(s_mm.h, k)
                if k >= 2:
                    a.wait_ge(s_done.h, (k - 1) * 2)
                a.copy(xbf[i][:], xb[i][:]).then_inc(s_cast.h, 1)
                a.wait_ge(s_mm1.h, k + 1)
                a.activation(meanS[i][:], p_sum[i], AF.Copy, scale=1.0 / D).then_inc(s_psf.h, 1)
                a.activation(xbf[i][:], xbf[i][:], AF.Square).then_inc(s_csq.h, 1)
                a.wait_ge(s_psf.h, 2 * (k + 1))
                if double_ln:
                    a.activation(varS[:], varS[:], AF.Ln, scale=(1.0 + LN_EPS),
                                 bias=ctx.sb["epscol2"][:, 0:1])
                else:
                    a.activation(varS[:], varS[:], AF.Ln, bias=ctx.sb["epscol"][:, 0:1])
                a.activation(rstd[i][:], varS[:], AF.Exp, scale=-0.5).then_inc(s_stat.h, 1)
        s_cast.n = NCK
        s_csq.n = NCK
        s_stat.n = NCK

        @b.tensor
        def _(t):
            for k in range(NCK):
                i = k % 2
                t.wait_ge(s_cast.h, k + 1)
                if k >= 2:
                    t.wait_ge(s_psf.h, (k - 1) * 2)
                for c in range(NCH):
                    ins = t.matmul(p_sum[i], ones128[:], xbf[i][:, c * CHUNK:(c + 1) * CHUNK],
                                   start=(c == 0), stop=(c == NCH - 1))
                ins.then_inc(s_mm1.h, 1)
                t.wait_ge(s_csq.h, k + 1)
                for c in range(NCH):
                    ins = t.matmul(p_sq[i], ones128[:], xbf[i][:, c * CHUNK:(c + 1) * CHUNK],
                                   start=(c == 0), stop=(c == NCH - 1))
                ins.then_inc(s_mm.h, 1)
        s_mm1.n = NCK
        s_mm.n = NCK

        @b.vector
        def _(v):
            for k in range(NCK):
                i = k % 2
                v.wait_ge(s_mm.h, k + 1)
                v.wait_ge(s_psf.h, 2 * k + 1)
                if k >= 1:
                    v.wait_ge(s_stat.h, k)
                v.tensor_mul(m2[:], meanS[i][:], meanS[i][:])
                v.scalar_tensor_tensor(varS[:], p_sq[i], 1.0 / D, m2[:],
                                       ALU.mult, ALU.subtract).then_inc(s_psf.h, 1)
                v.wait_ge(s_stat.h, k + 1)
                if ring is not None and k >= ring:
                    if to_hbm is not None:
                        v.wait_ge(s_lnst[k % 2].h, ((k - ring) // 2 + 1) * 16)
                    else:
                        v.wait_ge(cons[0].h, cons[1](k - ring))
                for c in range(NCH):
                    sl = slice(c * CHUNK, (c + 1) * CHUNK)
                    v.tensor_sub(xb[i][:, sl], xb[i][:, sl], meanS[i][:])
                ins = None
                for c in range(NCH):
                    ins = v.tensor_mul(lnx_at(c, k),
                                       xb[i][:, c * CHUNK:(c + 1) * CHUNK], rstd[i][:])
                ins.then_inc(s_done.h, 2)
        s_psf.n += NCK
        s_done.n = NCK * 2

    if to_hbm is not None:
        return s_done, s_lnst
    return s_done



def emit_residual_out(ctx, tag, src_name, dst_name, s_blk, blk_count_fn, emit_mm_fn,
                      s_pe=None, extra_sb=None, extra_stream=None, bufs=None,
                      sync_extra=None):
    nc = ctx.nc
    src = ctx.dram[src_name].rearrange("(c p) t -> p c t", p=128)
    dst = ctx.dram[dst_name].rearrange("(c p) t -> p c t", p=128)
    s_xin = ctx.sem(tag + "_xin")
    if s_pe is None:
        s_pe = ctx.sem(tag + "_pe")
    s_add = ctx.sem(tag + "_add")
    s_out = ctx.sem(tag + "_out")
    xb_ = ctx.sbuf(tag + "_xb", [128, NCH * CHUNK]) if bufs is None or "xb" not in bufs else bufs["xb"]
    yb_ = ctx.sbuf(tag + "_yb", [128, NCH * CHUNK]) if bufs is None or "yb" not in bufs else bufs["yb"]
    xb = [xb_, xb_]
    yb = [yb_, yb_]
    if extra_stream is not None:
        ename, edt = (extra_stream, BF16) if isinstance(extra_stream, str) else extra_stream
        eb = ctx.sbuf(tag + "_eb", [128, NCH * CHUNK], edt) if bufs is None or "eb" not in bufs else bufs["eb"]
        esrc = ctx.dram[ename].rearrange("(c p) t -> p c t", p=128)
        s_ein = ctx.sem(tag + "_ein")
        extra_sb = lambda k, o: eb[:, o * CHUNK:(o + 1) * CHUNK]

    with nc.Block() as b:
        @b.sync
        def _(q):
            for k in range(NCK):
                if k >= 1:
                    q.wait_ge(s_add.h, k * 8)
                q.dma_start(xb[k % 2][:].rearrange("p (c f) -> p c f", c=NCH),
                            src[:, :, k * CHUNK:(k + 1) * CHUNK]).then_inc(s_xin.h, 16)
                if extra_stream is not None:
                    q.dma_start(eb[:].rearrange("p (c f) -> p c f", c=NCH),
                                esrc[:, :, k * CHUNK:(k + 1) * CHUNK]).then_inc(s_ein.h, 16)
                if sync_extra is not None:
                    sync_extra(q, k)
        s_xin.n = NCK * 16

        @b.gpsimd
        def _(g):
            for k in range(NCK):
                g.wait_ge(s_add.h, (k + 1) * 8)
                g.dma_start(dst[:, :, k * CHUNK:(k + 1) * CHUNK],
                            yb[k % 2][:].rearrange("p (c f) -> p c f", c=NCH)).then_inc(s_out.h, 16)
            g.wait_ge(s_out.h, NCK * 16)
        s_out.n = NCK * 16

        @b.tensor
        def _(t):
            if s_blk is not None:
                t.wait_ge(s_blk.h, s_blk.n)
            for k in range(NCK):
                for o in range(NCH):
                    n = k * 8 + o
                    j = 4 + n % 4
                    if n >= 4:
                        t.wait_ge(s_add.h, n - 4 + 1)
                    if blk_count_fn is not None:
                        t.wait_ge(blk_count_fn[0].h, blk_count_fn[1](k))
                    emit_mm_fn(t, k, o, ctx.bank(j)).then_inc(s_pe.h, 1)
        s_pe.n = NCK * 8

        @b.vector
        def _(v):
            for k in range(NCK):
                i = k % 2
                v.wait_ge(s_xin.h, (k + 1) * 16)
                if extra_stream is not None:
                    v.wait_ge(s_ein.h, (k + 1) * 16)
                if k >= 1:
                    v.wait_ge(s_out.h, k * 16)
                for o in range(NCH):
                    n = k * 8 + o
                    j = 4 + n % 4
                    v.wait_ge(s_pe.h, n + 1)
                    if extra_sb is not None:
                        v.tensor_add(yb[i][:, o * CHUNK:(o + 1) * CHUNK],
                                     xb[i][:, o * CHUNK:(o + 1) * CHUNK],
                                     ctx.bank(j))
                        v.tensor_add(yb[i][:, o * CHUNK:(o + 1) * CHUNK],
                                     yb[i][:, o * CHUNK:(o + 1) * CHUNK],
                                     extra_sb(k, o)).then_inc(s_add.h, 1)
                    else:
                        v.tensor_add(yb[i][:, o * CHUNK:(o + 1) * CHUNK],
                                     xb[i][:, o * CHUNK:(o + 1) * CHUNK],
                                     ctx.bank(j)).then_inc(s_add.h, 1)
        s_add.n = NCK * 8
    return s_out



def emit_l0ffn_weights(ctx):
    nc = ctx.nc
    ident = ctx.sb["ident"]
    onesrow = ctx.sb["onesrow"]
    W = ctx.dram["L0_ffn_W"]
    bvec = ctx.dram["L0_ffn_b"]

    s_win = ctx.sem("f0_win")
    s_pe = ctx.sem("f0_pe")
    s_ev = ctx.sem("f0_ev")

    ldT = ctx.sbuf("f0_ldT", [100, 4], BF16)
    luT = ctx.sbuf("f0_luT", [4, 50], BF16)
    daS = ctx.sbuf("f0_da", [100, 1024], BF16)
    uaT = ctx.sbuf("f0_uaT", [50, 1024], BF16)
    ld_f = ctx.sbuf("f0_ld_f", [4, 100], BF16)
    lu_f = ctx.sbuf("f0_lu_f", [50, 4], BF16)
    ua_f = ctx.sbuf("f0_ua_f", [128, 8 * 50], BF16)
    r1 = ctx.sbuf("f0_r1", [4, 1024], BF16)
    r2 = ctx.sbuf("f0_r2", [50, 1024], BF16)
    brow = ctx.sbuf("f0_brow", [1, 1024], BF16)

    p_a = ctx.bank(4)
    p_b = ctx.bank(5)
    p_t = ctx.bank(6, 128, 256, BF16)

    with nc.Block() as b:
        @b.gpsimd
        def _(g):
            g.dma_start(ld_f[:], ctx.dram["L0_ffn_lora_down"]).then_inc(s_win.h, 16)
            g.dma_start(lu_f[:], ctx.dram["L0_ffn_lora_up"]).then_inc(s_win.h, 16)
            g.dma_start(daS[:], ctx.dram["L0_ffn_down_aux"]).then_inc(s_win.h, 16)
            g.dma_start(ua_f[:].rearrange("p (c f) -> p c f", c=8),
                        ctx.dram["L0_ffn_up_aux"].rearrange("(c p) f -> p c f", p=128)).then_inc(s_win.h, 16)
            g.dma_start(brow[:], bvec.rearrange("(one n) -> one n", one=1)).then_inc(s_win.h, 16)
        s_win.n = 5 * 16

        @b.tensor
        def _(t):
            t.wait_ge(ctx.s_const.h, ctx.s_const.n)
            t.wait_ge(s_win.h, s_win.n)
            t.transpose(p_t[0:100, 0:4], ld_f[:], ident[0:4, 0:4]).then_inc(s_pe.h, 1)
            t.wait_ge(s_ev.h, 1)
            t.transpose(p_t[0:4, 0:50], lu_f[:], ident[0:50, 0:50]).then_inc(s_pe.h, 1)
            t.wait_ge(s_ev.h, 2)
            for c in range(8):
                t.transpose(p_t[0:50, 0:128], ua_f[:, c * 50:(c + 1) * 50], ident[:]).then_inc(s_pe.h, 1)
                t.wait_ge(s_ev.h, 2 + c + 1)
            t.matmul(p_a[0:4, :], ldT[:], daS[:, 0:512], start=True, stop=True)
            t.matmul(p_b[0:4, :], ldT[:], daS[:, 512:1024], start=True, stop=True).then_inc(s_pe.h, 1)
            t.wait_ge(s_ev.h, 11)
            t.matmul(p_a[0:50, :], luT[:], r1[:, 0:512], start=True, stop=True)
            t.matmul(p_b[0:50, :], luT[:], r1[:, 512:1024], start=True, stop=True).then_inc(s_pe.h, 1)
            t.wait_ge(s_ev.h, 12)
        s_pe.n = 12

        @b.scalar
        def _(a):
            a.wait_ge(s_pe.h, 1)
            a.copy(ldT[:], p_t[0:100, 0:4]).then_inc(s_ev.h, 1)
            a.wait_ge(s_pe.h, 2)
            a.copy(luT[:], p_t[0:4, 0:50]).then_inc(s_ev.h, 1)
            for c in range(8):
                a.wait_ge(s_pe.h, 2 + c + 1)
                a.copy(uaT[:, c * 128:(c + 1) * 128], p_t[0:50, 0:128]).then_inc(s_ev.h, 1)
            a.wait_ge(s_pe.h, 11)
            a.copy(r1[:, 0:512], p_a[0:4, :])
            a.copy(r1[:, 512:1024], p_b[0:4, :]).then_inc(s_ev.h, 1)
            a.wait_ge(s_pe.h, 12)
            a.copy(r2[:, 0:512], p_a[0:50, :])
            a.copy(r2[:, 512:1024], p_b[0:50, :]).then_inc(s_ev.h, 1)
        s_ev.n = 12

    s_wld = [ctx.sem("f0_wld0"), ctx.sem("f0_wld1")]
    s_lmm = ctx.sem("f0_lmm")
    s_wadd = ctx.sem("f0_wadd")
    s_tr = ctx.sem("f0_tr")
    s_trev = ctx.sem("f0_trev")
    wtile = [ctx.sbuf("f0_wt0", [128, 1024]), ctx.sbuf("f0_wt1", [128, 1024])]
    p_t2 = [ctx.bank(6, 128, 128, BF16), ctx.bank(7, 128, 128, BF16)]
    weff_bf = [ctx.sbuf("f0_wb0", [128, 1024], BF16), ctx.sbuf("f0_wb1", [128, 1024], BF16)]
    lhsT = ctx.sbuf("f0_lhsT", [128, NCH * 1024], BF16)

    with nc.Block() as b:
        @b.sync
        def _(q):
            for o in range(8):
                if o >= 2:
                    q.wait_ge(s_wadd.h, 2 * (o - 1))
                q.dma_start(wtile[o % 2][:], W.rearrange("(o p) i -> p o i", p=128)[:, o]).then_inc(s_wld[o % 2].h, 16)
        s_wld[0].n = 4 * 16
        s_wld[1].n = 4 * 16

        @b.tensor
        def _(t):
            t.wait_ge(s_ev.h, 12)
            for o in range(8):
                if o >= 1:
                    t.wait_ge(s_wadd.h, 2 * o)
                t.matmul(p_a[:], uaT[:, o * 128:(o + 1) * 128], r2[:, 0:512], start=True, stop=True)
                t.matmul(p_b[:], uaT[:, o * 128:(o + 1) * 128], r2[:, 512:1024],
                         start=True, stop=True).then_inc(s_lmm.h, 1)
                t.wait_ge(s_wadd.h, 2 * (o + 1))
                for i in range(8):
                    n = o * 8 + i
                    if n >= 2:
                        t.wait_ge(s_trev.h, n - 1)
                    t.transpose(p_t2[n % 2], weff_bf[o % 2][:, i * 128:(i + 1) * 128],
                                ident[:]).then_inc(s_tr.h, 1)
        s_lmm.n = 8
        s_tr.n = 64

        @b.vector
        def _(v):
            for o in range(8):
                v.wait_ge(s_wld[o % 2].h, (o // 2 + 1) * 16)
                v.wait_ge(s_lmm.h, o + 1)
                if o >= 2:
                    v.wait_ge(s_tr.h, (o - 1) * 8)
                v.tensor_add(weff_bf[o % 2][:, 0:512], wtile[o % 2][:, 0:512], p_a[:]).then_inc(s_wadd.h, 1)
                v.tensor_add(weff_bf[o % 2][:, 512:1024], wtile[o % 2][:, 512:1024], p_b[:]).then_inc(s_wadd.h, 1)
        s_wadd.n = 16

        @b.scalar
        def _(a):
            for o in range(8):
                for i in range(8):
                    n = o * 8 + i
                    a.wait_ge(s_tr.h, n + 1)
                    a.copy(lhsT[:, i * 1024 + o * 128: i * 1024 + (o + 1) * 128],
                           p_t2[n % 2]).then_inc(s_trev.h, 1)
        s_trev.n = 64

    return dict(lhsT=lhsT, brow=brow, s_trev=s_trev)


def emit_l0ffn_main(ctx, w, lnx_at, s_ln, src_name, out_name):
    onesrow = ctx.sb["onesrow"]
    lhsT, brow = w["lhsT"], w["brow"]

    def mm_fn(t, k, o, bank_ap):
        pairs = [(lhsT[:, i * 1024 + o * 128: i * 1024 + (o + 1) * 128],
                  lnx_at(i, k)) for i in range(8)]
        pairs.append((brow[0:1, o * 128:(o + 1) * 128], onesrow[0:1, k * CHUNK:(k + 1) * CHUNK]))
        return mm_chain(t, bank_ap, pairs)

    return emit_residual_out(ctx, "f0m", src_name, out_name,
                             w["s_trev"], (s_ln, lambda k: 2 * (k + 1)), mm_fn)



def emit_l3ffn(ctx, s_lnst, lnh_name, src_name, out_name, pre=None):
    nc = ctx.nc
    I32 = mybir.dt.int32
    ident = ctx.sb["ident"]
    onesrow = ctx.sb["onesrow"]
    lnh = ctx.dram[lnh_name].rearrange("(c p) t -> p c t", p=128)

    wbase = ctx.sbuf("t3_wbase", [128, 8 * 1024], BF16)
    wdown = ctx.sbuf("t3_wdown", [128, 8 * 8 * 64], BF16)
    wup = ctx.sbuf("t3_wup", [64, 8 * 1024], BF16)
    bup8 = ctx.sbuf("t3_bup8", [8, 1024], BF16)
    r_bbase = ctx.sbuf("t3_rbbase", [1, 1024], BF16)
    r_benc = ctx.sbuf("t3_rbenc", [1, 2048], BF16)
    r_bdown = ctx.sbuf("t3_rbdown", [1, 512], BF16)
    r_bdec = ctx.sbuf("t3_rbdec", [1, 1024], BF16)
    encbuf = ctx.sbuf("t3_enc", [128, 2048], BF16)
    decbuf = ctx.sbuf("t3_dec", [128, 2048], BF16)
    hbuf = ctx.sbuf("t3_h", [128, 4 * 512], BF16)
    lnr1 = ctx.sbuf("t3_lnr1", [128, 16 * 512], BF16)
    lnr3 = ctx.sbuf("t3_lnr3", [128, 16 * 512], BF16)
    mids = ctx.sbuf("t3_mids", [64, 8 * 512], BF16)
    dscr = ctx.sbuf("t3_dscr", [128, 512])
    dslots = ctx.sbuf("t3_dslots", [128, 64])
    lossparts = ctx.sbuf("t3_lparts", [128, 8])
    ones128f = ctx.sbuf("t3_ones128f", [128, 1])
    lossrow = ctx.sbuf("t3_lossrow", [1, 8])
    minrow = ctx.sbuf("t3_minrow", [1, 1])
    selrow = ctx.sbuf("t3_selrow", [1, 8])
    selrowbf = ctx.sbuf("t3_selrowbf", [1, 8], BF16)
    selcolbf = ctx.sbuf("t3_selcolbf", [8, 1], BF16)
    iota8 = ctx.sbuf("t3_iota8", [8, 8])
    cidxi = ctx.sbuf("t3_cidxi", [8, 1], I32)
    cidxf = ctx.sbuf("t3_cidxf", [8, 1])
    Mbf = ctx.sbuf("t3_M", [8, 8], BF16)
    selexpbf = ctx.sbuf("t3_selexp", [8, 1], BF16)
    selexprow = ctx.sbuf("t3_selexprow", [1, 8], BF16)
    SEL = ctx.sbuf("t3_SEL", [128, 8])
    badapt = ctx.sbuf("t3_badapt", [1, 1024], BF16)

    W_enc = ctx.dram["L3_ffn_W_enc"]
    W_dec = ctx.dram["L3_ffn_W_dec"]

    s_w0 = ctx.sem("t3_w0")
    s_w0b = ctx.sem("t3_w0b")

    with nc.Block() as b:
        @b.gpsimd
        def _(g):
            g.dma_start(wbase[:].rearrange("p (c o) -> p c o", c=8),
                        ctx.dram["L3_ffn_W_base"].rearrange("(c p) o -> p c o", p=128)).then_inc(s_w0.h, 16)
            g.dma_start(wdown[:].rearrange("p (e c w) -> p e c w", e=8, c=8),
                        ctx.dram["L3_ffn_W_down"].rearrange("e (c p) w -> p e c w", p=128)).then_inc(s_w0.h, 16)
            g.dma_start(wup[:].rearrange("r (e o) -> r e o", e=8),
                        ctx.dram["L3_ffn_W_up"].rearrange("e r o -> r e o")).then_inc(s_w0.h, 16)
            g.dma_start(bup8[:], ctx.dram["L3_ffn_b_up"]).then_inc(s_w0.h, 16)
            g.dma_start(r_bbase[:], ctx.dram["L3_ffn_b_base"].rearrange("(o n) -> o n", o=1)).then_inc(s_w0.h, 16)
            g.dma_start(r_benc[:], ctx.dram["L3_ffn_b_enc"].rearrange("e w -> (e w)").rearrange("(o n) -> o n", o=1)).then_inc(s_w0.h, 16)
            g.dma_start(r_bdown[:], ctx.dram["L3_ffn_b_down"].rearrange("e w -> (e w)").rearrange("(o n) -> o n", o=1)).then_inc(s_w0.h, 16)
            g.dma_start(cidxi[:], ctx.dram["L3_ffn_connected_idx"].rearrange("(p o) -> p o", o=1)).then_inc(s_w0.h, 16)
            g.iota(iota8[:], pattern=[[1, 8]], base=0, channel_multiplier=0,
                   allow_small_or_imprecise_dtypes=True).then_inc(s_w0b.h, 1)
        s_w0.n = 8 * 16
        s_w0b.n = 1

        @b.vector
        def _(v):
            v.memset(ones128f[:], 1.0)
            v.wait_ge(s_w0.h, 8 * 16)
            v.wait_ge(s_w0b.h, 1)
            v.tensor_copy(cidxf[:], cidxi[:]).then_inc(s_w0b.h, 1)
            v.wait_ge(s_w0b.h, 2)
            v.tensor_scalar(Mbf[:], iota8[:], cidxf[:, 0:1], None, ALU.is_equal).then_inc(s_w0b.h, 1)
        s_w0b.n = 3

    s_ed = ctx.sem("t3_ed")
    s_ln1 = [ctx.sem("t3_ln1a"), ctx.sem("t3_ln1b")]
    s_hmm = ctx.sem("t3_hmm")
    s_hev = ctx.sem("t3_hev")
    s_dmm = ctx.sem("t3_dmm")
    s_dve = ctx.sem("t3_dve")
    with nc.Block() as b:
        @b.gpsimd
        def _(g):
            for e in range(8):
                if e >= 1:
                    g.wait_ge(s_dmm.h, e * 64)
                g.dma_start(encbuf[:].rearrange("p (c w) -> p c w", c=8),
                            W_enc.rearrange("e (c p) w -> e p c w", p=128)[e]).then_inc(s_ed.h, 16)
                g.dma_start(decbuf[:].rearrange("p (rt o) -> p rt o", rt=2),
                            W_dec.rearrange("e (rt p) o -> e p rt o", p=128)[e]).then_inc(s_ed.h, 16)
                g.dma_start(r_bdec[:], ctx.dram["L3_ffn_b_dec"][e].rearrange("(o n) -> o n", o=1)).then_inc(s_ed.h, 16)

        @b.sync
        def _(q):
            for e in range(8):
                for k in range(NCK):
                    m = e * 8 + k
                    if m >= 2:
                        e2, k2 = divmod(m - 2, 8)
                        q.wait_ge(s_dve.h, e2 * 65 + (k2 + 1) * 8)
                    q.wait_ge(s_lnst[k % 2].h, (k // 2 + 1) * 16)
                    q.dma_start(lnr1[:].rearrange("p (c s f) -> p c s f", c=8, s=2)[:, :, k % 2, :],
                                lnh[:, :, k * CHUNK:(k + 1) * CHUNK]).then_inc(s_ln1[k % 2].h, 16)
        s_ed.n = 8 * 48
        s_ln1[0].n = 8 * 4 * 16
        s_ln1[1].n = 8 * 4 * 16

        @b.tensor
        def _(t):
            t.wait_ge(ctx.s_const.h, ctx.s_const.n)
            t.wait_ge(s_w0.h, 8 * 16)
            t.wait_ge(s_w0b.h, 3)
            for e in range(8):
                t.wait_ge(s_ed.h, (e + 1) * 48)
                for k in range(NCK):
                    t.wait_ge(s_ln1[k % 2].h, (e * 4 + k // 2 + 1) * 16)
                    for rt in range(2):
                        g = e * 16 + k * 2 + rt
                        if g >= 2:
                            t.wait_ge(s_hev.h, g - 1)
                        pairs = [(encbuf[:, c * 256 + rt * 128: c * 256 + (rt + 1) * 128],
                                  lnr1[:, (c * 2 + k % 2) * 512:(c * 2 + k % 2 + 1) * 512])
                                 for c in range(8)]
                        pairs.append((r_benc[0:1, e * 256 + rt * 128: e * 256 + (rt + 1) * 128],
                                      onesrow[0:1, k * CHUNK:(k + 1) * CHUNK]))
                        mm_chain(t, ctx.bank(6 + g % 2), pairs).then_inc(s_hmm.h, 1)
                    for o in range(NCH):
                        n = e * 64 + k * 8 + o
                        t.wait_ge(s_hev.h, e * 16 + k * 2 + 2)
                        if n >= 2:
                            e3, r3 = divmod(n - 2, 64)
                            t.wait_ge(s_dve.h, e3 * 65 + r3 + 1)
                        pairs = [(decbuf[:, rt * 1024 + o * 128: rt * 1024 + (o + 1) * 128],
                                  hbuf[:, (rt * 2 + k % 2) * 512:(rt * 2 + k % 2 + 1) * 512])
                                 for rt in range(2)]
                        pairs.append((r_bdec[0:1, o * 128:(o + 1) * 128],
                                      onesrow[0:1, k * CHUNK:(k + 1) * CHUNK]))
                        mm_chain(t, ctx.bank(4 + n % 2), pairs).then_inc(s_dmm.h, 1)
        s_hmm.n = 8 * 16
        s_dmm.n = 8 * 64

        @b.scalar
        def _(a):
            for e in range(8):
                for k in range(NCK):
                    for rt in range(2):
                        g = e * 16 + k * 2 + rt
                        a.wait_ge(s_hmm.h, g + 1)
                        a.activation(hbuf[:, (rt * 2 + k % 2) * 512:(rt * 2 + k % 2 + 1) * 512],
                                     ctx.bank(6 + g % 2), AF.Relu).then_inc(s_hev.h, 1)
        s_hev.n = 8 * 16

        @b.vector
        def _(v):
            for e in range(8):
                for k in range(NCK):
                    for o in range(NCH):
                        n = e * 64 + k * 8 + o
                        v.wait_ge(s_dmm.h, n + 1)
                        v.tensor_sub(dscr[:], ctx.bank(4 + n % 2),
                                     lnr1[:, (o * 2 + k % 2) * 512:(o * 2 + k % 2 + 1) * 512])
                        v.scalar_tensor_tensor(dscr[:], dscr[:], 0.0, dscr[:],
                                               ALU.bypass, ALU.mult,
                                               accum_out=dslots[:, k * 8 + o: k * 8 + o + 1]).then_inc(s_dve.h, 1)
                v.wait_ge(s_dve.h, e * 65 + 64)
                v.tensor_reduce(lossparts[:, e:e + 1], dslots[:], AX.X, ALU.add).then_inc(s_dve.h, 1)
        s_dve.n = 8 * 65

    s_p2 = ctx.sem("t3_p2")
    s_p2v = ctx.sem("t3_p2v")
    with nc.Block() as b:
        @b.tensor
        def _(t):
            t.wait_ge(s_dve.h, 8 * 65)
            t.matmul(ctx.bank(4, 1, 8), ones128f[:], lossparts[:], start=True, stop=True)
            t.matmul(ctx.bank(5, 8, 1), lossparts[:], ones128f[:], start=True, stop=True).then_inc(s_p2.h, 1)
            t.wait_ge(s_p2.h, 4)
            t.transpose(ctx.bank(6, 8, 1, BF16), selrowbf[:], ident[0:1, 0:1]).then_inc(s_p2.h, 1)
            t.wait_ge(s_p2.h, 6)
            t.matmul(ctx.bank(4, 8, 1), Mbf[:], selcolbf[:], start=True, stop=True).then_inc(s_p2.h, 1)
            t.wait_ge(s_p2.h, 8)
            t.transpose(ctx.bank(6, 1, 8, BF16), selexpbf[:], ident[0:8, 0:8]).then_inc(s_p2.h, 1)
            t.wait_ge(s_p2.h, 10)
            t.matmul(ctx.bank(4, 128, 8), onesrow[0:1, 0:128], selexprow[:], start=True, stop=True)
            t.matmul(ctx.bank(5, 1, 512), selexpbf[:], bup8[:, 0:512], start=True, stop=True)
            t.matmul(ctx.bank(7, 1, 512), selexpbf[:], bup8[:, 512:1024], start=True, stop=True).then_inc(s_p2.h, 1)
        @b.vector
        def _(v):
            v.wait_ge(s_p2.h, 1)
            v.tensor_copy(lossrow[:], ctx.bank(4, 1, 8)).then_inc(s_p2v.h, 1)
            v.wait_ge(s_p2v.h, 1)
            v.tensor_reduce(minrow[:], lossrow[:], AX.X, ALU.min).then_inc(s_p2v.h, 1)
            v.wait_ge(s_p2v.h, 2)
            v.tensor_scalar(selrow[:], lossrow[:], minrow[0:1, 0:1], None, ALU.is_le).then_inc(s_p2.h, 1)
        @b.scalar
        def _(a):
            a.wait_ge(s_p2.h, 2)
            a.copy(selrowbf[:], selrow[:]).then_inc(s_p2.h, 2)
            a.wait_ge(s_p2.h, 5)
            a.copy(selcolbf[:], ctx.bank(6, 8, 1, BF16)).then_inc(s_p2.h, 1)
            a.wait_ge(s_p2.h, 7)
            a.copy(selexpbf[:], ctx.bank(4, 8, 1)).then_inc(s_p2.h, 1)
            a.wait_ge(s_p2.h, 9)
            a.copy(selexprow[:], ctx.bank(6, 1, 8, BF16)).then_inc(s_p2.h, 1)
            a.wait_ge(s_p2.h, 11)
            a.copy(SEL[:], ctx.bank(4, 128, 8))
            a.copy(badapt[0:1, 0:512], ctx.bank(5, 1, 512))
            a.copy(badapt[0:1, 512:1024], ctx.bank(7, 1, 512)).then_inc(s_p2.h, 1)
    s_p2.n = 12

    src = ctx.dram[src_name].rearrange("(c p) t -> p c t", p=128)
    dst = ctx.dram[out_name].rearrange("(c p) t -> p c t", p=128)
    s_ln3 = [ctx.sem("t3_ln3a"), ctx.sem("t3_ln3b")]
    s_xin = ctx.sem("t3_xin")
    s_midmm = ctx.sem("t3_midmm")
    s_midev = ctx.sem("t3_midev")
    s_pe3 = ctx.sem("t3_pe3")
    s_add = ctx.sem("t3_add")
    s_out = [ctx.sem("t3_out0"), ctx.sem("t3_out1")]
    xb = ctx.sbuf("t3_xb", [128, NCH * CHUNK])
    ybr = ctx.sbuf("t3_ybr", [128, 2 * CHUNK])

    with nc.Block() as b:
        @b.sync
        def _(q):
            for k in range(NCK):
                if k >= 1:
                    q.wait_ge(s_add.h, k * 8)
                q.dma_start(xb[:].rearrange("p (c f) -> p c f", c=NCH),
                            src[:, :, k * CHUNK:(k + 1) * CHUNK]).then_inc(s_xin.h, 16)
                if k >= 2:
                    q.wait_ge(s_pe3.h, (k - 1) * 8)
                q.wait_ge(s_lnst[k % 2].h, (k // 2 + 1) * 16)
                q.dma_start(lnr3[:].rearrange("p (c s f) -> p c s f", c=8, s=2)[:, :, k % 2, :],
                            lnh[:, :, k * CHUNK:(k + 1) * CHUNK]).then_inc(s_ln3[k % 2].h, 16)
        s_xin.n = NCK * 16
        s_ln3[0].n = 4 * 16
        s_ln3[1].n = 4 * 16

        @b.tensor
        def _(t):
            t.wait_ge(s_p2.h, 12)
            for k in range(NCK):
                t.wait_ge(s_ln3[k % 2].h, (k // 2 + 1) * 16)
                for e in range(8):
                    g = k * 8 + e
                    if g >= 2:
                        t.wait_ge(s_midev.h, g - 1)
                    if k >= 1:
                        t.wait_ge(s_pe3.h, k * 8)
                    pairs = [(wdown[:, (e * 8 + c) * 64:(e * 8 + c + 1) * 64],
                              lnr3[:, (c * 2 + k % 2) * 512:(c * 2 + k % 2 + 1) * 512])
                             for c in range(8)]
                    pairs.append((r_bdown[0:1, e * 64:(e + 1) * 64],
                                  onesrow[0:1, k * CHUNK:(k + 1) * CHUNK]))
                    mm_chain(t, ctx.bank(6 + g % 2, 64, 512), pairs).then_inc(s_midmm.h, 1)
                for o in range(NCH):
                    n = k * 8 + o
                    if n >= 2:
                        t.wait_ge(s_add.h, n - 1)
                    t.wait_ge(s_midev.h, (k + 1) * 8)
                    pairs = [(wbase[:, c * 1024 + o * 128: c * 1024 + (o + 1) * 128],
                              lnr3[:, (c * 2 + k % 2) * 512:(c * 2 + k % 2 + 1) * 512])
                             for c in range(8)]
                    pairs.append((r_bbase[0:1, o * 128:(o + 1) * 128],
                                  onesrow[0:1, k * CHUNK:(k + 1) * CHUNK]))
                    for e in range(8):
                        pairs.append((wup[:, e * 1024 + o * 128: e * 1024 + (o + 1) * 128],
                                      mids[:, e * 512:(e + 1) * 512]))
                    pairs.append((badapt[0:1, o * 128:(o + 1) * 128],
                                  onesrow[0:1, k * CHUNK:(k + 1) * CHUNK]))
                    mm_chain(t, ctx.bank(4 + n % 2), pairs).then_inc(s_pe3.h, 1)
        s_midmm.n = NCK * 8
        s_pe3.n = NCK * 8

        @b.scalar
        def _(a):
            for k in range(NCK):
                for e in range(8):
                    g = k * 8 + e
                    a.wait_ge(s_midmm.h, g + 1)
                    a.activation(mids[:, e * 512:(e + 1) * 512], ctx.bank(6 + g % 2, 64, 512),
                                 AF.Relu, scale=SEL[0:64, e:e + 1]).then_inc(s_midev.h, 1)
        s_midev.n = NCK * 8

        @b.vector
        def _(v):
            for k in range(NCK):
                v.wait_ge(s_xin.h, (k + 1) * 16)
                for o in range(NCH):
                    n = k * 8 + o
                    if n >= 2:
                        v.wait_ge(s_out[n % 2].h, ((n - 2) // 2 + 1) * 16)
                    v.wait_ge(s_pe3.h, n + 1)
                    v.tensor_add(ybr[:, (n % 2) * CHUNK:(n % 2 + 1) * CHUNK],
                                 xb[:, o * CHUNK:(o + 1) * CHUNK],
                                 ctx.bank(4 + n % 2)).then_inc(s_add.h, 1)
        s_add.n = NCK * 8

        @b.gpsimd
        def _(g):
            for k in range(NCK):
                for o in range(NCH):
                    n = k * 8 + o
                    g.wait_ge(s_add.h, n + 1)
                    g.dma_start(dst[:, o, k * CHUNK:(k + 1) * CHUNK],
                                ybr[:, (n % 2) * CHUNK:(n % 2 + 1) * CHUNK]).then_inc(s_out[n % 2].h, 16)
            g.wait_ge(s_out[0].h, NCK * 4 * 16)
            g.wait_ge(s_out[1].h, NCK * 4 * 16)
        s_out[0].n = NCK * 4 * 16
        s_out[1].n = NCK * 4 * 16

    if "t3dbg" in ctx.dram:
        s_dbg = ctx.sem("t3_dbg")
        with nc.Block() as b:
            @b.gpsimd
            def _(g):
                g.wait_ge(s_p2.h, 12)
                g.dma_start(ctx.dram["t3dbg"][0:128, 0:8], lossparts[:]).then_inc(s_dbg.h, 16)
                g.dma_start(ctx.dram["t3dbg"][0:128, 8:16], SEL[:]).then_inc(s_dbg.h, 16)
                g.dma_start(ctx.dram["t3dbg"][0:1, 16:24], lossrow[:]).then_inc(s_dbg.h, 16)
                g.dma_start(ctx.dram["t3dbg"][0:1, 24:32], selrow[:]).then_inc(s_dbg.h, 16)
                g.dma_start(ctx.dram["t3dbg"][0:1, 32:33], minrow[:]).then_inc(s_dbg.h, 16)
                g.wait_ge(s_dbg.h, 80)
    return s_out[0]


def build_launch_l3ffn():
    nc = bass.Bass("TRN2", target_bir_lowering=False, detect_race_conditions=False)
    ctx = Ctx(nc)
    I32 = mybir.dt.int32
    ctx.din("xT", [D, T], F32)
    ctx.din("L3_ffn_W_base", [1024, 1024]);  ctx.din("L3_ffn_b_base", [1024])
    ctx.din("L3_ffn_W_enc", [8, 1024, 256]); ctx.din("L3_ffn_b_enc", [8, 256])
    ctx.din("L3_ffn_W_dec", [8, 256, 1024]); ctx.din("L3_ffn_b_dec", [8, 1024])
    ctx.din("L3_ffn_W_down", [8, 1024, 64]); ctx.din("L3_ffn_b_down", [8, 64])
    ctx.din("L3_ffn_W_up", [8, 64, 1024]);   ctx.din("L3_ffn_b_up", [8, 1024])
    ctx.din("L3_ffn_connected_idx", [8], I32)
    ctx.dout("yT", [D, T], F32)
    ctx.dout("t3dbg", [128, 64], F32)
    ctx.dint("lnh", [D, T], BF16)
    ctx.load_consts()
    RING = 2
    lnx = ctx.sbuf("lnring", [128, NCH * RING * CHUNK], BF16)
    lnx_at = lambda c, k: lnx[:, (c * RING + k % RING) * CHUNK:
                              (c * RING + k % RING + 1) * CHUNK]
    lnx_view = lambda k: lnx[:].rearrange("p (c s f) -> p c s f", c=NCH, s=RING)[:, :, k % RING, :]
    s_ln, s_lnst = emit_ln(ctx, "xT", lnx_at, "ln0", bank0=0, ring=RING, to_hbm="lnh",
                           lnx_view=lnx_view)
    emit_l3ffn(ctx, s_lnst, "lnh", "xT", "yT")
    return ctx



def emit_l1ffn(ctx, s_lnst, lnh_name, src_name, out_name):
    nc = ctx.nc
    ident = ctx.sb["ident"]
    onesrow = ctx.sb["onesrow"]
    lnh = ctx.dram[lnh_name].rearrange("(c p) t -> p c t", p=128)
    NT = T // 128

    gatew = ctx.sbuf("m1_gatew", [128, 8 * 8], BF16)
    r_gateb = ctx.sbuf("m1_gateb", [1, 8], BF16)
    lgT = ctx.sbuf("m1_lgT", [8, T], BF16)
    LGall = ctx.sbuf("m1_LGall", [128, NT * 8])
    ex = ctx.sbuf("m1_ex", [128, NT * 8])
    pp = ctx.sbuf("m1_pp", [128, NT * 8])
    msk = ctx.sbuf("m1_msk", [128, NT * 8])
    keep = ex
    cwf = LGall
    cwb = ctx.sbuf("m1_cwb", [128, NT * 8], BF16)
    mx32 = ctx.sbuf("m1_mx32", [128, NT])
    sm32 = ctx.sbuf("m1_sm32", [128, NT])
    rc32 = ctx.sbuf("m1_rc32", [128, NT])
    m132 = ctx.sbuf("m1_m132", [128, NT])
    m232 = ctx.sbuf("m1_m232", [128, NT])
    cw8 = ctx.sbuf("m1_cw8", [8, T], BF16)
    lnr = ctx.sbuf("m1_lnr", [128, NCH * 2 * CHUNK], BF16)
    rw1f = ctx.sbuf("m1_rw1f", [128, 8 * 2048], BF16)
    rw2b = ctx.sbuf("m1_rw2", [128, 8 * 1024], BF16)
    r_rb1 = ctx.sbuf("m1_rb1", [1, 2048], BF16)
    rb2s = ctx.sbuf("m1_rb2", [8, 1024], BF16)
    r_sb2 = ctx.sbuf("m1_sb2", [1, 1024], BF16)
    hbuf = ctx.sbuf("m1_hbuf", [128, 8 * CHUNK], BF16)
    repsb = ctx.sbuf("m1_repsb", [128, CHUNK], BF16)
    tmpsl = ctx.sbuf("m1_tmpsl", [128, CHUNK], BF16)
    oev = ctx.sbuf("m1_oev", [128, 2 * CHUNK], BF16)
    c_sel8 = ctx.sb["sel8"]
    acch = ctx.dram["acch"].rearrange("(c p) t -> p c t", p=128)

    NE = 9

    def b8(v):
        return v[:].rearrange("p (t e) -> p t e", e=8)

    def bc32(v):
        return v[:].rearrange("p (t o) -> p t o", o=1).broadcast_to([128, NT, 8])

    s_w0 = ctx.sem("m1_w0")
    with nc.Block() as b:
        @b.gpsimd
        def _(g):
            g.dma_start(gatew[:].rearrange("p (c w) -> p c w", c=8),
                        ctx.dram["L1_ffn_gate_w"].rearrange("(c p) w -> p c w", p=128)).then_inc(s_w0.h, 16)
            g.dma_start(r_gateb[:], ctx.dram["L1_ffn_gate_b"].rearrange("(o n) -> o n", o=1)).then_inc(s_w0.h, 16)
            g.dma_start(rb2s[:], ctx.dram["L1_ffn_rb2"]).then_inc(s_w0.h, 16)
            g.dma_start(r_sb2[:], ctx.dram["L1_ffn_sb2"].rearrange("e w -> (e w)").rearrange("(o n) -> o n", o=1)).then_inc(s_w0.h, 16)
        s_w0.n = 4 * 16

    s_lr = [ctx.sem("m1_lr0"), ctx.sem("m1_lr1")]
    s_gmm = ctx.sem("m1_gmm")
    s_gev = ctx.sem("m1_gev")
    s_tr1 = ctx.sem("m1_tr1")
    s_tr1e = ctx.sem("m1_tr1e")
    s_v = ctx.sem("m1_v")
    s_cwbf = ctx.sem("m1_cwbf")
    s_tr2 = ctx.sem("m1_tr2")
    s_tr2e = ctx.sem("m1_tr2e")
    with nc.Block() as b:
        @b.sync
        def _(q):
            for k in range(NCK):
                q.wait_ge(s_lnst[k % 2].h, (k // 2 + 1) * 16)
                if k >= 2:
                    q.wait_ge(s_gmm.h, k - 1)
                q.dma_start(lnr[:].rearrange("p (c s f) -> p c s f", c=NCH, s=2)[:, :, k % 2, :],
                            lnh[:, :, k * CHUNK:(k + 1) * CHUNK]).then_inc(s_lr[k % 2].h, 16)
        s_lr[0].n = 4 * 16
        s_lr[1].n = 4 * 16

        @b.tensor
        def _(t):
            t.wait_ge(ctx.s_const.h, ctx.s_const.n)
            t.wait_ge(s_w0.h, s_w0.n)
            for k in range(NCK):
                t.wait_ge(s_lr[k % 2].h, (k // 2 + 1) * 16)
                if k >= 2:
                    t.wait_ge(s_gev.h, k - 1)
                pairs = [(gatew[:, c * 8:(c + 1) * 8],
                          lnr[:, (c * 2 + k % 2) * CHUNK:(c * 2 + k % 2 + 1) * CHUNK])
                         for c in range(NCH)]
                pairs.append((r_gateb[0:1, 0:8], onesrow[0:1, k * CHUNK:(k + 1) * CHUNK]))
                mm_chain(t, ctx.bank(k % 2, 8, 512), pairs).then_inc(s_gmm.h, 1)
            for j in range(NT):
                t.wait_ge(s_gev.h, j // 4 + 1)
                if j >= 2:
                    t.wait_ge(s_tr1e.h, j - 1)
                t.transpose(ctx.bank(2 + j % 2, 128, 8, BF16),
                            lgT[0:8, j * 128:(j + 1) * 128], ident[0:8, 0:8]).then_inc(s_tr1.h, 1)
            for j in range(NT):
                t.wait_ge(s_cwbf.h, 1)
                if j >= 2:
                    t.wait_ge(s_tr2e.h, j - 1)
                t.transpose(ctx.bank(4 + j % 2, 8, 128, BF16),
                            cwb[:, j * 8:(j + 1) * 8], ident[:]).then_inc(s_tr2.h, 1)
        s_gmm.n = NCK
        s_tr1.n = NT
        s_tr2.n = NT

        @b.scalar
        def _(a):
            for k in range(NCK):
                a.wait_ge(s_gmm.h, k + 1)
                a.copy(lgT[0:8, k * CHUNK:(k + 1) * CHUNK], ctx.bank(k % 2, 8, 512)).then_inc(s_gev.h, 1)
            for j in range(NT):
                a.wait_ge(s_tr1.h, j + 1)
                a.copy(LGall[:, j * 8:(j + 1) * 8], ctx.bank(2 + j % 2, 128, 8, BF16)).then_inc(s_tr1e.h, 1)
            a.wait_ge(s_v.h, 2)
            a.activation(ex[:], ex[:], AF.Exp).then_inc(s_v.h, 1)
            for j in range(NT):
                a.wait_ge(s_tr2.h, j + 1)
                a.copy(cw8[0:8, j * 128:(j + 1) * 128], ctx.bank(4 + j % 2, 8, 128, BF16)).then_inc(s_tr2e.h, 1)
        s_gev.n = NCK
        s_tr1e.n = NT
        s_tr2e.n = NT

        @b.vector
        def _(v):
            v.wait_ge(s_tr1e.h, NT)
            v.tensor_reduce(mx32[:], b8(LGall), AX.X, ALU.max).then_inc(s_v.h, 1)
            v.wait_ge(s_v.h, 1)
            v.tensor_sub(b8(ex), b8(LGall), bc32(mx32)).then_inc(s_v.h, 1)
            v.wait_ge(s_v.h, 3)
            v.tensor_reduce(sm32[:], b8(ex), AX.X, ALU.add).then_inc(s_v.h, 1)
            v.wait_ge(s_v.h, 4)
            v.reciprocal(rc32[:], sm32[:]).then_inc(s_v.h, 1)
            v.wait_ge(s_v.h, 5)
            v.tensor_mul(b8(pp), b8(ex), bc32(rc32)).then_inc(s_v.h, 1)
            v.wait_ge(s_v.h, 6)
            v.tensor_reduce(m132[:], b8(pp), AX.X, ALU.max).then_inc(s_v.h, 1)
            v.wait_ge(s_v.h, 7)
            v.tensor_tensor(b8(msk), b8(pp), bc32(m132), ALU.is_ge).then_inc(s_v.h, 1)
            v.wait_ge(s_v.h, 8)
            v.scalar_tensor_tensor(msk[:], msk[:], -1e30, pp[:], ALU.mult, ALU.add).then_inc(s_v.h, 1)
            v.wait_ge(s_v.h, 9)
            v.tensor_reduce(m232[:], b8(msk), AX.X, ALU.max).then_inc(s_v.h, 1)
            v.wait_ge(s_v.h, 10)
            v.tensor_tensor(b8(keep), b8(pp), bc32(m232), ALU.is_ge).then_inc(s_v.h, 1)
            v.wait_ge(s_v.h, 11)
            v.tensor_mul(cwf[:], pp[:], keep[:]).then_inc(s_v.h, 1)
            v.wait_ge(s_v.h, 12)
            v.tensor_copy(cwb[:], cwf[:]).then_inc(s_cwbf.h, 1)
        s_v.n = 12
        s_cwbf.n = 1

    s_w1 = ctx.sem("m1_w1")
    s_lnr2 = [ctx.sem("m1_ln2a"), ctx.sem("m1_ln2b")]
    s_ab = ctx.sem("m1_ab")
    s_rep = ctx.sem("m1_rep")
    s_repe = ctx.sem("m1_repe")
    s_sw = ctx.sem("m1_sw")
    s_sil = ctx.sem("m1_sil")
    s_omm = ctx.sem("m1_omm")
    s_acc = ctx.sem("m1_acc")
    s_acd = [ctx.sem("m1_acd0"), ctx.sem("m1_acd1")]

    def s_acd_wait(no):
        return ((no - 2) // 2 + 1) * 16

    rw1d = ctx.dram["L1_ffn_rw1"]
    rw2d = ctx.dram["L1_ffn_rw2"]
    sw1d = ctx.dram["L1_ffn_sw1"]
    sw2d = ctx.dram["L1_ffn_sw2"]
    rb1d = ctx.dram["L1_ffn_rb1"]
    sb1d = ctx.dram["L1_ffn_sb1"]

    with nc.Block() as b:
        @b.gpsimd
        def _(g):
            for e in range(NE):
                w1 = rw1d[e] if e < 8 else sw1d[0]
                w2 = rw2d[e] if e < 8 else sw2d[0]
                b1 = rb1d[e] if e < 8 else sb1d[0]
                if e >= 1:
                    g.wait_ge(s_omm.h, e * 64)
                g.dma_start(rw2b[:].rearrange("p (c w) -> p c w", c=8),
                            w2.rearrange("(c p) w -> p c w", p=128)).then_inc(s_w1.h, 16)
                if e >= 1:
                    g.wait_ge(s_ab.h, e * 128)
                g.dma_start(rw1f[:].rearrange("p (c w) -> p c w", c=8),
                            w1.rearrange("(c p) w -> p c w", p=128)).then_inc(s_w1.h, 16)
                g.dma_start(r_rb1[:], b1.rearrange("(o n) -> o n", o=1)).then_inc(s_w1.h, 16)
                for k in range(NCK):
                    for o in range(NCH):
                        no = e * 64 + k * 8 + o
                        g.wait_ge(s_acc.h, no + 1)
                        g.dma_start(acch[:, o, k * CHUNK:(k + 1) * CHUNK],
                                    oev[:, (no % 2) * CHUNK:(no % 2 + 1) * CHUNK],
                                    accum_op=(ALU.bypass if e == 0 else ALU.add)).then_inc(s_acd[no % 2].h, 16)
        s_w1.n = NE * 48
        s_acd[0].n = NE * NCK * 4 * 16
        s_acd[1].n = NE * NCK * 4 * 16

        @b.sync
        def _(q):
            for e in range(NE):
                for k in range(NCK):
                    m = e * 8 + k
                    if m >= 2:
                        e2, k2 = divmod(m - 2, 8)
                        q.wait_ge(s_ab.h, (e2 * 8 + k2) * 16 + 16)
                    q.wait_ge(s_lnst[k % 2].h, (k // 2 + 1) * 16)
                    q.dma_start(lnr[:].rearrange("p (c s f) -> p c s f", c=NCH, s=2)[:, :, k % 2, :],
                                lnh[:, :, k * CHUNK:(k + 1) * CHUNK]).then_inc(s_lnr2[k % 2].h, 16)
        s_lnr2[0].n = NE * 4 * 16
        s_lnr2[1].n = NE * 4 * 16

        @b.tensor
        def _(t):
            for e in range(NE):
                t.wait_ge(s_w1.h, (e + 1) * 48)
                for k in range(NCK):
                    t.wait_ge(s_lnr2[k % 2].h, (e * 4 + k // 2 + 1) * 16)
                    if e < 8:
                        if e * 8 + k >= 1:
                            t.wait_ge(s_repe.h, e * 8 + k)
                        t.matmul(ctx.bank(2), c_sel8[:, e * 128:(e + 1) * 128],
                                 cw8[0:8, k * CHUNK:(k + 1) * CHUNK], start=True, stop=True).then_inc(s_rep.h, 1)
                    for j in range(8):
                        g2 = e * 64 + k * 8 + j
                        if g2 >= 2:
                            t.wait_ge(s_sw.h, (g2 - 1) * 2)
                        if e < 8 and j == 1:
                            t.wait_ge(s_repe.h, e * 8 + k + 1)
                        pairs_a = [(rw1f[:, c * 2048 + j * 128: c * 2048 + (j + 1) * 128],
                                    lnr[:, (c * 2 + k % 2) * CHUNK:(c * 2 + k % 2 + 1) * CHUNK])
                                   for c in range(NCH)]
                        pairs_a.append((r_rb1[0:1, j * 128:(j + 1) * 128],
                                        onesrow[0:1, k * CHUNK:(k + 1) * CHUNK]))
                        mm_chain(t, ctx.bank(0 + 2 * (j % 2)), pairs_a)
                        pairs_b = [(rw1f[:, c * 2048 + 1024 + j * 128: c * 2048 + 1024 + (j + 1) * 128],
                                    lnr[:, (c * 2 + k % 2) * CHUNK:(c * 2 + k % 2 + 1) * CHUNK])
                                   for c in range(NCH)]
                        pairs_b.append((r_rb1[0:1, 1024 + j * 128: 1024 + (j + 1) * 128],
                                        onesrow[0:1, k * CHUNK:(k + 1) * CHUNK]))
                        mm_chain(t, ctx.bank(1 + 2 * (j % 2)), pairs_b).then_inc(s_ab.h, 2)
                    for o in range(NCH):
                        no = e * 64 + k * 8 + o
                        t.wait_ge(s_sw.h, (e * 64 + k * 8 + 8) * 2)
                        if no >= 4:
                            t.wait_ge(s_acc.h, no - 3)
                        pairs = [(rw2b[:, c * 1024 + o * 128: c * 1024 + (o + 1) * 128],
                                  hbuf[:, c * CHUNK:(c + 1) * CHUNK]) for c in range(NCH)]
                        mm_chain(t, ctx.bank(4 + no % 4), pairs).then_inc(s_omm.h, 1)
        s_ab.n = NE * NCK * 8 * 2
        s_rep.n = 8 * NCK
        s_omm.n = NE * NCK * 8

        @b.scalar
        def _(a):
            for e in range(NE):
                for k in range(NCK):
                    if e < 8:
                        a.wait_ge(s_rep.h, e * 8 + k + 1)
                        a.copy(repsb[:], ctx.bank(2)).then_inc(s_repe.h, 1)
                    for j in range(8):
                        g2 = e * 64 + k * 8 + j
                        a.wait_ge(s_ab.h, (g2 + 1) * 2)
                        if g2 >= 1:
                            a.wait_ge(s_sw.h, g2 * 2)
                        a.activation(tmpsl[:], ctx.bank(0 + 2 * (j % 2)), AF.Silu).then_inc(s_sil.h, 1)
                    for o in range(NCH):
                        no = e * 64 + k * 8 + o
                        a.wait_ge(s_omm.h, no + 1)
                        if no >= 2:
                            a.wait_ge(s_acd[no % 2].h, s_acd_wait(no))
                        a.copy(oev[:, (no % 2) * CHUNK:(no % 2 + 1) * CHUNK],
                               ctx.bank(4 + no % 4)).then_inc(s_acc.h, 1)
        s_repe.n = 8 * NCK
        s_sil.n = NE * NCK * 8
        s_acc.n = NE * NCK * 8

        @b.vector
        def _(v):
            for e in range(NE):
                for k in range(NCK):
                    m = e * 8 + k
                    if m >= 1:
                        v.wait_ge(s_omm.h, m * 8)
                    for j in range(8):
                        g2 = e * 64 + k * 8 + j
                        v.wait_ge(s_sil.h, g2 + 1)
                        if e < 8:
                            v.wait_ge(s_repe.h, e * 8 + k + 1)
                            v.tensor_mul(tmpsl[:], tmpsl[:], repsb[:]).then_inc(s_sw.h, 1)
                        else:
                            v.engine_nop().then_inc(s_sw.h, 1)
                        v.tensor_mul(hbuf[:, j * CHUNK:(j + 1) * CHUNK],
                                     tmpsl[:], ctx.bank(1 + 2 * (j % 2))).then_inc(s_sw.h, 1)
        s_sw.n = NE * NCK * 8 * 2

    def mm_fn(t, k, o, bank_ap):
        pairs = [(rb2s[:, o * 128:(o + 1) * 128], cw8[0:8, k * CHUNK:(k + 1) * CHUNK]),
                 (r_sb2[0:1, o * 128:(o + 1) * 128], onesrow[0:1, k * CHUNK:(k + 1) * CHUNK])]
        return mm_chain(t, bank_ap, pairs)

    return emit_residual_out(ctx, "m1f", src_name, out_name,
                             s_acd[0], None, mm_fn, extra_stream="acch",
                             bufs={"xb": ctx.ln_bufs()["xb"], "eb": ctx.ln_bufs()["xbf"]})



def build_launch_l0ffn():
    nc = bass.Bass("TRN2", target_bir_lowering=False, detect_race_conditions=False)
    ctx = Ctx(nc)
    ctx.din("xT", [D, T], F32)
    ctx.din("L0_ffn_W", [1024, 1024])
    ctx.din("L0_ffn_b", [1024])
    ctx.din("L0_ffn_lora_down", [4, 100])
    ctx.din("L0_ffn_lora_up", [50, 4])
    ctx.din("L0_ffn_down_aux", [100, 1024])
    ctx.din("L0_ffn_up_aux", [1024, 50])
    ctx.dout("yT", [D, T], F32)
    ctx.load_consts()
    lnx = ctx.sbuf("lnx0", [128, NCH * T], BF16)
    lnx_at = lambda c, k: lnx[:, c * T + k * CHUNK: c * T + (k + 1) * CHUNK]
    w = emit_l0ffn_weights(ctx)
    s_ln = emit_ln(ctx, "xT", lnx_at, "ln0", bank0=0)
    emit_l0ffn_main(ctx, w, lnx_at, s_ln, "xT", "yT")
    return ctx


def build_launch_l0ffn_weights_only():
    nc = bass.Bass("TRN2", target_bir_lowering=False, detect_race_conditions=False)
    ctx = Ctx(nc)
    for nm, shp in [("L0_ffn_W", [1024, 1024]), ("L0_ffn_b", [1024]),
                    ("L0_ffn_lora_down", [4, 100]), ("L0_ffn_lora_up", [50, 4]),
                    ("L0_ffn_down_aux", [100, 1024]), ("L0_ffn_up_aux", [1024, 50])]:
        ctx.din(nm, shp)
    ctx.din("xT", [D, T], F32)
    ctx.dout("wT", [128, NCH * 1024], F32)
    ctx.load_consts()
    w = emit_l0ffn_weights(ctx)
    s_out = ctx.sem("wout")
    with ctx.nc.Block() as b:
        @b.gpsimd
        def _(g):
            g.wait_ge(w["s_trev"].h, w["s_trev"].n)
            g.dma_start(ctx.dram["wT"], w["lhsT"][:]).then_inc(s_out.h, 16)
            g.wait_ge(s_out.h, 16)
    return ctx


def build_launch_ln_only():
    nc = bass.Bass("TRN2", target_bir_lowering=False, detect_race_conditions=False)
    ctx = Ctx(nc)
    ctx.din("xT", [D, T], F32)
    ctx.dout("yT", [D, T], F32)
    ctx.load_consts()
    lnx = ctx.sbuf("lnx0", [128, NCH * T], BF16)
    lnx_at = lambda c, k: lnx[:, c * T + k * CHUNK: c * T + (k + 1) * CHUNK]
    s_ln = emit_ln(ctx, "xT", lnx_at, "ln0", bank0=0)
    dst = ctx.dram["yT"].rearrange("(c p) t -> p c t", p=128)
    s_out = ctx.sem("lnout")
    with nc.Block() as b:
        @b.gpsimd
        def _(g):
            g.wait_ge(s_ln.h, s_ln.n)
            g.dma_start(dst, lnx[:].rearrange("p (c t) -> p c t", c=NCH)).then_inc(s_out.h, 16)
            g.wait_ge(s_out.h, 16)
    return ctx


def build_launch_l1ffn():
    nc = bass.Bass("TRN2", target_bir_lowering=False, detect_race_conditions=False)
    ctx = Ctx(nc)
    ctx.din("xT", [D, T], F32)
    ctx.din("L1_ffn_gate_w", [1024, 8]); ctx.din("L1_ffn_gate_b", [8])
    ctx.din("L1_ffn_sw1", [1, 1024, 2048]); ctx.din("L1_ffn_sb1", [1, 2048])
    ctx.din("L1_ffn_sw2", [1, 1024, 1024]); ctx.din("L1_ffn_sb2", [1, 1024])
    ctx.din("L1_ffn_rw1", [8, 1024, 2048]); ctx.din("L1_ffn_rb1", [8, 2048])
    ctx.din("L1_ffn_rw2", [8, 1024, 1024]); ctx.din("L1_ffn_rb2", [8, 1024])
    ctx.dout("yT", [D, T], F32)
    ctx.dint("lnh", [D, T], BF16)
    ctx.dint("acch", [D, T], BF16)
    ctx.load_consts(extra=("sel8",))
    RING = 2
    lnx = ctx.sbuf("lnring", [128, NCH * RING * CHUNK], BF16)
    lnx_at = lambda c, k: lnx[:, (c * RING + k % RING) * CHUNK:
                              (c * RING + k % RING + 1) * CHUNK]
    lnx_view = lambda k: lnx[:].rearrange("p (c s f) -> p c s f", c=NCH, s=RING)[:, :, k % RING, :]
    s_ln, s_lnst = emit_ln(ctx, "xT", lnx_at, "ln1f", bank0=0, ring=RING, to_hbm="lnh",
                           lnx_view=lnx_view)
    emit_l1ffn(ctx, s_lnst, "lnh", "xT", "yT")
    return ctx


def build_launch_l2attn_a():
    nc = bass.Bass("TRN2", target_bir_lowering=False, detect_race_conditions=False)
    ctx = Ctx(nc)
    ctx.din("xT", [D, T], F32)
    ctx.din("L2_attn_conv_dw_w", [1024, 1, 3]); ctx.din("L2_attn_conv_dw_b", [1024])
    ctx.din("L2_attn_conv_pw_w", [1024, 1024]); ctx.din("L2_attn_conv_pw_b", [1024])
    ctx.din("L2_attn_conv_norm_g", [1024]); ctx.din("L2_attn_gru_norm_g", [1024])
    ctx.din("L2_attn_gru_w", [2048, 1024])
    ctx.dout("x2T", [D, T], F32)
    ctx.dint("lnh", [D, T], BF16)
    ctx.dint("rn1h", [D, T], BF16)
    ctx.dint("rn2h", [D, T], BF16)
    ctx.dint("x1h", [D, T], F32)
    ctx.load_consts()
    RING = 2
    lnx = ctx.sbuf("lnring", [128, NCH * RING * CHUNK], BF16)
    lnx_at = lambda c, k: lnx[:, (c * RING + k % RING) * CHUNK:
                              (c * RING + k % RING + 1) * CHUNK]
    lnx_view = lambda k: lnx[:].rearrange("p (c s f) -> p c s f", c=NCH, s=RING)[:, :, k % RING, :]
    s_ln, s_lnst = emit_ln(ctx, "xT", lnx_at, "ln2a", bank0=0, ring=RING, to_hbm="lnh",
                           lnx_view=lnx_view)
    emit_l2attn_a(ctx, s_lnst, "lnh", "x2T")
    return ctx



def kernel(**inputs):
    x = np.asarray(inputs["L0_attn_x"])
    ctx = build_launch_fused(9)
    consts = {k: np.asarray(v) for k, v in CONSTS.items()}
    big = {k: np.asarray(v) for k, v in BIGCONSTS.items()}
    ims = []
    for i in range(8):
        im = dict(consts)
        im.update(big)
        im["xT"] = np.ascontiguousarray(x[i].T)
        for k in FUSED_WKEYS:
            im[k] = np.asarray(inputs[k])
        ims.append(im)
    res = run_bass_kernel_spmd(ctx.nc, ims, core_ids=list(range(8)))
    return np.stack([res.results[i]["yT"].T for i in range(8)]).astype(np.float32)



def emit_l3attn(ctx, s_lnst, lnh_name, src_name, out_name):
    nc = ctx.nc
    ident = ctx.sb["ident"]
    onesrow = ctx.sb["onesrow"]
    ones128 = ctx.sb["ones128"]
    c_triu = ctx.sb["triu"]
    c_triuf = ctx.sb["triuf"]
    lnh = ctx.dram[lnh_name].rearrange("(c p) t -> p c t", p=128)
    NT = T // 128
    EPS = 1e-6

    wqT = ctx.sbuf("a3_wqT", [128, 8 * 256], BF16)
    wkT = ctx.sbuf("a3_wkT", [128, 8 * 256], BF16)
    wvT = ctx.sbuf("a3_wvT", [128, 8 * 256], BF16)
    woT = ctx.sbuf("a3_woT", [128, 2 * 1024], BF16)
    wrow = ctx.sbuf("a3_wrow", [128, 2048], BF16)
    lnr = ctx.sbuf("a3_lnr", [128, NCH * 2 * CHUNK], BF16)
    Qh = ctx.sbuf("a3_Q", [128, NT * 256], BF16)
    Kh = ctx.sbuf("a3_K", [128, NT * 256], BF16)
    Vh = ctx.sbuf("a3_V", [128, NT * 256], BF16)
    YT = ctx.sbuf("a3_YT", [128, 2 * T], BF16)
    t1s = ctx.sbuf("a3_t1", [128, 256])
    t2s = ctx.sbuf("a3_t2", [128, 256])
    pscr = ctx.sbuf("a3_pscr", [128, 256])
    psq = ctx.sbuf("a3_psq", [128, 256])
    n2 = ctx.sbuf("a3_n2", [128, 4])
    nl = ctx.sbuf("a3_nl", [128, 4])
    dnm = ctx.sbuf("a3_dnm", [128, 4])
    rcpn = ctx.sbuf("a3_rcpn", [128, 4])
    S_sb = ctx.sbuf("a3_Ssb", [64, 64], BF16)
    S_f32 = ctx.sbuf("a3_Sf32", [64, 64])
    kst = ctx.sbuf("a3_kst", [1, 64], BF16)
    kst_f32 = ctx.sbuf("a3_kstf", [1, 64])
    Pm = ctx.sbuf("a3_Pm", [128, 128], BF16)
    KcT = ctx.sbuf("a3_KcT", [64, 128], BF16)
    QcT = ctx.sbuf("a3_QcT", [64, 128], BF16)
    dp = ctx.sbuf("a3_dp", [128, 64])
    den = ctx.sbuf("a3_den", [128, 1])
    denf = ctx.sbuf("a3_denf", [128, 1])
    denr = ctx.sbuf("a3_denr", [128, 1], BF16)
    denrT = ctx.sbuf("a3_denrT", [1, 128], BF16)
    Oev = ctx.sbuf("a3_Oev", [64, 128], BF16)

    s_w0 = ctx.sem("a3_w0")
    s_tp = ctx.sem("a3_tp")
    s_te = ctx.sem("a3_te")

    def subtiles(kind):
        subs = []
        if kind == "qkv":
            for c2 in range(2):
                for i in range(8):
                    subs.append((c2 * 1024 + i * 128, i * 256 + c2 * 128))
        else:
            for o8 in range(8):
                for c2 in range(2):
                    subs.append((o8 * 256 + c2 * 128, c2 * 1024 + o8 * 128))
        return subs

    specs = [("L3_attn_Wq", wqT, "qkv"), ("L3_attn_Wk", wkT, "qkv"),
             ("L3_attn_Wv", wvT, "qkv"), ("L3_attn_Wo", woT, "wo")]
    with nc.Block() as b:
        for si, (wname, dstT, kind) in enumerate(specs):
            @b.gpsimd
            def _(g, wname=wname):
                s_tp.wait(g)
                s_w0.inc(g.dma_start(wrow[:].rearrange("p (c w) -> p c w", c=2),
                                     ctx.dram[wname].rearrange("(c p) w -> p c w", p=128)), 16)

            @b.tensor
            def _(t, si=si, kind=kind):
                if si == 0:
                    t.wait_ge(ctx.s_const.h, ctx.s_const.n)
                s_w0.wait(t)
                te_base = s_te.n
                for n_, (srcoff, dstoff) in enumerate(subtiles(kind)):
                    if n_ >= 2:
                        t.wait_ge(s_te.h, te_base + n_ - 1)
                    s_tp.inc(t.transpose(ctx.bank(n_ % 2, 128, 128, BF16),
                                         wrow[:, srcoff:srcoff + 128], ident[:]))

            @b.scalar
            def _(a, dstT=dstT, kind=kind):
                tp_base = s_tp.n - 16
                for n_, (srcoff, dstoff) in enumerate(subtiles(kind)):
                    a.wait_ge(s_tp.h, tp_base + n_ + 1)
                    s_te.inc(a.copy(dstT[:, dstoff:dstoff + 128],
                                    ctx.bank(n_ % 2, 128, 128, BF16)))

    s_lr = [ctx.sem("a3_lr0"), ctx.sem("a3_lr1")]
    s_mm = ctx.sem("a3_mm")
    s_dv = ctx.sem("a3_dv")
    s_ac = ctx.sem("a3_ac")
    DVC = 13
    ACC_ = 8

    def relR(g):
        jt, wi = divmod(g, 3)
        return jt * DVC + (wi * 6 + 1 if wi < 2 else 13)

    with nc.Block() as b:
        @b.sync
        def _(q):
            for k in range(NCK):
                q.wait_ge(s_lnst[k % 2].h, (k // 2 + 1) * 16)
                if k >= 2:
                    q.wait_ge(s_mm.h, (k - 1) * 12)
                q.dma_start(lnr[:].rearrange("p (c s f) -> p c s f", c=NCH, s=2)[:, :, k % 2, :],
                            lnh[:, :, k * CHUNK:(k + 1) * CHUNK]).then_inc(s_lr[k % 2].h, 16)
        s_lr[0].n = 4 * 16
        s_lr[1].n = 4 * 16

        @b.tensor
        def _(t):
            s_te.wait(t)
            for k in range(NCK):
                t.wait_ge(s_lr[k % 2].h, (k // 2 + 1) * 16)
                for j4 in range(4):
                    jt = k * 4 + j4
                    for wi, wT in enumerate([wqT, wkT, wvT]):
                        g = jt * 3 + wi
                        if g >= 2:
                            t.wait_ge(s_dv.h, relR(g - 2))
                        pairs = [(lnr[:, (c * 2 + k % 2) * CHUNK + j4 * 128:
                                       (c * 2 + k % 2) * CHUNK + (j4 + 1) * 128],
                                  wT[:, c * 256:(c + 1) * 256]) for c in range(NCH)]
                        mm_chain(t, ctx.bank(g % 2, 128, 256), pairs).then_inc(s_mm.h, 1)
        s_mm.n = NCK * 12

        @b.vector
        def _(v):
            for jt in range(NT):
                for wi in range(2):
                    g = jt * 3 + wi
                    dvb = jt * DVC + wi * 6
                    acb = jt * ACC_ + wi * 4
                    dst = Qh if wi == 0 else Kh
                    v.wait_ge(s_mm.h, g + 1)
                    v.tensor_scalar_min(t1s[:], ctx.bank(g % 2, 128, 256), 0.0)
                    v.tensor_scalar_max(t2s[:], ctx.bank(g % 2, 128, 256), 0.0).then_inc(s_dv.h, 1)
                    v.wait_ge(s_ac.h, acb + 1)
                    v.tensor_add(pscr[:], t1s[:], t2s[:]).then_inc(s_dv.h, 1)
                    v.wait_ge(s_ac.h, acb + 2)
                    v.tensor_reduce(n2[:], psq[:].rearrange("p (h d) -> p h d", h=4),
                                    AX.X, ALU.add).then_inc(s_dv.h, 1)
                    v.wait_ge(s_ac.h, acb + 4)
                    v.tensor_scalar_add(dnm[:], dnm[:], EPS).then_inc(s_dv.h, 1)
                    v.wait_ge(s_dv.h, dvb + 4)
                    v.reciprocal(rcpn[:], dnm[:]).then_inc(s_dv.h, 1)
                    v.wait_ge(s_dv.h, dvb + 5)
                    v.tensor_mul(dst[:, jt * 256:(jt + 1) * 256].rearrange("p (h d) -> p h d", h=4),
                                 pscr[:].rearrange("p (h d) -> p h d", h=4),
                                 rcpn[:].rearrange("p (h o) -> p h o", o=1).broadcast_to([128, 4, 64])).then_inc(s_dv.h, 1)
                g = jt * 3 + 2
                v.wait_ge(s_mm.h, g + 1)
                v.tensor_copy(Vh[:, jt * 256:(jt + 1) * 256],
                              ctx.bank(g % 2, 128, 256)).then_inc(s_dv.h, 1)
        s_dv.n = NT * DVC

        @b.scalar
        def _(a):
            for jt in range(NT):
                for wi in range(2):
                    dvb = jt * DVC + wi * 6
                    acb = jt * ACC_ + wi * 4
                    a.wait_ge(s_dv.h, dvb + 1)
                    a.activation(t1s[:], t1s[:], AF.Exp).then_inc(s_ac.h, 1)
                    a.wait_ge(s_dv.h, dvb + 2)
                    a.activation(psq[:], pscr[:], AF.Square).then_inc(s_ac.h, 1)
                    a.wait_ge(s_dv.h, dvb + 3)
                    a.activation(nl[:], n2[:], AF.Ln).then_inc(s_ac.h, 1)
                    a.wait_ge(s_ac.h, acb + 3)
                    a.activation(dnm[:], nl[:], AF.Exp, scale=0.5).then_inc(s_ac.h, 1)
        s_ac.n = NT * ACC_

    s_pe3 = ctx.sem("a3_pe3")
    s_sc3 = ctx.sem("a3_sc3")
    s_dv3 = ctx.sem("a3_dv3")
    s_t3 = ctx.sem("a3_t3")
    with nc.Block() as b:
        for h in range(4):
            for c in range(NT):
                hsl = slice(c * 256 + h * 64, c * 256 + (h + 1) * 64)

                @b.tensor
                def _(t, h=h, c=c, hsl=hsl):
                    s_sc3.wait(t)
                    s_dv3.wait(t)
                    s_t3.inc(t.transpose(ctx.bank(3, 64, 128, BF16), Kh[:, hsl], ident[:]))
                    s_t3.inc(t.transpose(ctx.bank(7, 64, 128, BF16), Qh[:, hsl], ident[:]))

                @b.scalar
                def _(a, h=h, c=c):
                    s_t3.wait(a)
                    s_pe3.wait(a)
                    s_dv3.wait(a)
                    a.copy(KcT[:], ctx.bank(3, 64, 128, BF16))
                    ins = a.copy(QcT[:], ctx.bank(7, 64, 128, BF16))
                    if c > 0:
                        a.copy(S_sb[:], S_f32[:])
                        ins = a.copy(kst[:], kst_f32[:])
                    s_sc3.inc(ins)

                @b.tensor
                def _(t, h=h, c=c, hsl=hsl):
                    s_sc3.wait(t)
                    s_dv3.wait(t)
                    t.matmul(ctx.bank(2, 128, 128), KcT[:], QcT[:], start=True, stop=True)
                    ins = t.matmul(ctx.bank(6, 128, 64), c_triu[:], Kh[:, hsl],
                                   start=True, stop=(c == 0))
                    if c > 0:
                        ins = t.matmul(ctx.bank(6, 128, 64), onesrow[0:1, 0:128], kst[:],
                                       start=False, stop=True)
                        ins = t.matmul(ctx.bank(4, 64, 128), S_sb[:], QcT[:], start=True, stop=False)
                    s_pe3.inc(ins)

                @b.vector
                def _(v, h=h, c=c, hsl=hsl):
                    s_pe3.wait(v)
                    v.tensor_mul(Pm[:], ctx.bank(2, 128, 128), c_triuf[:]).then_inc(s_dv3.h, 1)
                    s_dv3.n += 1
                    v.tensor_mul(dp[:], Qh[:, hsl], ctx.bank(6, 128, 64)).then_inc(s_dv3.h, 1)
                    s_dv3.n += 1
                    s_dv3.wait(v)
                    v.tensor_reduce(den[:], dp[:], AX.X, ALU.add).then_inc(s_dv3.h, 1)
                    s_dv3.n += 1
                    s_dv3.wait(v)
                    v.reciprocal(denf[:], den[:]).then_inc(s_dv3.h, 1)
                    s_dv3.n += 1
                    s_dv3.wait(v)
                    v.tensor_copy(denr[:], denf[:]).then_inc(s_dv3.h, 1)
                    s_dv3.n += 1

                @b.tensor
                def _(t, h=h, c=c):
                    s_dv3.wait(t)
                    s_t3.inc(t.transpose(ctx.bank(3, 1, 128, BF16), denr[:], ident[0:128, 0:128]))

                @b.scalar
                def _(a):
                    s_t3.wait(a)
                    s_sc3.inc(a.copy(denrT[:], ctx.bank(3, 1, 128, BF16)))

                @b.tensor
                def _(t, h=h, c=c, hsl=hsl):
                    s_sc3.wait(t)
                    t.matmul(ctx.bank(4, 64, 128), Vh[:, hsl], Pm[:],
                             start=(c == 0), stop=True)
                    t.matmul(ctx.bank(0, 64, 64), Kh[:, hsl], Vh[:, hsl],
                             start=True, stop=True)
                    t.matmul(ctx.bank(1, 1, 64), ones128[:, 0:1], Kh[:, hsl],
                             start=True, stop=True)
                    s_pe3.inc(t.matmul(ctx.bank(2, 64, 128), onesrow[0:1, 0:64], denrT[:],
                                       start=True, stop=True))

                @b.scalar
                def _(a):
                    s_pe3.wait(a)
                    s_sc3.inc(a.copy(Oev[:], ctx.bank(4, 64, 128)))

                @b.vector
                def _(v, h=h, c=c):
                    s_sc3.wait(v)
                    s_pe3.wait(v)
                    v.tensor_mul(YT[(h % 2) * 64:(h % 2) * 64 + 64,
                                    (h // 2) * T + c * 128:(h // 2) * T + (c + 1) * 128],
                                 Oev[:], ctx.bank(2, 64, 128)).then_inc(s_dv3.h, 1)
                    s_dv3.n += 1
                    if c == 0:
                        v.tensor_copy(S_f32[:], ctx.bank(0, 64, 64))
                        v.tensor_copy(kst_f32[:], ctx.bank(1, 1, 64)).then_inc(s_dv3.h, 2)
                    else:
                        v.tensor_add(S_f32[:], S_f32[:], ctx.bank(0, 64, 64))
                        v.tensor_add(kst_f32[:], kst_f32[:], ctx.bank(1, 1, 64)).then_inc(s_dv3.h, 2)
                    s_dv3.n += 2

    def mm_fn(t, k, o, bank_ap):
        pairs = [(woT[:, c2 * 1024 + o * 128: c2 * 1024 + (o + 1) * 128],
                  YT[:, c2 * T + k * CHUNK: c2 * T + (k + 1) * CHUNK]) for c2 in range(2)]
        return mm_chain(t, bank_ap, pairs)

    return emit_residual_out(ctx, "a3f", src_name, out_name, s_dv3, None, mm_fn)


def build_launch_l3attn():
    nc = bass.Bass("TRN2", target_bir_lowering=False, detect_race_conditions=False)
    ctx = Ctx(nc)
    ctx.din("xT", [D, T], F32)
    ctx.din("L3_attn_Wq", [256, 1024]); ctx.din("L3_attn_Wk", [256, 1024])
    ctx.din("L3_attn_Wv", [256, 1024]); ctx.din("L3_attn_Wo", [1024, 256])
    ctx.dout("yT", [D, T], F32)
    ctx.dint("lnh", [D, T], BF16)
    ctx.load_consts(extra=("triu", "triuf"))
    RING = 2
    lnx = ctx.sbuf("lnring", [128, NCH * RING * CHUNK], BF16)
    lnx_at = lambda c, k: lnx[:, (c * RING + k % RING) * CHUNK:
                              (c * RING + k % RING + 1) * CHUNK]
    lnx_view = lambda k: lnx[:].rearrange("p (c s f) -> p c s f", c=NCH, s=RING)[:, :, k % RING, :]
    s_ln, s_lnst = emit_ln(ctx, "xT", lnx_at, "ln3a", bank0=0, ring=RING, to_hbm="lnh",
                           lnx_view=lnx_view)
    emit_l3attn(ctx, s_lnst, "lnh", "xT", "yT")
    return ctx



def emit_rms(ctx, src_name, lnx_at, tag, gcol, bank0=0, ring=None, to_hbm=None,
             lnx_view=None, src_bf16=False, src_sem=None):
    nc = ctx.nc
    ones128 = ctx.sb["ones128"]
    src = ctx.dram[src_name].rearrange("(c p) t -> p c t", p=128)
    lb = ctx.ln_bufs()
    xb = lb["xb"]
    xbf = lb["xbf"]
    rstd = lb["rstd"]
    varS = lb["varS"]

    s_in = [ctx.sem(tag + "_in0"), ctx.sem(tag + "_in1")]
    s_mm = ctx.sem(tag + "_mm")
    s_cast = ctx.sem(tag + "_cast")
    s_stat = ctx.sem(tag + "_stat")
    s_done = ctx.sem(tag + "_done")
    s_psf = ctx.sem(tag + "_psf")
    p_sq = [ctx.bank(bank0), ctx.bank(bank0 + 1)]

    if to_hbm is not None:
        lnx_dst = ctx.dram[to_hbm].rearrange("(c p) t -> p c t", p=128)
        s_lnst = [ctx.sem(tag + "_st0"), ctx.sem(tag + "_st1")]

    with nc.Block() as b:
        if to_hbm is not None and not src_bf16:
            @b.gpsimd
            def _(g):
                for k in range(NCK):
                    g.wait_ge(s_done.h, (k + 1) * 2)
                    g.dma_start(lnx_dst[:, :, k * CHUNK:(k + 1) * CHUNK],
                                lnx_view(k)).then_inc(s_lnst[k % 2].h, 16)
        if to_hbm is not None:
            s_lnst[0].n = (NCK // 2) * 16
            s_lnst[1].n = (NCK // 2) * 16

        if src_bf16:
            @b.gpsimd
            def _(g):
                for k in range(NCK):
                    if src_sem is not None:
                        for _sm, _fn in (src_sem if isinstance(src_sem, list) else [src_sem]):
                            g.wait_ge(_sm.h, _fn(k))
                    if k >= 1:
                        g.wait_ge(s_done.h, k * 2)
                    g.dma_start(xb[:].rearrange("p (c f) -> p c f", c=NCH),
                                src[:, :, k * CHUNK:(k + 1) * CHUNK]).then_inc(s_in[k % 2].h, 16)
                    if to_hbm is not None and k >= 1:
                        g.wait_ge(s_done.h, k * 2)
                        g.dma_start(lnx_dst[:, :, (k - 1) * CHUNK:k * CHUNK],
                                    lnx_view(k - 1)).then_inc(s_lnst[(k - 1) % 2].h, 16)
                if to_hbm is not None:
                    g.wait_ge(s_done.h, NCK * 2)
                    g.dma_start(lnx_dst[:, :, (NCK - 1) * CHUNK:NCK * CHUNK],
                                lnx_view(NCK - 1)).then_inc(s_lnst[(NCK - 1) % 2].h, 16)
        else:
            @b.sync
            def _(q):
                q.wait_ge(ctx.s_const.h, ctx.s_const.n)
                for k in range(NCK):
                    if src_sem is not None:
                        for _sm, _fn in (src_sem if isinstance(src_sem, list) else [src_sem]):
                            q.wait_ge(_sm.h, _fn(k))
                    if k >= 1:
                        q.wait_ge(s_done.h, k * 2)
                    q.dma_start(xb[:].rearrange("p (c f) -> p c f", c=NCH),
                                src[:, :, k * CHUNK:(k + 1) * CHUNK]).then_inc(s_in[k % 2].h, 16)
        s_in[0].n = (NCK - NCK // 2) * 16
        s_in[1].n = (NCK // 2) * 16

        @b.scalar
        def _(a):
            for k in range(NCK):
                i = k % 2
                a.wait_ge(s_in[k % 2].h, (k // 2 + 1) * 16)
                if k >= 1:
                    a.wait_ge(s_mm.h, k)
                if k >= 2:
                    a.wait_ge(s_done.h, (k - 1) * 2)
                a.activation(xbf[:], xb[:], AF.Square).then_inc(s_cast.h, 1)
                a.wait_ge(s_mm.h, k + 1)
                a.activation(varS[:], p_sq[i], AF.Ln).then_inc(s_psf.h, 1)
                a.activation(rstd[i][:], varS[:], AF.Exp, scale=-0.5).then_inc(s_stat.h, 1)
        s_cast.n = NCK
        s_stat.n = NCK

        @b.tensor
        def _(t):
            for k in range(NCK):
                i = k % 2
                t.wait_ge(s_cast.h, k + 1)
                if k >= 2:
                    t.wait_ge(s_psf.h, (k - 1) * 2)
                ins = None
                for c in range(NCH):
                    ins = t.matmul(p_sq[i], ones128[:], xbf[:, c * CHUNK:(c + 1) * CHUNK],
                                   start=(c == 0), stop=(c == NCH - 1))
                ins.then_inc(s_mm.h, 1)
        s_mm.n = NCK

        @b.vector
        def _(v):
            for k in range(NCK):
                i = k % 2
                v.wait_ge(s_stat.h, k + 1)
                v.tensor_scalar_min(rstd[i][:], rstd[i][:], 1e12).then_inc(s_psf.h, 1)
                if ring is not None and k >= ring:
                    if to_hbm is not None:
                        v.wait_ge(s_lnst[k % 2].h, ((k - ring) // 2 + 1) * 16)
                ins = None
                for c in range(NCH):
                    sl = slice(c * CHUNK, (c + 1) * CHUNK)
                    ins = v.scalar_tensor_tensor(lnx_at(c, k), xb[:, sl], gcol[:, c:c + 1],
                                                 rstd[i][:], ALU.mult, ALU.mult)
                ins.then_inc(s_done.h, 2)
        s_psf.n += NCK
        s_done.n = NCK * 2

    if to_hbm is not None:
        return s_done, s_lnst
    return s_done



def emit_weight_T(ctx, b, wname, wstage, dstT, nrow_tiles, ncol_tiles, s_w, s_tp, s_te,
                  dst_map=None, first=False, wslice=None):
    nc = ctx.nc
    ident = ctx.sb["ident"]
    C = ncol_tiles * 128
    if dst_map is None:
        dst_map = lambda r, c: c * (nrow_tiles * 128) + r * 128
    wsrc = ctx.dram[wname] if wslice is None else wslice(ctx.dram[wname])
    wsrc = wsrc.rearrange("(r p) w -> p r w", p=128)

    tp_base0 = s_tp.n

    @b.gpsimd
    def _(g):
        for r in range(nrow_tiles):
            if r >= 2:
                g.wait_ge(s_tp.h, tp_base0 + (r - 1) * ncol_tiles)
            s_w.inc(g.dma_start(wstage[r % 2][:, 0:C], wsrc[:, r, :]), 16)

    @b.tensor
    def _(t):
        if first:
            t.wait_ge(ctx.s_const.h, ctx.s_const.n)
        te_base0 = s_te.n
        w_base0 = s_w.n - nrow_tiles * 16
        n_ = 0
        for r in range(nrow_tiles):
            t.wait_ge(s_w.h, w_base0 + (r + 1) * 16)
            for c in range(ncol_tiles):
                if n_ >= 2:
                    t.wait_ge(s_te.h, te_base0 + n_ - 1)
                s_tp.inc(t.transpose(ctx.bank(6 + n_ % 2, 128, 128, BF16),
                                     wstage[r % 2][:, c * 128:(c + 1) * 128], ident[:]))
                n_ += 1

    @b.scalar
    def _(a):
        tp_base = s_tp.n - nrow_tiles * ncol_tiles
        n_ = 0
        for r in range(nrow_tiles):
            for c in range(ncol_tiles):
                a.wait_ge(s_tp.h, tp_base + n_ + 1)
                s_te.inc(a.copy(dstT[:, dst_map(r, c):dst_map(r, c) + 128],
                                ctx.bank(6 + n_ % 2, 128, 128, BF16)))
                n_ += 1


def emit_row_to_cols(ctx, b, row_ap, cols, ncols, s_tp, s_te, scale=None):
    nc = ctx.nc
    ident = ctx.sb["ident"]

    @b.tensor
    def _(t):
        for c in range(ncols):
            if c >= 1:
                s_te.wait(t)
            s_tp.inc(t.transpose(ctx.bank(5, 128, 1), row_ap[0:1, c * 128:(c + 1) * 128],
                                 ident[0:1, 0:1]))

    @b.scalar
    def _(a):
        tp_base = s_tp.n - ncols
        for c in range(ncols):
            a.wait_ge(s_tp.h, tp_base + c + 1)
            if scale is None:
                s_te.inc(a.copy(cols[:, c:c + 1], ctx.bank(5, 128, 1)))
            else:
                s_te.inc(a.activation(cols[:, c:c + 1], ctx.bank(5, 128, 1),
                                      AF.Copy, scale=scale))



def emit_l2attn_a(ctx, s_lnst, lnh_name, out_name):
    nc = ctx.nc
    ident = ctx.sb["ident"]
    onesrow = ctx.sb["onesrow"]
    lnh = ctx.dram[lnh_name].rearrange("(c p) t -> p c t", p=128)

    wstage = [ctx.sbuf("c2_ws0", [128, 1024], BF16), ctx.sbuf("c2_ws1", [128, 1024], BF16)]
    wbig = ctx.sbuf("c2_wbig", [128, 16 * 1024], BF16)
    rowbuf = ctx.sbuf("c2_rows", [1, 8 * 1024], BF16)
    gcol1 = ctx.sbuf("c2_g1", [128, 8])
    gcol2 = ctx.sbuf("c2_g2", [128, 8])
    wd = ctx.sbuf("c2_wd", [128, 3 * 8])
    bias2 = ctx.sbuf("c2_bias2", [1, 1024], BF16)
    bpwrow = ctx.sbuf("c2_bpw", [1, 1024], BF16)
    bdwcol = ctx.sbuf("c2_bdw", [128, 8], BF16)
    cbuf = [ctx.sbuf("c2_cb0", [128, 8 * 516], BF16), ctx.sbuf("c2_cb1", [128, 8 * 516], BF16)]
    dwbuf = ctx.sbuf("c2_dw", [128, 8 * CHUNK], BF16)
    abuf = ctx.sbuf("c2_a", [128, CHUNK])
    bscan = ctx.sbuf("c2_bs", [128, CHUNK])
    sgh = ctx.sbuf("c2_sgh", [128, CHUNK])
    relh = ctx.sbuf("c2_relh", [128, CHUNK])
    houts = ctx.sbuf("c2_h", [128, CHUNK])
    carry = ctx.sbuf("c2_carry", [128, 8])
    ybr = ctx.sbuf("c2_ybr", [128, 2 * CHUNK])

    s_r0 = ctx.sem("c2_r0")
    s_tp = ctx.sem("c2_tp")
    s_te = ctx.sem("c2_te")
    s_wl = ctx.sem("c2_wl")
    with nc.Block() as b:
        @b.gpsimd
        def _(g):
            s_r0.inc(g.dma_start(rowbuf[0:1, 0:1024],
                                 ctx.dram["L2_attn_conv_norm_g"].rearrange("(o n) -> o n", o=1)), 16)
            s_r0.inc(g.dma_start(rowbuf[0:1, 1024:2048],
                                 ctx.dram["L2_attn_gru_norm_g"].rearrange("(o n) -> o n", o=1)), 16)
            s_r0.inc(g.dma_start(rowbuf[0:1, 2048:3072],
                                 ctx.dram["L2_attn_conv_dw_b"].rearrange("(o n) -> o n", o=1)), 16)
            s_r0.inc(g.dma_start(bpwrow[:],
                                 ctx.dram["L2_attn_conv_pw_b"].rearrange("(o n) -> o n", o=1)), 16)
            s_r0.inc(g.dma_start(rowbuf[0:1, 3072:3072 + 3072],
                                 ctx.dram["L2_attn_conv_dw_w"].rearrange(
                                     "d one k -> one (d k)", one=1)), 16)

        @b.vector
        def _(v):
            pass

    s_tpr = ctx.sem("c2_tpr")
    s_ter = ctx.sem("c2_ter")
    s_gv = ctx.sem("c2_gv")
    with nc.Block() as b:
        @b.vector
        def _(v):
            s_r0.wait(v)
            v.tensor_scalar(rowbuf[0:1, 0:2048], rowbuf[0:1, 0:2048], 1.0, 32.0,
                            ALU.add, ALU.mult).then_inc(s_gv.h, 1)
        s_gv.n = 1

        @b.tensor
        def _(t):
            t.wait_ge(ctx.s_const.h, ctx.s_const.n)
            s_gv.wait(t)
            srcs = [rowbuf[0:1, c * 128:(c + 1) * 128] for c in range(8)]
            srcs += [rowbuf[0:1, 1024 + c * 128: 1024 + (c + 1) * 128] for c in range(8)]
            srcs += [rowbuf[0:1, 2048 + c * 128: 2048 + (c + 1) * 128] for c in range(8)]
            wdwview = rowbuf[0:1, 3072:3072 + 3072].rearrange("o (d k) -> o d k", k=3)
            for j in range(3):
                srcs += [wdwview[:, c * 128:(c + 1) * 128, j] for c in range(8)]
            for n_, sr in enumerate(srcs):
                if n_ >= 1:
                    t.wait_ge(s_ter.h, n_)
                s_tpr.inc(t.transpose(ctx.bank(5, 128, 1, BF16), sr, ident[0:1, 0:1]))
        s_tpr.n = 48

        @b.scalar
        def _(a):
            n_ = 0
            for dst in [gcol1, gcol2, bdwcol]:
                for c in range(8):
                    a.wait_ge(s_tpr.h, n_ + 1)
                    s_ter.inc(a.copy(dst[:, c:c + 1], ctx.bank(5, 128, 1, BF16)))
                    n_ += 1
            for j in range(3):
                for c in range(8):
                    a.wait_ge(s_tpr.h, n_ + 1)
                    s_ter.inc(a.copy(wd[:, j * 8 + c: j * 8 + c + 1], ctx.bank(5, 128, 1, BF16)))
                    n_ += 1
        s_ter.n = 48

    with nc.Block() as b:
        emit_weight_T(ctx, b, "L2_attn_conv_pw_w", wstage, wbig, 8, 8, s_wl, s_tp, s_te)

    s_b2 = ctx.sem("c2_b2")
    with nc.Block() as b:
        @b.tensor
        def _(t):
            s_te.wait(t)
            s_ter.wait(t)
            for half in range(2):
                ins = None
                for c in range(8):
                    ins = t.matmul(ctx.bank(3 + half, 1, 512), bdwcol[:, c:c + 1],
                                   wbig[:, c * 1024 + half * 512: c * 1024 + (half + 1) * 512],
                                   start=(c == 0), stop=(c == 7))
                s_b2.inc(ins)

        @b.vector
        def _(v):
            s_b2.wait(v)
            for half in range(2):
                v.tensor_add(bias2[0:1, half * 512:(half + 1) * 512],
                             bpwrow[0:1, half * 512:(half + 1) * 512],
                             ctx.bank(3 + half, 1, 512)).then_inc(s_b2.h, 1)
                s_b2.n += 1

    RING = 2
    rn1 = ctx.sbuf("c2_rn1", [128, NCH * RING * CHUNK], BF16)
    rn1_at = lambda c, k: rn1[:, (c * RING + k % RING) * CHUNK:
                              (c * RING + k % RING + 1) * CHUNK]
    rn1_view = lambda k: rn1[:].rearrange("p (c s f) -> p c s f", c=NCH, s=RING)[:, :, k % RING, :]
    s_rms1, s_r1st = emit_rms(ctx, lnh_name, rn1_at, "c2r1", gcol1, bank0=0, ring=RING,
                              to_hbm="rn1h", lnx_view=rn1_view, src_bf16=True,
                              src_sem=[(s_lnst[0], lambda k: (k // 2 + 1) * 16),
                                       (s_lnst[1], lambda k: (k // 2 + 1) * 16)])

    rn1h = ctx.dram["rn1h"].rearrange("(c p) t -> p c t", p=128)
    s_cbin = [ctx.sem("c2_cb0s"), ctx.sem("c2_cb1s")]
    s_dw = ctx.sem("c2_dws")
    s_pwm = ctx.sem("c2_pwm")
    s_x0in = ctx.sem("c2_x0in")
    s_add = ctx.sem("c2_adds")
    s_st = [ctx.sem("c2_st0"), ctx.sem("c2_st1")]
    x0b = ctx.sbuf("c2_x0b", [128, NCH * CHUNK])
    lnhd = ctx.dram[lnh_name].rearrange("(c p) t -> p c t", p=128)
    x1d = ctx.dram["x1h"].rearrange("(c p) t -> p c t", p=128)

    with nc.Block() as b:
        @b.gpsimd
        def _(g):
            for k in range(NCK):
                if k >= 1:
                    for o in range(NCH):
                        n = (k - 1) * 8 + o
                        g.wait_ge(s_add.h, n + 1)
                        g.dma_start(x1d[:, o, (k - 1) * CHUNK:k * CHUNK],
                                    ybr[:, (n % 2) * CHUNK:(n % 2 + 1) * CHUNK]).then_inc(s_st[n % 2].h, 16)
                g.wait_ge(s_r1st[k % 2].h, (k // 2 + 1) * 16)
                if k >= 1:
                    g.wait_ge(s_r1st[(k - 1) % 2].h, ((k - 1) // 2 + 1) * 16)
                if k >= 2:
                    g.wait_ge(s_dw.h, 1 + (k - 1) * 8)
                if k == 0:
                    g.dma_start(cbuf[0][:].rearrange("p (c f) -> p c f", c=NCH)[:, :, 4:516],
                                rn1h[:, :, 0:512]).then_inc(s_cbin[0].h, 16)
                else:
                    g.dma_start(cbuf[k % 2][:].rearrange("p (c f) -> p c f", c=NCH)[:, :, 0:514],
                                rn1h[:, :, k * CHUNK - 2:(k + 1) * CHUNK]).then_inc(s_cbin[k % 2].h, 16)
                if k >= 1:
                    g.wait_ge(s_add.h, k * 8)
                g.dma_start(x0b[:].rearrange("p (c f) -> p c f", c=NCH),
                            lnhd[:, :, k * CHUNK:(k + 1) * CHUNK]).then_inc(s_x0in.h, 16)
            for o in range(NCH):
                n = (NCK - 1) * 8 + o
                g.wait_ge(s_add.h, n + 1)
                g.dma_start(x1d[:, o, (NCK - 1) * CHUNK:NCK * CHUNK],
                            ybr[:, (n % 2) * CHUNK:(n % 2 + 1) * CHUNK]).then_inc(s_st[n % 2].h, 16)
            g.wait_ge(s_st[0].h, NCK * 4 * 16)
            g.wait_ge(s_st[1].h, NCK * 4 * 16)
        s_cbin[0].n = 4 * 16
        s_cbin[1].n = 4 * 16
        s_x0in.n = NCK * 16
        s_st[0].n = NCK * 4 * 16
        s_st[1].n = NCK * 4 * 16

        @b.vector
        def _(v):
            v.memset(cbuf[0][:].rearrange("p (c f) -> p c f", c=NCH)[:, :, 0:4], 0.0).then_inc(s_dw.h, 1)
            for k in range(NCK):
                v.wait_ge(s_cbin[k % 2].h, (k // 2 + 1) * 16)
                if k >= 1:
                    v.wait_ge(s_pwm.h, k * 8)
                base = 4 if k == 0 else 2
                for c in range(NCH):
                    cb = cbuf[k % 2][:, c * 516:(c + 1) * 516]
                    v.tensor_scalar(dwbuf[:, c * CHUNK:(c + 1) * CHUNK], cb[:, base - 2:base + 510],
                                    wd[:, 0 * 8 + c:0 * 8 + c + 1], None, ALU.mult)
                    v.scalar_tensor_tensor(dwbuf[:, c * CHUNK:(c + 1) * CHUNK],
                                           cb[:, base - 1:base + 511], wd[:, 1 * 8 + c:1 * 8 + c + 1],
                                           dwbuf[:, c * CHUNK:(c + 1) * CHUNK], ALU.mult, ALU.add)
                    v.scalar_tensor_tensor(dwbuf[:, c * CHUNK:(c + 1) * CHUNK],
                                           cb[:, base:base + 512], wd[:, 2 * 8 + c:2 * 8 + c + 1],
                                           dwbuf[:, c * CHUNK:(c + 1) * CHUNK],
                                           ALU.mult, ALU.add).then_inc(s_dw.h, 1)
                for o in range(NCH):
                    n = k * 8 + o
                    v.wait_ge(s_pwm.h, n + 1)
                    v.wait_ge(s_x0in.h, (k + 1) * 16)
                    if n >= 2:
                        v.wait_ge(s_st[n % 2].h, ((n - 2) // 2 + 1) * 16)
                    v.tensor_add(ybr[:, (n % 2) * CHUNK:(n % 2 + 1) * CHUNK],
                                 x0b[:, o * CHUNK:(o + 1) * CHUNK],
                                 ctx.bank(4 + n % 2)).then_inc(s_add.h, 1)
        s_dw.n = 1 + NCK * 8
        s_add.n = NCK * 8

        @b.tensor
        def _(t):
            s_b2.wait(t)
            for k in range(NCK):
                for o in range(NCH):
                    n = k * 8 + o
                    t.wait_ge(s_dw.h, 1 + k * 8 + 8)
                    if n >= 2:
                        t.wait_ge(s_add.h, n - 1)
                    pairs = [(wbig[:, c * 1024 + o * 128: c * 1024 + (o + 1) * 128],
                              dwbuf[:, c * CHUNK:(c + 1) * CHUNK]) for c in range(NCH)]
                    pairs.append((bias2[0:1, o * 128:(o + 1) * 128],
                                  onesrow[0:1, k * CHUNK:(k + 1) * CHUNK]))
                    mm_chain(t, ctx.bank(4 + n % 2), pairs).then_inc(s_pwm.h, 1)
        s_pwm.n = NCK * 8

    rn2_at = rn1_at
    rn2_view = rn1_view
    s_rms2, s_r2st = emit_rms(ctx, "x1h", rn2_at, "c2r2", gcol2, bank0=0, ring=RING,
                              to_hbm="rn2h", lnx_view=rn2_view, src_bf16=False,
                              src_sem=[(s_st[0], lambda k: (k + 1) * 64),
                                       (s_st[1], lambda k: (k + 1) * 64)])

    s_wl2 = ctx.sem("c2_wl2")
    s_tp2 = ctx.sem("c2_tp2")
    s_te2 = ctx.sem("c2_te2")
    with nc.Block() as b:
        emit_weight_T(ctx, b, "L2_attn_gru_w", wstage, wbig, 16, 8, s_wl2, s_tp2, s_te2)

    rn2h = ctx.dram["rn2h"].rearrange("(c p) t -> p c t", p=128)
    x1hd = ctx.dram["x1h"].rearrange("(c p) t -> p c t", p=128)
    x2d = ctx.dram[out_name].rearrange("(c p) t -> p c t", p=128)
    lnr2 = rn1
    s_ln2 = [ctx.sem("c2_l2a"), ctx.sem("c2_l2b")]
    s_x1in = ctx.sem("c2_x1in")
    s_hm = ctx.sem("c2_hm")
    s_el = ctx.sem("c2_el")
    s_sg = ctx.sem("c2_sg")
    s_sc = ctx.sem("c2_sc")
    s_st2 = [ctx.sem("c2_st2a"), ctx.sem("c2_st2b")]
    x1b = x0b

    with nc.Block() as b:
        @b.sync
        def _(q):
            for k in range(NCK):
                q.wait_ge(s_r2st[k % 2].h, (k // 2 + 1) * 16)
                if k >= 2:
                    q.wait_ge(s_hm.h, (k - 1) * 16)
                q.dma_start(lnr2[:].rearrange("p (c s f) -> p c s f", c=NCH, s=2)[:, :, k % 2, :],
                            rn2h[:, :, k * CHUNK:(k + 1) * CHUNK]).then_inc(s_ln2[k % 2].h, 16)
                if k >= 1:
                    q.wait_ge(s_sc.h, k * (NCH * 3))
                q.dma_start(x1b[:].rearrange("p (c f) -> p c f", c=NCH),
                            x1hd[:, :, k * CHUNK:(k + 1) * CHUNK]).then_inc(s_x1in.h, 16)
        s_ln2[0].n = 4 * 16
        s_ln2[1].n = 4 * 16
        s_x1in.n = NCK * 16

        @b.tensor
        def _(t):
            s_te2.wait(t)
            for k in range(NCK):
                t.wait_ge(s_ln2[k % 2].h, (k // 2 + 1) * 16)
                for c in range(NCH):
                    n = k * 8 + c
                    if n >= 1:
                        t.wait_ge(s_sg.h, n * 3)
                        t.wait_ge(s_el.h, n * 3 - 1)
                    pairs_h = [(wbig[:, ic * 2048 + c * 128: ic * 2048 + (c + 1) * 128],
                                lnr2[:, (ic * 2 + k % 2) * CHUNK:(ic * 2 + k % 2 + 1) * CHUNK])
                               for ic in range(NCH)]
                    mm_chain(t, ctx.bank(0), pairs_h)
                    pairs_g = [(wbig[:, ic * 2048 + 1024 + c * 128: ic * 2048 + 1024 + (c + 1) * 128],
                                lnr2[:, (ic * 2 + k % 2) * CHUNK:(ic * 2 + k % 2 + 1) * CHUNK])
                               for ic in range(NCH)]
                    mm_chain(t, ctx.bank(1), pairs_g).then_inc(s_hm.h, 2)
        s_hm.n = NCK * 16

        @b.scalar
        def _(a):
            for k in range(NCK):
                for c in range(NCH):
                    n = k * 8 + c
                    a.wait_ge(s_hm.h, (n + 1) * 2)
                    if n >= 1:
                        a.wait_ge(s_el.h, n * 3)
                        a.wait_ge(s_sc.h, n * 3 - 2)
                    a.activation(sgh[:], ctx.bank(0), AF.Sigmoid)
                    a.activation(abuf[:], ctx.bank(1), AF.Sigmoid, scale=-1.0).then_inc(s_sg.h, 2)
                    a.activation(relh[:], ctx.bank(1), AF.Sigmoid).then_inc(s_sg.h, 1)
        s_sg.n = NCK * 8 * 3

        @b.vector
        def _(v):
            for k in range(NCK):
                for c in range(NCH):
                    n = k * 8 + c
                    v.wait_ge(s_sg.h, n * 3 + 2)
                    v.tensor_scalar_min(sgh[:], sgh[:], 0.5)
                    v.scalar_tensor_tensor(bscan[:], ctx.bank(0), 0.0, sgh[:],
                                           ALU.max, ALU.add).then_inc(s_el.h, 2)
                    v.wait_ge(s_sg.h, n * 3 + 3)
                    v.tensor_mul(bscan[:], bscan[:], relh[:]).then_inc(s_el.h, 1)
                    v.wait_ge(s_el.h, n * 3 + 3)
                    if k == 0:
                        v.tensor_tensor_scan(houts[:], abuf[:], bscan[:], 0.0,
                                             ALU.mult, ALU.add).then_inc(s_sc.h, 1)
                    else:
                        v.tensor_tensor_scan(houts[:], abuf[:], bscan[:], carry[:, c:c + 1],
                                             ALU.mult, ALU.add).then_inc(s_sc.h, 1)
                    v.wait_ge(s_sc.h, n * 3 + 1)
                    v.tensor_copy(carry[:, c:c + 1], houts[:, CHUNK - 1:CHUNK]).then_inc(s_sc.h, 1)
                    v.wait_ge(s_x1in.h, (k + 1) * 16)
                    if n >= 2:
                        v.wait_ge(s_st2[n % 2].h, ((n - 2) // 2 + 1) * 16)
                    v.tensor_add(ybr[:, (n % 2) * CHUNK:(n % 2 + 1) * CHUNK],
                                 x1b[:, c * CHUNK:(c + 1) * CHUNK], houts[:]).then_inc(s_sc.h, 1)
        s_el.n = NCK * 8 * 3
        s_sc.n = NCK * 8 * 3

        @b.gpsimd
        def _(g):
            for k in range(NCK):
                for c in range(NCH):
                    n = k * 8 + c
                    g.wait_ge(s_sc.h, n * 3 + 3)
                    g.dma_start(x2d[:, c, k * CHUNK:(k + 1) * CHUNK],
                                ybr[:, (n % 2) * CHUNK:(n % 2 + 1) * CHUNK]).then_inc(s_st2[n % 2].h, 16)
            g.wait_ge(s_st2[0].h, NCK * 4 * 16)
            g.wait_ge(s_st2[1].h, NCK * 4 * 16)
        s_st2[0].n = NCK * 4 * 16
        s_st2[1].n = NCK * 4 * 16

    return s_st2




def emit_l2attn_b(ctx, src_outer, x2_name, out_name):
    nc = ctx.nc
    onesrow = ctx.sb["onesrow"]
    ident = ctx.sb["ident"]

    wbig = ctx.sbuf("f2_wbig", [128, 32 * 1024], BF16)
    h1b = ctx.sbuf("f2_h1b", [128, 32 * CHUNK], BF16)
    wstage = [h1b[:, 0:4096], h1b[:, 4096:8192]]
    rowbuf = ctx.sbuf("f2_rows", [1, 1024], BF16)
    b1row = ctx.sbuf("f2_b1", [1, 4096], BF16)
    b2row = ctx.sbuf("f2_b2", [1, 1024], BF16)
    gcol3 = ctx.sbuf("f2_g3", [128, 8])
    gel = ctx.sbuf("f2_gel", [128, 2 * CHUNK], BF16)

    s_r0 = ctx.sem("f2_r0")
    s_gv2 = ctx.sem("f2_gv2")
    s_tpr = ctx.sem("f2_tpr")
    s_ter = ctx.sem("f2_ter")
    with nc.Block() as b:
        @b.gpsimd
        def _(g):
            s_r0.inc(g.dma_start(rowbuf[0:1, 0:1024],
                                 ctx.dram["L2_attn_ff_norm_g"].rearrange("(o n) -> o n", o=1)), 16)
            s_r0.inc(g.dma_start(b1row[:],
                                 ctx.dram["L2_attn_ff_b1"].rearrange("(o n) -> o n", o=1)), 16)
            s_r0.inc(g.dma_start(b2row[:],
                                 ctx.dram["L2_attn_ff_b2"].rearrange("(o n) -> o n", o=1)), 16)

        @b.vector
        def _(v):
            s_r0.wait(v)
            v.tensor_scalar(rowbuf[0:1, 0:1024], rowbuf[0:1, 0:1024], 1.0, 32.0,
                            ALU.add, ALU.mult).then_inc(s_gv2.h, 1)
        s_gv2.n = 1

        @b.tensor
        def _(t):
            t.wait_ge(ctx.s_const.h, ctx.s_const.n)
            s_r0.wait(t)
            s_gv2.wait(t)
            for c in range(8):
                if c >= 1:
                    t.wait_ge(s_ter.h, c)
                s_tpr.inc(t.transpose(ctx.bank(5, 128, 1, BF16),
                                      rowbuf[0:1, c * 128:(c + 1) * 128], ident[0:1, 0:1]))

        @b.scalar
        def _(a):
            for c in range(8):
                a.wait_ge(s_tpr.h, c + 1)
                s_ter.inc(a.copy(gcol3[:, c:c + 1], ctx.bank(5, 128, 1, BF16)))

    RING = 2
    rn3 = ctx.sbuf("f2_rn3", [128, NCH * RING * CHUNK], BF16)
    rn3_at = lambda c, k: rn3[:, (c * RING + k % RING) * CHUNK:
                              (c * RING + k % RING + 1) * CHUNK]
    rn3_view = lambda k: rn3[:].rearrange("p (c s f) -> p c s f", c=NCH, s=RING)[:, :, k % RING, :]
    s_rms3, s_r3st = emit_rms(ctx, x2_name, rn3_at, "f2r3", gcol3, bank0=0, ring=RING,
                              to_hbm="rn3h", lnx_view=rn3_view, src_bf16=False)

    s_wl = ctx.sem("f2_wl")
    s_tp = ctx.sem("f2_tp")
    s_te = ctx.sem("f2_te")
    with nc.Block() as b:
        emit_weight_T(ctx, b, "L2_attn_ff_w1", wstage, wbig, 32, 8, s_wl, s_tp, s_te,
                      dst_map=lambda r, c: c * 4096 + r * 128)

    rn3h = ctx.dram["rn3h"].rearrange("(c p) t -> p c t", p=128)
    h1d = ctx.dram["h1h"].rearrange("(ft p) t -> p ft t", p=128)
    lnr = rn3
    s_lr = [ctx.sem("f2_lr0"), ctx.sem("f2_lr1")]
    s_hmm = ctx.sem("f2_hmm")
    s_gev = ctx.sem("f2_gev")
    s_hst = [ctx.sem("f2_hst0"), ctx.sem("f2_hst1")]
    with nc.Block() as b:
        @b.sync
        def _(q):
            for k in range(NCK):
                q.wait_ge(s_r3st[k % 2].h, (k // 2 + 1) * 16)
                if k >= 2:
                    q.wait_ge(s_hmm.h, (k - 1) * 32)
                q.dma_start(lnr[:].rearrange("p (c s f) -> p c s f", c=NCH, s=2)[:, :, k % 2, :],
                            rn3h[:, :, k * CHUNK:(k + 1) * CHUNK]).then_inc(s_lr[k % 2].h, 16)
        s_lr[0].n = 4 * 16
        s_lr[1].n = 4 * 16

        @b.tensor
        def _(t):
            s_te.wait(t)
            for k in range(NCK):
                t.wait_ge(s_lr[k % 2].h, (k // 2 + 1) * 16)
                for fo in range(32):
                    n = k * 32 + fo
                    if n >= 2:
                        t.wait_ge(s_gev.h, n - 1)
                    pairs = [(wbig[:, c * 4096 + fo * 128: c * 4096 + (fo + 1) * 128],
                              lnr[:, (c * 2 + k % 2) * CHUNK:(c * 2 + k % 2 + 1) * CHUNK])
                             for c in range(NCH)]
                    pairs.append((b1row[0:1, fo * 128:(fo + 1) * 128],
                                  onesrow[0:1, k * CHUNK:(k + 1) * CHUNK]))
                    mm_chain(t, ctx.bank(2 + n % 2), pairs).then_inc(s_hmm.h, 1)
        s_hmm.n = NCK * 32

        @b.scalar
        def _(a):
            for k in range(NCK):
                for fo in range(32):
                    n = k * 32 + fo
                    a.wait_ge(s_hmm.h, n + 1)
                    if n >= 2:
                        a.wait_ge(s_hst[n % 2].h, ((n - 2) // 2 + 1) * 16)
                    a.activation(gel[:, (n % 2) * CHUNK:(n % 2 + 1) * CHUNK],
                                 ctx.bank(2 + n % 2), AF.Gelu).then_inc(s_gev.h, 1)
        s_gev.n = NCK * 32

        @b.gpsimd
        def _(g):
            for k in range(NCK):
                for fo in range(32):
                    n = k * 32 + fo
                    g.wait_ge(s_gev.h, n + 1)
                    g.dma_start(h1d[:, fo, k * CHUNK:(k + 1) * CHUNK],
                                gel[:, (n % 2) * CHUNK:(n % 2 + 1) * CHUNK]).then_inc(s_hst[n % 2].h, 16)
            g.wait_ge(s_hst[0].h, NCK * 16 * 16)
            g.wait_ge(s_hst[1].h, NCK * 16 * 16)
        s_hst[0].n = NCK * 16 * 16
        s_hst[1].n = NCK * 16 * 16

    s_wl2 = ctx.sem("f2_wl2")
    s_tp2 = ctx.sem("f2_tp2")
    s_te2 = ctx.sem("f2_te2")
    with nc.Block() as b:
        emit_weight_T(ctx, b, "L2_attn_ff_w2", wstage, wbig, 8, 32, s_wl2, s_tp2, s_te2,
                      dst_map=lambda r, c: c * 1024 + r * 128)

    s_h1in = ctx.sem("f2_h1in")

    def sync_extra(q, k):
        if k >= 1:
            q.wait_ge(ctx.sems_f2m_pe.h, k * 8)
        q.dma_start(h1b[:].rearrange("p (ft f) -> p ft f", ft=32),
                    h1d[:, :, k * CHUNK:(k + 1) * CHUNK]).then_inc(s_h1in.h, 16)
        s_h1in.n += 16

    def mm_fn(t, k, o, bank_ap):
        t.wait_ge(s_h1in.h, (k + 1) * 16)
        pairs = [(wbig[:, c * 1024 + o * 128: c * 1024 + (o + 1) * 128],
                  h1b[:, c * CHUNK:(c + 1) * CHUNK]) for c in range(32)]
        pairs.append((b2row[0:1, o * 128:(o + 1) * 128],
                      onesrow[0:1, k * CHUNK:(k + 1) * CHUNK]))
        return mm_chain(t, bank_ap, pairs)

    return emit_residual_out(ctx, "f2m", src_outer, out_name,
                             s_te2, None, mm_fn, s_pe=ctx.sems_f2m_pe,
                             extra_stream=(x2_name, F32),
                             bufs={"xb": ctx.ln_bufs()["xb"]}, sync_extra=sync_extra)


def build_launch_l2attn_b():
    nc = bass.Bass("TRN2", target_bir_lowering=False, detect_race_conditions=False)
    ctx = Ctx(nc)
    ctx.din("xT", [D, T], F32)
    ctx.din("x2T", [D, T], F32)
    ctx.din("L2_attn_ff_norm_g", [1024])
    ctx.din("L2_attn_ff_w1", [4096, 1024]); ctx.din("L2_attn_ff_b1", [4096])
    ctx.din("L2_attn_ff_w2", [1024, 4096]); ctx.din("L2_attn_ff_b2", [1024])
    ctx.dout("yT", [D, T], F32)
    ctx.dint("rn3h", [D, T], BF16)
    ctx.dint("h1h", [4096, T], BF16)
    ctx.load_consts()
    ctx.sems_f2m_pe = ctx.sem("f2m_pe")
    emit_l2attn_b(ctx, "xT", "x2T", "yT")
    return ctx



MAGIC = float(np.float32(1.5 * 2 ** 23))


def emit_quant_chain(v, u_src, scr, W):
    r1, rb, q = scr["r1"], scr["rb"], scr["q"]
    v.tensor_scalar(r1[:, 0:W], u_src, 2.0, 2.0, ALU.min, ALU.mult)
    v.tensor_scalar(r1[:, 0:W], r1[:, 0:W], MAGIC, None, ALU.add)
    v.tensor_scalar(r1[:, 0:W], r1[:, 0:W], MAGIC, 0.5, ALU.subtract, ALU.mult)
    v.tensor_scalar(rb[:, 0:W], u_src, 2.0, 2.0, ALU.subtract, ALU.min)
    v.tensor_scalar(rb[:, 0:W], rb[:, 0:W], 0.0, MAGIC, ALU.max, ALU.add)
    v.tensor_scalar(rb[:, 0:W], rb[:, 0:W], MAGIC, None, ALU.subtract)
    v.tensor_scalar(q[:, 0:W], u_src, 5.0, 2.0, ALU.is_gt, ALU.mult)
    v.tensor_add(q[:, 0:W], q[:, 0:W], r1[:, 0:W])
    v.tensor_add(q[:, 0:W], q[:, 0:W], rb[:, 0:W])
    return 9


def alloc_quant_scratch(ctx, tag, W):
    return {
        "bm": ctx.sbuf(tag + "_bm", [128, W // 16]),
        "sc6": ctx.sbuf(tag + "_sc6", [128, W // 16]),
        "rcp": ctx.sbuf(tag + "_rcp", [128, W // 16]),
        "ab": ctx.sbuf(tag + "_ab", [128, W]),
        "sg": ctx.sbuf(tag + "_sg", [128, W]),
        "r1": ctx.sbuf(tag + "_r1", [128, W]),
        "rb": ctx.sbuf(tag + "_rb", [128, W]),
        "q": ctx.sbuf(tag + "_q", [128, W]),
    }


def emit_quant_tile_dve(v, src_ap, dst_ap, scr, W, s_dv, s_ac, acb):
    nb = W // 16
    bmv = src_ap.rearrange("p (b k) -> p b k", k=16)
    v.tensor_reduce(scr["bm"][:, 0:nb], bmv, AX.X, ALU.max,
                    apply_absolute_value=True).then_inc(s_dv.h, 1)
    s_dv.n += 1
    s_dv.wait(v)
    v.tensor_scalar(scr["sc6"][:, 0:nb], scr["bm"][:, 0:nb], 1.0 / 6.0, 1e-12,
                    ALU.mult, ALU.max).then_inc(s_dv.h, 1)
    s_dv.n += 1
    s_dv.wait(v)
    v.reciprocal(scr["rcp"][:, 0:nb], scr["sc6"][:, 0:nb]).then_inc(s_dv.h, 1)
    s_dv.n += 1
    s_dv.wait(v)
    v.wait_ge(s_ac.h, acb + 1)
    rcpb = scr["rcp"][:, 0:nb].rearrange("p (b o) -> p b o", o=1).broadcast_to([128, nb, 16])
    v.tensor_mul(scr["ab"][:, 0:W].rearrange("p (b k) -> p b k", k=16),
                 scr["ab"][:, 0:W].rearrange("p (b k) -> p b k", k=16), rcpb).then_inc(s_dv.h, 1)
    s_dv.n += 1
    emit_quant_chain(v, scr["ab"][:, 0:W], scr, W)
    v.wait_ge(s_ac.h, acb + 2)
    v.tensor_mul(scr["q"][:, 0:W], scr["q"][:, 0:W], scr["sg"][:, 0:W])
    sc6b = scr["sc6"][:, 0:nb].rearrange("p (b o) -> p b o", o=1).broadcast_to([128, nb, 16])
    v.tensor_mul(dst_ap.rearrange("p (b k) -> p b k", k=16),
                 scr["q"][:, 0:W].rearrange("p (b k) -> p b k", k=16), sc6b).then_inc(s_dv.h, 2)
    s_dv.n += 2


def emit_quant_tile_act(a, src_ap, scr, W, s_dv, dvb, s_ac):
    a.wait_ge(s_dv.h, dvb)
    a.activation(scr["ab"][:, 0:W], src_ap, AF.Abs).then_inc(s_ac.h, 1)
    s_ac.n += 1
    a.sign(scr["sg"][:, 0:W], src_ap).then_inc(s_ac.h, 1)
    s_ac.n += 1


DVE_Q = 6
ACT_Q = 2


def emit_quant_weights(ctx, wname, qt_name, R, C, scr, wstage, qcol):
    nc = ctx.nc
    ident = ctx.sb["ident"]
    identf = ctx.identf
    nrt, nct = R // 128, C // 128
    PW = 1408 if C > 1408 else C
    npass = (C + PW - 1) // PW
    QD = DVE_Q * npass
    QA = ACT_Q * npass
    tag = "qw_" + wname[-6:]
    s_w = ctx.sem(tag + "_w")
    s_dv = ctx.sem(tag + "_dv")
    s_ac = ctx.sem(tag + "_ac")
    s_tp = ctx.sem(tag + "_tp")
    s_te = ctx.sem(tag + "_te")
    s_st = ctx.sem(tag + "_st")
    wsrc = ctx.dram[wname].rearrange("(r p) w -> p r w", p=128)
    qdst = ctx.dram[qt_name].rearrange("(ct p) r -> p ct r", p=128)

    with nc.Block() as b:
        @b.gpsimd
        def _(g):
            for r in range(nrt):
                if r >= 1:
                    g.wait_ge(s_dv.h, r * QD)
                    g.wait_ge(s_tp.h, r * nct)
                g.dma_start(wstage[r % 2][:, 0:C], wsrc[:, r, :]).then_inc(s_w.h, 16)
                if r >= 1:
                    g.wait_ge(s_te.h, r * nct)
                    g.dma_start(qdst[:, :, (r - 1) * 128:r * 128],
                                qcol[(r - 1) % 2][:, 0:nct * 128].rearrange("p (ct w) -> p ct w", ct=nct)).then_inc(s_st.h, 16)
            g.wait_ge(s_te.h, nrt * nct)
            g.dma_start(qdst[:, :, (nrt - 1) * 128:nrt * 128],
                        qcol[(nrt - 1) % 2][:, 0:nct * 128].rearrange("p (ct w) -> p ct w", ct=nct)).then_inc(s_st.h, 16)
            g.wait_ge(s_st.h, nrt * 16)
        s_w.n = nrt * 16
        s_st.n = nrt * 16

        @b.vector
        def _(v):
            for r in range(nrt):
                v.wait_ge(s_w.h, (r + 1) * 16)
                for ps in range(npass):
                    W_ = min(PW, C - ps * PW)
                    sl = slice(ps * PW, ps * PW + W_)
                    emit_quant_tile_dve(v, wstage[r % 2][:, sl], wstage[r % 2][:, sl],
                                        scr, W_, s_dv, s_ac, r * QA + ps * ACT_Q)

        @b.scalar
        def _(a):
            for r in range(nrt):
                a.wait_ge(s_w.h, (r + 1) * 16)
                for ps in range(npass):
                    W_ = min(PW, C - ps * PW)
                    sl = slice(ps * PW, ps * PW + W_)
                    emit_quant_tile_act(a, wstage[r % 2][:, sl], scr, W_, s_dv,
                                        r * QD + ps * DVE_Q, s_ac)
                if r >= 1:
                    a.wait_ge(s_st.h, r * 16)
                for c in range(nct):
                    n = r * nct + c
                    a.wait_ge(s_tp.h, n + 1)
                    a.copy(qcol[r % 2][:, c * 128:(c + 1) * 128],
                           ctx.bank(6 + n % 2, 128, 128)).then_inc(s_te.h, 1)
        s_te.n = nrt * nct

        @b.tensor
        def _(t):
            t.wait_ge(ctx.s_const.h, ctx.s_const.n)
            for r in range(nrt):
                t.wait_ge(s_dv.h, (r + 1) * QD)
                for c in range(nct):
                    n = r * nct + c
                    if n >= 2:
                        t.wait_ge(s_te.h, n - 1)
                    t.transpose(ctx.bank(6 + n % 2, 128, 128),
                                wstage[r % 2][:, c * 128:(c + 1) * 128],
                                identf[:]).then_inc(s_tp.h, 1)
        s_tp.n = nrt * nct


def emit_l2ffn(ctx, s_lnst, lnh_name, src_name, out_name):
    nc = ctx.nc
    lnh = ctx.dram[lnh_name].rearrange("(c p) t -> p c t", p=128)
    src = ctx.dram[src_name].rearrange("(c p) t -> p c t", p=128)
    dst = ctx.dram[out_name].rearrange("(c p) t -> p c t", p=128)
    NI = 2816
    NIT = NI // 128

    ctx.dint("qgT", [1024, NI], BF16)
    ctx.dint("quT", [1024, NI], BF16)
    ctx.dint("qdT", [NI, 1024], BF16)
    scr = alloc_quant_scratch(ctx, "q2", 1408)
    ws_ = ctx.sbuf("q2_ws0", [128, NI])
    wstage = [ws_, ws_]
    qc_ = ctx.sbuf("q2_qc0", [128, NI], BF16)
    qcol = [qc_, qc_]
    emit_quant_weights(ctx, "L2_ffn_w_gate", "qgT", NI, 1024, scr, wstage, qcol)
    emit_quant_weights(ctx, "L2_ffn_w_up", "quT", NI, 1024, scr, wstage, qcol)
    emit_quant_weights(ctx, "L2_ffn_w_down", "qdT", 1024, NI, scr, wstage, qcol)

    qg = ctx.dram["qgT"].rearrange("(ic p) n -> p ic n", p=128)
    qu = ctx.dram["quT"].rearrange("(ic p) n -> p ic n", p=128)
    qd = ctx.dram["qdT"].rearrange("(nt p) o -> p nt o", p=128)

    lnr = ctx.sbuf("q2_lnr", [128, NCH * CHUNK])
    xq = ctx.sbuf("q2_xq", [128, NCH * CHUNK], BF16)
    btil = ctx.sbuf("q2_btil", [128, CHUNK])
    btq = ctx.sbuf("q2_btq", [128, CHUNK], BF16)
    hq_ = ctx.sbuf("q2_hq", [128, NIT * CHUNK], BF16)
    hq = hq_[:].rearrange("p f -> p f") if False else hq_
    tmps = ctx.sbuf("q2_tmps", [128, CHUNK])
    h1t = ctx.sbuf("q2_h1t", [128, CHUNK])
    gub = [ctx.sbuf("q2_gu0", [128, 2 * 1024], BF16), ctx.sbuf("q2_gu1", [128, 2 * 1024], BF16)]
    dw_ = ctx.sbuf("q2_dw0", [128, NIT * 128], BF16)
    dwb = [dw_, dw_]
    xb = ctx.ln_bufs()["xb"]
    ybr = ctx.sbuf("q2_ybr", [128, 2 * CHUNK])

    s_lr = [ctx.sem("q2_lr0"), ctx.sem("q2_lr1")]
    s_dv = ctx.sem("q2_dv")
    s_ac = ctx.sem("q2_ac")
    s_gu = [ctx.sem("q2_gua"), ctx.sem("q2_gub")]
    s_gum = ctx.sem("q2_gum")
    s_sil = ctx.sem("q2_sil")
    s_swg = ctx.sem("q2_swg")
    s_dwl = [ctx.sem("q2_dwl0"), ctx.sem("q2_dwl1")]
    s_dmm = ctx.sem("q2_dmm")
    s_xin = ctx.sem("q2_xin")
    s_add = ctx.sem("q2_add")
    s_st = [ctx.sem("q2_st0"), ctx.sem("q2_st1")]

    DVX = DVE_Q + 2
    CHV = (8 + NIT) * DVX

    def dv_chunk(k):
        return k * CHV

    def ac_chunk(k):
        return k * (8 + NIT) * ACT_Q

    with nc.Block() as b:
        @b.sync
        def _(q):
            for k in range(NCK):
                q.wait_ge(s_lnst[k % 2].h, (k // 2 + 1) * 16)
                if k >= 1:
                    q.wait_ge(s_dv.h, dv_chunk(k - 1) + 8 * DVX)
                q.dma_start(lnr[:].rearrange("p (c f) -> p c f", c=NCH),
                            lnh[:, :, k * CHUNK:(k + 1) * CHUNK]).then_inc(s_lr[k % 2].h, 16)
                if k >= 1:
                    q.wait_ge(s_add.h, k * 8)
                q.dma_start(xb[:].rearrange("p (c f) -> p c f", c=NCH),
                            src[:, :, k * CHUNK:(k + 1) * CHUNK]).then_inc(s_xin.h, 16)
                for n in range(NIT):
                    m = k * NIT + n
                    if m >= 2:
                        q.wait_ge(s_gum.h, (m - 1) * 2)
                    q.dma_start(gub[m % 2][:, 0:1024].rearrange("p (ic w) -> p ic w", ic=8),
                                qg[:, :, n * 128:(n + 1) * 128]).then_inc(s_gu[m % 2].h, 16)
                    q.dma_start(gub[m % 2][:, 1024:2048].rearrange("p (ic w) -> p ic w", ic=8),
                                qu[:, :, n * 128:(n + 1) * 128]).then_inc(s_gu[m % 2].h, 16)
                for o in range(NCH):
                    m2 = k * 8 + o
                    if m2 >= 1:
                        q.wait_ge(s_dmm.h, m2)
                    q.dma_start(dwb[m2 % 2][:].rearrange("p (nt w) -> p nt w", nt=NIT),
                                qd[:, :, o * 128:(o + 1) * 128]).then_inc(s_dwl[m2 % 2].h, 16)
        s_lr[0].n = 4 * 16
        s_lr[1].n = 4 * 16
        s_xin.n = NCK * 16
        s_gu[0].n = NCK * NIT * 16
        s_gu[1].n = NCK * NIT * 16
        s_dwl[0].n = NCK * 4 * 16
        s_dwl[1].n = NCK * 4 * 16

        @b.vector
        def _(v):
            for k in range(NCK):
                v.wait_ge(s_lr[k % 2].h, (k // 2 + 1) * 16)
                if k >= 1:
                    v.wait_ge(s_gum.h, k * NIT * 2)
                for c in range(NCH):
                    src_t = lnr[:, c * CHUNK:(c + 1) * CHUNK]
                    v.transpose(btil[:], src_t).then_inc(s_dv.h, 1)
                    s_dv.n += 1
                    s_dv.wait(v)
                    emit_quant_tile_dve(v, btil[:], btq[:], scr, CHUNK, s_dv, s_ac,
                                        ac_chunk(k) + c * ACT_Q)
                    s_dv.wait(v)
                    v.transpose(xq[:, c * CHUNK:(c + 1) * CHUNK], btq[:]).then_inc(s_dv.h, 1)
                    s_dv.n += 1
                hqs = hq_[:]
                for n in range(NIT):
                    m = k * NIT + n
                    v.wait_ge(s_sil.h, m + 1)
                    if k >= 1 and n == 0:
                        v.wait_ge(s_dmm.h, k * 8)
                    v.tensor_mul(h1t[:], tmps[:], ctx.bank(2 + m % 2, 128, CHUNK)).then_inc(s_swg.h, 1)
                    s_swg.n += 1
                    s_swg.wait(v)
                    v.transpose(btil[:], h1t[:]).then_inc(s_dv.h, 1)
                    s_dv.n += 1
                    s_dv.wait(v)
                    emit_quant_tile_dve(v, btil[:], btq[:], scr, CHUNK, s_dv, s_ac,
                                        ac_chunk(k) + (8 + n) * ACT_Q)
                    s_dv.wait(v)
                    v.transpose(hqs[:, n * CHUNK:(n + 1) * CHUNK], btq[:]).then_inc(s_dv.h, 1)
                    s_dv.n += 1
                v.wait_ge(s_xin.h, (k + 1) * 16)
                for o in range(NCH):
                    m2 = k * 8 + o
                    v.wait_ge(s_dmm.h, m2 + 1)
                    if m2 >= 2:
                        v.wait_ge(s_st[m2 % 2].h, ((m2 - 2) // 2 + 1) * 16)
                    v.tensor_add(ybr[:, (m2 % 2) * CHUNK:(m2 % 2 + 1) * CHUNK],
                                 xb[:, o * CHUNK:(o + 1) * CHUNK],
                                 ctx.bank(4 + m2 % 2)).then_inc(s_add.h, 1)
        s_add.n = NCK * 8

        @b.scalar
        def _(a):
            for k in range(NCK):
                for c in range(NCH):
                    a.wait_ge(s_dv.h, dv_chunk(k) + c * DVX + 1)
                    emit_quant_tile_act(a, btil[:], scr, CHUNK, s_dv,
                                        dv_chunk(k) + c * DVX, s_ac)
                for n in range(NIT):
                    m = k * NIT + n
                    a.wait_ge(s_gum.h, (m + 1) * 2)
                    if m >= 1:
                        a.wait_ge(s_swg.h, m)
                    a.activation(tmps[:], ctx.bank(0 + m % 2, 128, CHUNK), AF.Silu).then_inc(s_sil.h, 1)
                    s_sil.n += 1
                    a.wait_ge(s_dv.h, dv_chunk(k) + (8 + n) * DVX + 1)
                    emit_quant_tile_act(a, btil[:], scr, CHUNK, s_dv,
                                        dv_chunk(k) + (8 + n) * DVX, s_ac)

        @b.tensor
        def _(t):
            t.wait_ge(ctx.s_const.h, ctx.s_const.n)
            for k in range(NCK):
                t.wait_ge(s_dv.h, dv_chunk(k) + 8 * DVX)
                for n in range(NIT):
                    m = k * NIT + n
                    t.wait_ge(s_gu[m % 2].h, (m // 2 + 1) * 32)
                    if m >= 2:
                        t.wait_ge(s_sil.h, m - 1)
                        t.wait_ge(s_swg.h, m - 1)
                    pairs_g = [(gub[m % 2][:, c * 128:(c + 1) * 128],
                                xq[:, c * CHUNK:(c + 1) * CHUNK]) for c in range(8)]
                    mm_chain(t, ctx.bank(0 + m % 2), pairs_g)
                    pairs_u = [(gub[m % 2][:, 1024 + c * 128: 1024 + (c + 1) * 128],
                                xq[:, c * CHUNK:(c + 1) * CHUNK]) for c in range(8)]
                    mm_chain(t, ctx.bank(2 + m % 2), pairs_u).then_inc(s_gum.h, 2)
                hqs = hq_[:]
                t.wait_ge(s_dv.h, dv_chunk(k + 1))
                for o in range(NCH):
                    m2 = k * 8 + o
                    t.wait_ge(s_dwl[m2 % 2].h, (m2 // 2 + 1) * 16)
                    if m2 >= 2:
                        t.wait_ge(s_add.h, m2 - 1)
                    pairs = [(dwb[m2 % 2][:, c * 128:(c + 1) * 128],
                              hqs[:, c * CHUNK:(c + 1) * CHUNK]) for c in range(NIT)]
                    mm_chain(t, ctx.bank(4 + m2 % 2), pairs).then_inc(s_dmm.h, 1)
        s_gum.n = NCK * NIT * 2
        s_dmm.n = NCK * 8

        @b.gpsimd
        def _(g):
            for k in range(NCK):
                for o in range(NCH):
                    m2 = k * 8 + o
                    g.wait_ge(s_add.h, m2 + 1)
                    g.dma_start(dst[:, o, k * CHUNK:(k + 1) * CHUNK],
                                ybr[:, (m2 % 2) * CHUNK:(m2 % 2 + 1) * CHUNK]).then_inc(s_st[m2 % 2].h, 16)
            g.wait_ge(s_st[0].h, NCK * 4 * 16)
            g.wait_ge(s_st[1].h, NCK * 4 * 16)
        s_st[0].n = NCK * 4 * 16
        s_st[1].n = NCK * 4 * 16

    return s_st


def build_launch_l2ffn():
    nc = bass.Bass("TRN2", target_bir_lowering=False, detect_race_conditions=False)
    ctx = Ctx(nc)
    ctx.din("xT", [D, T], F32)
    ctx.din("L2_ffn_w_gate", [2816, 1024])
    ctx.din("L2_ffn_w_up", [2816, 1024])
    ctx.din("L2_ffn_w_down", [1024, 2816])
    ctx.dout("yT", [D, T], F32)
    ctx.dint("lnh", [D, T], F32)
    ctx.load_consts()
    RING = 2
    lnx = ctx.sbuf("lnring", [128, NCH * RING * CHUNK])
    lnx_at = lambda c, k: lnx[:, (c * RING + k % RING) * CHUNK:
                              (c * RING + k % RING + 1) * CHUNK]
    lnx_view = lambda k: lnx[:].rearrange("p (c s f) -> p c s f", c=NCH, s=RING)[:, :, k % RING, :]
    s_ln, s_lnst = emit_ln(ctx, "xT", lnx_at, "ln2f", bank0=0, ring=RING, to_hbm="lnh",
                           lnx_view=lnx_view)
    emit_l2ffn(ctx, s_lnst, "lnh", "xT", "yT")
    return ctx



def emit_l0attn(ctx, s_lnst, lnh_name, src_name, out_name, lnr):
    nc = ctx.nc
    ident = ctx.sb["ident"]
    identf = ctx.sb["identf"]
    ones128 = ctx.sb["ones128"]
    onesrow = ctx.sb["onesrow"]
    q512 = ctx.sb["q512"]
    blk64 = ctx.sb["blk64"]
    lnh = ctx.dram[lnh_name].rearrange("(c p) t -> p c t", p=128)
    SL = _alibi_slopes()
    CUT = 100.0
    nkv4 = ctx.dram["L0_attn_null_kv"].rearrange("h (j two) d -> h j two d", two=2)

    ctx.dint("oth", [1024, T], BF16)
    oth = ctx.dram["oth"].rearrange("(hp p) t -> p hp t", p=128)

    def ktmin(h, qc):
        for kt in range(0, 4 * qc + 4):
            if SL[h] * (qc * 512 - (kt * 128 + 127)) <= CUT:
                return kt
        return 4 * qc + 3

    pool = ctx.sbuf("a0_pool", [128, NCH * CHUNK])
    wstg = pool[:, 0:3072]
    grow = pool[0:1, 3072:4096]
    wpb = ctx.sbuf("a0_wpb", [128, 8 * 384], BF16)
    gcol = ctx.sbuf("a0_gcol", [128, 8])
    qs8col = ctx.sbuf("a0_qs8", [128, 1])
    kscol = ctx.sbuf("a0_ks", [128, 1])
    iq2col = ctx.sbuf("a0_iq2", [128, 1])
    ik2col = ctx.sbuf("a0_ik2", [128, 1])
    qsrow = ctx.sbuf("a0_qsrow", [1, 64])
    ksrow = ctx.sbuf("a0_ksrow", [1, 64])
    qaug = [ctx.sbuf("a0_qa", [66, T], BF16), ctx.sbuf("a0_qb", [66, T], BF16)]
    kaug = [ctx.sbuf("a0_ka", [66, T], BF16), ctx.sbuf("a0_kb", [66, T], BF16)]
    Vp = ctx.sbuf("a0_vp", [128, 32 * 130], BF16)
    OTp = ctx.sbuf("a0_otp", [128, T], BF16)
    q2sb = ctx.sbuf("a0_q2sb", [128, CHUNK], BF16)
    k2sb = ctx.sbuf("a0_k2sb", [128, CHUNK], BF16)
    qsq = ctx.sbuf("a0_qsq", [128, CHUNK], BF16)
    ksq = ctx.sbuf("a0_ksq", [128, CHUNK], BF16)
    kwsq = ctx.sbuf("a0_kwsq", [128, CHUNK], BF16)
    rowf = ctx.sbuf("a0_rowf", [1, CHUNK])
    rowqAb = ctx.sbuf("a0_rqab", [1, CHUNK], BF16)
    rowqBb = ctx.sbuf("a0_rqbb", [1, CHUNK], BF16)
    rowkAb = ctx.sbuf("a0_rkab", [1, CHUNK], BF16)
    rowkBb = ctx.sbuf("a0_rkbb", [1, CHUNK], BF16)
    rowkAf = ctx.sbuf("a0_rkaf", [1, CHUNK])
    rowkBf = ctx.sbuf("a0_rkbf", [1, CHUNK])
    r2s = ctx.sbuf("a0_r2s", [1, CHUNK])
    knrow = ctx.sbuf("a0_knrow", [1, CHUNK])
    cmax = ctx.sbuf("a0_cmax", [1, 1])
    kmax = [ctx.sbuf("a0_kmA", [1, 1]), ctx.sbuf("a0_kmB", [1, 1])]
    kml = ctx.sbuf("a0_kml", [1, 1])
    kmh = ctx.sbuf("a0_kmh", [1, 1])
    mrowfull = ctx.sbuf("a0_mrow", [1, T], BF16)
    nkb = ctx.sbuf("a0_nkb", [4, 64])
    nsq4 = ctx.sbuf("a0_nsq4", [4, 64])
    nn4 = ctx.sbuf("a0_nn4", [4, 1])
    nrs4 = ctx.sbuf("a0_nrs4", [4, 1])
    nks2 = ctx.sbuf("a0_nks2", [4, 64])
    nsq2 = ctx.sbuf("a0_nsq2", [64, 4], BF16)
    knullaug = [ctx.sbuf("a0_kna", [66, 2], BF16), ctx.sbuf("a0_knb", [66, 2], BF16)]
    vnullaug = [ctx.sbuf("a0_vna", [2, 65], BF16), ctx.sbuf("a0_vnb", [2, 65], BF16)]
    PT = ctx.sbuf("a0_pt", [128, 2 * 512], BF16)
    Osb = ctx.sbuf("a0_osb", [64, 512], BF16)
    denf = ctx.sbuf("a0_denf", [1, 512])
    denrb = ctx.sbuf("a0_denrb", [1, 512], BF16)
    woutres = ctx.sbuf("a0_wout", [128, 8 * 1024], BF16)
    othb = ctx.ln_bufs()["xbf"]

    s_r0 = ctx.sem("a0_r0")
    s_tp0 = ctx.sem("a0_tp0")
    s_te0 = ctx.sem("a0_te0")
    s_wl = ctx.sem("a0_wl")
    s_wsc = ctx.sem("a0_wsc")
    s_lr = [ctx.sem("a0_lr0"), ctx.sem("a0_lr1")]
    s_ape = ctx.sem("a0_ape")
    s_aac = ctx.sem("a0_aac")
    s_adv = ctx.sem("a0_adv")
    s_nl = ctx.sem("a0_nl")
    s_mr = ctx.sem("a0_mr")
    s_fpe = ctx.sem("a0_fpe")
    s_fex = ctx.sem("a0_fex")
    s_fac = ctx.sem("a0_fac")
    s_fdv = ctx.sem("a0_fdv")
    s_ots = ctx.sem("a0_ots")

    with nc.Block() as b:
        @b.gpsimd
        def _(g):
            s_r0.inc(g.dma_start(grow, ctx.dram["L0_attn_gamma"].rearrange("(o n) -> o n", o=1)), 16)
            s_r0.inc(g.dma_start(qsrow[:], ctx.dram["L0_attn_q_scale"].rearrange("(o n) -> o n", o=1)), 16)
            s_r0.inc(g.dma_start(ksrow[:], ctx.dram["L0_attn_k_scale"].rearrange("(o n) -> o n", o=1)), 16)
            s_r0.inc(g.dma_start(woutres[:].rearrange("p (c w) -> p c w", c=8),
                                 ctx.dram["L0_attn_w_out"].rearrange("(c p) w -> p c w", p=128)), 16)

        @b.tensor
        def _(t):
            t.wait_ge(ctx.s_const.h, ctx.s_const.n)
            t.wait_ge(s_r0.h, 64)
            for c in range(8):
                if c >= 1:
                    t.wait_ge(s_te0.h, c)
                s_tp0.inc(t.transpose(ctx.bank(5, 128, 1), grow[0:1, c * 128:(c + 1) * 128],
                                      identf[0:1, 0:1]))
            t.wait_ge(s_te0.h, 8)
            s_tp0.inc(t.transpose(ctx.bank(5, 64, 1), qsrow[:], identf[0:1, 0:1]))
            t.wait_ge(s_te0.h, 9)
            s_tp0.inc(t.transpose(ctx.bank(5, 64, 1), ksrow[:], identf[0:1, 0:1]))

        @b.scalar
        def _(a):
            for c in range(8):
                a.wait_ge(s_tp0.h, c + 1)
                s_te0.inc(a.copy(gcol[:, c:c + 1], ctx.bank(5, 128, 1)))
            a.wait_ge(s_tp0.h, 9)
            a.activation(qs8col[0:64, 0:1], ctx.bank(5, 64, 1), AF.Copy, scale=8.0)
            s_te0.inc(a.copy(qs8col[64:128, 0:1], qs8col[0:64, 0:1]))
            a.wait_ge(s_tp0.h, 10)
            a.copy(kscol[0:64, 0:1], ctx.bank(5, 64, 1))
            s_te0.inc(a.copy(kscol[64:128, 0:1], kscol[0:64, 0:1]))

        @b.vector
        def _(v):
            vpv = Vp[:].rearrange("p (t w) -> p t w", w=130)
            v.memset(vpv[:, :, 64:65], 1.0)
            v.memset(vpv[:, :, 129:130], 1.0)
            v.wait_ge(s_te0.h, 10)
            v.tensor_mul(iq2col[:], qs8col[:], qs8col[:])
            v.tensor_mul(ik2col[:], kscol[:], kscol[:]).then_inc(s_adv.h, 1)
            s_adv.n += 1
            v.wait_ge(s_adv.h, s_adv.n)
            v.reciprocal(iq2col[:], iq2col[:])
            v.reciprocal(ik2col[:], ik2col[:]).then_inc(s_adv.h, 1)
            s_adv.n += 1

    t1_end = {}
    v0_end = {}
    t2_end = {}
    t3_end = {}
    a1_end = {}
    a2_end = {}
    v1_end = {}

    for p in range(8):
        hA = 2 * p

        with nc.Block() as b:
            @b.sync
            def _(q, p=p):
                wq = ctx.dram["L0_attn_w_q"].rearrange("(c p) w -> p c w", p=128)
                wkv = ctx.dram["L0_attn_w_kv"].rearrange("(c p) w -> p c w", p=128)
                s_wl.inc(q.dma_start(wstg[:, 0:1024].rearrange("p (c w) -> p c w", c=8),
                                     wq[:, :, p * 128:(p + 1) * 128]), 16)
                s_wl.inc(q.dma_start(wstg[:, 1024:2048].rearrange("p (c w) -> p c w", c=8),
                                     wkv[:, :, p * 128:(p + 1) * 128]), 16)
                s_wl.inc(q.dma_start(wstg[:, 2048:3072].rearrange("p (c w) -> p c w", c=8),
                                     wkv[:, :, 1024 + p * 128:1024 + (p + 1) * 128]), 16)
                s_wl.inc(q.dma_start(kaug[0][64:66, :], ctx.dram["c_kaug2"]), 16)
                s_wl.inc(q.dma_start(kaug[1][64:66, :], ctx.dram["c_kaug2"]), 16)

            @b.vector
            def _(v, p=p, hA=hA):
                s_wl.wait(v)
                v.wait_ge(s_te0.h, 8)
                for c in range(8):
                    v.tensor_scalar(wpb[:, c * 384:c * 384 + 128],
                                    wstg[:, c * 128:(c + 1) * 128],
                                    gcol[:, c:c + 1], None, ALU.mult)
                    v.tensor_scalar(wpb[:, c * 384 + 128:c * 384 + 256],
                                    wstg[:, 1024 + c * 128:1024 + (c + 1) * 128],
                                    gcol[:, c:c + 1], None, ALU.mult)
                    v.tensor_scalar(wpb[:, c * 384 + 256:c * 384 + 384],
                                    wstg[:, 2048 + c * 128:2048 + (c + 1) * 128],
                                    gcol[:, c:c + 1], None, ALU.mult)
                v.memset(kmax[0][:], 0.0)
                v.memset(kmax[1][:], 0.0)
                v.memset(qaug[0][64:65, :], float(SL[hA]))
                s_wsc.inc(v.memset(qaug[1][64:65, :], float(SL[hA + 1])))

        with nc.Block() as b:
            for k in range(NCK):
                @b.sync
                def _(q, p=p, k=k):
                    q.wait_ge(s_lnst[k % 2].h, (k // 2 + 1) * 16)
                    if k >= 2:
                        q.wait_ge(s_ape.h, t1_end[(p, k - 2)])
                    q.dma_start(lnr[:].rearrange("p (c s f) -> p c s f", c=NCH, s=2)[:, :, k % 2, :],
                                lnh[:, :, k * CHUNK:(k + 1) * CHUNK]).then_inc(s_lr[k % 2].h, 16)
                    s_lr[k % 2].n += 16

                @b.tensor
                def _(t, p=p, k=k):
                    s_lr[k % 2].wait(t)
                    s_wsc.wait(t)
                    if k >= 1:
                        t.wait_ge(s_aac.h, a1_end[(p, k - 1)])
                    for c in range(8):
                        t.matmul(ctx.bank(4), wpb[:, c * 384:c * 384 + 128],
                                 lnr[:, (c * 2 + k % 2) * CHUNK:(c * 2 + k % 2 + 1) * CHUNK],
                                 start=(c == 0), stop=(c == 7))
                    for c in range(8):
                        t.matmul(ctx.bank(5), wpb[:, c * 384 + 128:c * 384 + 256],
                                 lnr[:, (c * 2 + k % 2) * CHUNK:(c * 2 + k % 2 + 1) * CHUNK],
                                 start=(c == 0), stop=(c == 7))
                    ins = None
                    for j4 in range(4):
                        for c in range(8):
                            ins = t.matmul(
                                ctx.bank(2)[:, j4 * 128:(j4 + 1) * 128],
                                lnr[:, (c * 2 + k % 2) * CHUNK + j4 * 128:
                                     (c * 2 + k % 2) * CHUNK + (j4 + 1) * 128],
                                wpb[:, c * 384 + 256:c * 384 + 384],
                                start=(c == 0), stop=(c == 7))
                    s_ape.inc(ins)
                    t1_end[(p, k)] = s_ape.n

                @b.scalar
                def _(a, p=p, k=k):
                    a.wait_ge(s_ape.h, t1_end[(p, k)])
                    if k >= 1:
                        a.wait_ge(s_adv.h, v1_end[(p, k - 1)])
                        a.wait_ge(s_ape.h, t2_end[(p, k - 1)])
                    a.activation(q2sb[:], ctx.bank(4), AF.Copy, scale=qs8col[:, 0:1])
                    a.activation(k2sb[:], ctx.bank(5), AF.Copy, scale=kscol[:, 0:1])
                    ins = None
                    for j4 in range(4):
                        jt = k * 4 + j4
                        a.copy(Vp[:, jt * 130 + 0:jt * 130 + 64],
                               ctx.bank(2)[:, j4 * 128:j4 * 128 + 64])
                        ins = a.copy(Vp[:, jt * 130 + 65:jt * 130 + 129],
                                     ctx.bank(2)[:, j4 * 128 + 64:j4 * 128 + 128])
                    s_aac.inc(ins)
                    a1_end[(p, k)] = s_aac.n

                @b.vector
                def _(v, p=p, k=k):
                    v.wait_ge(s_aac.h, a1_end[(p, k)])
                    v.tensor_mul(qsq[:], q2sb[:], q2sb[:])
                    v.tensor_mul(kwsq[:], k2sb[:], k2sb[:]).then_inc(s_adv.h, 1)
                    s_adv.n += 1
                    v.wait_ge(s_adv.h, s_adv.n)
                    v.tensor_scalar(qsq[:], qsq[:], iq2col[:, 0:1], None, ALU.mult)
                    s_adv.inc(v.tensor_scalar(ksq[:], kwsq[:], ik2col[:, 0:1], None, ALU.mult))
                    v0_end[(p, k)] = s_adv.n

                @b.tensor
                def _(t, p=p, k=k):
                    t.wait_ge(s_aac.h, a1_end[(p, k)])
                    t.wait_ge(s_adv.h, v0_end[(p, k)])
                    if k >= 1:
                        t.wait_ge(s_adv.h, v1_end[(p, k - 1)])
                    t.matmul(ctx.bank(6, 64, 512), blk64[:], qsq[:], start=True, stop=True)
                    t.matmul(ctx.bank(7, 64, 512), blk64[:], ksq[:], start=True, stop=True)
                    s_ape.inc(t.matmul(ctx.bank(3, 64, 512), blk64[:], kwsq[:], start=True, stop=True))
                    t2_end[(p, k)] = s_ape.n

                @b.scalar
                def _(a, p=p, k=k):
                    a.wait_ge(s_ape.h, t2_end[(p, k)])
                    ins = None
                    for src, dstb, dstf in ((ctx.bank(6, 64, 512)[0:1, :], rowqAb, None),
                                            (ctx.bank(6, 64, 512)[32:33, :], rowqBb, None),
                                            (ctx.bank(7, 64, 512)[0:1, :], rowkAb, rowkAf),
                                            (ctx.bank(7, 64, 512)[32:33, :], rowkBb, rowkBf)):
                        fdst = dstf if dstf is not None else rowf
                        a.activation(fdst[:], src, AF.Ln)
                        a.activation(fdst[:], fdst[:], AF.Exp, scale=-0.5)
                        ins = a.copy(dstb[:], fdst[:])
                    s_aac.inc(ins)
                    a2_end[(p, k)] = s_aac.n

                @b.tensor
                def _(t, p=p, k=k):
                    t.wait_ge(s_aac.h, a2_end[(p, k)])
                    t.matmul(ctx.bank(6, 64, 512), onesrow[0:1, 0:64], rowqAb[:], start=True, stop=True)
                    t.matmul(ctx.bank(0, 64, 512), onesrow[0:1, 0:64], rowqBb[:], start=True, stop=True)
                    t.matmul(ctx.bank(7, 64, 512), onesrow[0:1, 0:64], rowkAb[:], start=True, stop=True)
                    s_ape.inc(t.matmul(ctx.bank(1, 64, 512), onesrow[0:1, 0:64], rowkBb[:],
                                       start=True, stop=True))
                    t3_end[(p, k)] = s_ape.n

                @b.vector
                def _(v, p=p, k=k):
                    v.wait_ge(s_ape.h, t3_end[(p, k)])
                    sl_ = slice(k * CHUNK, (k + 1) * CHUNK)
                    v.tensor_mul(qaug[0][0:64, sl_], q2sb[0:64, :], ctx.bank(6, 64, 512))
                    v.tensor_mul(qaug[1][0:64, sl_], q2sb[64:128, :], ctx.bank(0, 64, 512))
                    v.tensor_mul(kaug[0][0:64, sl_], k2sb[0:64, :], ctx.bank(7, 64, 512))
                    v.tensor_mul(kaug[1][0:64, sl_], k2sb[64:128, :], ctx.bank(1, 64, 512))
                    for hh, rkf, b3r in ((0, rowkAf, ctx.bank(3, 64, 512)[0:1, :]),
                                         (1, rowkBf, ctx.bank(3, 64, 512)[32:33, :])):
                        v.tensor_mul(r2s[:], rkf[:], rkf[:])
                        v.tensor_mul(knrow[:], b3r, r2s[:]).then_inc(s_adv.h, 1)
                        s_adv.n += 1
                        v.wait_ge(s_adv.h, s_adv.n)
                        v.tensor_reduce(cmax[:], knrow[:], AX.X, ALU.max).then_inc(s_adv.h, 1)
                        s_adv.n += 1
                        v.wait_ge(s_adv.h, s_adv.n)
                        v.tensor_tensor(kmax[hh][:], kmax[hh][:], cmax[:], ALU.max).then_inc(s_adv.h, 1)
                        s_adv.n += 1
                        v.wait_ge(s_adv.h, s_adv.n)
                    s_adv.inc(v.engine_nop())
                    v1_end[(p, k)] = s_adv.n

        cnt = {}
        with nc.Block() as b:
            @b.gpsimd
            def _(g, p=p, hA=hA):
                s_nl.inc(g.dma_start(nkb[:],
                                     nkv4[hA:hA + 2, :, 0:1, :].rearrange("h j o d -> (h j o) d")), 16)
                s_nl.inc(g.dma_start(vnullaug[0][0:2, 0:64],
                                     nkv4[hA:hA + 1, :, 1:2, :].rearrange("h j o d -> (h j o) d")), 16)
                s_nl.inc(g.dma_start(vnullaug[1][0:2, 0:64],
                                     nkv4[hA + 1:hA + 2, :, 1:2, :].rearrange("h j o d -> (h j o) d")), 16)
                s_nl.inc(g.dma_start(knullaug[0][64:66, :], ctx.dram["c_nullrow"]), 16)
                s_nl.inc(g.dma_start(knullaug[1][64:66, :], ctx.dram["c_nullrow"]), 16)

            @b.vector
            def _(v, p=p):
                v.wait_ge(s_nl.h, s_nl.n - 64)
                v.tensor_mul(nsq4[:], nkb[:], nkb[:]).then_inc(s_adv.h, 1)
                s_adv.n += 1
                v.wait_ge(s_adv.h, s_adv.n)
                v.tensor_reduce(nn4[:], nsq4[:], AX.X, ALU.add).then_inc(s_adv.h, 1)
                s_adv.n += 1
                cnt["nn4"] = s_adv.n

            @b.scalar
            def _(a, p=p):
                a.wait_ge(s_adv.h, cnt["nn4"])
                a.activation(nrs4[:], nn4[:], AF.Ln)
                s_aac.inc(a.activation(nrs4[:], nrs4[:], AF.Exp, scale=-0.5))
                cnt["nrs4"] = s_aac.n

            @b.vector
            def _(v, p=p):
                v.wait_ge(s_aac.h, cnt["nrs4"])
                s_adv.inc(v.tensor_scalar(nks2[:], nkb[:], nrs4[0:4, 0:1], None, ALU.mult))
                cnt["nks2"] = s_adv.n

            @b.tensor
            def _(t, p=p):
                t.wait_ge(s_adv.h, cnt["nks2"])
                s_fpe.inc(t.transpose(ctx.bank(6, 64, 4), nks2[:], identf[0:4, 0:4]))
                cnt["ntp"] = s_fpe.n

            @b.vector
            def _(v, p=p):
                v.wait_ge(s_fpe.h, cnt["ntp"])
                v.tensor_scalar(knullaug[0][0:64, :], ctx.bank(6, 64, 4)[:, 0:2],
                                kscol[0:64, 0:1], None, ALU.mult)
                v.tensor_scalar(knullaug[1][0:64, :], ctx.bank(6, 64, 4)[:, 2:4],
                                kscol[0:64, 0:1], None, ALU.mult)
                v.memset(vnullaug[0][0:2, 64:65], 1.0)
                s_adv.inc(v.memset(vnullaug[1][0:2, 64:65], 1.0))
                cnt["knts"] = s_adv.n

            @b.scalar
            def _(a, p=p):
                a.wait_ge(s_adv.h, cnt["knts"])
                a.activation(nsq2[:, 0:2], knullaug[0][0:64, :], AF.Square)
                s_aac.inc(a.activation(nsq2[:, 2:4], knullaug[1][0:64, :], AF.Square))
                cnt["nsq2"] = s_aac.n

            @b.tensor
            def _(t, p=p):
                t.wait_ge(s_aac.h, cnt["nsq2"])
                s_fpe.inc(t.matmul(ctx.bank(7, 1, 4), ones128[0:64, 0:1], nsq2[:],
                                   start=True, stop=True))
                cnt["nmm"] = s_fpe.n

            @b.vector
            def _(v, p=p):
                v.wait_ge(s_fpe.h, cnt["nmm"])
                for hh in range(2):
                    v.tensor_reduce(cmax[:], ctx.bank(7, 1, 4)[:, 2 * hh:2 * hh + 2],
                                    AX.X, ALU.max).then_inc(s_adv.h, 1)
                    s_adv.n += 1
                    v.wait_ge(s_adv.h, s_adv.n)
                    v.tensor_tensor(kmax[hh][:], kmax[hh][:], cmax[:], ALU.max).then_inc(s_adv.h, 1)
                    s_adv.n += 1
                    v.wait_ge(s_adv.h, s_adv.n)
                cnt["kmax"] = s_adv.n

            for hh in range(2):
                @b.scalar
                def _(a, p=p, hh=hh):
                    a.wait_ge(s_adv.h, cnt["kmax"])
                    s_aac.inc(a.activation(kml[:], kmax[hh][:], AF.Ln))
                    cnt["kml"] = s_aac.n

                @b.vector
                def _(v, p=p, hh=hh):
                    v.wait_ge(s_aac.h, cnt["kml"])
                    if hh == 1:
                        v.wait_ge(s_mr.h, cnt["mr0"])
                    s_adv.inc(v.tensor_scalar(kmh[:], kml[:], 0.5, None, ALU.mult))
                    cnt["kmh"] = s_adv.n

                for kc in range(NCK):
                    @b.vector
                    def _(v, p=p, hh=hh, kc=kc):
                        v.wait_ge(s_adv.h, cnt["kmh"])
                        if "mln" in cnt:
                            v.wait_ge(s_aac.h, cnt["mln"])
                        sl_ = slice(kc * CHUNK, (kc + 1) * CHUNK)
                        s_adv.inc(v.tensor_mul(qsq[0:64, :], qaug[hh][0:64, sl_],
                                               qaug[hh][0:64, sl_]))
                        cnt["msq"] = s_adv.n

                    @b.tensor
                    def _(t, p=p, hh=hh, kc=kc):
                        t.wait_ge(s_adv.h, cnt["msq"])
                        if "mln" in cnt:
                            t.wait_ge(s_aac.h, cnt["mln"])
                        s_fpe.inc(t.matmul(ctx.bank(6, 1, 512), ones128[0:64, 0:1],
                                           qsq[0:64, :], start=True, stop=True))
                        cnt["mmm"] = s_fpe.n

                    @b.scalar
                    def _(a, p=p, hh=hh, kc=kc):
                        a.wait_ge(s_fpe.h, cnt["mmm"])
                        a.activation(rowf[:], ctx.bank(6, 1, 512), AF.Ln)
                        s_aac.inc(a.activation(rowqAb[:], rowf[:], AF.Exp, scale=0.5,
                                               bias=kmh[:, 0:1]))
                        cnt["mln"] = s_aac.n

                    @b.vector
                    def _(v, p=p, hh=hh, kc=kc, hA=hA):
                        v.wait_ge(s_aac.h, cnt["mln"])
                        sl_ = slice(kc * CHUNK, (kc + 1) * CHUNK)
                        s_adv.inc(v.scalar_tensor_tensor(
                            mrowfull[0:1, sl_], q512[0:1, sl_], -float(SL[hA + hh]),
                            rowqAb[:], ALU.mult, ALU.subtract))
                        cnt["mstt"] = s_adv.n

                @b.gpsimd
                def _(g, p=p, hh=hh):
                    g.wait_ge(s_adv.h, cnt["mstt"])
                    s_mr.inc(g.dma_start(qaug[hh][65:66, :], mrowfull[:]), 16)
                    cnt["mr%d" % hh] = s_mr.n

        for hh in range(2):
            emit_l0attn_flash(ctx, hh, hA + hh, qaug[hh], kaug[hh], knullaug[hh],
                              vnullaug[hh], Vp, OTp, PT, Osb, denf, denrb,
                              ktmin, s_nl, s_mr, s_fpe, s_fex, s_fac, s_fdv)

        with nc.Block() as b:
            @b.gpsimd
            def _(g, p=p):
                g.wait_ge(s_fdv.h, s_fdv.n)
                g.dma_start(oth[:, p, :], OTp[:]).then_inc(s_ots.h, 16)
                s_ots.n += 16
                g.wait_ge(s_ots.h, s_ots.n)

    s_oin = ctx.sem("a0_oin")
    s_res_pe = ctx.sem("a0_respe")

    def sync_extra(q, k):
        if k >= 1:
            q.wait_ge(s_res_pe.h, k * 8)
        q.dma_start(othb[:].rearrange("p (c f) -> p c f", c=NCH),
                    oth[:, :, k * CHUNK:(k + 1) * CHUNK]).then_inc(s_oin.h, 16)
        s_oin.n += 16

    def mm_fn(t, k, o, bank_ap):
        t.wait_ge(s_oin.h, (k + 1) * 16)
        pairs = [(woutres[:, c * 1024 + o * 128:c * 1024 + (o + 1) * 128],
                  othb[:, c * CHUNK:(c + 1) * CHUNK]) for c in range(8)]
        return mm_chain(t, bank_ap, pairs)

    return emit_residual_out(ctx, "a0m", src_name, out_name, s_ots, None, mm_fn,
                             s_pe=s_res_pe, sync_extra=sync_extra,
                             bufs={"xb": ctx.ln_bufs()["xb"], "yb": pool})


def emit_l0attn_flash(ctx, hh, h, qa, ka, knull, vnull, Vp, OTp, PT, Osb,
                      denf, denrb, ktmin, s_nl, s_mr, s_fpe, s_fex, s_fac, s_fdv):
    nc = ctx.nc
    ident = ctx.sb["ident"]
    onesrow = ctx.sb["onesrow"]
    alibit = ctx.sb["alibit"]
    maskq = ctx.sb["maskq"]
    NQC = T // 512
    prev = {}

    with nc.Block() as b:
        for qc in range(NQC):
            tiles = list(range(ktmin(h, qc), 4 * qc + 4)) + ["null"]
            nt = len(tiles)
            peb = s_fpe.n
            exb = s_fex.n

            @b.tensor
            def _(t, qc=qc, tiles=tiles, nt=nt, peb=peb, exb=exb):
                if qc == 0:
                    t.wait_ge(s_nl.h, s_nl.n)
                    t.wait_ge(s_mr.h, s_mr.n)

                def mm1(i, kt):
                    if i >= 2:
                        t.wait_ge(s_fex.h, exb + i - 1)
                    if kt == "null":
                        ins = t.matmul(ctx.bank(i % 2, 2, 512), knull[:],
                                       qa[:, qc * 512:(qc + 1) * 512], start=True, stop=True)
                    else:
                        dlt = 4 * qc - kt
                        if dlt in (0, -1, -2, -3):
                            t.matmul(ctx.bank(i % 2, 128, 512),
                                     ka[:, kt * 128:(kt + 1) * 128],
                                     qa[:, qc * 512:(qc + 1) * 512], start=True, stop=False)
                            ins = t.matmul(ctx.bank(i % 2, 128, 512), ident[:],
                                           maskq[:, -dlt * 512:(-dlt + 1) * 512],
                                           start=False, stop=True)
                        else:
                            ins = t.matmul(ctx.bank(i % 2, 128, 512),
                                           ka[:, kt * 128:(kt + 1) * 128],
                                           qa[:, qc * 512:(qc + 1) * 512], start=True, stop=True)
                    s_fpe.inc(ins)

                def mm2(i, kt):
                    t.wait_ge(s_fex.h, exb + i + 1)
                    if i == 0 and "recip" in prev:
                        t.wait_ge(s_fdv.h, prev["recip"])
                    if kt == "null":
                        ins2 = t.matmul(ctx.bank(2, 65, 512), vnull[:],
                                        PT[0:2, (i % 2) * 512:(i % 2) * 512 + 512],
                                        start=(i == 0), stop=True)
                    else:
                        ins2 = t.matmul(ctx.bank(2, 65, 512),
                                        Vp[:, kt * 130 + hh * 65:kt * 130 + hh * 65 + 65],
                                        PT[:, (i % 2) * 512:(i % 2) * 512 + 512],
                                        start=(i == 0), stop=(i == nt - 1))
                    s_fpe.inc(ins2)

                for i, kt in enumerate(tiles):
                    mm1(i, kt)
                    if i >= 1:
                        mm2(i - 1, tiles[i - 1])
                mm2(nt - 1, tiles[nt - 1])

            @b.scalar
            def _(a, qc=qc, tiles=tiles, nt=nt, peb=peb):
                for i, kt in enumerate(tiles):
                    a.wait_ge(s_fpe.h, peb + (1 if i == 0 else 2 * i))
                    dlt = 4 * qc if kt == "null" else 4 * qc - kt
                    col = h * 33 + dlt + 3
                    if kt == "null":
                        ins = a.activation(PT[0:2, (i % 2) * 512:(i % 2) * 512 + 512],
                                           ctx.bank(i % 2, 2, 512), AF.Exp,
                                           bias=alibit[0:2, col:col + 1])
                    else:
                        ins = a.activation(PT[:, (i % 2) * 512:(i % 2) * 512 + 512],
                                           ctx.bank(i % 2, 128, 512), AF.Exp,
                                           bias=alibit[:, col:col + 1])
                    s_fex.inc(ins)
                a.wait_ge(s_fpe.h, peb + 2 * nt)
                if "tt" in prev:
                    a.wait_ge(s_fdv.h, prev["tt"])
                s_fac.inc(a.copy(Osb[:], ctx.bank(2, 65, 512)[0:64, :]))

            @b.vector
            def _(v, qc=qc, nt=nt, peb=peb):
                v.wait_ge(s_fpe.h, peb + 2 * nt)
                v.tensor_copy(denf[:], ctx.bank(2, 65, 512)[64:65, :])
                v.reciprocal(denf[:], denf[:]).then_inc(s_fdv.h, 1)
                s_fdv.n += 1
                prev["recip"] = s_fdv.n
                v.wait_ge(s_fdv.h, s_fdv.n)
                if "rep" in prev:
                    v.wait_ge(s_fpe.h, prev["rep"])
                s_fdv.inc(v.tensor_copy(denrb[:], denf[:]))
                prev["denrb"] = s_fdv.n

            @b.tensor
            def _(t, qc=qc):
                t.wait_ge(s_fdv.h, prev["denrb"])
                s_fpe.inc(t.matmul(ctx.bank(3, 64, 512), onesrow[0:1, 0:64], denrb[:],
                                   start=True, stop=True))
                prev["rep"] = s_fpe.n

            @b.vector
            def _(v, qc=qc):
                v.wait_ge(s_fpe.h, prev["rep"])
                v.wait_ge(s_fac.h, s_fac.n)
                s_fdv.inc(v.tensor_mul(OTp[hh * 64:(hh + 1) * 64, qc * 512:(qc + 1) * 512],
                                       Osb[:], ctx.bank(3, 64, 512)))
                prev["tt"] = s_fdv.n


def build_launch_l0attn():
    nc = bass.Bass("TRN2", target_bir_lowering=False, detect_race_conditions=False)
    ctx = Ctx(nc)
    ctx.din("xT", [D, T], F32)
    ctx.din("L0_attn_gamma", [D])
    ctx.din("L0_attn_w_q", [1024, 1024])
    ctx.din("L0_attn_w_kv", [1024, 2048])
    ctx.din("L0_attn_null_kv", [16, 4, 64])
    ctx.din("L0_attn_q_scale", [64])
    ctx.din("L0_attn_k_scale", [64])
    ctx.din("L0_attn_w_out", [1024, 1024])
    ctx.dout("yT", [D, T], F32)
    ctx.dint("lnh", [D, T], BF16)
    ctx.load_consts(extra=("alibit", "maskq", "q512", "blk64"))
    RING = 2
    lnx = ctx.sbuf("lnring", [128, NCH * RING * CHUNK], BF16)
    lnx_at = lambda c, k: lnx[:, (c * RING + k % RING) * CHUNK:
                              (c * RING + k % RING + 1) * CHUNK]
    lnx_view = lambda k: lnx[:].rearrange("p (c s f) -> p c s f", c=NCH, s=RING)[:, :, k % RING, :]
    s_ln, s_lnst = emit_ln(ctx, "xT", lnx_at, "ln0a", double_ln=True, bank0=0,
                           ring=RING, to_hbm="lnh", lnx_view=lnx_view)
    emit_l0attn(ctx, s_lnst, "lnh", "xT", "yT", lnx)
    return ctx



def emit_l1attn(ctx, s_lnst, lnh_name, src_name, out_name, lnr):
    nc = ctx.nc
    ident = ctx.sb["ident"]
    onesrow = ctx.sb["onesrow"]
    CB = 64
    NCB = 1024 // CB
    NT2 = CB // 2
    GW = CB * 64

    ctx.dint("xph", [4096, T], BF16)
    ctx.dint("fh", [3072, T], BF16)
    ctx.dint("Ffile", [3, 128, 512 * 130])
    ctx.dint("yh0", [64, 1024 * 64], BF16)
    ctx.dint("yh1", [64, 1024 * 64], BF16)
    ctx.dint("y3h", [64, 1024 * 64])
    xph_ab = ctx.dram["xph"].rearrange("e (b a) -> b e a", a=64)
    fh_ab = ctx.dram["fh"].rearrange("e (b a) -> b e a", a=64)
    yh = [ctx.dram["yh0"], ctx.dram["yh1"]]

    Usb = ctx.sbuf("l1_usb", [128, 4160])
    Fsb = ctx.sbuf("l1_fsb", [128, 4160])
    Vb = ctx.sbuf("l1_v", [128, 4160], BF16)
    V2b = ctx.sbuf("l1_v2", [128, 4160], BF16)
    pool2 = ctx.sbuf("l1_pool2", [128, 4160])
    tmp1 = pool2[:, 0:2080]
    tmp2 = pool2[:, 2080:4160]
    WtTr = ctx.sbuf("l1_wttr", [65, 4096], BF16)
    WtTi = ctx.sbuf("l1_wtti", [65, 4096], BF16)
    ycur = ctx.sbuf("l1_ycur", [64, 4096], BF16)
    gate = ctx.sbuf("l1_gate", [64, 4096], BF16)
    ynext = ctx.sbuf("l1_yn", [64, 4096], BF16)
    ynf32 = Usb[0:64, 0:4096]
    xpb = ctx.sbuf("l1_xpb", [128, 4 * 512], BF16)
    bpr = ctx.sbuf("l1_bpr", [1, 4096], BF16)
    bfr = ctx.sbuf("l1_bfr", [1, 3072], BF16)
    fC1b = ctx.sbuf("l1_c1", [64, 130], BF16)
    fBDb = ctx.sbuf("l1_bd", [128, 384], BF16)
    fICbb = ctx.sbuf("l1_icb", [65, 128], BF16)
    fTW = ctx.sbuf("l1_tw", [128, 130])
    wslots = [Usb[:].bitcast(BF16)[:, 0:4096], Fsb[:].bitcast(BF16)[:, 0:4096]]
    ptslots = [pool2[:].bitcast(BF16)[:, 0:4096], Vb[:, 0:4096]]

    s_c = ctx.sem("l1_c")
    s_cb = ctx.sem("l1_cb")
    s_in = ctx.sem("l1_in")
    s_ws = ctx.sem("l1_ws")
    s_pe = ctx.sem("l1_pe")
    s_ac = ctx.sem("l1_ac")
    s_dv = ctx.sem("l1_dv")
    s_st = ctx.sem("l1_st")
    s_stg = ctx.sem("l1_stg")

    with nc.Block() as b:
        @b.sync
        def _(q):
            s_c.inc(q.dma_start(pool2[0:64, 0:130], ctx.dram["c_fC1"]), 16)
            s_c.inc(q.dma_start(pool2[:, 200:584], ctx.dram["c_fBD"]), 16)
            s_c.inc(q.dma_start(pool2[0:65, 600:728], ctx.dram["c_fICb"]), 16)
            s_c.inc(q.dma_start(fTW[:], ctx.dram["c_fTW"]), 16)

        @b.gpsimd
        def _(g):
            s_cb.inc(g.dma_start(bpr[:], ctx.dram["L1_attn_bp"].rearrange("(o n) -> o n", o=1)), 16)
            s_cb.inc(g.dma_start(bfr[:], ctx.dram["L1_attn_bf"].rearrange("(o n) -> o n", o=1)), 16)

        @b.scalar
        def _(a):
            s_c.wait(a)
            s_cb.wait(a)
            a.wait_ge(ctx.s_const.h, ctx.s_const.n)
            a.copy(fC1b[:], pool2[0:64, 0:130])
            a.copy(fBDb[:], pool2[:, 200:584])
            s_ac.inc(a.copy(fICbb[:], pool2[0:65, 600:728]))

    def emit_linear(wname, bias_sb, ne, src_kind, dst_name):
        w_d = ctx.dram[wname].rearrange("(db p) e -> p db e", p=128)
        dst = ctx.dram[dst_name].rearrange("(et p) t -> p et t", p=128)
        negr = ne // 4
        pe0, ac0 = s_pe.n, s_ac.n
        ws_after = {}
        st_after = {}

        with nc.Block() as b:
            @b.gpsimd
            def _(g):
                for k in range(NCK):
                    if src_kind == "pos":
                        posd = ctx.dram["c_posT"].rearrange("(db p) t -> p db t", p=128)
                        if k >= 2:
                            g.wait_ge(s_pe.h, pe0 + (k - 1) * ne)
                        g.dma_start(ptslots[k % 2].rearrange("p (db f) -> p db f", db=8),
                                    posd[:, :, k * 512:(k + 1) * 512]).then_inc(s_ws.h, 16)
                        s_ws.n += 16
                        ws_after[(k, "pos")] = s_ws.n
                    for eg in range(negr):
                        ng = k * negr + eg
                        if ng >= 2:
                            kk, gg = divmod(ng - 2, negr)
                            g.wait_ge(s_pe.h, pe0 + kk * ne + (gg + 1) * 4)
                        g.dma_start(wslots[ng % 2].rearrange("p (db e) -> p db e", db=8),
                                    w_d[:, :, eg * 512:(eg + 1) * 512]).then_inc(s_ws.h, 16)
                        s_ws.n += 16
                        ws_after[(k, eg)] = s_ws.n

            @b.sync
            def _(q):
                for k in range(NCK):
                    if src_kind == "lnh":
                        q.wait_ge(s_lnst[k % 2].h, (k // 2 + 1) * 16)
                        if k >= 2:
                            q.wait_ge(s_pe.h, pe0 + (k - 1) * ne)
                        q.dma_start(
                            lnr[:].rearrange("p (c s f) -> p c s f", c=NCH, s=2)[:, :, k % 2, :],
                            ctx.dram[lnh_name].rearrange("(c p) t -> p c t", p=128)[:, :, k * 512:(k + 1) * 512]
                        ).then_inc(s_in.h, 16)
                        s_in.n += 16
                        ws_after[(k, "ln")] = s_in.n
                    if k >= 1:
                        for et in range(ne):
                            n = (k - 1) * ne + et
                            q.wait_ge(s_ac.h, ac0 + n + 1)
                            q.dma_start(dst[:, et, (k - 1) * 512:k * 512],
                                        xpb[:, (n % 4) * 512:(n % 4 + 1) * 512]).then_inc(s_st.h, 16)
                            s_st.n += 16
                            st_after[n] = s_st.n
                for et in range(ne):
                    n = (NCK - 1) * ne + et
                    q.wait_ge(s_ac.h, ac0 + n + 1)
                    q.dma_start(dst[:, et, (NCK - 1) * 512:NCK * 512],
                                xpb[:, (n % 4) * 512:(n % 4 + 1) * 512]).then_inc(s_st.h, 16)
                    s_st.n += 16
                    st_after[n] = s_st.n
                q.wait_ge(s_st.h, s_st.n)

            @b.tensor
            def _(t):
                t.wait_ge(s_ac.h, 1)
                for k in range(NCK):
                    if src_kind == "lnh":
                        t.wait_ge(s_in.h, ws_after[(k, "ln")])
                    else:
                        t.wait_ge(s_ws.h, ws_after[(k, "pos")])
                    for et in range(ne):
                        n = k * ne + et
                        eg, er = et // 4, et % 4
                        t.wait_ge(s_ws.h, ws_after[(k, eg)])
                        if n >= 4:
                            t.wait_ge(s_ac.h, ac0 + n - 3)
                        pb = ctx.bank(4 + n % 4)
                        for db in range(8):
                            if src_kind == "lnh":
                                rhs = lnr[:, (db * 2 + k % 2) * 512:(db * 2 + k % 2 + 1) * 512]
                            else:
                                rhs = ptslots[k % 2][:, db * 512:(db + 1) * 512]
                            t.matmul(pb, wslots[(k * negr + eg) % 2][:, db * 512 + er * 128:
                                                                     db * 512 + (er + 1) * 128],
                                     rhs, start=(db == 0), stop=False)
                        s_pe.inc(t.matmul(pb, bias_sb[0:1, et * 128:(et + 1) * 128],
                                          onesrow[0:1, 0:512], start=False, stop=True))

            @b.scalar
            def _(a):
                for k in range(NCK):
                    for et in range(ne):
                        n = k * ne + et
                        a.wait_ge(s_pe.h, pe0 + n + 1)
                        if n >= 4:
                            a.wait_ge(s_st.h, st_after[n - 4])
                        s_ac.inc(a.copy(xpb[:, (n % 4) * 512:(n % 4 + 1) * 512],
                                        ctx.bank(4 + n % 4)))

    emit_linear("L1_attn_Wp", bpr, 32, "lnh", "xph")
    emit_linear("L1_attn_Wf", bfr, 24, "pos", "fh")

    Uv = Usb[:].rearrange("p (t g) -> p t g", g=130)
    Vv = Vb[:].rearrange("p (t g) -> p t g", g=130)
    V2v = V2b[:].rearrange("p (t g) -> p t g", g=130)
    NGR = (NT2 + 2) // 3
    GRP = [(g, g * 390, min(3, NT2 - 3 * g) * 130) for g in range(NGR)]
    Fv = Fsb[:].rearrange("p (t g) -> p t g", g=130)
    t1v = tmp1.rearrange("p (t m) -> p t m", m=65)
    t2v = tmp2.rearrange("p (t m) -> p t m", m=65)
    TWc = fTW[:, 0:65].rearrange("p (o m) -> p o m", o=1).broadcast_to([128, NT2, 65])
    TWs = fTW[:, 65:130].rearrange("p (o m) -> p o m", o=1).broadcast_to([128, NT2, 65])
    BDc, BDs, BDsn = fBDb[:, 0:128], fBDb[:, 128:256], fBDb[:, 256:384]

    def sec_s1(t, src_ab, pe0, ac0):
        for t_ in range(NT2):
            g = t_ // 3
            if g >= 8 and t_ % 3 == 0:
                t.wait_ge(s_ac.h, ac0 + g - 7)
            s_pe.inc(t.matmul(ctx.bank(g % 8)[:, (t_ % 3) * 130:(t_ % 3) * 130 + 130],
                              src_ab[:, t_ * 128:(t_ + 1) * 128],
                              fC1b[:], start=True, stop=True))

    def sec_evac(a, dst, pe0, mode="s1"):
        for g, off, w in GRP:
            if mode == "s1":
                a.wait_ge(s_pe.h, pe0 + min(3 * (g + 1), NT2))
            else:
                a.wait_ge(s_pe.h, pe0 + g + 1)
            s_ac.inc(a.copy(dst[:, off:off + w], ctx.bank(g % 8)[:, 0:w]))

    def sec_tw(v, ac0, inverse):
        v.wait_ge(s_ac.h, ac0 + NGR)
        Ur, Ui = Uv[:, :, 0:65], Uv[:, :, 65:130]
        Vr, Vi = Vv[:, :, 0:65], Vv[:, :, 65:130]
        if not inverse:
            v.tensor_mul(t1v[:], Ur, TWc)
            v.tensor_mul(t2v[:], Ui, TWs)
            v.tensor_add(Vr, t1v[:], t2v[:])
            v.tensor_mul(t1v[:], Ui, TWc)
            v.tensor_mul(t2v[:], Ur, TWs)
            v.tensor_sub(Vi, t1v[:], t2v[:])
            v.tensor_copy(V2v[:, :, 0:65], Vi)
            s_dv.inc(v.tensor_scalar(V2v[:, :, 65:130], Vr, -1.0, None, ALU.mult))
        else:
            v.tensor_mul(t1v[:], Ur, TWc)
            v.tensor_mul(t2v[:], Ui, TWs)
            v.tensor_sub(Vr, t1v[:], t2v[:])
            v.tensor_mul(t1v[:], Ui, TWc)
            v.tensor_mul(t2v[:], Ur, TWs)
            s_dv.inc(v.tensor_add(Vi, t1v[:], t2v[:]))

    def sec_s2(t, dv0, ac0, inverse):
        t.wait_ge(s_dv.h, dv0)
        t.wait_ge(s_ac.h, ac0 + NGR)
        for g, off, w in GRP:
            if g >= 8:
                t.wait_ge(s_ac.h, ac0 + NGR + g - 7)
            pb = ctx.bank(g % 8)[:, 0:w]
            t.matmul(pb, BDc, Vb[:, off:off + w], start=True, stop=False)
            s_pe.inc(t.matmul(pb, BDs, V2b[:, off:off + w], start=False, stop=True))

    def sec_pw(v, ac0, in0):
        v.wait_ge(s_ac.h, ac0 + NGR)
        v.wait_ge(s_in.h, in0)
        Yr, Yi = Uv[:, :, 0:65], Uv[:, :, 65:130]
        Fr, Fi = Fv[:, :, 0:65], Fv[:, :, 65:130]
        Zr, Zi = Vv[:, :, 0:65], Vv[:, :, 65:130]
        v.tensor_mul(t1v[:], Yr, Fr)
        v.tensor_mul(t2v[:], Yi, Fi)
        v.tensor_sub(Zr, t1v[:], t2v[:])
        v.tensor_mul(t1v[:], Yr, Fi)
        v.tensor_mul(t2v[:], Yi, Fr)
        v.tensor_add(Zi, t1v[:], t2v[:])
        v.tensor_scalar(V2v[:, :, 0:65], Zi, -1.0, None, ALU.mult)
        s_dv.inc(v.tensor_copy(V2v[:, :, 65:130], Zr))

    def sec_pivot(t, dv0, ac0):
        t.wait_ge(s_dv.h, dv0)
        for t_ in range(NT2):
            for ri in range(2):
                n = t_ * 2 + ri
                if n >= 4:
                    t.wait_ge(s_ac.h, ac0 + n - 3)
                s_pe.inc(t.transpose(ctx.bank(n % 4, 65, 128, BF16),
                                     Vb[:, t_ * 130 + ri * 65:t_ * 130 + (ri + 1) * 65],
                                     ident[:]))

    def sec_pivev(a, pe0):
        for t_ in range(NT2):
            for ri in range(2):
                n = t_ * 2 + ri
                a.wait_ge(s_pe.h, pe0 + n + 1)
                dst = WtTr if ri == 0 else WtTi
                s_ac.inc(a.copy(dst[:, t_ * 128:(t_ + 1) * 128], ctx.bank(n % 4, 65, 128, BF16)))

    def sec_is1(t, ac0, dv0):
        t.wait_ge(s_ac.h, ac0 + 2 * NT2)
        for ck in range(8):
            if ck >= 4:
                t.wait_ge(s_dv.h, dv0 + ck - 3)
            pb = ctx.bank(4 + ck % 4, 64, 512)
            t.matmul(pb, fICbb[:, 0:64], WtTr[:, ck * 512:(ck + 1) * 512], start=True, stop=False)
            s_pe.inc(t.matmul(pb, fICbb[:, 64:128], WtTi[:, ck * 512:(ck + 1) * 512],
                              start=False, stop=True))

    def sec_gate(v, pe0, in0, out_sb):
        for ck in range(8):
            v.wait_ge(s_pe.h, pe0 + ck + 1)
            if ck == 0:
                v.wait_ge(s_in.h, in0)
            s_dv.inc(v.tensor_mul(out_sb[:, ck * 512:(ck + 1) * 512],
                                  ctx.bank(4 + ck % 4, 64, 512),
                                  gate[:, ck * 512:(ck + 1) * 512]))

    for st in range(3):
        for chb in range(NCB):
            with nc.Block() as b:
                @b.sync
                def _(q, st=st, chb=chb):
                    q.dma_start(ycur[:].rearrange("b (e a) -> b e a", a=64),
                                fh_ab[:, st * 1024 + chb * CB:st * 1024 + (chb + 1) * CB, :]
                                ).then_inc(s_in.h, 16)
                    s_in.n += 16

                in0 = s_in.n
                pe0, ac0 = s_pe.n, s_ac.n

                @b.tensor
                def _(t, in0=in0, pe0=pe0, ac0=ac0):
                    t.wait_ge(s_in.h, in0)
                    sec_s1(t, ycur, pe0, ac0)

                @b.scalar
                def _(a, pe0=pe0):
                    sec_evac(a, Usb, pe0)

                dv0 = s_dv.n

                @b.vector
                def _(v, ac0=ac0):
                    sec_tw(v, ac0, inverse=False)

                pe1, ac1 = s_pe.n, s_ac.n

                @b.tensor
                def _(t, ac0=ac0):
                    sec_s2(t, s_dv.n, ac0, inverse=False)

                @b.scalar
                def _(a, pe1=pe1):
                    sec_evac(a, Fsb, pe1, mode="grp")

                @b.gpsimd
                def _(g, st=st, chb=chb, ac1=ac1):
                    g.wait_ge(s_ac.h, ac1 + NGR)
                    g.dma_start(ctx.dram["Ffile"][st][:, chb * NT2 * 130:(chb + 1) * NT2 * 130],
                                Fsb[:]).then_inc(s_stg.h, 16)
                    s_stg.n += 16
                    g.wait_ge(s_stg.h, s_stg.n)


    for st in range(3):
        for chb in range(NCB):
            with nc.Block() as b:
                @b.sync
                def _(q, st=st, chb=chb):
                    if st == 0:
                        q.dma_start(ycur[:].rearrange("b (e a) -> b e a", a=64),
                                    xph_ab[:, chb * CB:(chb + 1) * CB, :]).then_inc(s_in.h, 16)
                    else:
                        q.dma_start(ycur[:], yh[st % 2][:, chb * GW:(chb + 1) * GW]
                                    ).then_inc(s_in.h, 16)
                    s_in.n += 16
                    q.dma_start(Fsb[:], ctx.dram["Ffile"][st][:, chb * NT2 * 130:
                                                              (chb + 1) * NT2 * 130]).then_inc(s_in.h, 16)
                    s_in.n += 16
                    q.dma_start(gate[:].rearrange("b (e a) -> b e a", a=64),
                                xph_ab[:, (st + 1) * 1024 + chb * CB:
                                        (st + 1) * 1024 + (chb + 1) * CB, :]).then_inc(s_in.h, 16)
                    s_in.n += 16

                inY, inF, inG = s_in.n - 32, s_in.n - 16, s_in.n
                pe0, ac0 = s_pe.n, s_ac.n

                @b.tensor
                def _(t, inY=inY, pe0=pe0, ac0=ac0):
                    t.wait_ge(s_in.h, inY)
                    sec_s1(t, ycur, pe0, ac0)

                @b.scalar
                def _(a, pe0=pe0):
                    sec_evac(a, Usb, pe0)

                @b.vector
                def _(v, ac0=ac0):
                    sec_tw(v, ac0, inverse=False)

                dv1 = s_dv.n
                pe1 = s_pe.n

                @b.tensor
                def _(t, dv1=dv1, ac0=ac0):
                    sec_s2(t, dv1, ac0, inverse=False)

                @b.scalar
                def _(a, pe1=pe1):
                    sec_evac(a, Usb, pe1, mode="grp")

                ac1 = s_ac.n - NGR

                @b.vector
                def _(v, ac1=ac1, inF=inF):
                    sec_pw(v, ac1, inF)

                dv2 = s_dv.n
                pe2 = s_pe.n

                @b.tensor
                def _(t, dv2=dv2, ac1=ac1):
                    sec_s2(t, dv2, ac1, inverse=True)

                @b.scalar
                def _(a, pe2=pe2):
                    sec_evac(a, Usb, pe2, mode="grp")

                ac2 = s_ac.n - NGR

                @b.vector
                def _(v, ac2=ac2):
                    sec_tw(v, ac2, inverse=True)

                dv3 = s_dv.n
                pe3 = s_pe.n
                ac3 = s_ac.n

                @b.tensor
                def _(t, dv3=dv3, ac3=ac3):
                    sec_pivot(t, dv3, ac3)

                @b.scalar
                def _(a, pe3=pe3):
                    sec_pivev(a, pe3)

                ac4 = s_ac.n - 2 * NT2
                dv4 = s_dv.n
                pe4 = s_pe.n

                @b.tensor
                def _(t, ac4=ac4, dv4=dv4):
                    sec_is1(t, ac4, dv4)

                @b.vector
                def _(v, pe4=pe4, inG=inG, st=st):
                    sec_gate(v, pe4, inG, ynext if st < 2 else ynf32)

                dv5 = s_dv.n

                @b.gpsimd
                def _(g, st=st, chb=chb, dv5=dv5):
                    g.wait_ge(s_dv.h, dv5)
                    if st < 2:
                        g.dma_start(yh[(st + 1) % 2][:, chb * GW:(chb + 1) * GW],
                                    ynext[:]).then_inc(s_stg.h, 16)
                    else:
                        g.dma_start(ctx.dram["y3h"][:, chb * GW:(chb + 1) * GW],
                                    ynf32).then_inc(s_stg.h, 16)
                    s_stg.n += 16
                    g.wait_ge(s_stg.h, s_stg.n)

    xsrc = ctx.dram[src_name].rearrange("(c p) t -> p c t", p=128)
    ydst = ctx.dram[out_name].rearrange("(c p) t -> p c t", p=128)
    y3v = ctx.dram["y3h"].rearrange("b (cb p a) -> p cb b a", p=128, a=64)
    xb = ctx.ln_bufs()["xb"]
    y3b = Usb[:, 0:4096]
    s_r5 = ctx.sem("l1_r5")
    s_a5 = ctx.sem("l1_a5")
    s_o5 = ctx.sem("l1_o5")
    with nc.Block() as b:
        @b.sync
        def _(q):
            for k in range(NCK):
                if k >= 1:
                    q.wait_ge(s_o5.h, k * 16)
                q.dma_start(xb[:].rearrange("p (c f) -> p c f", c=NCH),
                            xsrc[:, :, k * 512:(k + 1) * 512]).then_inc(s_r5.h, 16)
                for cb in range(8):
                    q.dma_start(y3b[:, cb * 512:(cb + 1) * 512].rearrange(
                                    "p (b2 a) -> p b2 a", a=64),
                                y3v[:, cb, 8 * k:8 * k + 8, :]).then_inc(s_r5.h, 16)

        @b.vector
        def _(v):
            for k in range(NCK):
                v.wait_ge(s_r5.h, (k + 1) * 144)
                for c in range(NCH):
                    v.tensor_add(xb[:, c * 512:(c + 1) * 512], xb[:, c * 512:(c + 1) * 512],
                                 y3b[:, c * 512:(c + 1) * 512]).then_inc(s_a5.h, 1)

        @b.gpsimd
        def _(g):
            for k in range(NCK):
                g.wait_ge(s_a5.h, (k + 1) * 8)
                g.dma_start(ydst[:, :, k * 512:(k + 1) * 512],
                            xb[:].rearrange("p (c f) -> p c f", c=NCH)).then_inc(s_o5.h, 16)
            g.wait_ge(s_o5.h, NCK * 16)


def build_launch_l1attn():
    nc = bass.Bass("TRN2", target_bir_lowering=False, detect_race_conditions=False)
    ctx = Ctx(nc)
    ctx.din("xT", [D, T], F32)
    ctx.din("L1_attn_Wp", [1024, 4096])
    ctx.din("L1_attn_bp", [4096])
    ctx.din("L1_attn_Wf", [1024, 3072])
    ctx.din("L1_attn_bf", [3072])
    for k, v in BIGCONSTS.items():
        ctx.din(k, list(v.shape), F32)
    ctx.dout("yT", [D, T], F32)
    ctx.dint("lnh", [D, T], BF16)
    ctx.load_consts()
    RING = 2
    lnx = ctx.sbuf("lnring", [128, NCH * RING * CHUNK], BF16)
    lnx_at = lambda c, k: lnx[:, (c * RING + k % RING) * CHUNK:
                              (c * RING + k % RING + 1) * CHUNK]
    lnx_view = lambda k: lnx[:].rearrange("p (c s f) -> p c s f", c=NCH, s=RING)[:, :, k % RING, :]
    s_ln, s_lnst = emit_ln(ctx, "xT", lnx_at, "ln1a", bank0=0, ring=RING, to_hbm="lnh",
                           lnx_view=lnx_view)
    emit_l1attn(ctx, s_lnst, "lnh", "xT", "yT", lnx)
    return ctx



def _blk_l0attn(ctx, src, dst):
    ctx.dint("lnh_a0", [D, T], BF16)
    RING = 2
    lnx = ctx.sbuf("lnring", [128, NCH * RING * CHUNK], BF16)
    lnx_at = lambda c, k: lnx[:, (c * RING + k % RING) * CHUNK:
                              (c * RING + k % RING + 1) * CHUNK]
    lnx_view = lambda k: lnx[:].rearrange("p (c s f) -> p c s f", c=NCH, s=RING)[:, :, k % RING, :]
    s_ln, s_lnst = emit_ln(ctx, src, lnx_at, "ln0a", double_ln=True, bank0=0,
                           ring=RING, to_hbm="lnh_a0", lnx_view=lnx_view)
    emit_l0attn(ctx, s_lnst, "lnh_a0", src, dst, lnx)


def _blk_l0ffn(ctx, src, dst):
    lnx = ctx.sbuf("lnx0", [128, NCH * T], BF16)
    lnx_at = lambda c, k: lnx[:, c * T + k * CHUNK: c * T + (k + 1) * CHUNK]
    w = emit_l0ffn_weights(ctx)
    s_ln = emit_ln(ctx, src, lnx_at, "ln0", bank0=0)
    emit_l0ffn_main(ctx, w, lnx_at, s_ln, src, dst)


def _ln_ring_hbm(ctx, src, tag, lnh_name, double_ln=False, dtype=BF16):
    ctx.dint(lnh_name, [D, T], dtype)
    RING = 2
    lnx = ctx.sbuf("lnring", [128, NCH * RING * CHUNK], dtype)
    lnx_at = lambda c, k: lnx[:, (c * RING + k % RING) * CHUNK:
                              (c * RING + k % RING + 1) * CHUNK]
    lnx_view = lambda k: lnx[:].rearrange("p (c s f) -> p c s f", c=NCH, s=RING)[:, :, k % RING, :]
    s_ln, s_lnst = emit_ln(ctx, src, lnx_at, tag, double_ln=double_ln, bank0=0,
                           ring=RING, to_hbm=lnh_name, lnx_view=lnx_view)
    return lnx, s_lnst


def _blk_l1attn(ctx, src, dst):
    lnx, s_lnst = _ln_ring_hbm(ctx, src, "ln1a", "lnh_a1")
    emit_l1attn(ctx, s_lnst, "lnh_a1", src, dst, lnx)


def _blk_l1ffn(ctx, src, dst):
    ctx.dint("acch", [D, T], BF16)
    lnx, s_lnst = _ln_ring_hbm(ctx, src, "ln1f", "lnh_f1")
    emit_l1ffn(ctx, s_lnst, "lnh_f1", src, dst)


def _blk_l2attn_a(ctx, src, dst):
    ctx.dint("rn1h", [D, T], BF16)
    ctx.dint("rn2h", [D, T], BF16)
    ctx.dint("x1h", [D, T], F32)
    lnx, s_lnst = _ln_ring_hbm(ctx, src, "ln2a", "lnh_a2")
    emit_l2attn_a(ctx, s_lnst, "lnh_a2", dst)


def _blk_l2attn_b(ctx, src, x2name, dst):
    ctx.dint("rn3h", [D, T], BF16)
    ctx.dint("h1h", [4096, T], BF16)
    ctx.sems_f2m_pe = ctx.sem("f2m_pe")
    emit_l2attn_b(ctx, src, x2name, dst)


def _blk_l2ffn(ctx, src, dst):
    lnx, s_lnst = _ln_ring_hbm(ctx, src, "ln2f", "lnh_f2", dtype=F32)
    emit_l2ffn(ctx, s_lnst, "lnh_f2", src, dst)


def _blk_l3attn(ctx, src, dst):
    lnx, s_lnst = _ln_ring_hbm(ctx, src, "ln3a", "lnh_a3")
    emit_l3attn(ctx, s_lnst, "lnh_a3", src, dst)


def _blk_l3ffn(ctx, src, dst):
    lnx, s_lnst = _ln_ring_hbm(ctx, src, "ln3f", "lnh_f3")
    emit_l3ffn(ctx, s_lnst, "lnh_f3", src, dst)


FUSED_STEPS = [
    ("f0", ("alibit", "maskq", "q512", "blk64"), _blk_l0attn),
    ("f1", (), _blk_l0ffn),
    ("f2", (), _blk_l1attn),
    ("f3", ("sel8",), _blk_l1ffn),
    ("f4", (), _blk_l2attn_a),
    ("f5", (), _blk_l2attn_b),
    ("f6", (), _blk_l2ffn),
    ("f7", ("triu", "triuf"), _blk_l3attn),
    ("f8", (), _blk_l3ffn),
]

FUSED_WKEYS = (
    "L0_attn_gamma", "L0_attn_w_q", "L0_attn_w_kv", "L0_attn_null_kv",
    "L0_attn_q_scale", "L0_attn_k_scale", "L0_attn_w_out",
    "L0_ffn_W", "L0_ffn_b", "L0_ffn_lora_down", "L0_ffn_lora_up",
    "L0_ffn_down_aux", "L0_ffn_up_aux",
    "L1_attn_Wp", "L1_attn_bp", "L1_attn_Wf", "L1_attn_bf",
    "L1_ffn_gate_w", "L1_ffn_gate_b", "L1_ffn_sw1", "L1_ffn_sb1",
    "L1_ffn_sw2", "L1_ffn_sb2", "L1_ffn_rw1", "L1_ffn_rb1",
    "L1_ffn_rw2", "L1_ffn_rb2",
    "L2_attn_conv_dw_w", "L2_attn_conv_dw_b", "L2_attn_conv_pw_w",
    "L2_attn_conv_pw_b", "L2_attn_conv_norm_g", "L2_attn_gru_norm_g",
    "L2_attn_gru_w", "L2_attn_ff_norm_g", "L2_attn_ff_w1", "L2_attn_ff_b1",
    "L2_attn_ff_w2", "L2_attn_ff_b2",
    "L2_ffn_w_gate", "L2_ffn_w_up", "L2_ffn_w_down",
    "L3_attn_Wq", "L3_attn_Wk", "L3_attn_Wv", "L3_attn_Wo",
    "L3_ffn_W_base", "L3_ffn_b_base", "L3_ffn_W_enc", "L3_ffn_b_enc",
    "L3_ffn_W_dec", "L3_ffn_b_dec", "L3_ffn_W_down", "L3_ffn_b_down",
    "L3_ffn_W_up", "L3_ffn_b_up", "L3_ffn_connected_idx",
)

_WSHAPES = {
    "L0_attn_gamma": [1024], "L0_attn_w_q": [1024, 1024],
    "L0_attn_w_kv": [1024, 2048], "L0_attn_null_kv": [16, 4, 64],
    "L0_attn_q_scale": [64], "L0_attn_k_scale": [64], "L0_attn_w_out": [1024, 1024],
    "L0_ffn_W": [1024, 1024], "L0_ffn_b": [1024], "L0_ffn_lora_down": [4, 100],
    "L0_ffn_lora_up": [50, 4], "L0_ffn_down_aux": [100, 1024], "L0_ffn_up_aux": [1024, 50],
    "L1_attn_Wp": [1024, 4096], "L1_attn_bp": [4096],
    "L1_attn_Wf": [1024, 3072], "L1_attn_bf": [3072],
    "L1_ffn_gate_w": [1024, 8], "L1_ffn_gate_b": [8],
    "L1_ffn_sw1": [1, 1024, 2048], "L1_ffn_sb1": [1, 2048],
    "L1_ffn_sw2": [1, 1024, 1024], "L1_ffn_sb2": [1, 1024],
    "L1_ffn_rw1": [8, 1024, 2048], "L1_ffn_rb1": [8, 2048],
    "L1_ffn_rw2": [8, 1024, 1024], "L1_ffn_rb2": [8, 1024],
    "L2_attn_conv_dw_w": [1024, 1, 3], "L2_attn_conv_dw_b": [1024],
    "L2_attn_conv_pw_w": [1024, 1024], "L2_attn_conv_pw_b": [1024],
    "L2_attn_conv_norm_g": [1024], "L2_attn_gru_norm_g": [1024],
    "L2_attn_gru_w": [2048, 1024], "L2_attn_ff_norm_g": [1024],
    "L2_attn_ff_w1": [4096, 1024], "L2_attn_ff_b1": [4096],
    "L2_attn_ff_w2": [1024, 4096], "L2_attn_ff_b2": [1024],
    "L2_ffn_w_gate": [2816, 1024], "L2_ffn_w_up": [2816, 1024],
    "L2_ffn_w_down": [1024, 2816],
    "L3_attn_Wq": [256, 1024], "L3_attn_Wk": [256, 1024],
    "L3_attn_Wv": [256, 1024], "L3_attn_Wo": [1024, 256],
    "L3_ffn_W_base": [1024, 1024], "L3_ffn_b_base": [1024],
    "L3_ffn_W_enc": [8, 1024, 256], "L3_ffn_b_enc": [8, 256],
    "L3_ffn_W_dec": [8, 256, 1024], "L3_ffn_b_dec": [8, 1024],
    "L3_ffn_W_down": [8, 1024, 64], "L3_ffn_b_down": [8, 64],
    "L3_ffn_W_up": [8, 64, 1024], "L3_ffn_b_up": [8, 1024],
}


def build_launch_fused(nsteps=9):
    nc = bass.Bass("TRN2", target_bir_lowering=False, detect_race_conditions=False)
    ctx = Ctx(nc)
    I32 = mybir.dt.int32
    ctx.din("xT", [D, T], F32)
    for k in FUSED_WKEYS:
        ctx.din(k, _WSHAPES[k] if k != "L3_ffn_connected_idx" else [8],
                I32 if k == "L3_ffn_connected_idx" else F32)
    for k, v in BIGCONSTS.items():
        ctx.din(k, list(v.shape), F32)
    ctx.dout("yT", [D, T], F32)
    ctx.dout("t3dbg", [128, 64], F32)
    for i in range(1, 8):
        ctx.dint("hx%d" % i, [D, T], F32)
    ctx.dint("x2h", [D, T], F32)

    hand = ["xT", "hx1", "hx2", "hx3", "hx4", "hx5", "hx6", "hx7", "yT"]
    hi = 0
    for si in range(nsteps):
        tag, extras, fn = FUSED_STEPS[si]
        last = (si == nsteps - 1)
        with ctx.frame(tag, extras=extras):
            if fn is _blk_l2attn_a:
                fn(ctx, hand[hi], "x2h")
            elif fn is _blk_l2attn_b:
                fn(ctx, hand[hi], "x2h", "yT" if last else hand[hi + 1])
                hi += 1
            else:
                fn(ctx, hand[hi], "yT" if last else hand[hi + 1])
                hi += 1
    return ctx


# version 06
# speedup vs baseline: 1.1467x; 1.0047x over previous
import math
import numpy as np
import ml_dtypes
import concourse.bass as bass
import concourse.mybir as mybir
from concourse.bass_utils import run_bass_kernel_spmd

F32 = mybir.dt.float32
BF16 = mybir.dt.bfloat16
AF = mybir.ActivationFunctionType
ALU = mybir.AluOpType
AX = mybir.AxisListType
BF = ml_dtypes.bfloat16

T = 4096
D = 1024
NCH = D // 128
CHUNK = 512
NCK = T // CHUNK
LN_EPS = 1e-5


def build_constants():
    sel8 = np.zeros((8, 8 * 128), dtype=BF)
    for e in range(8):
        sel8[e, e * 128:(e + 1) * 128] = 1
    triu = np.triu(np.ones((128, 128), dtype=np.float32))
    return {
        "c_ident": np.eye(128).astype(BF),
        "c_ones128": np.ones((128, 128), dtype=BF),
        "c_onesrow": np.ones((1, T), dtype=BF),
        "c_sel8": sel8,
        "c_triu": triu.astype(BF),
        "c_triuf": triu,
        "c_identf": np.eye(128, dtype=np.float32),
        "c_alibi": _alibi_table(),
        "c_maskq": np.concatenate([_mask(o * 128) for o in range(4)], axis=1),
        "c_nullrow": np.array([[-2.0, -1.0], [1.0, 1.0]], dtype=BF),
        "c_kaug2": np.stack([(np.arange(T) % 128), np.ones(T)]).astype(BF),
        "c_q512": (np.arange(T) % 512).astype(BF).reshape(1, T),
        "c_blk64": _blk64().astype(BF),
        "c_blk64f": _blk64(),
    }


def _alibi_slopes(heads=16):
    start = 2.0 ** (-2.0 ** (-(math.log2(heads) - 3.0)))
    return np.array([start * start ** i for i in range(heads)], dtype=np.float64)


def _alibi_table():
    sl = _alibi_slopes()
    t = np.zeros((16, 33), dtype=np.float32)
    for h in range(16):
        for dlt in range(-3, 30):
            t[h, dlt + 3] = -sl[h] * 128.0 * dlt
    return np.tile(t.reshape(1, 16 * 33), (128, 1)).astype(np.float32)


def _mask(off):
    p = np.arange(128)[:, None]
    f = np.arange(512)[None, :]
    return np.where((p + off) <= f, 0.0, -30000.0).astype(BF)


def _blk64():
    m = np.zeros((128, 64), dtype=np.float32)
    m[0:64, 0] = 1.0
    m[64:128, 32] = 1.0
    return m

def _l1_consts():
    bb = np.arange(64)[:, None]
    mm = np.arange(65)[None, :]
    aa = np.arange(64)[:, None]
    cc = np.arange(64)[None, :]
    C1 = np.concatenate([np.cos(2 * np.pi * bb * mm / 128.0),
                         -np.sin(2 * np.pi * bb * mm / 128.0)], 1)
    C2c = np.cos(2 * np.pi * aa * cc / 64.0)
    C2s = np.sin(2 * np.pi * aa * cc / 64.0)
    z64 = np.zeros((64, 64))
    bd = lambda M: np.block([[M, z64], [z64, M]])
    BD = np.concatenate([bd(C2c), bd(C2s), bd(-C2s)], 1)
    TW = np.concatenate([np.tile(np.cos(2 * np.pi * aa * mm / 8192.0), (2, 1)),
                         np.tile(np.sin(2 * np.pi * aa * mm / 8192.0), (2, 1))], 1)
    wm = np.ones(65)
    wm[0] = wm[64] = 0.5
    mm2 = np.arange(65)[:, None]
    bb2 = np.arange(64)[None, :]
    ICb = np.concatenate([2 * wm[:, None] * np.cos(2 * np.pi * bb2 * mm2 / 128.0) / 8192.0,
                          -2 * wm[:, None] * np.sin(2 * np.pi * bb2 * mm2 / 128.0) / 8192.0], 1)
    return {"c_fC1": C1.astype(np.float32), "c_fBD": BD.astype(np.float32),
            "c_fTW": TW.astype(np.float32), "c_fICb": ICb.astype(np.float32)}


def _pos_enc():
    pos = np.arange(T, dtype=np.float64)[:, None]
    i = np.arange(D)[None, :]
    angle = pos / np.power(10000.0, (2 * (i // 2)) / float(D))
    p = np.where(i % 2 == 0, np.sin(angle), np.cos(angle))
    return np.ascontiguousarray(p.T).astype(np.float32)


CONSTS = build_constants()
CONSTS.update(_l1_consts())
BIGCONSTS = {"c_posT": _pos_enc()}
CONST_DTYPES = {k: v.dtype for k, v in CONSTS.items()}



class Sem:
    __slots__ = ("h", "n")

    def __init__(self, nc, name):
        self.h = nc.alloc_semaphore(name)
        self.n = 0

    def inc(self, ins, k=1):
        ins.then_inc(self.h, k)
        self.n += k
        return ins

    def wait(self, eng, v=None):
        eng.wait_ge(self.h, self.n if v is None else v)


class Ctx:
    def __init__(self, nc):
        self.nc = nc
        self.dram = {}
        self.sb = {}
        self._names = set()
        self._frame = ""
        self.banks = [nc.alloc_psum_tensor("bank%d" % j, [128, 512], F32) for j in range(8)]

    def _uniq(self, name):
        name = self._frame + name
        assert name not in self._names, name
        self._names.add(name)
        return name

    def frame(self, tag, extras=()):
        import contextlib

        @contextlib.contextmanager
        def _cm():
            self._frame = tag + "_"
            self.sb = {}
            if hasattr(self, "_ln_bufs"):
                del self._ln_bufs
            with self.nc.cleanup_on_exit():
                self.load_consts(extra=extras)
                yield
                self.nc.all_engine_barrier()
        return _cm()

    def din(self, name, shape, dtype=F32):
        ap = self.nc.dram_tensor(name, list(shape), dtype, kind="ExternalInput").ap()
        self.dram[name] = ap
        return ap

    def dout(self, name, shape, dtype=F32):
        ap = self.nc.dram_tensor(name, list(shape), dtype, kind="ExternalOutput").ap()
        self.dram[name] = ap
        return ap

    def dint(self, name, shape, dtype=F32):
        ap = self.nc.dram_tensor(name, list(shape), dtype, kind="Internal").ap()
        self.dram[name] = ap
        return ap

    def sbuf(self, name, shape, dtype=F32):
        t = self.nc.alloc_sbuf_tensor(self._uniq(name), list(shape), dtype)
        self.sb[name] = t
        return t

    def sem(self, name):
        return Sem(self.nc, self._uniq(name))

    def bank(self, j, rows=128, cols=512, dtype=F32):
        ap = self.banks[j][:]
        if dtype != F32:
            ap = ap.bitcast(dtype)
        return ap[0:rows, 0:cols]

    def ln_bufs(self):
        if not hasattr(self, "_ln_bufs"):
            self._ln_bufs = {
                "xb": self.sbuf("lnb_xb", [128, NCH * CHUNK]),
                "xbf": self.sbuf("lnb_xbf", [128, NCH * CHUNK], BF16),
                "meanS": [self.sbuf("lnb_m0", [128, CHUNK]), self.sbuf("lnb_m1", [128, CHUNK])],
                "rstd": [self.sbuf("lnb_r0", [128, CHUNK]), self.sbuf("lnb_r1", [128, CHUNK])],
                "m2": self.sbuf("lnb_m2", [128, CHUNK]),
                "varS": self.sbuf("lnb_varS", [128, CHUNK]),
            }
        return self._ln_bufs

    def load_consts(self, extra=()):
        for k, v in CONSTS.items():
            if k not in self.dram:
                self.din(k, list(v.shape), BF16 if v.dtype == BF else F32)
        specs = {
            "ident": ("c_ident", [128, 128], BF16),
            "ones128": ("c_ones128", [128, 128], BF16),
            "onesrow": ("c_onesrow", [1, T], BF16),
            "identf": ("c_identf", [128, 128], F32),
            "sel8": ("c_sel8", [8, 8 * 128], BF16),
            "triu": ("c_triu", [128, 128], BF16),
            "triuf": ("c_triuf", [128, 128], F32),
            "alibit": ("c_alibi", [128, 16 * 33], F32),
            "maskq": ("c_maskq", [128, 4 * 512], BF16),
            "q512": ("c_q512", [1, T], BF16),
            "blk64": ("c_blk64", [128, 64], BF16),
        }
        names = ["ident", "ones128", "onesrow", "identf"] + list(extra)
        tensors = {n: self.sbuf(n, specs[n][1], specs[n][2]) for n in names}
        self.identf = tensors["identf"]
        epscol = self.sbuf("epscol", [128, 1])
        epscol2 = self.sbuf("epscol2", [128, 1])
        s = self.sem("s_const")
        with self.nc.Block() as b:
            @b.sync
            def _(q):
                for n in names:
                    q.dma_start(tensors[n][:], self.dram[specs[n][0]]).then_inc(s.h, 16)
            s.n = len(names) * 16

            @b.vector
            def _(v):
                v.memset(epscol[:], LN_EPS)
                v.memset(epscol2[:], LN_EPS * LN_EPS).then_inc(s.h, 1)
            s.n += 1
        self.s_const = s
        return s


def tile(t2d, c, w=T):
    return t2d[:, c * w:(c + 1) * w]


def mm_chain(t_eng, psum_ap, pairs):
    n = len(pairs)
    ins = None
    for j, (lh, rh) in enumerate(pairs):
        ins = t_eng.matmul(psum_ap, lh, rh, start=(j == 0), stop=(j == n - 1))
    return ins



def emit_ln(ctx, src_name, lnx_at, tag, double_ln=False, bank0=0, src_sem=None,
            ring=None, cons=None, to_hbm=None, lnx_view=None):
    nc = ctx.nc
    ones128 = ctx.sb["ones128"]
    src = ctx.dram[src_name].rearrange("(c p) t -> p c t", p=128)

    s_in = [ctx.sem(tag + "_in0"), ctx.sem(tag + "_in1")]
    s_mm1 = ctx.sem(tag + "_mm1")
    s_mm = ctx.sem(tag + "_mm")
    s_cast = ctx.sem(tag + "_cast")
    s_csq = ctx.sem(tag + "_csq")
    s_stat = ctx.sem(tag + "_stat")
    s_done = ctx.sem(tag + "_done")
    s_psf = ctx.sem(tag + "_psf")

    lb = ctx.ln_bufs()
    xb_ = lb["xb"]
    xb = [xb_, xb_]
    xbf_ = lb["xbf"]
    xbf = [xbf_, xbf_]
    meanS = lb["meanS"]
    rstd = lb["rstd"]
    m2 = lb["m2"]
    varS = lb["varS"]
    p_sum = [ctx.bank(bank0), ctx.bank(bank0 + 1)]
    p_sq = [ctx.bank(bank0 + 2), ctx.bank(bank0 + 3)]

    if to_hbm is not None:
        lnx_dst = ctx.dram[to_hbm].rearrange("(c p) t -> p c t", p=128)
        s_lnst = [ctx.sem(tag + "_st0"), ctx.sem(tag + "_st1")]

    with nc.Block() as b:
        if to_hbm is not None:
            @b.gpsimd
            def _(g):
                for k in range(NCK):
                    g.wait_ge(s_done.h, (k + 1) * 2)
                    g.dma_start(lnx_dst[:, :, k * CHUNK:(k + 1) * CHUNK],
                                lnx_view(k)).then_inc(s_lnst[k % 2].h, 16)
            s_lnst[0].n = (NCK // 2) * 16
            s_lnst[1].n = (NCK // 2) * 16

        @b.sync
        def _(q):
            q.wait_ge(ctx.s_const.h, ctx.s_const.n)
            for k in range(NCK):
                if src_sem is not None:
                    for _sm, _fn in (src_sem if isinstance(src_sem, list) else [src_sem]):
                        q.wait_ge(_sm.h, _fn(k))
                if k >= 1:
                    q.wait_ge(s_done.h, k * 2)
                q.dma_start(xb[k % 2][:].rearrange("p (c f) -> p c f", c=NCH),
                            src[:, :, k * CHUNK:(k + 1) * CHUNK]).then_inc(s_in[k % 2].h, 16)
        s_in[0].n = (NCK // 2) * 16
        s_in[1].n = (NCK - NCK // 2) * 16

        @b.scalar
        def _(a):
            for k in range(NCK):
                i = k % 2
                a.wait_ge(s_in[k % 2].h, (k // 2 + 1) * 16)
                if k >= 1:
                    a.wait_ge(s_mm.h, k)
                if k >= 2:
                    a.wait_ge(s_done.h, (k - 1) * 2)
                a.copy(xbf[i][:], xb[i][:]).then_inc(s_cast.h, 1)
                a.wait_ge(s_mm1.h, k + 1)
                a.activation(meanS[i][:], p_sum[i], AF.Copy, scale=1.0 / D).then_inc(s_psf.h, 1)
                a.activation(xbf[i][:], xbf[i][:], AF.Square).then_inc(s_csq.h, 1)
                a.wait_ge(s_psf.h, 2 * (k + 1))
                if double_ln:
                    a.activation(varS[:], varS[:], AF.Ln, scale=(1.0 + LN_EPS),
                                 bias=ctx.sb["epscol2"][:, 0:1])
                else:
                    a.activation(varS[:], varS[:], AF.Ln, bias=ctx.sb["epscol"][:, 0:1])
                a.activation(rstd[i][:], varS[:], AF.Exp, scale=-0.5).then_inc(s_stat.h, 1)
        s_cast.n = NCK
        s_csq.n = NCK
        s_stat.n = NCK

        @b.tensor
        def _(t):
            for k in range(NCK):
                i = k % 2
                t.wait_ge(s_cast.h, k + 1)
                if k >= 2:
                    t.wait_ge(s_psf.h, (k - 1) * 2)
                for c in range(NCH):
                    ins = t.matmul(p_sum[i], ones128[:], xbf[i][:, c * CHUNK:(c + 1) * CHUNK],
                                   start=(c == 0), stop=(c == NCH - 1))
                ins.then_inc(s_mm1.h, 1)
                t.wait_ge(s_csq.h, k + 1)
                for c in range(NCH):
                    ins = t.matmul(p_sq[i], ones128[:], xbf[i][:, c * CHUNK:(c + 1) * CHUNK],
                                   start=(c == 0), stop=(c == NCH - 1))
                ins.then_inc(s_mm.h, 1)
        s_mm1.n = NCK
        s_mm.n = NCK

        @b.vector
        def _(v):
            for k in range(NCK):
                i = k % 2
                v.wait_ge(s_mm.h, k + 1)
                v.wait_ge(s_psf.h, 2 * k + 1)
                if k >= 1:
                    v.wait_ge(s_stat.h, k)
                v.tensor_mul(m2[:], meanS[i][:], meanS[i][:])
                v.scalar_tensor_tensor(varS[:], p_sq[i], 1.0 / D, m2[:],
                                       ALU.mult, ALU.subtract).then_inc(s_psf.h, 1)
                v.wait_ge(s_stat.h, k + 1)
                if ring is not None and k >= ring:
                    if to_hbm is not None:
                        v.wait_ge(s_lnst[k % 2].h, ((k - ring) // 2 + 1) * 16)
                    else:
                        v.wait_ge(cons[0].h, cons[1](k - ring))
                for c in range(NCH):
                    sl = slice(c * CHUNK, (c + 1) * CHUNK)
                    v.tensor_sub(xb[i][:, sl], xb[i][:, sl], meanS[i][:])
                ins = None
                for c in range(NCH):
                    ins = v.tensor_mul(lnx_at(c, k),
                                       xb[i][:, c * CHUNK:(c + 1) * CHUNK], rstd[i][:])
                ins.then_inc(s_done.h, 2)
        s_psf.n += NCK
        s_done.n = NCK * 2

    if to_hbm is not None:
        return s_done, s_lnst
    return s_done



def emit_residual_out(ctx, tag, src_name, dst_name, s_blk, blk_count_fn, emit_mm_fn,
                      s_pe=None, extra_sb=None, extra_stream=None, bufs=None,
                      sync_extra=None):
    nc = ctx.nc
    src = ctx.dram[src_name].rearrange("(c p) t -> p c t", p=128)
    dst = ctx.dram[dst_name].rearrange("(c p) t -> p c t", p=128)
    s_xin = ctx.sem(tag + "_xin")
    if s_pe is None:
        s_pe = ctx.sem(tag + "_pe")
    s_add = ctx.sem(tag + "_add")
    s_out = ctx.sem(tag + "_out")
    xb_ = ctx.sbuf(tag + "_xb", [128, NCH * CHUNK]) if bufs is None or "xb" not in bufs else bufs["xb"]
    yb_ = ctx.sbuf(tag + "_yb", [128, NCH * CHUNK]) if bufs is None or "yb" not in bufs else bufs["yb"]
    xb = [xb_, xb_]
    yb = [yb_, yb_]
    if extra_stream is not None:
        ename, edt = (extra_stream, BF16) if isinstance(extra_stream, str) else extra_stream
        eb = ctx.sbuf(tag + "_eb", [128, NCH * CHUNK], edt) if bufs is None or "eb" not in bufs else bufs["eb"]
        esrc = ctx.dram[ename].rearrange("(c p) t -> p c t", p=128)
        s_ein = ctx.sem(tag + "_ein")
        extra_sb = lambda k, o: eb[:, o * CHUNK:(o + 1) * CHUNK]

    with nc.Block() as b:
        @b.sync
        def _(q):
            for k in range(NCK):
                if k >= 1:
                    q.wait_ge(s_add.h, k * 8)
                q.dma_start(xb[k % 2][:].rearrange("p (c f) -> p c f", c=NCH),
                            src[:, :, k * CHUNK:(k + 1) * CHUNK]).then_inc(s_xin.h, 16)
                if extra_stream is not None:
                    q.dma_start(eb[:].rearrange("p (c f) -> p c f", c=NCH),
                                esrc[:, :, k * CHUNK:(k + 1) * CHUNK]).then_inc(s_ein.h, 16)
                if sync_extra is not None:
                    sync_extra(q, k)
        s_xin.n = NCK * 16

        @b.gpsimd
        def _(g):
            for k in range(NCK):
                g.wait_ge(s_add.h, (k + 1) * 8)
                g.dma_start(dst[:, :, k * CHUNK:(k + 1) * CHUNK],
                            yb[k % 2][:].rearrange("p (c f) -> p c f", c=NCH)).then_inc(s_out.h, 16)
            g.wait_ge(s_out.h, NCK * 16)
        s_out.n = NCK * 16

        @b.tensor
        def _(t):
            if s_blk is not None:
                t.wait_ge(s_blk.h, s_blk.n)
            for k in range(NCK):
                for o in range(NCH):
                    n = k * 8 + o
                    j = 4 + n % 4
                    if n >= 4:
                        t.wait_ge(s_add.h, n - 4 + 1)
                    if blk_count_fn is not None:
                        t.wait_ge(blk_count_fn[0].h, blk_count_fn[1](k))
                    emit_mm_fn(t, k, o, ctx.bank(j)).then_inc(s_pe.h, 1)
        s_pe.n = NCK * 8

        @b.vector
        def _(v):
            for k in range(NCK):
                i = k % 2
                v.wait_ge(s_xin.h, (k + 1) * 16)
                if extra_stream is not None:
                    v.wait_ge(s_ein.h, (k + 1) * 16)
                if k >= 1:
                    v.wait_ge(s_out.h, k * 16)
                for o in range(NCH):
                    n = k * 8 + o
                    j = 4 + n % 4
                    v.wait_ge(s_pe.h, n + 1)
                    if extra_sb is not None:
                        v.tensor_add(yb[i][:, o * CHUNK:(o + 1) * CHUNK],
                                     xb[i][:, o * CHUNK:(o + 1) * CHUNK],
                                     ctx.bank(j))
                        v.tensor_add(yb[i][:, o * CHUNK:(o + 1) * CHUNK],
                                     yb[i][:, o * CHUNK:(o + 1) * CHUNK],
                                     extra_sb(k, o)).then_inc(s_add.h, 1)
                    else:
                        v.tensor_add(yb[i][:, o * CHUNK:(o + 1) * CHUNK],
                                     xb[i][:, o * CHUNK:(o + 1) * CHUNK],
                                     ctx.bank(j)).then_inc(s_add.h, 1)
        s_add.n = NCK * 8
    return s_out



def emit_l0ffn_weights(ctx):
    nc = ctx.nc
    ident = ctx.sb["ident"]
    onesrow = ctx.sb["onesrow"]
    W = ctx.dram["L0_ffn_W"]
    bvec = ctx.dram["L0_ffn_b"]

    s_win = ctx.sem("f0_win")
    s_pe = ctx.sem("f0_pe")
    s_ev = ctx.sem("f0_ev")

    ldT = ctx.sbuf("f0_ldT", [100, 4], BF16)
    luT = ctx.sbuf("f0_luT", [4, 50], BF16)
    daS = ctx.sbuf("f0_da", [100, 1024], BF16)
    uaT = ctx.sbuf("f0_uaT", [50, 1024], BF16)
    ld_f = ctx.sbuf("f0_ld_f", [4, 100], BF16)
    lu_f = ctx.sbuf("f0_lu_f", [50, 4], BF16)
    ua_f = ctx.sbuf("f0_ua_f", [128, 8 * 50], BF16)
    r1 = ctx.sbuf("f0_r1", [4, 1024], BF16)
    r2 = ctx.sbuf("f0_r2", [50, 1024], BF16)
    brow = ctx.sbuf("f0_brow", [1, 1024], BF16)

    p_a = ctx.bank(4)
    p_b = ctx.bank(5)
    p_t = ctx.bank(6, 128, 256, BF16)

    with nc.Block() as b:
        @b.gpsimd
        def _(g):
            g.dma_start(ld_f[:], ctx.dram["L0_ffn_lora_down"]).then_inc(s_win.h, 16)
            g.dma_start(lu_f[:], ctx.dram["L0_ffn_lora_up"]).then_inc(s_win.h, 16)
            g.dma_start(daS[:], ctx.dram["L0_ffn_down_aux"]).then_inc(s_win.h, 16)
            g.dma_start(ua_f[:].rearrange("p (c f) -> p c f", c=8),
                        ctx.dram["L0_ffn_up_aux"].rearrange("(c p) f -> p c f", p=128)).then_inc(s_win.h, 16)
            g.dma_start(brow[:], bvec.rearrange("(one n) -> one n", one=1)).then_inc(s_win.h, 16)
        s_win.n = 5 * 16

        @b.tensor
        def _(t):
            t.wait_ge(ctx.s_const.h, ctx.s_const.n)
            t.wait_ge(s_win.h, s_win.n)
            t.transpose(p_t[0:100, 0:4], ld_f[:], ident[0:4, 0:4]).then_inc(s_pe.h, 1)
            t.wait_ge(s_ev.h, 1)
            t.transpose(p_t[0:4, 0:50], lu_f[:], ident[0:50, 0:50]).then_inc(s_pe.h, 1)
            t.wait_ge(s_ev.h, 2)
            for c in range(8):
                t.transpose(p_t[0:50, 0:128], ua_f[:, c * 50:(c + 1) * 50], ident[:]).then_inc(s_pe.h, 1)
                t.wait_ge(s_ev.h, 2 + c + 1)
            t.matmul(p_a[0:4, :], ldT[:], daS[:, 0:512], start=True, stop=True)
            t.matmul(p_b[0:4, :], ldT[:], daS[:, 512:1024], start=True, stop=True).then_inc(s_pe.h, 1)
            t.wait_ge(s_ev.h, 11)
            t.matmul(p_a[0:50, :], luT[:], r1[:, 0:512], start=True, stop=True)
            t.matmul(p_b[0:50, :], luT[:], r1[:, 512:1024], start=True, stop=True).then_inc(s_pe.h, 1)
            t.wait_ge(s_ev.h, 12)
        s_pe.n = 12

        @b.scalar
        def _(a):
            a.wait_ge(s_pe.h, 1)
            a.copy(ldT[:], p_t[0:100, 0:4]).then_inc(s_ev.h, 1)
            a.wait_ge(s_pe.h, 2)
            a.copy(luT[:], p_t[0:4, 0:50]).then_inc(s_ev.h, 1)
            for c in range(8):
                a.wait_ge(s_pe.h, 2 + c + 1)
                a.copy(uaT[:, c * 128:(c + 1) * 128], p_t[0:50, 0:128]).then_inc(s_ev.h, 1)
            a.wait_ge(s_pe.h, 11)
            a.copy(r1[:, 0:512], p_a[0:4, :])
            a.copy(r1[:, 512:1024], p_b[0:4, :]).then_inc(s_ev.h, 1)
            a.wait_ge(s_pe.h, 12)
            a.copy(r2[:, 0:512], p_a[0:50, :])
            a.copy(r2[:, 512:1024], p_b[0:50, :]).then_inc(s_ev.h, 1)
        s_ev.n = 12

    s_wld = [ctx.sem("f0_wld0"), ctx.sem("f0_wld1")]
    s_lmm = ctx.sem("f0_lmm")
    s_wadd = ctx.sem("f0_wadd")
    s_tr = ctx.sem("f0_tr")
    s_trev = ctx.sem("f0_trev")
    wtile = [ctx.sbuf("f0_wt0", [128, 1024]), ctx.sbuf("f0_wt1", [128, 1024])]
    p_t2 = [ctx.bank(6, 128, 128, BF16), ctx.bank(7, 128, 128, BF16)]
    weff_bf = [ctx.sbuf("f0_wb0", [128, 1024], BF16), ctx.sbuf("f0_wb1", [128, 1024], BF16)]
    lhsT = ctx.sbuf("f0_lhsT", [128, NCH * 1024], BF16)

    with nc.Block() as b:
        @b.sync
        def _(q):
            for o in range(8):
                if o >= 2:
                    q.wait_ge(s_wadd.h, 2 * (o - 1))
                q.dma_start(wtile[o % 2][:], W.rearrange("(o p) i -> p o i", p=128)[:, o]).then_inc(s_wld[o % 2].h, 16)
        s_wld[0].n = 4 * 16
        s_wld[1].n = 4 * 16

        @b.tensor
        def _(t):
            t.wait_ge(s_ev.h, 12)
            for o in range(8):
                if o >= 1:
                    t.wait_ge(s_wadd.h, 2 * o)
                t.matmul(p_a[:], uaT[:, o * 128:(o + 1) * 128], r2[:, 0:512], start=True, stop=True)
                t.matmul(p_b[:], uaT[:, o * 128:(o + 1) * 128], r2[:, 512:1024],
                         start=True, stop=True).then_inc(s_lmm.h, 1)
                t.wait_ge(s_wadd.h, 2 * (o + 1))
                for i in range(8):
                    n = o * 8 + i
                    if n >= 2:
                        t.wait_ge(s_trev.h, n - 1)
                    t.transpose(p_t2[n % 2], weff_bf[o % 2][:, i * 128:(i + 1) * 128],
                                ident[:]).then_inc(s_tr.h, 1)
        s_lmm.n = 8
        s_tr.n = 64

        @b.vector
        def _(v):
            for o in range(8):
                v.wait_ge(s_wld[o % 2].h, (o // 2 + 1) * 16)
                v.wait_ge(s_lmm.h, o + 1)
                if o >= 2:
                    v.wait_ge(s_tr.h, (o - 1) * 8)
                v.tensor_add(weff_bf[o % 2][:, 0:512], wtile[o % 2][:, 0:512], p_a[:]).then_inc(s_wadd.h, 1)
                v.tensor_add(weff_bf[o % 2][:, 512:1024], wtile[o % 2][:, 512:1024], p_b[:]).then_inc(s_wadd.h, 1)
        s_wadd.n = 16

        @b.scalar
        def _(a):
            for o in range(8):
                for i in range(8):
                    n = o * 8 + i
                    a.wait_ge(s_tr.h, n + 1)
                    a.copy(lhsT[:, i * 1024 + o * 128: i * 1024 + (o + 1) * 128],
                           p_t2[n % 2]).then_inc(s_trev.h, 1)
        s_trev.n = 64

    return dict(lhsT=lhsT, brow=brow, s_trev=s_trev)


def emit_l0ffn_main(ctx, w, lnx_at, s_ln, src_name, out_name):
    onesrow = ctx.sb["onesrow"]
    lhsT, brow = w["lhsT"], w["brow"]

    def mm_fn(t, k, o, bank_ap):
        pairs = [(lhsT[:, i * 1024 + o * 128: i * 1024 + (o + 1) * 128],
                  lnx_at(i, k)) for i in range(8)]
        pairs.append((brow[0:1, o * 128:(o + 1) * 128], onesrow[0:1, k * CHUNK:(k + 1) * CHUNK]))
        return mm_chain(t, bank_ap, pairs)

    return emit_residual_out(ctx, "f0m", src_name, out_name,
                             w["s_trev"], (s_ln, lambda k: 2 * (k + 1)), mm_fn)



def emit_l3ffn(ctx, s_lnst, lnh_name, src_name, out_name, pre=None):
    nc = ctx.nc
    I32 = mybir.dt.int32
    ident = ctx.sb["ident"]
    onesrow = ctx.sb["onesrow"]
    lnh = ctx.dram[lnh_name].rearrange("(c p) t -> p c t", p=128)

    wbase = ctx.sbuf("t3_wbase", [128, 8 * 1024], BF16)
    wdown = ctx.sbuf("t3_wdown", [128, 8 * 8 * 64], BF16)
    wup = ctx.sbuf("t3_wup", [64, 8 * 1024], BF16)
    bup8 = ctx.sbuf("t3_bup8", [8, 1024], BF16)
    r_bbase = ctx.sbuf("t3_rbbase", [1, 1024], BF16)
    r_benc = ctx.sbuf("t3_rbenc", [1, 2048], BF16)
    r_bdown = ctx.sbuf("t3_rbdown", [1, 512], BF16)
    r_bdec = ctx.sbuf("t3_rbdec", [1, 1024], BF16)
    encbuf = ctx.sbuf("t3_enc", [128, 2048], BF16)
    decbuf = ctx.sbuf("t3_dec", [128, 2048], BF16)
    hbuf = ctx.sbuf("t3_h", [128, 4 * 512], BF16)
    lnr1 = ctx.sbuf("t3_lnr1", [128, 16 * 512], BF16)
    lnr3 = ctx.sbuf("t3_lnr3", [128, 16 * 512], BF16)
    mids = ctx.sbuf("t3_mids", [64, 8 * 512], BF16)
    dscr = ctx.sbuf("t3_dscr", [128, 512])
    dslots = ctx.sbuf("t3_dslots", [128, 64])
    lossparts = ctx.sbuf("t3_lparts", [128, 8])
    ones128f = ctx.sbuf("t3_ones128f", [128, 1])
    lossrow = ctx.sbuf("t3_lossrow", [1, 8])
    minrow = ctx.sbuf("t3_minrow", [1, 1])
    selrow = ctx.sbuf("t3_selrow", [1, 8])
    selrowbf = ctx.sbuf("t3_selrowbf", [1, 8], BF16)
    selcolbf = ctx.sbuf("t3_selcolbf", [8, 1], BF16)
    iota8 = ctx.sbuf("t3_iota8", [8, 8])
    cidxi = ctx.sbuf("t3_cidxi", [8, 1], I32)
    cidxf = ctx.sbuf("t3_cidxf", [8, 1])
    Mbf = ctx.sbuf("t3_M", [8, 8], BF16)
    selexpbf = ctx.sbuf("t3_selexp", [8, 1], BF16)
    selexprow = ctx.sbuf("t3_selexprow", [1, 8], BF16)
    SEL = ctx.sbuf("t3_SEL", [128, 8])
    badapt = ctx.sbuf("t3_badapt", [1, 1024], BF16)

    W_enc = ctx.dram["L3_ffn_W_enc"]
    W_dec = ctx.dram["L3_ffn_W_dec"]

    s_w0 = ctx.sem("t3_w0")
    s_w0b = ctx.sem("t3_w0b")

    with nc.Block() as b:
        @b.gpsimd
        def _(g):
            g.dma_start(wbase[:].rearrange("p (c o) -> p c o", c=8),
                        ctx.dram["L3_ffn_W_base"].rearrange("(c p) o -> p c o", p=128)).then_inc(s_w0.h, 16)
            g.dma_start(wdown[:].rearrange("p (e c w) -> p e c w", e=8, c=8),
                        ctx.dram["L3_ffn_W_down"].rearrange("e (c p) w -> p e c w", p=128)).then_inc(s_w0.h, 16)
            g.dma_start(wup[:].rearrange("r (e o) -> r e o", e=8),
                        ctx.dram["L3_ffn_W_up"].rearrange("e r o -> r e o")).then_inc(s_w0.h, 16)
            g.dma_start(bup8[:], ctx.dram["L3_ffn_b_up"]).then_inc(s_w0.h, 16)
            g.dma_start(r_bbase[:], ctx.dram["L3_ffn_b_base"].rearrange("(o n) -> o n", o=1)).then_inc(s_w0.h, 16)
            g.dma_start(r_benc[:], ctx.dram["L3_ffn_b_enc"].rearrange("e w -> (e w)").rearrange("(o n) -> o n", o=1)).then_inc(s_w0.h, 16)
            g.dma_start(r_bdown[:], ctx.dram["L3_ffn_b_down"].rearrange("e w -> (e w)").rearrange("(o n) -> o n", o=1)).then_inc(s_w0.h, 16)
            g.dma_start(cidxi[:], ctx.dram["L3_ffn_connected_idx"].rearrange("(p o) -> p o", o=1)).then_inc(s_w0.h, 16)
            g.iota(iota8[:], pattern=[[1, 8]], base=0, channel_multiplier=0,
                   allow_small_or_imprecise_dtypes=True).then_inc(s_w0b.h, 1)
        s_w0.n = 8 * 16
        s_w0b.n = 1

        @b.vector
        def _(v):
            v.memset(ones128f[:], 1.0)
            v.wait_ge(s_w0.h, 8 * 16)
            v.wait_ge(s_w0b.h, 1)
            v.tensor_copy(cidxf[:], cidxi[:]).then_inc(s_w0b.h, 1)
            v.wait_ge(s_w0b.h, 2)
            v.tensor_scalar(Mbf[:], iota8[:], cidxf[:, 0:1], None, ALU.is_equal).then_inc(s_w0b.h, 1)
        s_w0b.n = 3

    s_ed = ctx.sem("t3_ed")
    s_ln1 = [ctx.sem("t3_ln1a"), ctx.sem("t3_ln1b")]
    s_hmm = ctx.sem("t3_hmm")
    s_hev = ctx.sem("t3_hev")
    s_dmm = ctx.sem("t3_dmm")
    s_dve = ctx.sem("t3_dve")
    with nc.Block() as b:
        @b.gpsimd
        def _(g):
            for e in range(8):
                if e >= 1:
                    g.wait_ge(s_dmm.h, e * 64)
                g.dma_start(encbuf[:].rearrange("p (c w) -> p c w", c=8),
                            W_enc.rearrange("e (c p) w -> e p c w", p=128)[e]).then_inc(s_ed.h, 16)
                g.dma_start(decbuf[:].rearrange("p (rt o) -> p rt o", rt=2),
                            W_dec.rearrange("e (rt p) o -> e p rt o", p=128)[e]).then_inc(s_ed.h, 16)
                g.dma_start(r_bdec[:], ctx.dram["L3_ffn_b_dec"][e].rearrange("(o n) -> o n", o=1)).then_inc(s_ed.h, 16)

        @b.sync
        def _(q):
            for e in range(8):
                for k in range(NCK):
                    m = e * 8 + k
                    if m >= 2:
                        e2, k2 = divmod(m - 2, 8)
                        q.wait_ge(s_dve.h, e2 * 65 + (k2 + 1) * 8)
                    q.wait_ge(s_lnst[k % 2].h, (k // 2 + 1) * 16)
                    q.dma_start(lnr1[:].rearrange("p (c s f) -> p c s f", c=8, s=2)[:, :, k % 2, :],
                                lnh[:, :, k * CHUNK:(k + 1) * CHUNK]).then_inc(s_ln1[k % 2].h, 16)
        s_ed.n = 8 * 48
        s_ln1[0].n = 8 * 4 * 16
        s_ln1[1].n = 8 * 4 * 16

        @b.tensor
        def _(t):
            t.wait_ge(ctx.s_const.h, ctx.s_const.n)
            t.wait_ge(s_w0.h, 8 * 16)
            t.wait_ge(s_w0b.h, 3)
            for e in range(8):
                t.wait_ge(s_ed.h, (e + 1) * 48)
                for k in range(NCK):
                    t.wait_ge(s_ln1[k % 2].h, (e * 4 + k // 2 + 1) * 16)
                    for rt in range(2):
                        g = e * 16 + k * 2 + rt
                        if g >= 2:
                            t.wait_ge(s_hev.h, g - 1)
                        pairs = [(encbuf[:, c * 256 + rt * 128: c * 256 + (rt + 1) * 128],
                                  lnr1[:, (c * 2 + k % 2) * 512:(c * 2 + k % 2 + 1) * 512])
                                 for c in range(8)]
                        pairs.append((r_benc[0:1, e * 256 + rt * 128: e * 256 + (rt + 1) * 128],
                                      onesrow[0:1, k * CHUNK:(k + 1) * CHUNK]))
                        mm_chain(t, ctx.bank(6 + g % 2), pairs).then_inc(s_hmm.h, 1)
                    for o in range(NCH):
                        n = e * 64 + k * 8 + o
                        t.wait_ge(s_hev.h, e * 16 + k * 2 + 2)
                        if n >= 2:
                            e3, r3 = divmod(n - 2, 64)
                            t.wait_ge(s_dve.h, e3 * 65 + r3 + 1)
                        pairs = [(decbuf[:, rt * 1024 + o * 128: rt * 1024 + (o + 1) * 128],
                                  hbuf[:, (rt * 2 + k % 2) * 512:(rt * 2 + k % 2 + 1) * 512])
                                 for rt in range(2)]
                        pairs.append((r_bdec[0:1, o * 128:(o + 1) * 128],
                                      onesrow[0:1, k * CHUNK:(k + 1) * CHUNK]))
                        mm_chain(t, ctx.bank(4 + n % 2), pairs).then_inc(s_dmm.h, 1)
        s_hmm.n = 8 * 16
        s_dmm.n = 8 * 64

        @b.scalar
        def _(a):
            for e in range(8):
                for k in range(NCK):
                    for rt in range(2):
                        g = e * 16 + k * 2 + rt
                        a.wait_ge(s_hmm.h, g + 1)
                        a.activation(hbuf[:, (rt * 2 + k % 2) * 512:(rt * 2 + k % 2 + 1) * 512],
                                     ctx.bank(6 + g % 2), AF.Relu).then_inc(s_hev.h, 1)
        s_hev.n = 8 * 16

        @b.vector
        def _(v):
            for e in range(8):
                for k in range(NCK):
                    for o in range(NCH):
                        n = e * 64 + k * 8 + o
                        v.wait_ge(s_dmm.h, n + 1)
                        v.tensor_sub(dscr[:], ctx.bank(4 + n % 2),
                                     lnr1[:, (o * 2 + k % 2) * 512:(o * 2 + k % 2 + 1) * 512])
                        v.scalar_tensor_tensor(dscr[:], dscr[:], 0.0, dscr[:],
                                               ALU.bypass, ALU.mult,
                                               accum_out=dslots[:, k * 8 + o: k * 8 + o + 1]).then_inc(s_dve.h, 1)
                v.wait_ge(s_dve.h, e * 65 + 64)
                v.tensor_reduce(lossparts[:, e:e + 1], dslots[:], AX.X, ALU.add).then_inc(s_dve.h, 1)
        s_dve.n = 8 * 65

    s_p2 = ctx.sem("t3_p2")
    s_p2v = ctx.sem("t3_p2v")
    with nc.Block() as b:
        @b.tensor
        def _(t):
            t.wait_ge(s_dve.h, 8 * 65)
            t.matmul(ctx.bank(4, 1, 8), ones128f[:], lossparts[:], start=True, stop=True)
            t.matmul(ctx.bank(5, 8, 1), lossparts[:], ones128f[:], start=True, stop=True).then_inc(s_p2.h, 1)
            t.wait_ge(s_p2.h, 4)
            t.transpose(ctx.bank(6, 8, 1, BF16), selrowbf[:], ident[0:1, 0:1]).then_inc(s_p2.h, 1)
            t.wait_ge(s_p2.h, 6)
            t.matmul(ctx.bank(4, 8, 1), Mbf[:], selcolbf[:], start=True, stop=True).then_inc(s_p2.h, 1)
            t.wait_ge(s_p2.h, 8)
            t.transpose(ctx.bank(6, 1, 8, BF16), selexpbf[:], ident[0:8, 0:8]).then_inc(s_p2.h, 1)
            t.wait_ge(s_p2.h, 10)
            t.matmul(ctx.bank(4, 128, 8), onesrow[0:1, 0:128], selexprow[:], start=True, stop=True)
            t.matmul(ctx.bank(5, 1, 512), selexpbf[:], bup8[:, 0:512], start=True, stop=True)
            t.matmul(ctx.bank(7, 1, 512), selexpbf[:], bup8[:, 512:1024], start=True, stop=True).then_inc(s_p2.h, 1)
        @b.vector
        def _(v):
            v.wait_ge(s_p2.h, 1)
            v.tensor_copy(lossrow[:], ctx.bank(4, 1, 8)).then_inc(s_p2v.h, 1)
            v.wait_ge(s_p2v.h, 1)
            v.tensor_reduce(minrow[:], lossrow[:], AX.X, ALU.min).then_inc(s_p2v.h, 1)
            v.wait_ge(s_p2v.h, 2)
            v.tensor_scalar(selrow[:], lossrow[:], minrow[0:1, 0:1], None, ALU.is_le).then_inc(s_p2.h, 1)
        @b.scalar
        def _(a):
            a.wait_ge(s_p2.h, 2)
            a.copy(selrowbf[:], selrow[:]).then_inc(s_p2.h, 2)
            a.wait_ge(s_p2.h, 5)
            a.copy(selcolbf[:], ctx.bank(6, 8, 1, BF16)).then_inc(s_p2.h, 1)
            a.wait_ge(s_p2.h, 7)
            a.copy(selexpbf[:], ctx.bank(4, 8, 1)).then_inc(s_p2.h, 1)
            a.wait_ge(s_p2.h, 9)
            a.copy(selexprow[:], ctx.bank(6, 1, 8, BF16)).then_inc(s_p2.h, 1)
            a.wait_ge(s_p2.h, 11)
            a.copy(SEL[:], ctx.bank(4, 128, 8))
            a.copy(badapt[0:1, 0:512], ctx.bank(5, 1, 512))
            a.copy(badapt[0:1, 512:1024], ctx.bank(7, 1, 512)).then_inc(s_p2.h, 1)
    s_p2.n = 12

    src = ctx.dram[src_name].rearrange("(c p) t -> p c t", p=128)
    dst = ctx.dram[out_name].rearrange("(c p) t -> p c t", p=128)
    s_ln3 = [ctx.sem("t3_ln3a"), ctx.sem("t3_ln3b")]
    s_xin = ctx.sem("t3_xin")
    s_midmm = ctx.sem("t3_midmm")
    s_midev = ctx.sem("t3_midev")
    s_pe3 = ctx.sem("t3_pe3")
    s_add = ctx.sem("t3_add")
    s_out = [ctx.sem("t3_out0"), ctx.sem("t3_out1")]
    xb = ctx.sbuf("t3_xb", [128, NCH * CHUNK])
    ybr = ctx.sbuf("t3_ybr", [128, 2 * CHUNK])

    with nc.Block() as b:
        @b.sync
        def _(q):
            for k in range(NCK):
                if k >= 1:
                    q.wait_ge(s_add.h, k * 8)
                q.dma_start(xb[:].rearrange("p (c f) -> p c f", c=NCH),
                            src[:, :, k * CHUNK:(k + 1) * CHUNK]).then_inc(s_xin.h, 16)
                if k >= 2:
                    q.wait_ge(s_pe3.h, (k - 1) * 8)
                q.wait_ge(s_lnst[k % 2].h, (k // 2 + 1) * 16)
                q.dma_start(lnr3[:].rearrange("p (c s f) -> p c s f", c=8, s=2)[:, :, k % 2, :],
                            lnh[:, :, k * CHUNK:(k + 1) * CHUNK]).then_inc(s_ln3[k % 2].h, 16)
        s_xin.n = NCK * 16
        s_ln3[0].n = 4 * 16
        s_ln3[1].n = 4 * 16

        @b.tensor
        def _(t):
            t.wait_ge(s_p2.h, 12)
            for k in range(NCK):
                t.wait_ge(s_ln3[k % 2].h, (k // 2 + 1) * 16)
                for e in range(8):
                    g = k * 8 + e
                    if g >= 2:
                        t.wait_ge(s_midev.h, g - 1)
                    if k >= 1:
                        t.wait_ge(s_pe3.h, k * 8)
                    pairs = [(wdown[:, (e * 8 + c) * 64:(e * 8 + c + 1) * 64],
                              lnr3[:, (c * 2 + k % 2) * 512:(c * 2 + k % 2 + 1) * 512])
                             for c in range(8)]
                    pairs.append((r_bdown[0:1, e * 64:(e + 1) * 64],
                                  onesrow[0:1, k * CHUNK:(k + 1) * CHUNK]))
                    mm_chain(t, ctx.bank(6 + g % 2, 64, 512), pairs).then_inc(s_midmm.h, 1)
                for o in range(NCH):
                    n = k * 8 + o
                    if n >= 2:
                        t.wait_ge(s_add.h, n - 1)
                    t.wait_ge(s_midev.h, (k + 1) * 8)
                    pairs = [(wbase[:, c * 1024 + o * 128: c * 1024 + (o + 1) * 128],
                              lnr3[:, (c * 2 + k % 2) * 512:(c * 2 + k % 2 + 1) * 512])
                             for c in range(8)]
                    pairs.append((r_bbase[0:1, o * 128:(o + 1) * 128],
                                  onesrow[0:1, k * CHUNK:(k + 1) * CHUNK]))
                    for e in range(8):
                        pairs.append((wup[:, e * 1024 + o * 128: e * 1024 + (o + 1) * 128],
                                      mids[:, e * 512:(e + 1) * 512]))
                    pairs.append((badapt[0:1, o * 128:(o + 1) * 128],
                                  onesrow[0:1, k * CHUNK:(k + 1) * CHUNK]))
                    mm_chain(t, ctx.bank(4 + n % 2), pairs).then_inc(s_pe3.h, 1)
        s_midmm.n = NCK * 8
        s_pe3.n = NCK * 8

        @b.scalar
        def _(a):
            for k in range(NCK):
                for e in range(8):
                    g = k * 8 + e
                    a.wait_ge(s_midmm.h, g + 1)
                    a.activation(mids[:, e * 512:(e + 1) * 512], ctx.bank(6 + g % 2, 64, 512),
                                 AF.Relu, scale=SEL[0:64, e:e + 1]).then_inc(s_midev.h, 1)
        s_midev.n = NCK * 8

        @b.vector
        def _(v):
            for k in range(NCK):
                v.wait_ge(s_xin.h, (k + 1) * 16)
                for o in range(NCH):
                    n = k * 8 + o
                    if n >= 2:
                        v.wait_ge(s_out[n % 2].h, ((n - 2) // 2 + 1) * 16)
                    v.wait_ge(s_pe3.h, n + 1)
                    v.tensor_add(ybr[:, (n % 2) * CHUNK:(n % 2 + 1) * CHUNK],
                                 xb[:, o * CHUNK:(o + 1) * CHUNK],
                                 ctx.bank(4 + n % 2)).then_inc(s_add.h, 1)
        s_add.n = NCK * 8

        @b.gpsimd
        def _(g):
            for k in range(NCK):
                for o in range(NCH):
                    n = k * 8 + o
                    g.wait_ge(s_add.h, n + 1)
                    g.dma_start(dst[:, o, k * CHUNK:(k + 1) * CHUNK],
                                ybr[:, (n % 2) * CHUNK:(n % 2 + 1) * CHUNK]).then_inc(s_out[n % 2].h, 16)
            g.wait_ge(s_out[0].h, NCK * 4 * 16)
            g.wait_ge(s_out[1].h, NCK * 4 * 16)
        s_out[0].n = NCK * 4 * 16
        s_out[1].n = NCK * 4 * 16

    if "t3dbg" in ctx.dram:
        s_dbg = ctx.sem("t3_dbg")
        with nc.Block() as b:
            @b.gpsimd
            def _(g):
                g.wait_ge(s_p2.h, 12)
                g.dma_start(ctx.dram["t3dbg"][0:128, 0:8], lossparts[:]).then_inc(s_dbg.h, 16)
                g.dma_start(ctx.dram["t3dbg"][0:128, 8:16], SEL[:]).then_inc(s_dbg.h, 16)
                g.dma_start(ctx.dram["t3dbg"][0:1, 16:24], lossrow[:]).then_inc(s_dbg.h, 16)
                g.dma_start(ctx.dram["t3dbg"][0:1, 24:32], selrow[:]).then_inc(s_dbg.h, 16)
                g.dma_start(ctx.dram["t3dbg"][0:1, 32:33], minrow[:]).then_inc(s_dbg.h, 16)
                g.wait_ge(s_dbg.h, 80)
    return s_out[0]


def build_launch_l3ffn():
    nc = bass.Bass("TRN2", target_bir_lowering=False, detect_race_conditions=False)
    ctx = Ctx(nc)
    I32 = mybir.dt.int32
    ctx.din("xT", [D, T], F32)
    ctx.din("L3_ffn_W_base", [1024, 1024]);  ctx.din("L3_ffn_b_base", [1024])
    ctx.din("L3_ffn_W_enc", [8, 1024, 256]); ctx.din("L3_ffn_b_enc", [8, 256])
    ctx.din("L3_ffn_W_dec", [8, 256, 1024]); ctx.din("L3_ffn_b_dec", [8, 1024])
    ctx.din("L3_ffn_W_down", [8, 1024, 64]); ctx.din("L3_ffn_b_down", [8, 64])
    ctx.din("L3_ffn_W_up", [8, 64, 1024]);   ctx.din("L3_ffn_b_up", [8, 1024])
    ctx.din("L3_ffn_connected_idx", [8], I32)
    ctx.dout("yT", [D, T], F32)
    ctx.dout("t3dbg", [128, 64], F32)
    ctx.dint("lnh", [D, T], BF16)
    ctx.load_consts()
    RING = 2
    lnx = ctx.sbuf("lnring", [128, NCH * RING * CHUNK], BF16)
    lnx_at = lambda c, k: lnx[:, (c * RING + k % RING) * CHUNK:
                              (c * RING + k % RING + 1) * CHUNK]
    lnx_view = lambda k: lnx[:].rearrange("p (c s f) -> p c s f", c=NCH, s=RING)[:, :, k % RING, :]
    s_ln, s_lnst = emit_ln(ctx, "xT", lnx_at, "ln0", bank0=0, ring=RING, to_hbm="lnh",
                           lnx_view=lnx_view)
    emit_l3ffn(ctx, s_lnst, "lnh", "xT", "yT")
    return ctx



def emit_l1ffn(ctx, s_lnst, lnh_name, src_name, out_name):
    nc = ctx.nc
    ident = ctx.sb["ident"]
    onesrow = ctx.sb["onesrow"]
    lnh = ctx.dram[lnh_name].rearrange("(c p) t -> p c t", p=128)
    NT = T // 128

    gatew = ctx.sbuf("m1_gatew", [128, 8 * 8], BF16)
    r_gateb = ctx.sbuf("m1_gateb", [1, 8], BF16)
    lgT = ctx.sbuf("m1_lgT", [8, T], BF16)
    LGall = ctx.sbuf("m1_LGall", [128, NT * 8])
    ex = ctx.sbuf("m1_ex", [128, NT * 8])
    pp = ctx.sbuf("m1_pp", [128, NT * 8])
    msk = ctx.sbuf("m1_msk", [128, NT * 8])
    keep = ex
    cwf = LGall
    cwb = ctx.sbuf("m1_cwb", [128, NT * 8], BF16)
    mx32 = ctx.sbuf("m1_mx32", [128, NT])
    sm32 = ctx.sbuf("m1_sm32", [128, NT])
    rc32 = ctx.sbuf("m1_rc32", [128, NT])
    m132 = ctx.sbuf("m1_m132", [128, NT])
    m232 = ctx.sbuf("m1_m232", [128, NT])
    cw8 = ctx.sbuf("m1_cw8", [8, T], BF16)
    lnr = ctx.sbuf("m1_lnr", [128, NCH * 2 * CHUNK], BF16)
    rw1f = ctx.sbuf("m1_rw1f", [128, 8 * 2048], BF16)
    rw2b = ctx.sbuf("m1_rw2", [128, 8 * 1024], BF16)
    r_rb1 = ctx.sbuf("m1_rb1", [1, 2048], BF16)
    rb2s = ctx.sbuf("m1_rb2", [8, 1024], BF16)
    r_sb2 = ctx.sbuf("m1_sb2", [1, 1024], BF16)
    hbuf = ctx.sbuf("m1_hbuf", [128, 8 * CHUNK], BF16)
    repsb = ctx.sbuf("m1_repsb", [128, CHUNK], BF16)
    tmpsl = ctx.sbuf("m1_tmpsl", [128, CHUNK], BF16)
    oev = ctx.sbuf("m1_oev", [128, 2 * CHUNK], BF16)
    c_sel8 = ctx.sb["sel8"]
    acch = ctx.dram["acch"].rearrange("(c p) t -> p c t", p=128)

    NE = 9

    def b8(v):
        return v[:].rearrange("p (t e) -> p t e", e=8)

    def bc32(v):
        return v[:].rearrange("p (t o) -> p t o", o=1).broadcast_to([128, NT, 8])

    s_w0 = ctx.sem("m1_w0")
    with nc.Block() as b:
        @b.gpsimd
        def _(g):
            g.dma_start(gatew[:].rearrange("p (c w) -> p c w", c=8),
                        ctx.dram["L1_ffn_gate_w"].rearrange("(c p) w -> p c w", p=128)).then_inc(s_w0.h, 16)
            g.dma_start(r_gateb[:], ctx.dram["L1_ffn_gate_b"].rearrange("(o n) -> o n", o=1)).then_inc(s_w0.h, 16)
            g.dma_start(rb2s[:], ctx.dram["L1_ffn_rb2"]).then_inc(s_w0.h, 16)
            g.dma_start(r_sb2[:], ctx.dram["L1_ffn_sb2"].rearrange("e w -> (e w)").rearrange("(o n) -> o n", o=1)).then_inc(s_w0.h, 16)
        s_w0.n = 4 * 16

    s_lr = [ctx.sem("m1_lr0"), ctx.sem("m1_lr1")]
    s_gmm = ctx.sem("m1_gmm")
    s_gev = ctx.sem("m1_gev")
    s_tr1 = ctx.sem("m1_tr1")
    s_tr1e = ctx.sem("m1_tr1e")
    s_v = ctx.sem("m1_v")
    s_cwbf = ctx.sem("m1_cwbf")
    s_tr2 = ctx.sem("m1_tr2")
    s_tr2e = ctx.sem("m1_tr2e")
    with nc.Block() as b:
        @b.sync
        def _(q):
            for k in range(NCK):
                q.wait_ge(s_lnst[k % 2].h, (k // 2 + 1) * 16)
                if k >= 2:
                    q.wait_ge(s_gmm.h, k - 1)
                q.dma_start(lnr[:].rearrange("p (c s f) -> p c s f", c=NCH, s=2)[:, :, k % 2, :],
                            lnh[:, :, k * CHUNK:(k + 1) * CHUNK]).then_inc(s_lr[k % 2].h, 16)
        s_lr[0].n = 4 * 16
        s_lr[1].n = 4 * 16

        @b.tensor
        def _(t):
            t.wait_ge(ctx.s_const.h, ctx.s_const.n)
            t.wait_ge(s_w0.h, s_w0.n)
            for k in range(NCK):
                t.wait_ge(s_lr[k % 2].h, (k // 2 + 1) * 16)
                if k >= 2:
                    t.wait_ge(s_gev.h, k - 1)
                pairs = [(gatew[:, c * 8:(c + 1) * 8],
                          lnr[:, (c * 2 + k % 2) * CHUNK:(c * 2 + k % 2 + 1) * CHUNK])
                         for c in range(NCH)]
                pairs.append((r_gateb[0:1, 0:8], onesrow[0:1, k * CHUNK:(k + 1) * CHUNK]))
                mm_chain(t, ctx.bank(k % 2, 8, 512), pairs).then_inc(s_gmm.h, 1)
            for j in range(NT):
                t.wait_ge(s_gev.h, j // 4 + 1)
                if j >= 2:
                    t.wait_ge(s_tr1e.h, j - 1)
                t.transpose(ctx.bank(2 + j % 2, 128, 8, BF16),
                            lgT[0:8, j * 128:(j + 1) * 128], ident[0:8, 0:8]).then_inc(s_tr1.h, 1)
            for j in range(NT):
                t.wait_ge(s_cwbf.h, 1)
                if j >= 2:
                    t.wait_ge(s_tr2e.h, j - 1)
                t.transpose(ctx.bank(4 + j % 2, 8, 128, BF16),
                            cwb[:, j * 8:(j + 1) * 8], ident[:]).then_inc(s_tr2.h, 1)
        s_gmm.n = NCK
        s_tr1.n = NT
        s_tr2.n = NT

        @b.scalar
        def _(a):
            for k in range(NCK):
                a.wait_ge(s_gmm.h, k + 1)
                a.copy(lgT[0:8, k * CHUNK:(k + 1) * CHUNK], ctx.bank(k % 2, 8, 512)).then_inc(s_gev.h, 1)
            for j in range(NT):
                a.wait_ge(s_tr1.h, j + 1)
                a.copy(LGall[:, j * 8:(j + 1) * 8], ctx.bank(2 + j % 2, 128, 8, BF16)).then_inc(s_tr1e.h, 1)
            a.wait_ge(s_v.h, 2)
            a.activation(ex[:], ex[:], AF.Exp).then_inc(s_v.h, 1)
            for j in range(NT):
                a.wait_ge(s_tr2.h, j + 1)
                a.copy(cw8[0:8, j * 128:(j + 1) * 128], ctx.bank(4 + j % 2, 8, 128, BF16)).then_inc(s_tr2e.h, 1)
        s_gev.n = NCK
        s_tr1e.n = NT
        s_tr2e.n = NT

        @b.vector
        def _(v):
            v.wait_ge(s_tr1e.h, NT)
            v.tensor_reduce(mx32[:], b8(LGall), AX.X, ALU.max).then_inc(s_v.h, 1)
            v.wait_ge(s_v.h, 1)
            v.tensor_sub(b8(ex), b8(LGall), bc32(mx32)).then_inc(s_v.h, 1)
            v.wait_ge(s_v.h, 3)
            v.tensor_reduce(sm32[:], b8(ex), AX.X, ALU.add).then_inc(s_v.h, 1)
            v.wait_ge(s_v.h, 4)
            v.reciprocal(rc32[:], sm32[:]).then_inc(s_v.h, 1)
            v.wait_ge(s_v.h, 5)
            v.tensor_mul(b8(pp), b8(ex), bc32(rc32)).then_inc(s_v.h, 1)
            v.wait_ge(s_v.h, 6)
            v.tensor_reduce(m132[:], b8(pp), AX.X, ALU.max).then_inc(s_v.h, 1)
            v.wait_ge(s_v.h, 7)
            v.tensor_tensor(b8(msk), b8(pp), bc32(m132), ALU.is_ge).then_inc(s_v.h, 1)
            v.wait_ge(s_v.h, 8)
            v.scalar_tensor_tensor(msk[:], msk[:], -1e30, pp[:], ALU.mult, ALU.add).then_inc(s_v.h, 1)
            v.wait_ge(s_v.h, 9)
            v.tensor_reduce(m232[:], b8(msk), AX.X, ALU.max).then_inc(s_v.h, 1)
            v.wait_ge(s_v.h, 10)
            v.tensor_tensor(b8(keep), b8(pp), bc32(m232), ALU.is_ge).then_inc(s_v.h, 1)
            v.wait_ge(s_v.h, 11)
            v.tensor_mul(cwf[:], pp[:], keep[:]).then_inc(s_v.h, 1)
            v.wait_ge(s_v.h, 12)
            v.tensor_copy(cwb[:], cwf[:]).then_inc(s_cwbf.h, 1)
        s_v.n = 12
        s_cwbf.n = 1

    s_w1 = ctx.sem("m1_w1")
    s_lnr2 = [ctx.sem("m1_ln2a"), ctx.sem("m1_ln2b")]
    s_ab = ctx.sem("m1_ab")
    s_rep = ctx.sem("m1_rep")
    s_repe = ctx.sem("m1_repe")
    s_sw = ctx.sem("m1_sw")
    s_sil = ctx.sem("m1_sil")
    s_omm = ctx.sem("m1_omm")
    s_acc = ctx.sem("m1_acc")
    s_acd = [ctx.sem("m1_acd0"), ctx.sem("m1_acd1")]

    def s_acd_wait(no):
        return ((no - 2) // 2 + 1) * 16

    rw1d = ctx.dram["L1_ffn_rw1"]
    rw2d = ctx.dram["L1_ffn_rw2"]
    sw1d = ctx.dram["L1_ffn_sw1"]
    sw2d = ctx.dram["L1_ffn_sw2"]
    rb1d = ctx.dram["L1_ffn_rb1"]
    sb1d = ctx.dram["L1_ffn_sb1"]

    with nc.Block() as b:
        @b.gpsimd
        def _(g):
            for e in range(NE):
                w1 = rw1d[e] if e < 8 else sw1d[0]
                w2 = rw2d[e] if e < 8 else sw2d[0]
                b1 = rb1d[e] if e < 8 else sb1d[0]
                if e >= 1:
                    g.wait_ge(s_omm.h, e * 64)
                g.dma_start(rw2b[:].rearrange("p (c w) -> p c w", c=8),
                            w2.rearrange("(c p) w -> p c w", p=128)).then_inc(s_w1.h, 16)
                if e >= 1:
                    g.wait_ge(s_ab.h, e * 128)
                g.dma_start(rw1f[:].rearrange("p (c w) -> p c w", c=8),
                            w1.rearrange("(c p) w -> p c w", p=128)).then_inc(s_w1.h, 16)
                g.dma_start(r_rb1[:], b1.rearrange("(o n) -> o n", o=1)).then_inc(s_w1.h, 16)
                for k in range(NCK):
                    for o in range(NCH):
                        no = e * 64 + k * 8 + o
                        g.wait_ge(s_acc.h, no + 1)
                        g.dma_start(acch[:, o, k * CHUNK:(k + 1) * CHUNK],
                                    oev[:, (no % 2) * CHUNK:(no % 2 + 1) * CHUNK],
                                    accum_op=(ALU.bypass if e == 0 else ALU.add)).then_inc(s_acd[no % 2].h, 16)
        s_w1.n = NE * 48
        s_acd[0].n = NE * NCK * 4 * 16
        s_acd[1].n = NE * NCK * 4 * 16

        @b.sync
        def _(q):
            for e in range(NE):
                for k in range(NCK):
                    m = e * 8 + k
                    if m >= 2:
                        e2, k2 = divmod(m - 2, 8)
                        q.wait_ge(s_ab.h, (e2 * 8 + k2) * 16 + 16)
                    q.wait_ge(s_lnst[k % 2].h, (k // 2 + 1) * 16)
                    q.dma_start(lnr[:].rearrange("p (c s f) -> p c s f", c=NCH, s=2)[:, :, k % 2, :],
                                lnh[:, :, k * CHUNK:(k + 1) * CHUNK]).then_inc(s_lnr2[k % 2].h, 16)
        s_lnr2[0].n = NE * 4 * 16
        s_lnr2[1].n = NE * 4 * 16

        @b.tensor
        def _(t):
            for e in range(NE):
                t.wait_ge(s_w1.h, (e + 1) * 48)
                for k in range(NCK):
                    t.wait_ge(s_lnr2[k % 2].h, (e * 4 + k // 2 + 1) * 16)
                    if e < 8:
                        if e * 8 + k >= 1:
                            t.wait_ge(s_repe.h, e * 8 + k)
                        t.matmul(ctx.bank(2), c_sel8[:, e * 128:(e + 1) * 128],
                                 cw8[0:8, k * CHUNK:(k + 1) * CHUNK], start=True, stop=True).then_inc(s_rep.h, 1)
                    for j in range(8):
                        g2 = e * 64 + k * 8 + j
                        if g2 >= 2:
                            t.wait_ge(s_sw.h, (g2 - 1) * 2)
                        if e < 8 and j == 1:
                            t.wait_ge(s_repe.h, e * 8 + k + 1)
                        pairs_a = [(rw1f[:, c * 2048 + j * 128: c * 2048 + (j + 1) * 128],
                                    lnr[:, (c * 2 + k % 2) * CHUNK:(c * 2 + k % 2 + 1) * CHUNK])
                                   for c in range(NCH)]
                        pairs_a.append((r_rb1[0:1, j * 128:(j + 1) * 128],
                                        onesrow[0:1, k * CHUNK:(k + 1) * CHUNK]))
                        mm_chain(t, ctx.bank(0 + 2 * (j % 2)), pairs_a)
                        pairs_b = [(rw1f[:, c * 2048 + 1024 + j * 128: c * 2048 + 1024 + (j + 1) * 128],
                                    lnr[:, (c * 2 + k % 2) * CHUNK:(c * 2 + k % 2 + 1) * CHUNK])
                                   for c in range(NCH)]
                        pairs_b.append((r_rb1[0:1, 1024 + j * 128: 1024 + (j + 1) * 128],
                                        onesrow[0:1, k * CHUNK:(k + 1) * CHUNK]))
                        mm_chain(t, ctx.bank(1 + 2 * (j % 2)), pairs_b).then_inc(s_ab.h, 2)
                    for o in range(NCH):
                        no = e * 64 + k * 8 + o
                        t.wait_ge(s_sw.h, (e * 64 + k * 8 + 8) * 2)
                        if no >= 4:
                            t.wait_ge(s_acc.h, no - 3)
                        pairs = [(rw2b[:, c * 1024 + o * 128: c * 1024 + (o + 1) * 128],
                                  hbuf[:, c * CHUNK:(c + 1) * CHUNK]) for c in range(NCH)]
                        mm_chain(t, ctx.bank(4 + no % 4), pairs).then_inc(s_omm.h, 1)
        s_ab.n = NE * NCK * 8 * 2
        s_rep.n = 8 * NCK
        s_omm.n = NE * NCK * 8

        @b.scalar
        def _(a):
            for e in range(NE):
                for k in range(NCK):
                    if e < 8:
                        a.wait_ge(s_rep.h, e * 8 + k + 1)
                        a.copy(repsb[:], ctx.bank(2)).then_inc(s_repe.h, 1)
                    for j in range(8):
                        g2 = e * 64 + k * 8 + j
                        a.wait_ge(s_ab.h, (g2 + 1) * 2)
                        if g2 >= 1:
                            a.wait_ge(s_sw.h, g2 * 2)
                        a.activation(tmpsl[:], ctx.bank(0 + 2 * (j % 2)), AF.Silu).then_inc(s_sil.h, 1)
                    for o in range(NCH):
                        no = e * 64 + k * 8 + o
                        a.wait_ge(s_omm.h, no + 1)
                        if no >= 2:
                            a.wait_ge(s_acd[no % 2].h, s_acd_wait(no))
                        a.copy(oev[:, (no % 2) * CHUNK:(no % 2 + 1) * CHUNK],
                               ctx.bank(4 + no % 4)).then_inc(s_acc.h, 1)
        s_repe.n = 8 * NCK
        s_sil.n = NE * NCK * 8
        s_acc.n = NE * NCK * 8

        @b.vector
        def _(v):
            for e in range(NE):
                for k in range(NCK):
                    m = e * 8 + k
                    if m >= 1:
                        v.wait_ge(s_omm.h, m * 8)
                    for j in range(8):
                        g2 = e * 64 + k * 8 + j
                        v.wait_ge(s_sil.h, g2 + 1)
                        if e < 8:
                            v.wait_ge(s_repe.h, e * 8 + k + 1)
                            v.tensor_mul(tmpsl[:], tmpsl[:], repsb[:]).then_inc(s_sw.h, 1)
                        else:
                            v.engine_nop().then_inc(s_sw.h, 1)
                        v.tensor_mul(hbuf[:, j * CHUNK:(j + 1) * CHUNK],
                                     tmpsl[:], ctx.bank(1 + 2 * (j % 2))).then_inc(s_sw.h, 1)
        s_sw.n = NE * NCK * 8 * 2

    def mm_fn(t, k, o, bank_ap):
        pairs = [(rb2s[:, o * 128:(o + 1) * 128], cw8[0:8, k * CHUNK:(k + 1) * CHUNK]),
                 (r_sb2[0:1, o * 128:(o + 1) * 128], onesrow[0:1, k * CHUNK:(k + 1) * CHUNK])]
        return mm_chain(t, bank_ap, pairs)

    return emit_residual_out(ctx, "m1f", src_name, out_name,
                             s_acd[0], None, mm_fn, extra_stream="acch",
                             bufs={"xb": ctx.ln_bufs()["xb"], "eb": ctx.ln_bufs()["xbf"]})



def build_launch_l0ffn():
    nc = bass.Bass("TRN2", target_bir_lowering=False, detect_race_conditions=False)
    ctx = Ctx(nc)
    ctx.din("xT", [D, T], F32)
    ctx.din("L0_ffn_W", [1024, 1024])
    ctx.din("L0_ffn_b", [1024])
    ctx.din("L0_ffn_lora_down", [4, 100])
    ctx.din("L0_ffn_lora_up", [50, 4])
    ctx.din("L0_ffn_down_aux", [100, 1024])
    ctx.din("L0_ffn_up_aux", [1024, 50])
    ctx.dout("yT", [D, T], F32)
    ctx.load_consts()
    lnx = ctx.sbuf("lnx0", [128, NCH * T], BF16)
    lnx_at = lambda c, k: lnx[:, c * T + k * CHUNK: c * T + (k + 1) * CHUNK]
    w = emit_l0ffn_weights(ctx)
    s_ln = emit_ln(ctx, "xT", lnx_at, "ln0", bank0=0)
    emit_l0ffn_main(ctx, w, lnx_at, s_ln, "xT", "yT")
    return ctx


def build_launch_l0ffn_weights_only():
    nc = bass.Bass("TRN2", target_bir_lowering=False, detect_race_conditions=False)
    ctx = Ctx(nc)
    for nm, shp in [("L0_ffn_W", [1024, 1024]), ("L0_ffn_b", [1024]),
                    ("L0_ffn_lora_down", [4, 100]), ("L0_ffn_lora_up", [50, 4]),
                    ("L0_ffn_down_aux", [100, 1024]), ("L0_ffn_up_aux", [1024, 50])]:
        ctx.din(nm, shp)
    ctx.din("xT", [D, T], F32)
    ctx.dout("wT", [128, NCH * 1024], F32)
    ctx.load_consts()
    w = emit_l0ffn_weights(ctx)
    s_out = ctx.sem("wout")
    with ctx.nc.Block() as b:
        @b.gpsimd
        def _(g):
            g.wait_ge(w["s_trev"].h, w["s_trev"].n)
            g.dma_start(ctx.dram["wT"], w["lhsT"][:]).then_inc(s_out.h, 16)
            g.wait_ge(s_out.h, 16)
    return ctx


def build_launch_ln_only():
    nc = bass.Bass("TRN2", target_bir_lowering=False, detect_race_conditions=False)
    ctx = Ctx(nc)
    ctx.din("xT", [D, T], F32)
    ctx.dout("yT", [D, T], F32)
    ctx.load_consts()
    lnx = ctx.sbuf("lnx0", [128, NCH * T], BF16)
    lnx_at = lambda c, k: lnx[:, c * T + k * CHUNK: c * T + (k + 1) * CHUNK]
    s_ln = emit_ln(ctx, "xT", lnx_at, "ln0", bank0=0)
    dst = ctx.dram["yT"].rearrange("(c p) t -> p c t", p=128)
    s_out = ctx.sem("lnout")
    with nc.Block() as b:
        @b.gpsimd
        def _(g):
            g.wait_ge(s_ln.h, s_ln.n)
            g.dma_start(dst, lnx[:].rearrange("p (c t) -> p c t", c=NCH)).then_inc(s_out.h, 16)
            g.wait_ge(s_out.h, 16)
    return ctx


def build_launch_l1ffn():
    nc = bass.Bass("TRN2", target_bir_lowering=False, detect_race_conditions=False)
    ctx = Ctx(nc)
    ctx.din("xT", [D, T], F32)
    ctx.din("L1_ffn_gate_w", [1024, 8]); ctx.din("L1_ffn_gate_b", [8])
    ctx.din("L1_ffn_sw1", [1, 1024, 2048]); ctx.din("L1_ffn_sb1", [1, 2048])
    ctx.din("L1_ffn_sw2", [1, 1024, 1024]); ctx.din("L1_ffn_sb2", [1, 1024])
    ctx.din("L1_ffn_rw1", [8, 1024, 2048]); ctx.din("L1_ffn_rb1", [8, 2048])
    ctx.din("L1_ffn_rw2", [8, 1024, 1024]); ctx.din("L1_ffn_rb2", [8, 1024])
    ctx.dout("yT", [D, T], F32)
    ctx.dint("lnh", [D, T], BF16)
    ctx.dint("acch", [D, T], BF16)
    ctx.load_consts(extra=("sel8",))
    RING = 2
    lnx = ctx.sbuf("lnring", [128, NCH * RING * CHUNK], BF16)
    lnx_at = lambda c, k: lnx[:, (c * RING + k % RING) * CHUNK:
                              (c * RING + k % RING + 1) * CHUNK]
    lnx_view = lambda k: lnx[:].rearrange("p (c s f) -> p c s f", c=NCH, s=RING)[:, :, k % RING, :]
    s_ln, s_lnst = emit_ln(ctx, "xT", lnx_at, "ln1f", bank0=0, ring=RING, to_hbm="lnh",
                           lnx_view=lnx_view)
    emit_l1ffn(ctx, s_lnst, "lnh", "xT", "yT")
    return ctx


def build_launch_l2attn_a():
    nc = bass.Bass("TRN2", target_bir_lowering=False, detect_race_conditions=False)
    ctx = Ctx(nc)
    ctx.din("xT", [D, T], F32)
    ctx.din("L2_attn_conv_dw_w", [1024, 1, 3]); ctx.din("L2_attn_conv_dw_b", [1024])
    ctx.din("L2_attn_conv_pw_w", [1024, 1024]); ctx.din("L2_attn_conv_pw_b", [1024])
    ctx.din("L2_attn_conv_norm_g", [1024]); ctx.din("L2_attn_gru_norm_g", [1024])
    ctx.din("L2_attn_gru_w", [2048, 1024])
    ctx.dout("x2T", [D, T], F32)
    ctx.dint("lnh", [D, T], BF16)
    ctx.dint("rn1h", [D, T], BF16)
    ctx.dint("rn2h", [D, T], BF16)
    ctx.dint("x1h", [D, T], F32)
    ctx.load_consts()
    RING = 2
    lnx = ctx.sbuf("lnring", [128, NCH * RING * CHUNK], BF16)
    lnx_at = lambda c, k: lnx[:, (c * RING + k % RING) * CHUNK:
                              (c * RING + k % RING + 1) * CHUNK]
    lnx_view = lambda k: lnx[:].rearrange("p (c s f) -> p c s f", c=NCH, s=RING)[:, :, k % RING, :]
    s_ln, s_lnst = emit_ln(ctx, "xT", lnx_at, "ln2a", bank0=0, ring=RING, to_hbm="lnh",
                           lnx_view=lnx_view)
    emit_l2attn_a(ctx, s_lnst, "lnh", "x2T")
    return ctx



def kernel(**inputs):
    x = np.asarray(inputs["L0_attn_x"])
    ctx = build_launch_fused(9)
    consts = {k: np.asarray(v) for k, v in CONSTS.items()}
    big = {k: np.asarray(v) for k, v in BIGCONSTS.items()}
    ims = []
    for i in range(8):
        im = dict(consts)
        im.update(big)
        im["xT"] = np.ascontiguousarray(x[i].T)
        for k in FUSED_WKEYS:
            im[k] = np.asarray(inputs[k])
        ims.append(im)
    res = run_bass_kernel_spmd(ctx.nc, ims, core_ids=list(range(8)))
    return np.stack([res.results[i]["yT"].T for i in range(8)]).astype(np.float32)



def emit_l3attn(ctx, s_lnst, lnh_name, src_name, out_name):
    nc = ctx.nc
    ident = ctx.sb["ident"]
    onesrow = ctx.sb["onesrow"]
    ones128 = ctx.sb["ones128"]
    c_triu = ctx.sb["triu"]
    c_triuf = ctx.sb["triuf"]
    lnh = ctx.dram[lnh_name].rearrange("(c p) t -> p c t", p=128)
    NT = T // 128
    EPS = 1e-6

    wqT = ctx.sbuf("a3_wqT", [128, 8 * 256], BF16)
    wkT = ctx.sbuf("a3_wkT", [128, 8 * 256], BF16)
    wvT = ctx.sbuf("a3_wvT", [128, 8 * 256], BF16)
    woT = ctx.sbuf("a3_woT", [128, 2 * 1024], BF16)
    wrow = ctx.sbuf("a3_wrow", [128, 2048], BF16)
    lnr = ctx.sbuf("a3_lnr", [128, NCH * 2 * CHUNK], BF16)
    Qh = ctx.sbuf("a3_Q", [128, NT * 256], BF16)
    Kh = ctx.sbuf("a3_K", [128, NT * 256], BF16)
    Vh = ctx.sbuf("a3_V", [128, NT * 256], BF16)
    YT = ctx.sbuf("a3_YT", [128, 2 * T], BF16)
    t1s = ctx.sbuf("a3_t1", [128, 256])
    t2s = ctx.sbuf("a3_t2", [128, 256])
    pscr = ctx.sbuf("a3_pscr", [128, 256])
    psq = ctx.sbuf("a3_psq", [128, 256])
    n2 = ctx.sbuf("a3_n2", [128, 4])
    nl = ctx.sbuf("a3_nl", [128, 4])
    dnm = ctx.sbuf("a3_dnm", [128, 4])
    rcpn = ctx.sbuf("a3_rcpn", [128, 4])
    S_sb = ctx.sbuf("a3_Ssb", [64, 64], BF16)
    S_f32 = ctx.sbuf("a3_Sf32", [64, 64])
    kst = ctx.sbuf("a3_kst", [1, 64], BF16)
    kst_f32 = ctx.sbuf("a3_kstf", [1, 64])
    Pm = ctx.sbuf("a3_Pm", [128, 128], BF16)
    KcT = ctx.sbuf("a3_KcT", [64, 128], BF16)
    QcT = ctx.sbuf("a3_QcT", [64, 128], BF16)
    dp = ctx.sbuf("a3_dp", [128, 64])
    den = ctx.sbuf("a3_den", [128, 1])
    denf = ctx.sbuf("a3_denf", [128, 1])
    denr = ctx.sbuf("a3_denr", [128, 1], BF16)
    denrT = ctx.sbuf("a3_denrT", [1, 128], BF16)
    Oev = ctx.sbuf("a3_Oev", [64, 128], BF16)

    s_w0 = ctx.sem("a3_w0")
    s_tp = ctx.sem("a3_tp")
    s_te = ctx.sem("a3_te")

    def subtiles(kind):
        subs = []
        if kind == "qkv":
            for c2 in range(2):
                for i in range(8):
                    subs.append((c2 * 1024 + i * 128, i * 256 + c2 * 128))
        else:
            for o8 in range(8):
                for c2 in range(2):
                    subs.append((o8 * 256 + c2 * 128, c2 * 1024 + o8 * 128))
        return subs

    specs = [("L3_attn_Wq", wqT, "qkv"), ("L3_attn_Wk", wkT, "qkv"),
             ("L3_attn_Wv", wvT, "qkv"), ("L3_attn_Wo", woT, "wo")]
    with nc.Block() as b:
        for si, (wname, dstT, kind) in enumerate(specs):
            @b.gpsimd
            def _(g, wname=wname):
                s_tp.wait(g)
                s_w0.inc(g.dma_start(wrow[:].rearrange("p (c w) -> p c w", c=2),
                                     ctx.dram[wname].rearrange("(c p) w -> p c w", p=128)), 16)

            @b.tensor
            def _(t, si=si, kind=kind):
                if si == 0:
                    t.wait_ge(ctx.s_const.h, ctx.s_const.n)
                s_w0.wait(t)
                te_base = s_te.n
                for n_, (srcoff, dstoff) in enumerate(subtiles(kind)):
                    if n_ >= 2:
                        t.wait_ge(s_te.h, te_base + n_ - 1)
                    s_tp.inc(t.transpose(ctx.bank(n_ % 2, 128, 128, BF16),
                                         wrow[:, srcoff:srcoff + 128], ident[:]))

            @b.scalar
            def _(a, dstT=dstT, kind=kind):
                tp_base = s_tp.n - 16
                for n_, (srcoff, dstoff) in enumerate(subtiles(kind)):
                    a.wait_ge(s_tp.h, tp_base + n_ + 1)
                    s_te.inc(a.copy(dstT[:, dstoff:dstoff + 128],
                                    ctx.bank(n_ % 2, 128, 128, BF16)))

    s_lr = [ctx.sem("a3_lr0"), ctx.sem("a3_lr1")]
    s_mm = ctx.sem("a3_mm")
    s_dv = ctx.sem("a3_dv")
    s_ac = ctx.sem("a3_ac")
    DVC = 13
    ACC_ = 8

    def relR(g):
        jt, wi = divmod(g, 3)
        return jt * DVC + (wi * 6 + 1 if wi < 2 else 13)

    with nc.Block() as b:
        @b.sync
        def _(q):
            for k in range(NCK):
                q.wait_ge(s_lnst[k % 2].h, (k // 2 + 1) * 16)
                if k >= 2:
                    q.wait_ge(s_mm.h, (k - 1) * 12)
                q.dma_start(lnr[:].rearrange("p (c s f) -> p c s f", c=NCH, s=2)[:, :, k % 2, :],
                            lnh[:, :, k * CHUNK:(k + 1) * CHUNK]).then_inc(s_lr[k % 2].h, 16)
        s_lr[0].n = 4 * 16
        s_lr[1].n = 4 * 16

        @b.tensor
        def _(t):
            s_te.wait(t)
            for k in range(NCK):
                t.wait_ge(s_lr[k % 2].h, (k // 2 + 1) * 16)
                for j4 in range(4):
                    jt = k * 4 + j4
                    for wi, wT in enumerate([wqT, wkT, wvT]):
                        g = jt * 3 + wi
                        if g >= 2:
                            t.wait_ge(s_dv.h, relR(g - 2))
                        pairs = [(lnr[:, (c * 2 + k % 2) * CHUNK + j4 * 128:
                                       (c * 2 + k % 2) * CHUNK + (j4 + 1) * 128],
                                  wT[:, c * 256:(c + 1) * 256]) for c in range(NCH)]
                        mm_chain(t, ctx.bank(g % 2, 128, 256), pairs).then_inc(s_mm.h, 1)
        s_mm.n = NCK * 12

        @b.vector
        def _(v):
            for jt in range(NT):
                for wi in range(2):
                    g = jt * 3 + wi
                    dvb = jt * DVC + wi * 6
                    acb = jt * ACC_ + wi * 4
                    dst = Qh if wi == 0 else Kh
                    v.wait_ge(s_mm.h, g + 1)
                    v.tensor_scalar_min(t1s[:], ctx.bank(g % 2, 128, 256), 0.0)
                    v.tensor_scalar_max(t2s[:], ctx.bank(g % 2, 128, 256), 0.0).then_inc(s_dv.h, 1)
                    v.wait_ge(s_ac.h, acb + 1)
                    v.tensor_add(pscr[:], t1s[:], t2s[:]).then_inc(s_dv.h, 1)
                    v.wait_ge(s_ac.h, acb + 2)
                    v.tensor_reduce(n2[:], psq[:].rearrange("p (h d) -> p h d", h=4),
                                    AX.X, ALU.add).then_inc(s_dv.h, 1)
                    v.wait_ge(s_ac.h, acb + 4)
                    v.tensor_scalar_add(dnm[:], dnm[:], EPS).then_inc(s_dv.h, 1)
                    v.wait_ge(s_dv.h, dvb + 4)
                    v.reciprocal(rcpn[:], dnm[:]).then_inc(s_dv.h, 1)
                    v.wait_ge(s_dv.h, dvb + 5)
                    v.tensor_mul(dst[:, jt * 256:(jt + 1) * 256].rearrange("p (h d) -> p h d", h=4),
                                 pscr[:].rearrange("p (h d) -> p h d", h=4),
                                 rcpn[:].rearrange("p (h o) -> p h o", o=1).broadcast_to([128, 4, 64])).then_inc(s_dv.h, 1)
                g = jt * 3 + 2
                v.wait_ge(s_mm.h, g + 1)
                v.tensor_copy(Vh[:, jt * 256:(jt + 1) * 256],
                              ctx.bank(g % 2, 128, 256)).then_inc(s_dv.h, 1)
        s_dv.n = NT * DVC

        @b.scalar
        def _(a):
            for jt in range(NT):
                for wi in range(2):
                    dvb = jt * DVC + wi * 6
                    acb = jt * ACC_ + wi * 4
                    a.wait_ge(s_dv.h, dvb + 1)
                    a.activation(t1s[:], t1s[:], AF.Exp).then_inc(s_ac.h, 1)
                    a.wait_ge(s_dv.h, dvb + 2)
                    a.activation(psq[:], pscr[:], AF.Square).then_inc(s_ac.h, 1)
                    a.wait_ge(s_dv.h, dvb + 3)
                    a.activation(nl[:], n2[:], AF.Ln).then_inc(s_ac.h, 1)
                    a.wait_ge(s_ac.h, acb + 3)
                    a.activation(dnm[:], nl[:], AF.Exp, scale=0.5).then_inc(s_ac.h, 1)
        s_ac.n = NT * ACC_

    s_pe3 = ctx.sem("a3_pe3")
    s_sc3 = ctx.sem("a3_sc3")
    s_dv3 = ctx.sem("a3_dv3")
    s_t3 = ctx.sem("a3_t3")
    with nc.Block() as b:
        for h in range(4):
            for c in range(NT):
                hsl = slice(c * 256 + h * 64, c * 256 + (h + 1) * 64)

                @b.tensor
                def _(t, h=h, c=c, hsl=hsl):
                    s_sc3.wait(t)
                    s_dv3.wait(t)
                    s_t3.inc(t.transpose(ctx.bank(3, 64, 128, BF16), Kh[:, hsl], ident[:]))
                    s_t3.inc(t.transpose(ctx.bank(7, 64, 128, BF16), Qh[:, hsl], ident[:]))

                @b.scalar
                def _(a, h=h, c=c):
                    s_t3.wait(a)
                    s_pe3.wait(a)
                    s_dv3.wait(a)
                    a.copy(KcT[:], ctx.bank(3, 64, 128, BF16))
                    ins = a.copy(QcT[:], ctx.bank(7, 64, 128, BF16))
                    if c > 0:
                        a.copy(S_sb[:], S_f32[:])
                        ins = a.copy(kst[:], kst_f32[:])
                    s_sc3.inc(ins)

                @b.tensor
                def _(t, h=h, c=c, hsl=hsl):
                    s_sc3.wait(t)
                    s_dv3.wait(t)
                    t.matmul(ctx.bank(2, 128, 128), KcT[:], QcT[:], start=True, stop=True)
                    ins = t.matmul(ctx.bank(6, 128, 64), c_triu[:], Kh[:, hsl],
                                   start=True, stop=(c == 0))
                    if c > 0:
                        ins = t.matmul(ctx.bank(6, 128, 64), onesrow[0:1, 0:128], kst[:],
                                       start=False, stop=True)
                        ins = t.matmul(ctx.bank(4, 64, 128), S_sb[:], QcT[:], start=True, stop=False)
                    s_pe3.inc(ins)

                @b.vector
                def _(v, h=h, c=c, hsl=hsl):
                    s_pe3.wait(v)
                    v.tensor_mul(Pm[:], ctx.bank(2, 128, 128), c_triuf[:]).then_inc(s_dv3.h, 1)
                    s_dv3.n += 1
                    v.tensor_mul(dp[:], Qh[:, hsl], ctx.bank(6, 128, 64)).then_inc(s_dv3.h, 1)
                    s_dv3.n += 1
                    s_dv3.wait(v)
                    v.tensor_reduce(den[:], dp[:], AX.X, ALU.add).then_inc(s_dv3.h, 1)
                    s_dv3.n += 1
                    s_dv3.wait(v)
                    v.reciprocal(denf[:], den[:]).then_inc(s_dv3.h, 1)
                    s_dv3.n += 1
                    s_dv3.wait(v)
                    v.tensor_copy(denr[:], denf[:]).then_inc(s_dv3.h, 1)
                    s_dv3.n += 1

                @b.tensor
                def _(t, h=h, c=c):
                    s_dv3.wait(t)
                    s_t3.inc(t.transpose(ctx.bank(3, 1, 128, BF16), denr[:], ident[0:128, 0:128]))

                @b.scalar
                def _(a):
                    s_t3.wait(a)
                    s_sc3.inc(a.copy(denrT[:], ctx.bank(3, 1, 128, BF16)))

                @b.tensor
                def _(t, h=h, c=c, hsl=hsl):
                    s_sc3.wait(t)
                    t.matmul(ctx.bank(4, 64, 128), Vh[:, hsl], Pm[:],
                             start=(c == 0), stop=True)
                    t.matmul(ctx.bank(0, 64, 64), Kh[:, hsl], Vh[:, hsl],
                             start=True, stop=True)
                    t.matmul(ctx.bank(1, 1, 64), ones128[:, 0:1], Kh[:, hsl],
                             start=True, stop=True)
                    s_pe3.inc(t.matmul(ctx.bank(2, 64, 128), onesrow[0:1, 0:64], denrT[:],
                                       start=True, stop=True))

                @b.scalar
                def _(a):
                    s_pe3.wait(a)
                    s_sc3.inc(a.copy(Oev[:], ctx.bank(4, 64, 128)))

                @b.vector
                def _(v, h=h, c=c):
                    s_sc3.wait(v)
                    s_pe3.wait(v)
                    v.tensor_mul(YT[(h % 2) * 64:(h % 2) * 64 + 64,
                                    (h // 2) * T + c * 128:(h // 2) * T + (c + 1) * 128],
                                 Oev[:], ctx.bank(2, 64, 128)).then_inc(s_dv3.h, 1)
                    s_dv3.n += 1
                    if c == 0:
                        v.tensor_copy(S_f32[:], ctx.bank(0, 64, 64))
                        v.tensor_copy(kst_f32[:], ctx.bank(1, 1, 64)).then_inc(s_dv3.h, 2)
                    else:
                        v.tensor_add(S_f32[:], S_f32[:], ctx.bank(0, 64, 64))
                        v.tensor_add(kst_f32[:], kst_f32[:], ctx.bank(1, 1, 64)).then_inc(s_dv3.h, 2)
                    s_dv3.n += 2

    def mm_fn(t, k, o, bank_ap):
        pairs = [(woT[:, c2 * 1024 + o * 128: c2 * 1024 + (o + 1) * 128],
                  YT[:, c2 * T + k * CHUNK: c2 * T + (k + 1) * CHUNK]) for c2 in range(2)]
        return mm_chain(t, bank_ap, pairs)

    return emit_residual_out(ctx, "a3f", src_name, out_name, s_dv3, None, mm_fn)


def build_launch_l3attn():
    nc = bass.Bass("TRN2", target_bir_lowering=False, detect_race_conditions=False)
    ctx = Ctx(nc)
    ctx.din("xT", [D, T], F32)
    ctx.din("L3_attn_Wq", [256, 1024]); ctx.din("L3_attn_Wk", [256, 1024])
    ctx.din("L3_attn_Wv", [256, 1024]); ctx.din("L3_attn_Wo", [1024, 256])
    ctx.dout("yT", [D, T], F32)
    ctx.dint("lnh", [D, T], BF16)
    ctx.load_consts(extra=("triu", "triuf"))
    RING = 2
    lnx = ctx.sbuf("lnring", [128, NCH * RING * CHUNK], BF16)
    lnx_at = lambda c, k: lnx[:, (c * RING + k % RING) * CHUNK:
                              (c * RING + k % RING + 1) * CHUNK]
    lnx_view = lambda k: lnx[:].rearrange("p (c s f) -> p c s f", c=NCH, s=RING)[:, :, k % RING, :]
    s_ln, s_lnst = emit_ln(ctx, "xT", lnx_at, "ln3a", bank0=0, ring=RING, to_hbm="lnh",
                           lnx_view=lnx_view)
    emit_l3attn(ctx, s_lnst, "lnh", "xT", "yT")
    return ctx



def emit_rms(ctx, src_name, lnx_at, tag, gcol, bank0=0, ring=None, to_hbm=None,
             lnx_view=None, src_bf16=False, src_sem=None):
    nc = ctx.nc
    ones128 = ctx.sb["ones128"]
    src = ctx.dram[src_name].rearrange("(c p) t -> p c t", p=128)
    lb = ctx.ln_bufs()
    xb = lb["xb"]
    xbf = lb["xbf"]
    rstd = lb["rstd"]
    varS = lb["varS"]

    s_in = [ctx.sem(tag + "_in0"), ctx.sem(tag + "_in1")]
    s_mm = ctx.sem(tag + "_mm")
    s_cast = ctx.sem(tag + "_cast")
    s_stat = ctx.sem(tag + "_stat")
    s_done = ctx.sem(tag + "_done")
    s_psf = ctx.sem(tag + "_psf")
    p_sq = [ctx.bank(bank0), ctx.bank(bank0 + 1)]

    if to_hbm is not None:
        lnx_dst = ctx.dram[to_hbm].rearrange("(c p) t -> p c t", p=128)
        s_lnst = [ctx.sem(tag + "_st0"), ctx.sem(tag + "_st1")]

    with nc.Block() as b:
        if to_hbm is not None and not src_bf16:
            @b.gpsimd
            def _(g):
                for k in range(NCK):
                    g.wait_ge(s_done.h, (k + 1) * 2)
                    g.dma_start(lnx_dst[:, :, k * CHUNK:(k + 1) * CHUNK],
                                lnx_view(k)).then_inc(s_lnst[k % 2].h, 16)
        if to_hbm is not None:
            s_lnst[0].n = (NCK // 2) * 16
            s_lnst[1].n = (NCK // 2) * 16

        if src_bf16:
            @b.gpsimd
            def _(g):
                for k in range(NCK):
                    if src_sem is not None:
                        for _sm, _fn in (src_sem if isinstance(src_sem, list) else [src_sem]):
                            g.wait_ge(_sm.h, _fn(k))
                    if k >= 1:
                        g.wait_ge(s_done.h, k * 2)
                    g.dma_start(xb[:].rearrange("p (c f) -> p c f", c=NCH),
                                src[:, :, k * CHUNK:(k + 1) * CHUNK]).then_inc(s_in[k % 2].h, 16)
                    if to_hbm is not None and k >= 1:
                        g.wait_ge(s_done.h, k * 2)
                        g.dma_start(lnx_dst[:, :, (k - 1) * CHUNK:k * CHUNK],
                                    lnx_view(k - 1)).then_inc(s_lnst[(k - 1) % 2].h, 16)
                if to_hbm is not None:
                    g.wait_ge(s_done.h, NCK * 2)
                    g.dma_start(lnx_dst[:, :, (NCK - 1) * CHUNK:NCK * CHUNK],
                                lnx_view(NCK - 1)).then_inc(s_lnst[(NCK - 1) % 2].h, 16)
        else:
            @b.sync
            def _(q):
                q.wait_ge(ctx.s_const.h, ctx.s_const.n)
                for k in range(NCK):
                    if src_sem is not None:
                        for _sm, _fn in (src_sem if isinstance(src_sem, list) else [src_sem]):
                            q.wait_ge(_sm.h, _fn(k))
                    if k >= 1:
                        q.wait_ge(s_done.h, k * 2)
                    q.dma_start(xb[:].rearrange("p (c f) -> p c f", c=NCH),
                                src[:, :, k * CHUNK:(k + 1) * CHUNK]).then_inc(s_in[k % 2].h, 16)
        s_in[0].n = (NCK - NCK // 2) * 16
        s_in[1].n = (NCK // 2) * 16

        @b.scalar
        def _(a):
            for k in range(NCK):
                i = k % 2
                a.wait_ge(s_in[k % 2].h, (k // 2 + 1) * 16)
                if k >= 1:
                    a.wait_ge(s_mm.h, k)
                if k >= 2:
                    a.wait_ge(s_done.h, (k - 1) * 2)
                a.activation(xbf[:], xb[:], AF.Square).then_inc(s_cast.h, 1)
                a.wait_ge(s_mm.h, k + 1)
                a.activation(varS[:], p_sq[i], AF.Ln).then_inc(s_psf.h, 1)
                a.activation(rstd[i][:], varS[:], AF.Exp, scale=-0.5).then_inc(s_stat.h, 1)
        s_cast.n = NCK
        s_stat.n = NCK

        @b.tensor
        def _(t):
            for k in range(NCK):
                i = k % 2
                t.wait_ge(s_cast.h, k + 1)
                if k >= 2:
                    t.wait_ge(s_psf.h, (k - 1) * 2)
                ins = None
                for c in range(NCH):
                    ins = t.matmul(p_sq[i], ones128[:], xbf[:, c * CHUNK:(c + 1) * CHUNK],
                                   start=(c == 0), stop=(c == NCH - 1))
                ins.then_inc(s_mm.h, 1)
        s_mm.n = NCK

        @b.vector
        def _(v):
            for k in range(NCK):
                i = k % 2
                v.wait_ge(s_stat.h, k + 1)
                v.tensor_scalar_min(rstd[i][:], rstd[i][:], 1e12).then_inc(s_psf.h, 1)
                if ring is not None and k >= ring:
                    if to_hbm is not None:
                        v.wait_ge(s_lnst[k % 2].h, ((k - ring) // 2 + 1) * 16)
                ins = None
                for c in range(NCH):
                    sl = slice(c * CHUNK, (c + 1) * CHUNK)
                    ins = v.scalar_tensor_tensor(lnx_at(c, k), xb[:, sl], gcol[:, c:c + 1],
                                                 rstd[i][:], ALU.mult, ALU.mult)
                ins.then_inc(s_done.h, 2)
        s_psf.n += NCK
        s_done.n = NCK * 2

    if to_hbm is not None:
        return s_done, s_lnst
    return s_done



def emit_weight_T(ctx, b, wname, wstage, dstT, nrow_tiles, ncol_tiles, s_w, s_tp, s_te,
                  dst_map=None, first=False, wslice=None):
    nc = ctx.nc
    ident = ctx.sb["ident"]
    C = ncol_tiles * 128
    if dst_map is None:
        dst_map = lambda r, c: c * (nrow_tiles * 128) + r * 128
    wsrc = ctx.dram[wname] if wslice is None else wslice(ctx.dram[wname])
    wsrc = wsrc.rearrange("(r p) w -> p r w", p=128)

    tp_base0 = s_tp.n

    @b.gpsimd
    def _(g):
        for r in range(nrow_tiles):
            if r >= 2:
                g.wait_ge(s_tp.h, tp_base0 + (r - 1) * ncol_tiles)
            s_w.inc(g.dma_start(wstage[r % 2][:, 0:C], wsrc[:, r, :]), 16)

    @b.tensor
    def _(t):
        if first:
            t.wait_ge(ctx.s_const.h, ctx.s_const.n)
        te_base0 = s_te.n
        w_base0 = s_w.n - nrow_tiles * 16
        n_ = 0
        for r in range(nrow_tiles):
            t.wait_ge(s_w.h, w_base0 + (r + 1) * 16)
            for c in range(ncol_tiles):
                if n_ >= 2:
                    t.wait_ge(s_te.h, te_base0 + n_ - 1)
                s_tp.inc(t.transpose(ctx.bank(6 + n_ % 2, 128, 128, BF16),
                                     wstage[r % 2][:, c * 128:(c + 1) * 128], ident[:]))
                n_ += 1

    @b.scalar
    def _(a):
        tp_base = s_tp.n - nrow_tiles * ncol_tiles
        n_ = 0
        for r in range(nrow_tiles):
            for c in range(ncol_tiles):
                a.wait_ge(s_tp.h, tp_base + n_ + 1)
                s_te.inc(a.copy(dstT[:, dst_map(r, c):dst_map(r, c) + 128],
                                ctx.bank(6 + n_ % 2, 128, 128, BF16)))
                n_ += 1


def emit_row_to_cols(ctx, b, row_ap, cols, ncols, s_tp, s_te, scale=None):
    nc = ctx.nc
    ident = ctx.sb["ident"]

    @b.tensor
    def _(t):
        for c in range(ncols):
            if c >= 1:
                s_te.wait(t)
            s_tp.inc(t.transpose(ctx.bank(5, 128, 1), row_ap[0:1, c * 128:(c + 1) * 128],
                                 ident[0:1, 0:1]))

    @b.scalar
    def _(a):
        tp_base = s_tp.n - ncols
        for c in range(ncols):
            a.wait_ge(s_tp.h, tp_base + c + 1)
            if scale is None:
                s_te.inc(a.copy(cols[:, c:c + 1], ctx.bank(5, 128, 1)))
            else:
                s_te.inc(a.activation(cols[:, c:c + 1], ctx.bank(5, 128, 1),
                                      AF.Copy, scale=scale))



def emit_l2attn_a(ctx, s_lnst, lnh_name, out_name):
    nc = ctx.nc
    ident = ctx.sb["ident"]
    onesrow = ctx.sb["onesrow"]
    lnh = ctx.dram[lnh_name].rearrange("(c p) t -> p c t", p=128)

    wstage = [ctx.sbuf("c2_ws0", [128, 1024], BF16), ctx.sbuf("c2_ws1", [128, 1024], BF16)]
    wbig = ctx.sbuf("c2_wbig", [128, 16 * 1024], BF16)
    rowbuf = ctx.sbuf("c2_rows", [1, 8 * 1024], BF16)
    gcol1 = ctx.sbuf("c2_g1", [128, 8])
    gcol2 = ctx.sbuf("c2_g2", [128, 8])
    wd = ctx.sbuf("c2_wd", [128, 3 * 8])
    bias2 = ctx.sbuf("c2_bias2", [1, 1024], BF16)
    bpwrow = ctx.sbuf("c2_bpw", [1, 1024], BF16)
    bdwcol = ctx.sbuf("c2_bdw", [128, 8], BF16)
    cbuf = [ctx.sbuf("c2_cb0", [128, 8 * 516], BF16), ctx.sbuf("c2_cb1", [128, 8 * 516], BF16)]
    dwbuf = ctx.sbuf("c2_dw", [128, 8 * CHUNK], BF16)
    abuf = ctx.sbuf("c2_a", [128, CHUNK])
    bscan = ctx.sbuf("c2_bs", [128, CHUNK])
    sgh = ctx.sbuf("c2_sgh", [128, CHUNK])
    relh = ctx.sbuf("c2_relh", [128, CHUNK])
    houts = ctx.sbuf("c2_h", [128, CHUNK])
    carry = ctx.sbuf("c2_carry", [128, 8])
    ybr = ctx.sbuf("c2_ybr", [128, 2 * CHUNK])

    s_r0 = ctx.sem("c2_r0")
    s_tp = ctx.sem("c2_tp")
    s_te = ctx.sem("c2_te")
    s_wl = ctx.sem("c2_wl")
    with nc.Block() as b:
        @b.gpsimd
        def _(g):
            s_r0.inc(g.dma_start(rowbuf[0:1, 0:1024],
                                 ctx.dram["L2_attn_conv_norm_g"].rearrange("(o n) -> o n", o=1)), 16)
            s_r0.inc(g.dma_start(rowbuf[0:1, 1024:2048],
                                 ctx.dram["L2_attn_gru_norm_g"].rearrange("(o n) -> o n", o=1)), 16)
            s_r0.inc(g.dma_start(rowbuf[0:1, 2048:3072],
                                 ctx.dram["L2_attn_conv_dw_b"].rearrange("(o n) -> o n", o=1)), 16)
            s_r0.inc(g.dma_start(bpwrow[:],
                                 ctx.dram["L2_attn_conv_pw_b"].rearrange("(o n) -> o n", o=1)), 16)
            s_r0.inc(g.dma_start(rowbuf[0:1, 3072:3072 + 3072],
                                 ctx.dram["L2_attn_conv_dw_w"].rearrange(
                                     "d one k -> one (d k)", one=1)), 16)

        @b.vector
        def _(v):
            pass

    s_tpr = ctx.sem("c2_tpr")
    s_ter = ctx.sem("c2_ter")
    s_gv = ctx.sem("c2_gv")
    with nc.Block() as b:
        @b.vector
        def _(v):
            s_r0.wait(v)
            v.tensor_scalar(rowbuf[0:1, 0:2048], rowbuf[0:1, 0:2048], 1.0, 32.0,
                            ALU.add, ALU.mult).then_inc(s_gv.h, 1)
        s_gv.n = 1

        @b.tensor
        def _(t):
            t.wait_ge(ctx.s_const.h, ctx.s_const.n)
            s_gv.wait(t)
            srcs = [rowbuf[0:1, c * 128:(c + 1) * 128] for c in range(8)]
            srcs += [rowbuf[0:1, 1024 + c * 128: 1024 + (c + 1) * 128] for c in range(8)]
            srcs += [rowbuf[0:1, 2048 + c * 128: 2048 + (c + 1) * 128] for c in range(8)]
            wdwview = rowbuf[0:1, 3072:3072 + 3072].rearrange("o (d k) -> o d k", k=3)
            for j in range(3):
                srcs += [wdwview[:, c * 128:(c + 1) * 128, j] for c in range(8)]
            for n_, sr in enumerate(srcs):
                if n_ >= 1:
                    t.wait_ge(s_ter.h, n_)
                s_tpr.inc(t.transpose(ctx.bank(5, 128, 1, BF16), sr, ident[0:1, 0:1]))
        s_tpr.n = 48

        @b.scalar
        def _(a):
            n_ = 0
            for dst in [gcol1, gcol2, bdwcol]:
                for c in range(8):
                    a.wait_ge(s_tpr.h, n_ + 1)
                    s_ter.inc(a.copy(dst[:, c:c + 1], ctx.bank(5, 128, 1, BF16)))
                    n_ += 1
            for j in range(3):
                for c in range(8):
                    a.wait_ge(s_tpr.h, n_ + 1)
                    s_ter.inc(a.copy(wd[:, j * 8 + c: j * 8 + c + 1], ctx.bank(5, 128, 1, BF16)))
                    n_ += 1
        s_ter.n = 48

    with nc.Block() as b:
        emit_weight_T(ctx, b, "L2_attn_conv_pw_w", wstage, wbig, 8, 8, s_wl, s_tp, s_te)

    s_b2 = ctx.sem("c2_b2")
    with nc.Block() as b:
        @b.tensor
        def _(t):
            s_te.wait(t)
            s_ter.wait(t)
            for half in range(2):
                ins = None
                for c in range(8):
                    ins = t.matmul(ctx.bank(3 + half, 1, 512), bdwcol[:, c:c + 1],
                                   wbig[:, c * 1024 + half * 512: c * 1024 + (half + 1) * 512],
                                   start=(c == 0), stop=(c == 7))
                s_b2.inc(ins)

        @b.vector
        def _(v):
            s_b2.wait(v)
            for half in range(2):
                v.tensor_add(bias2[0:1, half * 512:(half + 1) * 512],
                             bpwrow[0:1, half * 512:(half + 1) * 512],
                             ctx.bank(3 + half, 1, 512)).then_inc(s_b2.h, 1)
                s_b2.n += 1

    RING = 2
    rn1 = ctx.sbuf("c2_rn1", [128, NCH * RING * CHUNK], BF16)
    rn1_at = lambda c, k: rn1[:, (c * RING + k % RING) * CHUNK:
                              (c * RING + k % RING + 1) * CHUNK]
    rn1_view = lambda k: rn1[:].rearrange("p (c s f) -> p c s f", c=NCH, s=RING)[:, :, k % RING, :]
    s_rms1, s_r1st = emit_rms(ctx, lnh_name, rn1_at, "c2r1", gcol1, bank0=0, ring=RING,
                              to_hbm="rn1h", lnx_view=rn1_view, src_bf16=True,
                              src_sem=[(s_lnst[0], lambda k: (k // 2 + 1) * 16),
                                       (s_lnst[1], lambda k: (k // 2 + 1) * 16)])

    rn1h = ctx.dram["rn1h"].rearrange("(c p) t -> p c t", p=128)
    s_cbin = [ctx.sem("c2_cb0s"), ctx.sem("c2_cb1s")]
    s_dw = ctx.sem("c2_dws")
    s_pwm = ctx.sem("c2_pwm")
    s_x0in = ctx.sem("c2_x0in")
    s_add = ctx.sem("c2_adds")
    s_st = [ctx.sem("c2_st0"), ctx.sem("c2_st1")]
    x0b = ctx.sbuf("c2_x0b", [128, NCH * CHUNK])
    lnhd = ctx.dram[lnh_name].rearrange("(c p) t -> p c t", p=128)
    x1d = ctx.dram["x1h"].rearrange("(c p) t -> p c t", p=128)

    with nc.Block() as b:
        @b.gpsimd
        def _(g):
            for k in range(NCK):
                if k >= 1:
                    for o in range(NCH):
                        n = (k - 1) * 8 + o
                        g.wait_ge(s_add.h, n + 1)
                        g.dma_start(x1d[:, o, (k - 1) * CHUNK:k * CHUNK],
                                    ybr[:, (n % 2) * CHUNK:(n % 2 + 1) * CHUNK]).then_inc(s_st[n % 2].h, 16)
                g.wait_ge(s_r1st[k % 2].h, (k // 2 + 1) * 16)
                if k >= 1:
                    g.wait_ge(s_r1st[(k - 1) % 2].h, ((k - 1) // 2 + 1) * 16)
                if k >= 2:
                    g.wait_ge(s_dw.h, 1 + (k - 1) * 8)
                if k == 0:
                    g.dma_start(cbuf[0][:].rearrange("p (c f) -> p c f", c=NCH)[:, :, 4:516],
                                rn1h[:, :, 0:512]).then_inc(s_cbin[0].h, 16)
                else:
                    g.dma_start(cbuf[k % 2][:].rearrange("p (c f) -> p c f", c=NCH)[:, :, 0:514],
                                rn1h[:, :, k * CHUNK - 2:(k + 1) * CHUNK]).then_inc(s_cbin[k % 2].h, 16)
                if k >= 1:
                    g.wait_ge(s_add.h, k * 8)
                g.dma_start(x0b[:].rearrange("p (c f) -> p c f", c=NCH),
                            lnhd[:, :, k * CHUNK:(k + 1) * CHUNK]).then_inc(s_x0in.h, 16)
            for o in range(NCH):
                n = (NCK - 1) * 8 + o
                g.wait_ge(s_add.h, n + 1)
                g.dma_start(x1d[:, o, (NCK - 1) * CHUNK:NCK * CHUNK],
                            ybr[:, (n % 2) * CHUNK:(n % 2 + 1) * CHUNK]).then_inc(s_st[n % 2].h, 16)
            g.wait_ge(s_st[0].h, NCK * 4 * 16)
            g.wait_ge(s_st[1].h, NCK * 4 * 16)
        s_cbin[0].n = 4 * 16
        s_cbin[1].n = 4 * 16
        s_x0in.n = NCK * 16
        s_st[0].n = NCK * 4 * 16
        s_st[1].n = NCK * 4 * 16

        @b.vector
        def _(v):
            v.memset(cbuf[0][:].rearrange("p (c f) -> p c f", c=NCH)[:, :, 0:4], 0.0).then_inc(s_dw.h, 1)
            for k in range(NCK):
                v.wait_ge(s_cbin[k % 2].h, (k // 2 + 1) * 16)
                if k >= 1:
                    v.wait_ge(s_pwm.h, k * 8)
                base = 4 if k == 0 else 2
                for c in range(NCH):
                    cb = cbuf[k % 2][:, c * 516:(c + 1) * 516]
                    v.tensor_scalar(dwbuf[:, c * CHUNK:(c + 1) * CHUNK], cb[:, base - 2:base + 510],
                                    wd[:, 0 * 8 + c:0 * 8 + c + 1], None, ALU.mult)
                    v.scalar_tensor_tensor(dwbuf[:, c * CHUNK:(c + 1) * CHUNK],
                                           cb[:, base - 1:base + 511], wd[:, 1 * 8 + c:1 * 8 + c + 1],
                                           dwbuf[:, c * CHUNK:(c + 1) * CHUNK], ALU.mult, ALU.add)
                    v.scalar_tensor_tensor(dwbuf[:, c * CHUNK:(c + 1) * CHUNK],
                                           cb[:, base:base + 512], wd[:, 2 * 8 + c:2 * 8 + c + 1],
                                           dwbuf[:, c * CHUNK:(c + 1) * CHUNK],
                                           ALU.mult, ALU.add).then_inc(s_dw.h, 1)
                for o in range(NCH):
                    n = k * 8 + o
                    v.wait_ge(s_pwm.h, n + 1)
                    v.wait_ge(s_x0in.h, (k + 1) * 16)
                    if n >= 2:
                        v.wait_ge(s_st[n % 2].h, ((n - 2) // 2 + 1) * 16)
                    v.tensor_add(ybr[:, (n % 2) * CHUNK:(n % 2 + 1) * CHUNK],
                                 x0b[:, o * CHUNK:(o + 1) * CHUNK],
                                 ctx.bank(4 + n % 2)).then_inc(s_add.h, 1)
        s_dw.n = 1 + NCK * 8
        s_add.n = NCK * 8

        @b.tensor
        def _(t):
            s_b2.wait(t)
            for k in range(NCK):
                for o in range(NCH):
                    n = k * 8 + o
                    t.wait_ge(s_dw.h, 1 + k * 8 + 8)
                    if n >= 2:
                        t.wait_ge(s_add.h, n - 1)
                    pairs = [(wbig[:, c * 1024 + o * 128: c * 1024 + (o + 1) * 128],
                              dwbuf[:, c * CHUNK:(c + 1) * CHUNK]) for c in range(NCH)]
                    pairs.append((bias2[0:1, o * 128:(o + 1) * 128],
                                  onesrow[0:1, k * CHUNK:(k + 1) * CHUNK]))
                    mm_chain(t, ctx.bank(4 + n % 2), pairs).then_inc(s_pwm.h, 1)
        s_pwm.n = NCK * 8

    rn2_at = rn1_at
    rn2_view = rn1_view
    s_rms2, s_r2st = emit_rms(ctx, "x1h", rn2_at, "c2r2", gcol2, bank0=0, ring=RING,
                              to_hbm="rn2h", lnx_view=rn2_view, src_bf16=False,
                              src_sem=[(s_st[0], lambda k: (k + 1) * 64),
                                       (s_st[1], lambda k: (k + 1) * 64)])

    s_wl2 = ctx.sem("c2_wl2")
    s_tp2 = ctx.sem("c2_tp2")
    s_te2 = ctx.sem("c2_te2")
    with nc.Block() as b:
        emit_weight_T(ctx, b, "L2_attn_gru_w", wstage, wbig, 16, 8, s_wl2, s_tp2, s_te2)

    rn2h = ctx.dram["rn2h"].rearrange("(c p) t -> p c t", p=128)
    x1hd = ctx.dram["x1h"].rearrange("(c p) t -> p c t", p=128)
    x2d = ctx.dram[out_name].rearrange("(c p) t -> p c t", p=128)
    lnr2 = rn1
    s_ln2 = [ctx.sem("c2_l2a"), ctx.sem("c2_l2b")]
    s_x1in = ctx.sem("c2_x1in")
    s_hm = ctx.sem("c2_hm")
    s_el = ctx.sem("c2_el")
    s_sg = ctx.sem("c2_sg")
    s_sc = ctx.sem("c2_sc")
    s_st2 = [ctx.sem("c2_st2a"), ctx.sem("c2_st2b")]
    x1b = x0b

    with nc.Block() as b:
        @b.sync
        def _(q):
            for k in range(NCK):
                q.wait_ge(s_r2st[k % 2].h, (k // 2 + 1) * 16)
                if k >= 2:
                    q.wait_ge(s_hm.h, (k - 1) * 16)
                q.dma_start(lnr2[:].rearrange("p (c s f) -> p c s f", c=NCH, s=2)[:, :, k % 2, :],
                            rn2h[:, :, k * CHUNK:(k + 1) * CHUNK]).then_inc(s_ln2[k % 2].h, 16)
                if k >= 1:
                    q.wait_ge(s_sc.h, k * (NCH * 3))
                q.dma_start(x1b[:].rearrange("p (c f) -> p c f", c=NCH),
                            x1hd[:, :, k * CHUNK:(k + 1) * CHUNK]).then_inc(s_x1in.h, 16)
        s_ln2[0].n = 4 * 16
        s_ln2[1].n = 4 * 16
        s_x1in.n = NCK * 16

        @b.tensor
        def _(t):
            s_te2.wait(t)
            for k in range(NCK):
                t.wait_ge(s_ln2[k % 2].h, (k // 2 + 1) * 16)
                for c in range(NCH):
                    n = k * 8 + c
                    if n >= 1:
                        t.wait_ge(s_sg.h, n * 3)
                        t.wait_ge(s_el.h, n * 3 - 1)
                    pairs_h = [(wbig[:, ic * 2048 + c * 128: ic * 2048 + (c + 1) * 128],
                                lnr2[:, (ic * 2 + k % 2) * CHUNK:(ic * 2 + k % 2 + 1) * CHUNK])
                               for ic in range(NCH)]
                    mm_chain(t, ctx.bank(0), pairs_h)
                    pairs_g = [(wbig[:, ic * 2048 + 1024 + c * 128: ic * 2048 + 1024 + (c + 1) * 128],
                                lnr2[:, (ic * 2 + k % 2) * CHUNK:(ic * 2 + k % 2 + 1) * CHUNK])
                               for ic in range(NCH)]
                    mm_chain(t, ctx.bank(1), pairs_g).then_inc(s_hm.h, 2)
        s_hm.n = NCK * 16

        @b.scalar
        def _(a):
            for k in range(NCK):
                for c in range(NCH):
                    n = k * 8 + c
                    a.wait_ge(s_hm.h, (n + 1) * 2)
                    if n >= 1:
                        a.wait_ge(s_el.h, n * 3)
                        a.wait_ge(s_sc.h, n * 3 - 2)
                    a.activation(sgh[:], ctx.bank(0), AF.Sigmoid)
                    a.activation(abuf[:], ctx.bank(1), AF.Sigmoid, scale=-1.0).then_inc(s_sg.h, 2)
                    a.activation(relh[:], ctx.bank(1), AF.Sigmoid).then_inc(s_sg.h, 1)
        s_sg.n = NCK * 8 * 3

        @b.vector
        def _(v):
            for k in range(NCK):
                for c in range(NCH):
                    n = k * 8 + c
                    v.wait_ge(s_sg.h, n * 3 + 2)
                    v.tensor_scalar_min(sgh[:], sgh[:], 0.5)
                    v.scalar_tensor_tensor(bscan[:], ctx.bank(0), 0.0, sgh[:],
                                           ALU.max, ALU.add).then_inc(s_el.h, 2)
                    v.wait_ge(s_sg.h, n * 3 + 3)
                    v.tensor_mul(bscan[:], bscan[:], relh[:]).then_inc(s_el.h, 1)
                    v.wait_ge(s_el.h, n * 3 + 3)
                    if k == 0:
                        v.tensor_tensor_scan(houts[:], abuf[:], bscan[:], 0.0,
                                             ALU.mult, ALU.add).then_inc(s_sc.h, 1)
                    else:
                        v.tensor_tensor_scan(houts[:], abuf[:], bscan[:], carry[:, c:c + 1],
                                             ALU.mult, ALU.add).then_inc(s_sc.h, 1)
                    v.wait_ge(s_sc.h, n * 3 + 1)
                    v.tensor_copy(carry[:, c:c + 1], houts[:, CHUNK - 1:CHUNK]).then_inc(s_sc.h, 1)
                    v.wait_ge(s_x1in.h, (k + 1) * 16)
                    if n >= 2:
                        v.wait_ge(s_st2[n % 2].h, ((n - 2) // 2 + 1) * 16)
                    v.tensor_add(ybr[:, (n % 2) * CHUNK:(n % 2 + 1) * CHUNK],
                                 x1b[:, c * CHUNK:(c + 1) * CHUNK], houts[:]).then_inc(s_sc.h, 1)
        s_el.n = NCK * 8 * 3
        s_sc.n = NCK * 8 * 3

        @b.gpsimd
        def _(g):
            for k in range(NCK):
                for c in range(NCH):
                    n = k * 8 + c
                    g.wait_ge(s_sc.h, n * 3 + 3)
                    g.dma_start(x2d[:, c, k * CHUNK:(k + 1) * CHUNK],
                                ybr[:, (n % 2) * CHUNK:(n % 2 + 1) * CHUNK]).then_inc(s_st2[n % 2].h, 16)
            g.wait_ge(s_st2[0].h, NCK * 4 * 16)
            g.wait_ge(s_st2[1].h, NCK * 4 * 16)
        s_st2[0].n = NCK * 4 * 16
        s_st2[1].n = NCK * 4 * 16

    return s_st2




def emit_l2attn_b(ctx, src_outer, x2_name, out_name):
    nc = ctx.nc
    onesrow = ctx.sb["onesrow"]
    ident = ctx.sb["ident"]

    wbig = ctx.sbuf("f2_wbig", [128, 32 * 1024], BF16)
    h1b = ctx.sbuf("f2_h1b", [128, 32 * CHUNK], BF16)
    wstage = [h1b[:, 0:4096], h1b[:, 4096:8192]]
    rowbuf = ctx.sbuf("f2_rows", [1, 1024], BF16)
    b1row = ctx.sbuf("f2_b1", [1, 4096], BF16)
    b2row = ctx.sbuf("f2_b2", [1, 1024], BF16)
    gcol3 = ctx.sbuf("f2_g3", [128, 8])
    gel = ctx.sbuf("f2_gel", [128, 2 * CHUNK], BF16)

    s_r0 = ctx.sem("f2_r0")
    s_gv2 = ctx.sem("f2_gv2")
    s_tpr = ctx.sem("f2_tpr")
    s_ter = ctx.sem("f2_ter")
    with nc.Block() as b:
        @b.gpsimd
        def _(g):
            s_r0.inc(g.dma_start(rowbuf[0:1, 0:1024],
                                 ctx.dram["L2_attn_ff_norm_g"].rearrange("(o n) -> o n", o=1)), 16)
            s_r0.inc(g.dma_start(b1row[:],
                                 ctx.dram["L2_attn_ff_b1"].rearrange("(o n) -> o n", o=1)), 16)
            s_r0.inc(g.dma_start(b2row[:],
                                 ctx.dram["L2_attn_ff_b2"].rearrange("(o n) -> o n", o=1)), 16)

        @b.vector
        def _(v):
            s_r0.wait(v)
            v.tensor_scalar(rowbuf[0:1, 0:1024], rowbuf[0:1, 0:1024], 1.0, 32.0,
                            ALU.add, ALU.mult).then_inc(s_gv2.h, 1)
        s_gv2.n = 1

        @b.tensor
        def _(t):
            t.wait_ge(ctx.s_const.h, ctx.s_const.n)
            s_r0.wait(t)
            s_gv2.wait(t)
            for c in range(8):
                if c >= 1:
                    t.wait_ge(s_ter.h, c)
                s_tpr.inc(t.transpose(ctx.bank(5, 128, 1, BF16),
                                      rowbuf[0:1, c * 128:(c + 1) * 128], ident[0:1, 0:1]))

        @b.scalar
        def _(a):
            for c in range(8):
                a.wait_ge(s_tpr.h, c + 1)
                s_ter.inc(a.copy(gcol3[:, c:c + 1], ctx.bank(5, 128, 1, BF16)))

    RING = 2
    rn3 = ctx.sbuf("f2_rn3", [128, NCH * RING * CHUNK], BF16)
    rn3_at = lambda c, k: rn3[:, (c * RING + k % RING) * CHUNK:
                              (c * RING + k % RING + 1) * CHUNK]
    rn3_view = lambda k: rn3[:].rearrange("p (c s f) -> p c s f", c=NCH, s=RING)[:, :, k % RING, :]
    s_rms3, s_r3st = emit_rms(ctx, x2_name, rn3_at, "f2r3", gcol3, bank0=0, ring=RING,
                              to_hbm="rn3h", lnx_view=rn3_view, src_bf16=False)

    s_wl = ctx.sem("f2_wl")
    s_tp = ctx.sem("f2_tp")
    s_te = ctx.sem("f2_te")
    with nc.Block() as b:
        emit_weight_T(ctx, b, "L2_attn_ff_w1", wstage, wbig, 32, 8, s_wl, s_tp, s_te,
                      dst_map=lambda r, c: c * 4096 + r * 128)

    rn3h = ctx.dram["rn3h"].rearrange("(c p) t -> p c t", p=128)
    h1d = ctx.dram["h1h"].rearrange("(ft p) t -> p ft t", p=128)
    lnr = rn3
    s_lr = [ctx.sem("f2_lr0"), ctx.sem("f2_lr1")]
    s_hmm = ctx.sem("f2_hmm")
    s_gev = ctx.sem("f2_gev")
    s_hst = [ctx.sem("f2_hst0"), ctx.sem("f2_hst1")]
    with nc.Block() as b:
        @b.sync
        def _(q):
            for k in range(NCK):
                q.wait_ge(s_r3st[k % 2].h, (k // 2 + 1) * 16)
                if k >= 2:
                    q.wait_ge(s_hmm.h, (k - 1) * 32)
                q.dma_start(lnr[:].rearrange("p (c s f) -> p c s f", c=NCH, s=2)[:, :, k % 2, :],
                            rn3h[:, :, k * CHUNK:(k + 1) * CHUNK]).then_inc(s_lr[k % 2].h, 16)
        s_lr[0].n = 4 * 16
        s_lr[1].n = 4 * 16

        @b.tensor
        def _(t):
            s_te.wait(t)
            for k in range(NCK):
                t.wait_ge(s_lr[k % 2].h, (k // 2 + 1) * 16)
                for fo in range(32):
                    n = k * 32 + fo
                    if n >= 2:
                        t.wait_ge(s_gev.h, n - 1)
                    pairs = [(wbig[:, c * 4096 + fo * 128: c * 4096 + (fo + 1) * 128],
                              lnr[:, (c * 2 + k % 2) * CHUNK:(c * 2 + k % 2 + 1) * CHUNK])
                             for c in range(NCH)]
                    pairs.append((b1row[0:1, fo * 128:(fo + 1) * 128],
                                  onesrow[0:1, k * CHUNK:(k + 1) * CHUNK]))
                    mm_chain(t, ctx.bank(2 + n % 2), pairs).then_inc(s_hmm.h, 1)
        s_hmm.n = NCK * 32

        @b.scalar
        def _(a):
            for k in range(NCK):
                for fo in range(32):
                    n = k * 32 + fo
                    a.wait_ge(s_hmm.h, n + 1)
                    if n >= 2:
                        a.wait_ge(s_hst[n % 2].h, ((n - 2) // 2 + 1) * 16)
                    a.activation(gel[:, (n % 2) * CHUNK:(n % 2 + 1) * CHUNK],
                                 ctx.bank(2 + n % 2), AF.Gelu).then_inc(s_gev.h, 1)
        s_gev.n = NCK * 32

        @b.gpsimd
        def _(g):
            for k in range(NCK):
                for fo in range(32):
                    n = k * 32 + fo
                    g.wait_ge(s_gev.h, n + 1)
                    g.dma_start(h1d[:, fo, k * CHUNK:(k + 1) * CHUNK],
                                gel[:, (n % 2) * CHUNK:(n % 2 + 1) * CHUNK]).then_inc(s_hst[n % 2].h, 16)
            g.wait_ge(s_hst[0].h, NCK * 16 * 16)
            g.wait_ge(s_hst[1].h, NCK * 16 * 16)
        s_hst[0].n = NCK * 16 * 16
        s_hst[1].n = NCK * 16 * 16

    s_wl2 = ctx.sem("f2_wl2")
    s_tp2 = ctx.sem("f2_tp2")
    s_te2 = ctx.sem("f2_te2")
    with nc.Block() as b:
        emit_weight_T(ctx, b, "L2_attn_ff_w2", wstage, wbig, 8, 32, s_wl2, s_tp2, s_te2,
                      dst_map=lambda r, c: c * 1024 + r * 128)

    s_h1in = ctx.sem("f2_h1in")

    def sync_extra(q, k):
        if k >= 1:
            q.wait_ge(ctx.sems_f2m_pe.h, k * 8)
        q.dma_start(h1b[:].rearrange("p (ft f) -> p ft f", ft=32),
                    h1d[:, :, k * CHUNK:(k + 1) * CHUNK]).then_inc(s_h1in.h, 16)
        s_h1in.n += 16

    def mm_fn(t, k, o, bank_ap):
        t.wait_ge(s_h1in.h, (k + 1) * 16)
        pairs = [(wbig[:, c * 1024 + o * 128: c * 1024 + (o + 1) * 128],
                  h1b[:, c * CHUNK:(c + 1) * CHUNK]) for c in range(32)]
        pairs.append((b2row[0:1, o * 128:(o + 1) * 128],
                      onesrow[0:1, k * CHUNK:(k + 1) * CHUNK]))
        return mm_chain(t, bank_ap, pairs)

    return emit_residual_out(ctx, "f2m", src_outer, out_name,
                             s_te2, None, mm_fn, s_pe=ctx.sems_f2m_pe,
                             extra_stream=(x2_name, F32),
                             bufs={"xb": ctx.ln_bufs()["xb"]}, sync_extra=sync_extra)


def build_launch_l2attn_b():
    nc = bass.Bass("TRN2", target_bir_lowering=False, detect_race_conditions=False)
    ctx = Ctx(nc)
    ctx.din("xT", [D, T], F32)
    ctx.din("x2T", [D, T], F32)
    ctx.din("L2_attn_ff_norm_g", [1024])
    ctx.din("L2_attn_ff_w1", [4096, 1024]); ctx.din("L2_attn_ff_b1", [4096])
    ctx.din("L2_attn_ff_w2", [1024, 4096]); ctx.din("L2_attn_ff_b2", [1024])
    ctx.dout("yT", [D, T], F32)
    ctx.dint("rn3h", [D, T], BF16)
    ctx.dint("h1h", [4096, T], BF16)
    ctx.load_consts()
    ctx.sems_f2m_pe = ctx.sem("f2m_pe")
    emit_l2attn_b(ctx, "xT", "x2T", "yT")
    return ctx



MAGIC = float(np.float32(1.5 * 2 ** 23))


def emit_quant_chain(v, u_src, scr, W):
    r1, rb, q = scr["r1"], scr["rb"], scr["q"]
    v.tensor_scalar(r1[:, 0:W], u_src, 2.0, 2.0, ALU.min, ALU.mult)
    v.tensor_scalar(r1[:, 0:W], r1[:, 0:W], MAGIC, None, ALU.add)
    v.tensor_scalar(r1[:, 0:W], r1[:, 0:W], MAGIC, 0.5, ALU.subtract, ALU.mult)
    v.tensor_scalar(rb[:, 0:W], u_src, 2.0, 2.0, ALU.subtract, ALU.min)
    v.tensor_scalar(rb[:, 0:W], rb[:, 0:W], 0.0, MAGIC, ALU.max, ALU.add)
    v.tensor_scalar(rb[:, 0:W], rb[:, 0:W], MAGIC, None, ALU.subtract)
    v.tensor_scalar(q[:, 0:W], u_src, 5.0, 2.0, ALU.is_gt, ALU.mult)
    v.tensor_add(q[:, 0:W], q[:, 0:W], r1[:, 0:W])
    v.tensor_add(q[:, 0:W], q[:, 0:W], rb[:, 0:W])
    return 9


def alloc_quant_scratch(ctx, tag, W):
    return {
        "bm": ctx.sbuf(tag + "_bm", [128, W // 16]),
        "sc6": ctx.sbuf(tag + "_sc6", [128, W // 16]),
        "rcp": ctx.sbuf(tag + "_rcp", [128, W // 16]),
        "ab": ctx.sbuf(tag + "_ab", [128, W]),
        "sg": ctx.sbuf(tag + "_sg", [128, W]),
        "r1": ctx.sbuf(tag + "_r1", [128, W]),
        "rb": ctx.sbuf(tag + "_rb", [128, W]),
        "q": ctx.sbuf(tag + "_q", [128, W]),
    }


def emit_quant_tile_dve(v, src_ap, dst_ap, scr, W, s_dv, s_ac, acb):
    nb = W // 16
    bmv = src_ap.rearrange("p (b k) -> p b k", k=16)
    v.tensor_reduce(scr["bm"][:, 0:nb], bmv, AX.X, ALU.max,
                    apply_absolute_value=True).then_inc(s_dv.h, 1)
    s_dv.n += 1
    s_dv.wait(v)
    v.tensor_scalar(scr["sc6"][:, 0:nb], scr["bm"][:, 0:nb], 1.0 / 6.0, 1e-12,
                    ALU.mult, ALU.max).then_inc(s_dv.h, 1)
    s_dv.n += 1
    s_dv.wait(v)
    v.reciprocal(scr["rcp"][:, 0:nb], scr["sc6"][:, 0:nb]).then_inc(s_dv.h, 1)
    s_dv.n += 1
    s_dv.wait(v)
    v.wait_ge(s_ac.h, acb + 1)
    rcpb = scr["rcp"][:, 0:nb].rearrange("p (b o) -> p b o", o=1).broadcast_to([128, nb, 16])
    v.tensor_mul(scr["ab"][:, 0:W].rearrange("p (b k) -> p b k", k=16),
                 scr["ab"][:, 0:W].rearrange("p (b k) -> p b k", k=16), rcpb).then_inc(s_dv.h, 1)
    s_dv.n += 1
    emit_quant_chain(v, scr["ab"][:, 0:W], scr, W)
    v.wait_ge(s_ac.h, acb + 2)
    v.tensor_mul(scr["q"][:, 0:W], scr["q"][:, 0:W], scr["sg"][:, 0:W])
    sc6b = scr["sc6"][:, 0:nb].rearrange("p (b o) -> p b o", o=1).broadcast_to([128, nb, 16])
    v.tensor_mul(dst_ap.rearrange("p (b k) -> p b k", k=16),
                 scr["q"][:, 0:W].rearrange("p (b k) -> p b k", k=16), sc6b).then_inc(s_dv.h, 2)
    s_dv.n += 2


def emit_quant_tile_act(a, src_ap, scr, W, s_dv, dvb, s_ac):
    a.wait_ge(s_dv.h, dvb)
    a.activation(scr["ab"][:, 0:W], src_ap, AF.Abs).then_inc(s_ac.h, 1)
    s_ac.n += 1
    a.sign(scr["sg"][:, 0:W], src_ap).then_inc(s_ac.h, 1)
    s_ac.n += 1


DVE_Q = 6
ACT_Q = 2


def emit_quant_weights(ctx, wname, qt_name, R, C, scr, wstage, qcol):
    nc = ctx.nc
    ident = ctx.sb["ident"]
    identf = ctx.identf
    nrt, nct = R // 128, C // 128
    PW = 1408 if C > 1408 else C
    npass = (C + PW - 1) // PW
    QD = DVE_Q * npass
    QA = ACT_Q * npass
    tag = "qw_" + wname[-6:]
    s_w = ctx.sem(tag + "_w")
    s_dv = ctx.sem(tag + "_dv")
    s_ac = ctx.sem(tag + "_ac")
    s_tp = ctx.sem(tag + "_tp")
    s_te = ctx.sem(tag + "_te")
    s_st = ctx.sem(tag + "_st")
    wsrc = ctx.dram[wname].rearrange("(r p) w -> p r w", p=128)
    qdst = ctx.dram[qt_name].rearrange("(ct p) r -> p ct r", p=128)

    with nc.Block() as b:
        @b.gpsimd
        def _(g):
            for r in range(nrt):
                if r >= 1:
                    g.wait_ge(s_dv.h, r * QD)
                    g.wait_ge(s_tp.h, r * nct)
                g.dma_start(wstage[r % 2][:, 0:C], wsrc[:, r, :]).then_inc(s_w.h, 16)
                if r >= 1:
                    g.wait_ge(s_te.h, r * nct)
                    g.dma_start(qdst[:, :, (r - 1) * 128:r * 128],
                                qcol[(r - 1) % 2][:, 0:nct * 128].rearrange("p (ct w) -> p ct w", ct=nct)).then_inc(s_st.h, 16)
            g.wait_ge(s_te.h, nrt * nct)
            g.dma_start(qdst[:, :, (nrt - 1) * 128:nrt * 128],
                        qcol[(nrt - 1) % 2][:, 0:nct * 128].rearrange("p (ct w) -> p ct w", ct=nct)).then_inc(s_st.h, 16)
            g.wait_ge(s_st.h, nrt * 16)
        s_w.n = nrt * 16
        s_st.n = nrt * 16

        @b.vector
        def _(v):
            for r in range(nrt):
                v.wait_ge(s_w.h, (r + 1) * 16)
                for ps in range(npass):
                    W_ = min(PW, C - ps * PW)
                    sl = slice(ps * PW, ps * PW + W_)
                    emit_quant_tile_dve(v, wstage[r % 2][:, sl], wstage[r % 2][:, sl],
                                        scr, W_, s_dv, s_ac, r * QA + ps * ACT_Q)

        @b.scalar
        def _(a):
            for r in range(nrt):
                a.wait_ge(s_w.h, (r + 1) * 16)
                for ps in range(npass):
                    W_ = min(PW, C - ps * PW)
                    sl = slice(ps * PW, ps * PW + W_)
                    emit_quant_tile_act(a, wstage[r % 2][:, sl], scr, W_, s_dv,
                                        r * QD + ps * DVE_Q, s_ac)
                if r >= 1:
                    a.wait_ge(s_st.h, r * 16)
                for c in range(nct):
                    n = r * nct + c
                    a.wait_ge(s_tp.h, n + 1)
                    a.copy(qcol[r % 2][:, c * 128:(c + 1) * 128],
                           ctx.bank(6 + n % 2, 128, 128)).then_inc(s_te.h, 1)
        s_te.n = nrt * nct

        @b.tensor
        def _(t):
            t.wait_ge(ctx.s_const.h, ctx.s_const.n)
            for r in range(nrt):
                t.wait_ge(s_dv.h, (r + 1) * QD)
                for c in range(nct):
                    n = r * nct + c
                    if n >= 2:
                        t.wait_ge(s_te.h, n - 1)
                    t.transpose(ctx.bank(6 + n % 2, 128, 128),
                                wstage[r % 2][:, c * 128:(c + 1) * 128],
                                identf[:]).then_inc(s_tp.h, 1)
        s_tp.n = nrt * nct


def emit_l2ffn(ctx, s_lnst, lnh_name, src_name, out_name):
    nc = ctx.nc
    lnh = ctx.dram[lnh_name].rearrange("(c p) t -> p c t", p=128)
    src = ctx.dram[src_name].rearrange("(c p) t -> p c t", p=128)
    dst = ctx.dram[out_name].rearrange("(c p) t -> p c t", p=128)
    NI = 2816
    NIT = NI // 128

    ctx.dint("qgT", [1024, NI], BF16)
    ctx.dint("quT", [1024, NI], BF16)
    ctx.dint("qdT", [NI, 1024], BF16)
    scr = alloc_quant_scratch(ctx, "q2", 1408)
    ws_ = ctx.sbuf("q2_ws0", [128, NI])
    wstage = [ws_, ws_]
    qc_ = ctx.sbuf("q2_qc0", [128, NI], BF16)
    qcol = [qc_, qc_]
    emit_quant_weights(ctx, "L2_ffn_w_gate", "qgT", NI, 1024, scr, wstage, qcol)
    emit_quant_weights(ctx, "L2_ffn_w_up", "quT", NI, 1024, scr, wstage, qcol)
    emit_quant_weights(ctx, "L2_ffn_w_down", "qdT", 1024, NI, scr, wstage, qcol)

    qg = ctx.dram["qgT"].rearrange("(ic p) n -> p ic n", p=128)
    qu = ctx.dram["quT"].rearrange("(ic p) n -> p ic n", p=128)
    qd = ctx.dram["qdT"].rearrange("(nt p) o -> p nt o", p=128)

    lnr = ctx.sbuf("q2_lnr", [128, NCH * CHUNK])
    xq = ctx.sbuf("q2_xq", [128, NCH * CHUNK], BF16)
    btil = ctx.sbuf("q2_btil", [128, CHUNK])
    btq = ctx.sbuf("q2_btq", [128, CHUNK], BF16)
    hq_ = ctx.sbuf("q2_hq", [128, NIT * CHUNK], BF16)
    hq = hq_[:].rearrange("p f -> p f") if False else hq_
    tmps = ctx.sbuf("q2_tmps", [128, CHUNK])
    h1t = ctx.sbuf("q2_h1t", [128, CHUNK])
    gub = [ctx.sbuf("q2_gu0", [128, 2 * 1024], BF16), ctx.sbuf("q2_gu1", [128, 2 * 1024], BF16)]
    dw_ = ctx.sbuf("q2_dw0", [128, NIT * 128], BF16)
    dwb = [dw_, dw_]
    xb = ctx.ln_bufs()["xb"]
    ybr = ctx.sbuf("q2_ybr", [128, 2 * CHUNK])

    s_lr = [ctx.sem("q2_lr0"), ctx.sem("q2_lr1")]
    s_dv = ctx.sem("q2_dv")
    s_ac = ctx.sem("q2_ac")
    s_gu = [ctx.sem("q2_gua"), ctx.sem("q2_gub")]
    s_gum = ctx.sem("q2_gum")
    s_sil = ctx.sem("q2_sil")
    s_swg = ctx.sem("q2_swg")
    s_dwl = [ctx.sem("q2_dwl0"), ctx.sem("q2_dwl1")]
    s_dmm = ctx.sem("q2_dmm")
    s_xin = ctx.sem("q2_xin")
    s_add = ctx.sem("q2_add")
    s_st = [ctx.sem("q2_st0"), ctx.sem("q2_st1")]

    DVX = DVE_Q + 2
    CHV = (8 + NIT) * DVX

    def dv_chunk(k):
        return k * CHV

    def ac_chunk(k):
        return k * (8 + NIT) * ACT_Q

    with nc.Block() as b:
        @b.sync
        def _(q):
            for k in range(NCK):
                q.wait_ge(s_lnst[k % 2].h, (k // 2 + 1) * 16)
                if k >= 1:
                    q.wait_ge(s_dv.h, dv_chunk(k - 1) + 8 * DVX)
                q.dma_start(lnr[:].rearrange("p (c f) -> p c f", c=NCH),
                            lnh[:, :, k * CHUNK:(k + 1) * CHUNK]).then_inc(s_lr[k % 2].h, 16)
                if k >= 1:
                    q.wait_ge(s_add.h, k * 8)
                q.dma_start(xb[:].rearrange("p (c f) -> p c f", c=NCH),
                            src[:, :, k * CHUNK:(k + 1) * CHUNK]).then_inc(s_xin.h, 16)
                for n in range(NIT):
                    m = k * NIT + n
                    if m >= 2:
                        q.wait_ge(s_gum.h, (m - 1) * 2)
                    q.dma_start(gub[m % 2][:, 0:1024].rearrange("p (ic w) -> p ic w", ic=8),
                                qg[:, :, n * 128:(n + 1) * 128]).then_inc(s_gu[m % 2].h, 16)
                    q.dma_start(gub[m % 2][:, 1024:2048].rearrange("p (ic w) -> p ic w", ic=8),
                                qu[:, :, n * 128:(n + 1) * 128]).then_inc(s_gu[m % 2].h, 16)
                for o in range(NCH):
                    m2 = k * 8 + o
                    if m2 >= 1:
                        q.wait_ge(s_dmm.h, m2)
                    q.dma_start(dwb[m2 % 2][:].rearrange("p (nt w) -> p nt w", nt=NIT),
                                qd[:, :, o * 128:(o + 1) * 128]).then_inc(s_dwl[m2 % 2].h, 16)
        s_lr[0].n = 4 * 16
        s_lr[1].n = 4 * 16
        s_xin.n = NCK * 16
        s_gu[0].n = NCK * NIT * 16
        s_gu[1].n = NCK * NIT * 16
        s_dwl[0].n = NCK * 4 * 16
        s_dwl[1].n = NCK * 4 * 16

        @b.vector
        def _(v):
            for k in range(NCK):
                v.wait_ge(s_lr[k % 2].h, (k // 2 + 1) * 16)
                if k >= 1:
                    v.wait_ge(s_gum.h, k * NIT * 2)
                for c in range(NCH):
                    src_t = lnr[:, c * CHUNK:(c + 1) * CHUNK]
                    v.transpose(btil[:], src_t).then_inc(s_dv.h, 1)
                    s_dv.n += 1
                    s_dv.wait(v)
                    emit_quant_tile_dve(v, btil[:], btq[:], scr, CHUNK, s_dv, s_ac,
                                        ac_chunk(k) + c * ACT_Q)
                    s_dv.wait(v)
                    v.transpose(xq[:, c * CHUNK:(c + 1) * CHUNK], btq[:]).then_inc(s_dv.h, 1)
                    s_dv.n += 1
                hqs = hq_[:]
                for n in range(NIT):
                    m = k * NIT + n
                    v.wait_ge(s_sil.h, m + 1)
                    if k >= 1 and n == 0:
                        v.wait_ge(s_dmm.h, k * 8)
                    v.tensor_mul(h1t[:], tmps[:], ctx.bank(2 + m % 2, 128, CHUNK)).then_inc(s_swg.h, 1)
                    s_swg.n += 1
                    s_swg.wait(v)
                    v.transpose(btil[:], h1t[:]).then_inc(s_dv.h, 1)
                    s_dv.n += 1
                    s_dv.wait(v)
                    emit_quant_tile_dve(v, btil[:], btq[:], scr, CHUNK, s_dv, s_ac,
                                        ac_chunk(k) + (8 + n) * ACT_Q)
                    s_dv.wait(v)
                    v.transpose(hqs[:, n * CHUNK:(n + 1) * CHUNK], btq[:]).then_inc(s_dv.h, 1)
                    s_dv.n += 1
                v.wait_ge(s_xin.h, (k + 1) * 16)
                for o in range(NCH):
                    m2 = k * 8 + o
                    v.wait_ge(s_dmm.h, m2 + 1)
                    if m2 >= 2:
                        v.wait_ge(s_st[m2 % 2].h, ((m2 - 2) // 2 + 1) * 16)
                    v.tensor_add(ybr[:, (m2 % 2) * CHUNK:(m2 % 2 + 1) * CHUNK],
                                 xb[:, o * CHUNK:(o + 1) * CHUNK],
                                 ctx.bank(4 + m2 % 2)).then_inc(s_add.h, 1)
        s_add.n = NCK * 8

        @b.scalar
        def _(a):
            for k in range(NCK):
                for c in range(NCH):
                    a.wait_ge(s_dv.h, dv_chunk(k) + c * DVX + 1)
                    emit_quant_tile_act(a, btil[:], scr, CHUNK, s_dv,
                                        dv_chunk(k) + c * DVX, s_ac)
                for n in range(NIT):
                    m = k * NIT + n
                    a.wait_ge(s_gum.h, (m + 1) * 2)
                    if m >= 1:
                        a.wait_ge(s_swg.h, m)
                    a.activation(tmps[:], ctx.bank(0 + m % 2, 128, CHUNK), AF.Silu).then_inc(s_sil.h, 1)
                    s_sil.n += 1
                    a.wait_ge(s_dv.h, dv_chunk(k) + (8 + n) * DVX + 1)
                    emit_quant_tile_act(a, btil[:], scr, CHUNK, s_dv,
                                        dv_chunk(k) + (8 + n) * DVX, s_ac)

        @b.tensor
        def _(t):
            t.wait_ge(ctx.s_const.h, ctx.s_const.n)
            for k in range(NCK):
                t.wait_ge(s_dv.h, dv_chunk(k) + 8 * DVX)
                for n in range(NIT):
                    m = k * NIT + n
                    t.wait_ge(s_gu[m % 2].h, (m // 2 + 1) * 32)
                    if m >= 2:
                        t.wait_ge(s_sil.h, m - 1)
                        t.wait_ge(s_swg.h, m - 1)
                    pairs_g = [(gub[m % 2][:, c * 128:(c + 1) * 128],
                                xq[:, c * CHUNK:(c + 1) * CHUNK]) for c in range(8)]
                    mm_chain(t, ctx.bank(0 + m % 2), pairs_g)
                    pairs_u = [(gub[m % 2][:, 1024 + c * 128: 1024 + (c + 1) * 128],
                                xq[:, c * CHUNK:(c + 1) * CHUNK]) for c in range(8)]
                    mm_chain(t, ctx.bank(2 + m % 2), pairs_u).then_inc(s_gum.h, 2)
                hqs = hq_[:]
                t.wait_ge(s_dv.h, dv_chunk(k + 1))
                for o in range(NCH):
                    m2 = k * 8 + o
                    t.wait_ge(s_dwl[m2 % 2].h, (m2 // 2 + 1) * 16)
                    if m2 >= 2:
                        t.wait_ge(s_add.h, m2 - 1)
                    pairs = [(dwb[m2 % 2][:, c * 128:(c + 1) * 128],
                              hqs[:, c * CHUNK:(c + 1) * CHUNK]) for c in range(NIT)]
                    mm_chain(t, ctx.bank(4 + m2 % 2), pairs).then_inc(s_dmm.h, 1)
        s_gum.n = NCK * NIT * 2
        s_dmm.n = NCK * 8

        @b.gpsimd
        def _(g):
            for k in range(NCK):
                for o in range(NCH):
                    m2 = k * 8 + o
                    g.wait_ge(s_add.h, m2 + 1)
                    g.dma_start(dst[:, o, k * CHUNK:(k + 1) * CHUNK],
                                ybr[:, (m2 % 2) * CHUNK:(m2 % 2 + 1) * CHUNK]).then_inc(s_st[m2 % 2].h, 16)
            g.wait_ge(s_st[0].h, NCK * 4 * 16)
            g.wait_ge(s_st[1].h, NCK * 4 * 16)
        s_st[0].n = NCK * 4 * 16
        s_st[1].n = NCK * 4 * 16

    return s_st


def build_launch_l2ffn():
    nc = bass.Bass("TRN2", target_bir_lowering=False, detect_race_conditions=False)
    ctx = Ctx(nc)
    ctx.din("xT", [D, T], F32)
    ctx.din("L2_ffn_w_gate", [2816, 1024])
    ctx.din("L2_ffn_w_up", [2816, 1024])
    ctx.din("L2_ffn_w_down", [1024, 2816])
    ctx.dout("yT", [D, T], F32)
    ctx.dint("lnh", [D, T], F32)
    ctx.load_consts()
    RING = 2
    lnx = ctx.sbuf("lnring", [128, NCH * RING * CHUNK])
    lnx_at = lambda c, k: lnx[:, (c * RING + k % RING) * CHUNK:
                              (c * RING + k % RING + 1) * CHUNK]
    lnx_view = lambda k: lnx[:].rearrange("p (c s f) -> p c s f", c=NCH, s=RING)[:, :, k % RING, :]
    s_ln, s_lnst = emit_ln(ctx, "xT", lnx_at, "ln2f", bank0=0, ring=RING, to_hbm="lnh",
                           lnx_view=lnx_view)
    emit_l2ffn(ctx, s_lnst, "lnh", "xT", "yT")
    return ctx



def emit_l0attn(ctx, s_lnst, lnh_name, src_name, out_name, lnr):
    nc = ctx.nc
    ident = ctx.sb["ident"]
    identf = ctx.sb["identf"]
    ones128 = ctx.sb["ones128"]
    onesrow = ctx.sb["onesrow"]
    q512 = ctx.sb["q512"]
    blk64 = ctx.sb["blk64"]
    lnh = ctx.dram[lnh_name].rearrange("(c p) t -> p c t", p=128)
    SL = _alibi_slopes()
    CUT = 60.0
    nkv4 = ctx.dram["L0_attn_null_kv"].rearrange("h (j two) d -> h j two d", two=2)

    ctx.dint("oth", [1024, T], BF16)
    oth = ctx.dram["oth"].rearrange("(hp p) t -> p hp t", p=128)

    def ktmin(h, qc):
        for kt in range(0, 4 * qc + 4):
            if SL[h] * (qc * 512 - (kt * 128 + 127)) <= CUT:
                return kt
        return 4 * qc + 3

    pool = ctx.sbuf("a0_pool", [128, NCH * CHUNK])
    wstg = pool[:, 0:3072]
    grow = pool[0:1, 3072:4096]
    wpb = ctx.sbuf("a0_wpb", [128, 8 * 384], BF16)
    gcol = ctx.sbuf("a0_gcol", [128, 8])
    qs8col = ctx.sbuf("a0_qs8", [128, 1])
    kscol = ctx.sbuf("a0_ks", [128, 1])
    iq2col = ctx.sbuf("a0_iq2", [128, 1])
    ik2col = ctx.sbuf("a0_ik2", [128, 1])
    qsrow = ctx.sbuf("a0_qsrow", [1, 64])
    ksrow = ctx.sbuf("a0_ksrow", [1, 64])
    qaug = [ctx.sbuf("a0_qa", [66, T], BF16), ctx.sbuf("a0_qb", [66, T], BF16)]
    kaug = [ctx.sbuf("a0_ka", [66, T], BF16), ctx.sbuf("a0_kb", [66, T], BF16)]
    Vp = ctx.sbuf("a0_vp", [128, 32 * 130], BF16)
    OTp = ctx.sbuf("a0_otp", [128, T], BF16)
    q2sb = ctx.sbuf("a0_q2sb", [128, CHUNK], BF16)
    k2sb = ctx.sbuf("a0_k2sb", [128, CHUNK], BF16)
    qsq = ctx.sbuf("a0_qsq", [128, CHUNK], BF16)
    ksq = ctx.sbuf("a0_ksq", [128, CHUNK], BF16)
    kwsq = ctx.sbuf("a0_kwsq", [128, CHUNK], BF16)
    rowf = ctx.sbuf("a0_rowf", [1, CHUNK])
    rowqAb = ctx.sbuf("a0_rqab", [1, CHUNK], BF16)
    rowqBb = ctx.sbuf("a0_rqbb", [1, CHUNK], BF16)
    rowkAb = ctx.sbuf("a0_rkab", [1, CHUNK], BF16)
    rowkBb = ctx.sbuf("a0_rkbb", [1, CHUNK], BF16)
    rowkAf = ctx.sbuf("a0_rkaf", [1, CHUNK])
    rowkBf = ctx.sbuf("a0_rkbf", [1, CHUNK])
    r2s = ctx.sbuf("a0_r2s", [1, CHUNK])
    knrow = ctx.sbuf("a0_knrow", [1, CHUNK])
    cmax = ctx.sbuf("a0_cmax", [1, 1])
    kmax = [ctx.sbuf("a0_kmA", [1, 1]), ctx.sbuf("a0_kmB", [1, 1])]
    kml = ctx.sbuf("a0_kml", [1, 1])
    kmh = ctx.sbuf("a0_kmh", [1, 1])
    mrowfull = ctx.sbuf("a0_mrow", [1, T], BF16)
    nkb = ctx.sbuf("a0_nkb", [4, 64])
    nsq4 = ctx.sbuf("a0_nsq4", [4, 64])
    nn4 = ctx.sbuf("a0_nn4", [4, 1])
    nrs4 = ctx.sbuf("a0_nrs4", [4, 1])
    nks2 = ctx.sbuf("a0_nks2", [4, 64])
    nsq2 = ctx.sbuf("a0_nsq2", [64, 4], BF16)
    knullaug = [ctx.sbuf("a0_kna", [66, 2], BF16), ctx.sbuf("a0_knb", [66, 2], BF16)]
    vnullaug = [ctx.sbuf("a0_vna", [2, 65], BF16), ctx.sbuf("a0_vnb", [2, 65], BF16)]
    PT = ctx.sbuf("a0_pt", [128, 2 * 512], BF16)
    Osb = ctx.sbuf("a0_osb", [64, 512], BF16)
    denf = ctx.sbuf("a0_denf", [1, 512])
    denrb = ctx.sbuf("a0_denrb", [1, 512], BF16)
    woutres = ctx.sbuf("a0_wout", [128, 8 * 1024], BF16)
    othb = ctx.ln_bufs()["xbf"]

    s_r0 = ctx.sem("a0_r0")
    s_tp0 = ctx.sem("a0_tp0")
    s_te0 = ctx.sem("a0_te0")
    s_wl = ctx.sem("a0_wl")
    s_wsc = ctx.sem("a0_wsc")
    s_lr = [ctx.sem("a0_lr0"), ctx.sem("a0_lr1")]
    s_ape = ctx.sem("a0_ape")
    s_aac = ctx.sem("a0_aac")
    s_adv = ctx.sem("a0_adv")
    s_nl = ctx.sem("a0_nl")
    s_mr = ctx.sem("a0_mr")
    s_fpe = ctx.sem("a0_fpe")
    s_fex = ctx.sem("a0_fex")
    s_fac = ctx.sem("a0_fac")
    s_fdv = ctx.sem("a0_fdv")
    s_ots = ctx.sem("a0_ots")

    with nc.Block() as b:
        @b.gpsimd
        def _(g):
            s_r0.inc(g.dma_start(grow, ctx.dram["L0_attn_gamma"].rearrange("(o n) -> o n", o=1)), 16)
            s_r0.inc(g.dma_start(qsrow[:], ctx.dram["L0_attn_q_scale"].rearrange("(o n) -> o n", o=1)), 16)
            s_r0.inc(g.dma_start(ksrow[:], ctx.dram["L0_attn_k_scale"].rearrange("(o n) -> o n", o=1)), 16)
            s_r0.inc(g.dma_start(woutres[:].rearrange("p (c w) -> p c w", c=8),
                                 ctx.dram["L0_attn_w_out"].rearrange("(c p) w -> p c w", p=128)), 16)

        @b.tensor
        def _(t):
            t.wait_ge(ctx.s_const.h, ctx.s_const.n)
            t.wait_ge(s_r0.h, 64)
            for c in range(8):
                if c >= 1:
                    t.wait_ge(s_te0.h, c)
                s_tp0.inc(t.transpose(ctx.bank(5, 128, 1), grow[0:1, c * 128:(c + 1) * 128],
                                      identf[0:1, 0:1]))
            t.wait_ge(s_te0.h, 8)
            s_tp0.inc(t.transpose(ctx.bank(5, 64, 1), qsrow[:], identf[0:1, 0:1]))
            t.wait_ge(s_te0.h, 9)
            s_tp0.inc(t.transpose(ctx.bank(5, 64, 1), ksrow[:], identf[0:1, 0:1]))

        @b.scalar
        def _(a):
            for c in range(8):
                a.wait_ge(s_tp0.h, c + 1)
                s_te0.inc(a.copy(gcol[:, c:c + 1], ctx.bank(5, 128, 1)))
            a.wait_ge(s_tp0.h, 9)
            a.activation(qs8col[0:64, 0:1], ctx.bank(5, 64, 1), AF.Copy, scale=8.0)
            s_te0.inc(a.copy(qs8col[64:128, 0:1], qs8col[0:64, 0:1]))
            a.wait_ge(s_tp0.h, 10)
            a.copy(kscol[0:64, 0:1], ctx.bank(5, 64, 1))
            s_te0.inc(a.copy(kscol[64:128, 0:1], kscol[0:64, 0:1]))

        @b.vector
        def _(v):
            vpv = Vp[:].rearrange("p (t w) -> p t w", w=130)
            v.memset(vpv[:, :, 64:65], 1.0)
            v.memset(vpv[:, :, 129:130], 1.0)
            v.wait_ge(s_te0.h, 10)
            v.tensor_mul(iq2col[:], qs8col[:], qs8col[:])
            v.tensor_mul(ik2col[:], kscol[:], kscol[:]).then_inc(s_adv.h, 1)
            s_adv.n += 1
            v.wait_ge(s_adv.h, s_adv.n)
            v.reciprocal(iq2col[:], iq2col[:])
            v.reciprocal(ik2col[:], ik2col[:]).then_inc(s_adv.h, 1)
            s_adv.n += 1

    t1_end = {}
    v0_end = {}
    t2_end = {}
    t3_end = {}
    a1_end = {}
    a2_end = {}
    v1_end = {}

    for p in range(8):
        hA = 2 * p

        with nc.Block() as b:
            @b.sync
            def _(q, p=p):
                wq = ctx.dram["L0_attn_w_q"].rearrange("(c p) w -> p c w", p=128)
                wkv = ctx.dram["L0_attn_w_kv"].rearrange("(c p) w -> p c w", p=128)
                s_wl.inc(q.dma_start(wstg[:, 0:1024].rearrange("p (c w) -> p c w", c=8),
                                     wq[:, :, p * 128:(p + 1) * 128]), 16)
                s_wl.inc(q.dma_start(wstg[:, 1024:2048].rearrange("p (c w) -> p c w", c=8),
                                     wkv[:, :, p * 128:(p + 1) * 128]), 16)
                s_wl.inc(q.dma_start(wstg[:, 2048:3072].rearrange("p (c w) -> p c w", c=8),
                                     wkv[:, :, 1024 + p * 128:1024 + (p + 1) * 128]), 16)
                s_wl.inc(q.dma_start(kaug[0][64:66, :], ctx.dram["c_kaug2"]), 16)
                s_wl.inc(q.dma_start(kaug[1][64:66, :], ctx.dram["c_kaug2"]), 16)

            @b.vector
            def _(v, p=p, hA=hA):
                s_wl.wait(v)
                v.wait_ge(s_te0.h, 8)
                for c in range(8):
                    v.tensor_scalar(wpb[:, c * 384:c * 384 + 128],
                                    wstg[:, c * 128:(c + 1) * 128],
                                    gcol[:, c:c + 1], None, ALU.mult)
                    v.tensor_scalar(wpb[:, c * 384 + 128:c * 384 + 256],
                                    wstg[:, 1024 + c * 128:1024 + (c + 1) * 128],
                                    gcol[:, c:c + 1], None, ALU.mult)
                    v.tensor_scalar(wpb[:, c * 384 + 256:c * 384 + 384],
                                    wstg[:, 2048 + c * 128:2048 + (c + 1) * 128],
                                    gcol[:, c:c + 1], None, ALU.mult)
                v.memset(kmax[0][:], 0.0)
                v.memset(kmax[1][:], 0.0)
                v.memset(qaug[0][64:65, :], float(SL[hA]))
                s_wsc.inc(v.memset(qaug[1][64:65, :], float(SL[hA + 1])))

        with nc.Block() as b:
            for k in range(NCK):
                @b.sync
                def _(q, p=p, k=k):
                    q.wait_ge(s_lnst[k % 2].h, (k // 2 + 1) * 16)
                    if k >= 2:
                        q.wait_ge(s_ape.h, t1_end[(p, k - 2)])
                    q.dma_start(lnr[:].rearrange("p (c s f) -> p c s f", c=NCH, s=2)[:, :, k % 2, :],
                                lnh[:, :, k * CHUNK:(k + 1) * CHUNK]).then_inc(s_lr[k % 2].h, 16)
                    s_lr[k % 2].n += 16

                @b.tensor
                def _(t, p=p, k=k):
                    s_lr[k % 2].wait(t)
                    s_wsc.wait(t)
                    if k >= 1:
                        t.wait_ge(s_aac.h, a1_end[(p, k - 1)])
                    for c in range(8):
                        t.matmul(ctx.bank(4), wpb[:, c * 384:c * 384 + 128],
                                 lnr[:, (c * 2 + k % 2) * CHUNK:(c * 2 + k % 2 + 1) * CHUNK],
                                 start=(c == 0), stop=(c == 7))
                    for c in range(8):
                        t.matmul(ctx.bank(5), wpb[:, c * 384 + 128:c * 384 + 256],
                                 lnr[:, (c * 2 + k % 2) * CHUNK:(c * 2 + k % 2 + 1) * CHUNK],
                                 start=(c == 0), stop=(c == 7))
                    ins = None
                    for j4 in range(4):
                        for c in range(8):
                            ins = t.matmul(
                                ctx.bank(2)[:, j4 * 128:(j4 + 1) * 128],
                                lnr[:, (c * 2 + k % 2) * CHUNK + j4 * 128:
                                     (c * 2 + k % 2) * CHUNK + (j4 + 1) * 128],
                                wpb[:, c * 384 + 256:c * 384 + 384],
                                start=(c == 0), stop=(c == 7))
                    s_ape.inc(ins)
                    t1_end[(p, k)] = s_ape.n

                @b.scalar
                def _(a, p=p, k=k):
                    a.wait_ge(s_ape.h, t1_end[(p, k)])
                    if k >= 1:
                        a.wait_ge(s_adv.h, v1_end[(p, k - 1)])
                        a.wait_ge(s_ape.h, t2_end[(p, k - 1)])
                    a.activation(q2sb[:], ctx.bank(4), AF.Copy, scale=qs8col[:, 0:1])
                    a.activation(k2sb[:], ctx.bank(5), AF.Copy, scale=kscol[:, 0:1])
                    ins = None
                    for j4 in range(4):
                        jt = k * 4 + j4
                        a.copy(Vp[:, jt * 130 + 0:jt * 130 + 64],
                               ctx.bank(2)[:, j4 * 128:j4 * 128 + 64])
                        ins = a.copy(Vp[:, jt * 130 + 65:jt * 130 + 129],
                                     ctx.bank(2)[:, j4 * 128 + 64:j4 * 128 + 128])
                    s_aac.inc(ins)
                    a1_end[(p, k)] = s_aac.n

                @b.vector
                def _(v, p=p, k=k):
                    v.wait_ge(s_aac.h, a1_end[(p, k)])
                    v.tensor_mul(qsq[:], q2sb[:], q2sb[:])
                    v.tensor_mul(kwsq[:], k2sb[:], k2sb[:]).then_inc(s_adv.h, 1)
                    s_adv.n += 1
                    v.wait_ge(s_adv.h, s_adv.n)
                    v.tensor_scalar(qsq[:], qsq[:], iq2col[:, 0:1], None, ALU.mult)
                    s_adv.inc(v.tensor_scalar(ksq[:], kwsq[:], ik2col[:, 0:1], None, ALU.mult))
                    v0_end[(p, k)] = s_adv.n

                @b.tensor
                def _(t, p=p, k=k):
                    t.wait_ge(s_aac.h, a1_end[(p, k)])
                    t.wait_ge(s_adv.h, v0_end[(p, k)])
                    if k >= 1:
                        t.wait_ge(s_adv.h, v1_end[(p, k - 1)])
                    t.matmul(ctx.bank(6, 64, 512), blk64[:], qsq[:], start=True, stop=True)
                    t.matmul(ctx.bank(7, 64, 512), blk64[:], ksq[:], start=True, stop=True)
                    s_ape.inc(t.matmul(ctx.bank(3, 64, 512), blk64[:], kwsq[:], start=True, stop=True))
                    t2_end[(p, k)] = s_ape.n

                @b.scalar
                def _(a, p=p, k=k):
                    a.wait_ge(s_ape.h, t2_end[(p, k)])
                    ins = None
                    for src, dstb, dstf in ((ctx.bank(6, 64, 512)[0:1, :], rowqAb, None),
                                            (ctx.bank(6, 64, 512)[32:33, :], rowqBb, None),
                                            (ctx.bank(7, 64, 512)[0:1, :], rowkAb, rowkAf),
                                            (ctx.bank(7, 64, 512)[32:33, :], rowkBb, rowkBf)):
                        fdst = dstf if dstf is not None else rowf
                        a.activation(fdst[:], src, AF.Ln)
                        a.activation(fdst[:], fdst[:], AF.Exp, scale=-0.5)
                        ins = a.copy(dstb[:], fdst[:])
                    s_aac.inc(ins)
                    a2_end[(p, k)] = s_aac.n

                @b.tensor
                def _(t, p=p, k=k):
                    t.wait_ge(s_aac.h, a2_end[(p, k)])
                    t.matmul(ctx.bank(6, 64, 512), onesrow[0:1, 0:64], rowqAb[:], start=True, stop=True)
                    t.matmul(ctx.bank(0, 64, 512), onesrow[0:1, 0:64], rowqBb[:], start=True, stop=True)
                    t.matmul(ctx.bank(7, 64, 512), onesrow[0:1, 0:64], rowkAb[:], start=True, stop=True)
                    s_ape.inc(t.matmul(ctx.bank(1, 64, 512), onesrow[0:1, 0:64], rowkBb[:],
                                       start=True, stop=True))
                    t3_end[(p, k)] = s_ape.n

                @b.vector
                def _(v, p=p, k=k):
                    v.wait_ge(s_ape.h, t3_end[(p, k)])
                    sl_ = slice(k * CHUNK, (k + 1) * CHUNK)
                    v.tensor_mul(qaug[0][0:64, sl_], q2sb[0:64, :], ctx.bank(6, 64, 512))
                    v.tensor_mul(qaug[1][0:64, sl_], q2sb[64:128, :], ctx.bank(0, 64, 512))
                    v.tensor_mul(kaug[0][0:64, sl_], k2sb[0:64, :], ctx.bank(7, 64, 512))
                    v.tensor_mul(kaug[1][0:64, sl_], k2sb[64:128, :], ctx.bank(1, 64, 512))
                    for hh, rkf, b3r in ((0, rowkAf, ctx.bank(3, 64, 512)[0:1, :]),
                                         (1, rowkBf, ctx.bank(3, 64, 512)[32:33, :])):
                        v.tensor_mul(r2s[:], rkf[:], rkf[:])
                        v.tensor_mul(knrow[:], b3r, r2s[:]).then_inc(s_adv.h, 1)
                        s_adv.n += 1
                        v.wait_ge(s_adv.h, s_adv.n)
                        v.tensor_reduce(cmax[:], knrow[:], AX.X, ALU.max).then_inc(s_adv.h, 1)
                        s_adv.n += 1
                        v.wait_ge(s_adv.h, s_adv.n)
                        v.tensor_tensor(kmax[hh][:], kmax[hh][:], cmax[:], ALU.max).then_inc(s_adv.h, 1)
                        s_adv.n += 1
                        v.wait_ge(s_adv.h, s_adv.n)
                    s_adv.inc(v.engine_nop())
                    v1_end[(p, k)] = s_adv.n

        cnt = {}
        with nc.Block() as b:
            @b.gpsimd
            def _(g, p=p, hA=hA):
                s_nl.inc(g.dma_start(nkb[:],
                                     nkv4[hA:hA + 2, :, 0:1, :].rearrange("h j o d -> (h j o) d")), 16)
                s_nl.inc(g.dma_start(vnullaug[0][0:2, 0:64],
                                     nkv4[hA:hA + 1, :, 1:2, :].rearrange("h j o d -> (h j o) d")), 16)
                s_nl.inc(g.dma_start(vnullaug[1][0:2, 0:64],
                                     nkv4[hA + 1:hA + 2, :, 1:2, :].rearrange("h j o d -> (h j o) d")), 16)
                s_nl.inc(g.dma_start(knullaug[0][64:66, :], ctx.dram["c_nullrow"]), 16)
                s_nl.inc(g.dma_start(knullaug[1][64:66, :], ctx.dram["c_nullrow"]), 16)

            @b.vector
            def _(v, p=p):
                v.wait_ge(s_nl.h, s_nl.n - 64)
                v.tensor_mul(nsq4[:], nkb[:], nkb[:]).then_inc(s_adv.h, 1)
                s_adv.n += 1
                v.wait_ge(s_adv.h, s_adv.n)
                v.tensor_reduce(nn4[:], nsq4[:], AX.X, ALU.add).then_inc(s_adv.h, 1)
                s_adv.n += 1
                cnt["nn4"] = s_adv.n

            @b.scalar
            def _(a, p=p):
                a.wait_ge(s_adv.h, cnt["nn4"])
                a.activation(nrs4[:], nn4[:], AF.Ln)
                s_aac.inc(a.activation(nrs4[:], nrs4[:], AF.Exp, scale=-0.5))
                cnt["nrs4"] = s_aac.n

            @b.vector
            def _(v, p=p):
                v.wait_ge(s_aac.h, cnt["nrs4"])
                s_adv.inc(v.tensor_scalar(nks2[:], nkb[:], nrs4[0:4, 0:1], None, ALU.mult))
                cnt["nks2"] = s_adv.n

            @b.tensor
            def _(t, p=p):
                t.wait_ge(s_adv.h, cnt["nks2"])
                s_fpe.inc(t.transpose(ctx.bank(6, 64, 4), nks2[:], identf[0:4, 0:4]))
                cnt["ntp"] = s_fpe.n

            @b.vector
            def _(v, p=p):
                v.wait_ge(s_fpe.h, cnt["ntp"])
                v.tensor_scalar(knullaug[0][0:64, :], ctx.bank(6, 64, 4)[:, 0:2],
                                kscol[0:64, 0:1], None, ALU.mult)
                v.tensor_scalar(knullaug[1][0:64, :], ctx.bank(6, 64, 4)[:, 2:4],
                                kscol[0:64, 0:1], None, ALU.mult)
                v.memset(vnullaug[0][0:2, 64:65], 1.0)
                s_adv.inc(v.memset(vnullaug[1][0:2, 64:65], 1.0))
                cnt["knts"] = s_adv.n

            @b.scalar
            def _(a, p=p):
                a.wait_ge(s_adv.h, cnt["knts"])
                a.activation(nsq2[:, 0:2], knullaug[0][0:64, :], AF.Square)
                s_aac.inc(a.activation(nsq2[:, 2:4], knullaug[1][0:64, :], AF.Square))
                cnt["nsq2"] = s_aac.n

            @b.tensor
            def _(t, p=p):
                t.wait_ge(s_aac.h, cnt["nsq2"])
                s_fpe.inc(t.matmul(ctx.bank(7, 1, 4), ones128[0:64, 0:1], nsq2[:],
                                   start=True, stop=True))
                cnt["nmm"] = s_fpe.n

            @b.vector
            def _(v, p=p):
                v.wait_ge(s_fpe.h, cnt["nmm"])
                for hh in range(2):
                    v.tensor_reduce(cmax[:], ctx.bank(7, 1, 4)[:, 2 * hh:2 * hh + 2],
                                    AX.X, ALU.max).then_inc(s_adv.h, 1)
                    s_adv.n += 1
                    v.wait_ge(s_adv.h, s_adv.n)
                    v.tensor_tensor(kmax[hh][:], kmax[hh][:], cmax[:], ALU.max).then_inc(s_adv.h, 1)
                    s_adv.n += 1
                    v.wait_ge(s_adv.h, s_adv.n)
                cnt["kmax"] = s_adv.n

            for hh in range(2):
                @b.scalar
                def _(a, p=p, hh=hh):
                    a.wait_ge(s_adv.h, cnt["kmax"])
                    s_aac.inc(a.activation(kml[:], kmax[hh][:], AF.Ln))
                    cnt["kml"] = s_aac.n

                @b.vector
                def _(v, p=p, hh=hh):
                    v.wait_ge(s_aac.h, cnt["kml"])
                    if hh == 1:
                        v.wait_ge(s_mr.h, cnt["mr0"])
                    s_adv.inc(v.tensor_scalar(kmh[:], kml[:], 0.5, None, ALU.mult))
                    cnt["kmh"] = s_adv.n

                for kc in range(NCK):
                    @b.vector
                    def _(v, p=p, hh=hh, kc=kc):
                        v.wait_ge(s_adv.h, cnt["kmh"])
                        if "mln" in cnt:
                            v.wait_ge(s_aac.h, cnt["mln"])
                        sl_ = slice(kc * CHUNK, (kc + 1) * CHUNK)
                        s_adv.inc(v.tensor_mul(qsq[0:64, :], qaug[hh][0:64, sl_],
                                               qaug[hh][0:64, sl_]))
                        cnt["msq"] = s_adv.n

                    @b.tensor
                    def _(t, p=p, hh=hh, kc=kc):
                        t.wait_ge(s_adv.h, cnt["msq"])
                        if "mln" in cnt:
                            t.wait_ge(s_aac.h, cnt["mln"])
                        s_fpe.inc(t.matmul(ctx.bank(6, 1, 512), ones128[0:64, 0:1],
                                           qsq[0:64, :], start=True, stop=True))
                        cnt["mmm"] = s_fpe.n

                    @b.scalar
                    def _(a, p=p, hh=hh, kc=kc):
                        a.wait_ge(s_fpe.h, cnt["mmm"])
                        a.activation(rowf[:], ctx.bank(6, 1, 512), AF.Ln)
                        s_aac.inc(a.activation(rowqAb[:], rowf[:], AF.Exp, scale=0.5,
                                               bias=kmh[:, 0:1]))
                        cnt["mln"] = s_aac.n

                    @b.vector
                    def _(v, p=p, hh=hh, kc=kc, hA=hA):
                        v.wait_ge(s_aac.h, cnt["mln"])
                        sl_ = slice(kc * CHUNK, (kc + 1) * CHUNK)
                        s_adv.inc(v.scalar_tensor_tensor(
                            mrowfull[0:1, sl_], q512[0:1, sl_], -float(SL[hA + hh]),
                            rowqAb[:], ALU.mult, ALU.subtract))
                        cnt["mstt"] = s_adv.n

                @b.gpsimd
                def _(g, p=p, hh=hh):
                    g.wait_ge(s_adv.h, cnt["mstt"])
                    s_mr.inc(g.dma_start(qaug[hh][65:66, :], mrowfull[:]), 16)
                    cnt["mr%d" % hh] = s_mr.n

        for hh in range(2):
            emit_l0attn_flash(ctx, hh, hA + hh, qaug[hh], kaug[hh], knullaug[hh],
                              vnullaug[hh], Vp, OTp, PT, Osb, denf, denrb,
                              ktmin, s_nl, s_mr, s_fpe, s_fex, s_fac, s_fdv)

        with nc.Block() as b:
            @b.gpsimd
            def _(g, p=p):
                g.wait_ge(s_fdv.h, s_fdv.n)
                g.dma_start(oth[:, p, :], OTp[:]).then_inc(s_ots.h, 16)
                s_ots.n += 16
                g.wait_ge(s_ots.h, s_ots.n)

    s_oin = ctx.sem("a0_oin")
    s_res_pe = ctx.sem("a0_respe")

    def sync_extra(q, k):
        if k >= 1:
            q.wait_ge(s_res_pe.h, k * 8)
        q.dma_start(othb[:].rearrange("p (c f) -> p c f", c=NCH),
                    oth[:, :, k * CHUNK:(k + 1) * CHUNK]).then_inc(s_oin.h, 16)
        s_oin.n += 16

    def mm_fn(t, k, o, bank_ap):
        t.wait_ge(s_oin.h, (k + 1) * 16)
        pairs = [(woutres[:, c * 1024 + o * 128:c * 1024 + (o + 1) * 128],
                  othb[:, c * CHUNK:(c + 1) * CHUNK]) for c in range(8)]
        return mm_chain(t, bank_ap, pairs)

    return emit_residual_out(ctx, "a0m", src_name, out_name, s_ots, None, mm_fn,
                             s_pe=s_res_pe, sync_extra=sync_extra,
                             bufs={"xb": ctx.ln_bufs()["xb"], "yb": pool})


def emit_l0attn_flash(ctx, hh, h, qa, ka, knull, vnull, Vp, OTp, PT, Osb,
                      denf, denrb, ktmin, s_nl, s_mr, s_fpe, s_fex, s_fac, s_fdv):
    nc = ctx.nc
    ident = ctx.sb["ident"]
    onesrow = ctx.sb["onesrow"]
    alibit = ctx.sb["alibit"]
    maskq = ctx.sb["maskq"]
    NQC = T // 512
    prev = {}

    with nc.Block() as b:
        for qc in range(NQC):
            tiles = list(range(ktmin(h, qc), 4 * qc + 4)) + ["null"]
            nt = len(tiles)
            peb = s_fpe.n
            exb = s_fex.n

            @b.tensor
            def _(t, qc=qc, tiles=tiles, nt=nt, peb=peb, exb=exb):
                if qc == 0:
                    t.wait_ge(s_nl.h, s_nl.n)
                    t.wait_ge(s_mr.h, s_mr.n)

                def mm1(i, kt):
                    if i >= 2:
                        t.wait_ge(s_fex.h, exb + i - 1)
                    if kt == "null":
                        ins = t.matmul(ctx.bank(i % 2, 2, 512), knull[:],
                                       qa[:, qc * 512:(qc + 1) * 512], start=True, stop=True)
                    else:
                        dlt = 4 * qc - kt
                        if dlt in (0, -1, -2, -3):
                            t.matmul(ctx.bank(i % 2, 128, 512),
                                     ka[:, kt * 128:(kt + 1) * 128],
                                     qa[:, qc * 512:(qc + 1) * 512], start=True, stop=False)
                            ins = t.matmul(ctx.bank(i % 2, 128, 512), ident[:],
                                           maskq[:, -dlt * 512:(-dlt + 1) * 512],
                                           start=False, stop=True)
                        else:
                            ins = t.matmul(ctx.bank(i % 2, 128, 512),
                                           ka[:, kt * 128:(kt + 1) * 128],
                                           qa[:, qc * 512:(qc + 1) * 512], start=True, stop=True)
                    s_fpe.inc(ins)

                def mm2(i, kt):
                    t.wait_ge(s_fex.h, exb + i + 1)
                    if i == 0 and "recip" in prev:
                        t.wait_ge(s_fdv.h, prev["recip"])
                    if kt == "null":
                        ins2 = t.matmul(ctx.bank(2, 65, 512), vnull[:],
                                        PT[0:2, (i % 2) * 512:(i % 2) * 512 + 512],
                                        start=(i == 0), stop=True)
                    else:
                        ins2 = t.matmul(ctx.bank(2, 65, 512),
                                        Vp[:, kt * 130 + hh * 65:kt * 130 + hh * 65 + 65],
                                        PT[:, (i % 2) * 512:(i % 2) * 512 + 512],
                                        start=(i == 0), stop=(i == nt - 1))
                    s_fpe.inc(ins2)

                for i, kt in enumerate(tiles):
                    mm1(i, kt)
                    if i >= 1:
                        mm2(i - 1, tiles[i - 1])
                mm2(nt - 1, tiles[nt - 1])

            @b.scalar
            def _(a, qc=qc, tiles=tiles, nt=nt, peb=peb):
                for i, kt in enumerate(tiles):
                    a.wait_ge(s_fpe.h, peb + (1 if i == 0 else 2 * i))
                    dlt = 4 * qc if kt == "null" else 4 * qc - kt
                    col = h * 33 + dlt + 3
                    if kt == "null":
                        ins = a.activation(PT[0:2, (i % 2) * 512:(i % 2) * 512 + 512],
                                           ctx.bank(i % 2, 2, 512), AF.Exp,
                                           bias=alibit[0:2, col:col + 1])
                    else:
                        ins = a.activation(PT[:, (i % 2) * 512:(i % 2) * 512 + 512],
                                           ctx.bank(i % 2, 128, 512), AF.Exp,
                                           bias=alibit[:, col:col + 1])
                    s_fex.inc(ins)
                a.wait_ge(s_fpe.h, peb + 2 * nt)
                if "tt" in prev:
                    a.wait_ge(s_fdv.h, prev["tt"])
                s_fac.inc(a.copy(Osb[:], ctx.bank(2, 65, 512)[0:64, :]))

            @b.vector
            def _(v, qc=qc, nt=nt, peb=peb):
                v.wait_ge(s_fpe.h, peb + 2 * nt)
                v.tensor_copy(denf[:], ctx.bank(2, 65, 512)[64:65, :])
                v.reciprocal(denf[:], denf[:]).then_inc(s_fdv.h, 1)
                s_fdv.n += 1
                prev["recip"] = s_fdv.n
                v.wait_ge(s_fdv.h, s_fdv.n)
                if "rep" in prev:
                    v.wait_ge(s_fpe.h, prev["rep"])
                s_fdv.inc(v.tensor_copy(denrb[:], denf[:]))
                prev["denrb"] = s_fdv.n

            @b.tensor
            def _(t, qc=qc):
                t.wait_ge(s_fdv.h, prev["denrb"])
                s_fpe.inc(t.matmul(ctx.bank(3, 64, 512), onesrow[0:1, 0:64], denrb[:],
                                   start=True, stop=True))
                prev["rep"] = s_fpe.n

            @b.vector
            def _(v, qc=qc):
                v.wait_ge(s_fpe.h, prev["rep"])
                v.wait_ge(s_fac.h, s_fac.n)
                s_fdv.inc(v.tensor_mul(OTp[hh * 64:(hh + 1) * 64, qc * 512:(qc + 1) * 512],
                                       Osb[:], ctx.bank(3, 64, 512)))
                prev["tt"] = s_fdv.n


def build_launch_l0attn():
    nc = bass.Bass("TRN2", target_bir_lowering=False, detect_race_conditions=False)
    ctx = Ctx(nc)
    ctx.din("xT", [D, T], F32)
    ctx.din("L0_attn_gamma", [D])
    ctx.din("L0_attn_w_q", [1024, 1024])
    ctx.din("L0_attn_w_kv", [1024, 2048])
    ctx.din("L0_attn_null_kv", [16, 4, 64])
    ctx.din("L0_attn_q_scale", [64])
    ctx.din("L0_attn_k_scale", [64])
    ctx.din("L0_attn_w_out", [1024, 1024])
    ctx.dout("yT", [D, T], F32)
    ctx.dint("lnh", [D, T], BF16)
    ctx.load_consts(extra=("alibit", "maskq", "q512", "blk64"))
    RING = 2
    lnx = ctx.sbuf("lnring", [128, NCH * RING * CHUNK], BF16)
    lnx_at = lambda c, k: lnx[:, (c * RING + k % RING) * CHUNK:
                              (c * RING + k % RING + 1) * CHUNK]
    lnx_view = lambda k: lnx[:].rearrange("p (c s f) -> p c s f", c=NCH, s=RING)[:, :, k % RING, :]
    s_ln, s_lnst = emit_ln(ctx, "xT", lnx_at, "ln0a", double_ln=True, bank0=0,
                           ring=RING, to_hbm="lnh", lnx_view=lnx_view)
    emit_l0attn(ctx, s_lnst, "lnh", "xT", "yT", lnx)
    return ctx



def emit_l1attn(ctx, s_lnst, lnh_name, src_name, out_name, lnr):
    nc = ctx.nc
    ident = ctx.sb["ident"]
    onesrow = ctx.sb["onesrow"]
    CB = 64
    NCB = 1024 // CB
    NT2 = CB // 2
    GW = CB * 64

    ctx.dint("xph", [4096, T], BF16)
    ctx.dint("fh", [3072, T], BF16)
    ctx.dint("Ffile", [3, 128, 512 * 130])
    ctx.dint("yh0", [64, 1024 * 64], BF16)
    ctx.dint("yh1", [64, 1024 * 64], BF16)
    ctx.dint("y3h", [64, 1024 * 64])
    xph_ab = ctx.dram["xph"].rearrange("e (b a) -> b e a", a=64)
    fh_ab = ctx.dram["fh"].rearrange("e (b a) -> b e a", a=64)
    yh = [ctx.dram["yh0"], ctx.dram["yh1"]]

    Usb = ctx.sbuf("l1_usb", [128, 4160])
    Fsb = ctx.sbuf("l1_fsb", [128, 4160])
    Vb = ctx.sbuf("l1_v", [128, 4160], BF16)
    V2b = ctx.sbuf("l1_v2", [128, 4160], BF16)
    pool2 = ctx.sbuf("l1_pool2", [128, 4160])
    tmp1 = pool2[:, 0:2080]
    tmp2 = pool2[:, 2080:4160]
    WtTr = ctx.sbuf("l1_wttr", [65, 4096], BF16)
    WtTi = ctx.sbuf("l1_wtti", [65, 4096], BF16)
    ycur = ctx.sbuf("l1_ycur", [64, 4096], BF16)
    gate = ctx.sbuf("l1_gate", [64, 4096], BF16)
    ynext = ctx.sbuf("l1_yn", [64, 4096], BF16)
    ynf32 = Usb[0:64, 0:4096]
    xpb = ctx.sbuf("l1_xpb", [128, 4 * 512], BF16)
    bpr = ctx.sbuf("l1_bpr", [1, 4096], BF16)
    bfr = ctx.sbuf("l1_bfr", [1, 3072], BF16)
    fC1b = ctx.sbuf("l1_c1", [64, 130], BF16)
    fBDb = ctx.sbuf("l1_bd", [128, 384], BF16)
    fICbb = ctx.sbuf("l1_icb", [65, 128], BF16)
    fTW = ctx.sbuf("l1_tw", [128, 130])
    wslots = [Usb[:].bitcast(BF16)[:, 0:4096], Fsb[:].bitcast(BF16)[:, 0:4096]]
    ptslots = [pool2[:].bitcast(BF16)[:, 0:4096], Vb[:, 0:4096]]

    s_c = ctx.sem("l1_c")
    s_cb = ctx.sem("l1_cb")
    s_in = ctx.sem("l1_in")
    s_ws = ctx.sem("l1_ws")
    s_pe = ctx.sem("l1_pe")
    s_ac = ctx.sem("l1_ac")
    s_dv = ctx.sem("l1_dv")
    s_st = ctx.sem("l1_st")
    s_stg = ctx.sem("l1_stg")

    with nc.Block() as b:
        @b.sync
        def _(q):
            s_c.inc(q.dma_start(pool2[0:64, 0:130], ctx.dram["c_fC1"]), 16)
            s_c.inc(q.dma_start(pool2[:, 200:584], ctx.dram["c_fBD"]), 16)
            s_c.inc(q.dma_start(pool2[0:65, 600:728], ctx.dram["c_fICb"]), 16)
            s_c.inc(q.dma_start(fTW[:], ctx.dram["c_fTW"]), 16)

        @b.gpsimd
        def _(g):
            s_cb.inc(g.dma_start(bpr[:], ctx.dram["L1_attn_bp"].rearrange("(o n) -> o n", o=1)), 16)
            s_cb.inc(g.dma_start(bfr[:], ctx.dram["L1_attn_bf"].rearrange("(o n) -> o n", o=1)), 16)

        @b.scalar
        def _(a):
            s_c.wait(a)
            s_cb.wait(a)
            a.wait_ge(ctx.s_const.h, ctx.s_const.n)
            a.copy(fC1b[:], pool2[0:64, 0:130])
            a.copy(fBDb[:], pool2[:, 200:584])
            s_ac.inc(a.copy(fICbb[:], pool2[0:65, 600:728]))

    def emit_linear(wname, bias_sb, ne, src_kind, dst_name):
        w_d = ctx.dram[wname].rearrange("(db p) e -> p db e", p=128)
        dst = ctx.dram[dst_name].rearrange("(et p) t -> p et t", p=128)
        negr = ne // 4
        pe0, ac0 = s_pe.n, s_ac.n
        ws_after = {}
        st_after = {}

        with nc.Block() as b:
            @b.gpsimd
            def _(g):
                for k in range(NCK):
                    if src_kind == "pos":
                        posd = ctx.dram["c_posT"].rearrange("(db p) t -> p db t", p=128)
                        if k >= 2:
                            g.wait_ge(s_pe.h, pe0 + (k - 1) * ne)
                        g.dma_start(ptslots[k % 2].rearrange("p (db f) -> p db f", db=8),
                                    posd[:, :, k * 512:(k + 1) * 512]).then_inc(s_ws.h, 16)
                        s_ws.n += 16
                        ws_after[(k, "pos")] = s_ws.n
                    for eg in range(negr):
                        ng = k * negr + eg
                        if ng >= 2:
                            kk, gg = divmod(ng - 2, negr)
                            g.wait_ge(s_pe.h, pe0 + kk * ne + (gg + 1) * 4)
                        g.dma_start(wslots[ng % 2].rearrange("p (db e) -> p db e", db=8),
                                    w_d[:, :, eg * 512:(eg + 1) * 512]).then_inc(s_ws.h, 16)
                        s_ws.n += 16
                        ws_after[(k, eg)] = s_ws.n

            @b.sync
            def _(q):
                for k in range(NCK):
                    if src_kind == "lnh":
                        q.wait_ge(s_lnst[k % 2].h, (k // 2 + 1) * 16)
                        if k >= 2:
                            q.wait_ge(s_pe.h, pe0 + (k - 1) * ne)
                        q.dma_start(
                            lnr[:].rearrange("p (c s f) -> p c s f", c=NCH, s=2)[:, :, k % 2, :],
                            ctx.dram[lnh_name].rearrange("(c p) t -> p c t", p=128)[:, :, k * 512:(k + 1) * 512]
                        ).then_inc(s_in.h, 16)
                        s_in.n += 16
                        ws_after[(k, "ln")] = s_in.n
                    if k >= 1:
                        for et in range(ne):
                            n = (k - 1) * ne + et
                            q.wait_ge(s_ac.h, ac0 + n + 1)
                            q.dma_start(dst[:, et, (k - 1) * 512:k * 512],
                                        xpb[:, (n % 4) * 512:(n % 4 + 1) * 512]).then_inc(s_st.h, 16)
                            s_st.n += 16
                            st_after[n] = s_st.n
                for et in range(ne):
                    n = (NCK - 1) * ne + et
                    q.wait_ge(s_ac.h, ac0 + n + 1)
                    q.dma_start(dst[:, et, (NCK - 1) * 512:NCK * 512],
                                xpb[:, (n % 4) * 512:(n % 4 + 1) * 512]).then_inc(s_st.h, 16)
                    s_st.n += 16
                    st_after[n] = s_st.n
                q.wait_ge(s_st.h, s_st.n)

            @b.tensor
            def _(t):
                t.wait_ge(s_ac.h, 1)
                for k in range(NCK):
                    if src_kind == "lnh":
                        t.wait_ge(s_in.h, ws_after[(k, "ln")])
                    else:
                        t.wait_ge(s_ws.h, ws_after[(k, "pos")])
                    for et in range(ne):
                        n = k * ne + et
                        eg, er = et // 4, et % 4
                        t.wait_ge(s_ws.h, ws_after[(k, eg)])
                        if n >= 4:
                            t.wait_ge(s_ac.h, ac0 + n - 3)
                        pb = ctx.bank(4 + n % 4)
                        for db in range(8):
                            if src_kind == "lnh":
                                rhs = lnr[:, (db * 2 + k % 2) * 512:(db * 2 + k % 2 + 1) * 512]
                            else:
                                rhs = ptslots[k % 2][:, db * 512:(db + 1) * 512]
                            t.matmul(pb, wslots[(k * negr + eg) % 2][:, db * 512 + er * 128:
                                                                     db * 512 + (er + 1) * 128],
                                     rhs, start=(db == 0), stop=False)
                        s_pe.inc(t.matmul(pb, bias_sb[0:1, et * 128:(et + 1) * 128],
                                          onesrow[0:1, 0:512], start=False, stop=True))

            @b.scalar
            def _(a):
                for k in range(NCK):
                    for et in range(ne):
                        n = k * ne + et
                        a.wait_ge(s_pe.h, pe0 + n + 1)
                        if n >= 4:
                            a.wait_ge(s_st.h, st_after[n - 4])
                        s_ac.inc(a.copy(xpb[:, (n % 4) * 512:(n % 4 + 1) * 512],
                                        ctx.bank(4 + n % 4)))

    emit_linear("L1_attn_Wp", bpr, 32, "lnh", "xph")
    emit_linear("L1_attn_Wf", bfr, 24, "pos", "fh")

    Uv = Usb[:].rearrange("p (t g) -> p t g", g=130)
    Vv = Vb[:].rearrange("p (t g) -> p t g", g=130)
    V2v = V2b[:].rearrange("p (t g) -> p t g", g=130)
    NGR = (NT2 + 2) // 3
    GRP = [(g, g * 390, min(3, NT2 - 3 * g) * 130) for g in range(NGR)]
    Fv = Fsb[:].rearrange("p (t g) -> p t g", g=130)
    t1v = tmp1.rearrange("p (t m) -> p t m", m=65)
    t2v = tmp2.rearrange("p (t m) -> p t m", m=65)
    TWc = fTW[:, 0:65].rearrange("p (o m) -> p o m", o=1).broadcast_to([128, NT2, 65])
    TWs = fTW[:, 65:130].rearrange("p (o m) -> p o m", o=1).broadcast_to([128, NT2, 65])
    BDc, BDs, BDsn = fBDb[:, 0:128], fBDb[:, 128:256], fBDb[:, 256:384]

    def sec_s1(t, src_ab, pe0, ac0):
        for t_ in range(NT2):
            g = t_ // 3
            if g >= 8 and t_ % 3 == 0:
                t.wait_ge(s_ac.h, ac0 + g - 7)
            s_pe.inc(t.matmul(ctx.bank(g % 8)[:, (t_ % 3) * 130:(t_ % 3) * 130 + 130],
                              src_ab[:, t_ * 128:(t_ + 1) * 128],
                              fC1b[:], start=True, stop=True))

    def sec_evac(a, dst, pe0, mode="s1"):
        for g, off, w in GRP:
            if mode == "s1":
                a.wait_ge(s_pe.h, pe0 + min(3 * (g + 1), NT2))
            else:
                a.wait_ge(s_pe.h, pe0 + g + 1)
            s_ac.inc(a.copy(dst[:, off:off + w], ctx.bank(g % 8)[:, 0:w]))

    def sec_tw(v, ac0, inverse):
        v.wait_ge(s_ac.h, ac0 + NGR)
        Ur, Ui = Uv[:, :, 0:65], Uv[:, :, 65:130]
        Vr, Vi = Vv[:, :, 0:65], Vv[:, :, 65:130]
        if not inverse:
            v.tensor_mul(t1v[:], Ur, TWc)
            v.tensor_mul(t2v[:], Ui, TWs)
            v.tensor_add(Vr, t1v[:], t2v[:])
            v.tensor_mul(t1v[:], Ui, TWc)
            v.tensor_mul(t2v[:], Ur, TWs)
            v.tensor_sub(Vi, t1v[:], t2v[:])
            v.tensor_copy(V2v[:, :, 0:65], Vi)
            s_dv.inc(v.tensor_scalar(V2v[:, :, 65:130], Vr, -1.0, None, ALU.mult))
        else:
            v.tensor_mul(t1v[:], Ur, TWc)
            v.tensor_mul(t2v[:], Ui, TWs)
            v.tensor_sub(Vr, t1v[:], t2v[:])
            v.tensor_mul(t1v[:], Ui, TWc)
            v.tensor_mul(t2v[:], Ur, TWs)
            s_dv.inc(v.tensor_add(Vi, t1v[:], t2v[:]))

    def sec_s2(t, dv0, ac0, inverse):
        t.wait_ge(s_dv.h, dv0)
        t.wait_ge(s_ac.h, ac0 + NGR)
        for g, off, w in GRP:
            if g >= 8:
                t.wait_ge(s_ac.h, ac0 + NGR + g - 7)
            pb = ctx.bank(g % 8)[:, 0:w]
            t.matmul(pb, BDc, Vb[:, off:off + w], start=True, stop=False)
            s_pe.inc(t.matmul(pb, BDs, V2b[:, off:off + w], start=False, stop=True))

    def sec_pw(v, ac0, in0):
        v.wait_ge(s_ac.h, ac0 + NGR)
        v.wait_ge(s_in.h, in0)
        Yr, Yi = Uv[:, :, 0:65], Uv[:, :, 65:130]
        Fr, Fi = Fv[:, :, 0:65], Fv[:, :, 65:130]
        Zr, Zi = Vv[:, :, 0:65], Vv[:, :, 65:130]
        v.tensor_mul(t1v[:], Yr, Fr)
        v.tensor_mul(t2v[:], Yi, Fi)
        v.tensor_sub(Zr, t1v[:], t2v[:])
        v.tensor_mul(t1v[:], Yr, Fi)
        v.tensor_mul(t2v[:], Yi, Fr)
        v.tensor_add(Zi, t1v[:], t2v[:])
        v.tensor_scalar(V2v[:, :, 0:65], Zi, -1.0, None, ALU.mult)
        s_dv.inc(v.tensor_copy(V2v[:, :, 65:130], Zr))

    def sec_pivot(t, dv0, ac0):
        t.wait_ge(s_dv.h, dv0)
        for t_ in range(NT2):
            for ri in range(2):
                n = t_ * 2 + ri
                if n >= 4:
                    t.wait_ge(s_ac.h, ac0 + n - 3)
                s_pe.inc(t.transpose(ctx.bank(n % 4, 65, 128, BF16),
                                     Vb[:, t_ * 130 + ri * 65:t_ * 130 + (ri + 1) * 65],
                                     ident[:]))

    def sec_pivev(a, pe0):
        for t_ in range(NT2):
            for ri in range(2):
                n = t_ * 2 + ri
                a.wait_ge(s_pe.h, pe0 + n + 1)
                dst = WtTr if ri == 0 else WtTi
                s_ac.inc(a.copy(dst[:, t_ * 128:(t_ + 1) * 128], ctx.bank(n % 4, 65, 128, BF16)))

    def sec_is1(t, ac0, dv0):
        t.wait_ge(s_ac.h, ac0 + 2 * NT2)
        for ck in range(8):
            if ck >= 4:
                t.wait_ge(s_dv.h, dv0 + ck - 3)
            pb = ctx.bank(4 + ck % 4, 64, 512)
            t.matmul(pb, fICbb[:, 0:64], WtTr[:, ck * 512:(ck + 1) * 512], start=True, stop=False)
            s_pe.inc(t.matmul(pb, fICbb[:, 64:128], WtTi[:, ck * 512:(ck + 1) * 512],
                              start=False, stop=True))

    def sec_gate(v, pe0, in0, out_sb):
        for ck in range(8):
            v.wait_ge(s_pe.h, pe0 + ck + 1)
            if ck == 0:
                v.wait_ge(s_in.h, in0)
            s_dv.inc(v.tensor_mul(out_sb[:, ck * 512:(ck + 1) * 512],
                                  ctx.bank(4 + ck % 4, 64, 512),
                                  gate[:, ck * 512:(ck + 1) * 512]))

    for st in range(3):
        for chb in range(NCB):
            with nc.Block() as b:
                @b.sync
                def _(q, st=st, chb=chb):
                    q.dma_start(ycur[:].rearrange("b (e a) -> b e a", a=64),
                                fh_ab[:, st * 1024 + chb * CB:st * 1024 + (chb + 1) * CB, :]
                                ).then_inc(s_in.h, 16)
                    s_in.n += 16

                in0 = s_in.n
                pe0, ac0 = s_pe.n, s_ac.n

                @b.tensor
                def _(t, in0=in0, pe0=pe0, ac0=ac0):
                    t.wait_ge(s_in.h, in0)
                    sec_s1(t, ycur, pe0, ac0)

                @b.scalar
                def _(a, pe0=pe0):
                    sec_evac(a, Usb, pe0)

                dv0 = s_dv.n

                @b.vector
                def _(v, ac0=ac0):
                    sec_tw(v, ac0, inverse=False)

                pe1, ac1 = s_pe.n, s_ac.n

                @b.tensor
                def _(t, ac0=ac0):
                    sec_s2(t, s_dv.n, ac0, inverse=False)

                @b.scalar
                def _(a, pe1=pe1):
                    sec_evac(a, Fsb, pe1, mode="grp")

                @b.gpsimd
                def _(g, st=st, chb=chb, ac1=ac1):
                    g.wait_ge(s_ac.h, ac1 + NGR)
                    g.dma_start(ctx.dram["Ffile"][st][:, chb * NT2 * 130:(chb + 1) * NT2 * 130],
                                Fsb[:]).then_inc(s_stg.h, 16)
                    s_stg.n += 16
                    g.wait_ge(s_stg.h, s_stg.n)


    for st in range(3):
        for chb in range(NCB):
            with nc.Block() as b:
                @b.sync
                def _(q, st=st, chb=chb):
                    if st == 0:
                        q.dma_start(ycur[:].rearrange("b (e a) -> b e a", a=64),
                                    xph_ab[:, chb * CB:(chb + 1) * CB, :]).then_inc(s_in.h, 16)
                    else:
                        q.dma_start(ycur[:], yh[st % 2][:, chb * GW:(chb + 1) * GW]
                                    ).then_inc(s_in.h, 16)
                    s_in.n += 16
                    q.dma_start(Fsb[:], ctx.dram["Ffile"][st][:, chb * NT2 * 130:
                                                              (chb + 1) * NT2 * 130]).then_inc(s_in.h, 16)
                    s_in.n += 16
                    q.dma_start(gate[:].rearrange("b (e a) -> b e a", a=64),
                                xph_ab[:, (st + 1) * 1024 + chb * CB:
                                        (st + 1) * 1024 + (chb + 1) * CB, :]).then_inc(s_in.h, 16)
                    s_in.n += 16

                inY, inF, inG = s_in.n - 32, s_in.n - 16, s_in.n
                pe0, ac0 = s_pe.n, s_ac.n

                @b.tensor
                def _(t, inY=inY, pe0=pe0, ac0=ac0):
                    t.wait_ge(s_in.h, inY)
                    sec_s1(t, ycur, pe0, ac0)

                @b.scalar
                def _(a, pe0=pe0):
                    sec_evac(a, Usb, pe0)

                @b.vector
                def _(v, ac0=ac0):
                    sec_tw(v, ac0, inverse=False)

                dv1 = s_dv.n
                pe1 = s_pe.n

                @b.tensor
                def _(t, dv1=dv1, ac0=ac0):
                    sec_s2(t, dv1, ac0, inverse=False)

                @b.scalar
                def _(a, pe1=pe1):
                    sec_evac(a, Usb, pe1, mode="grp")

                ac1 = s_ac.n - NGR

                @b.vector
                def _(v, ac1=ac1, inF=inF):
                    sec_pw(v, ac1, inF)

                dv2 = s_dv.n
                pe2 = s_pe.n

                @b.tensor
                def _(t, dv2=dv2, ac1=ac1):
                    sec_s2(t, dv2, ac1, inverse=True)

                @b.scalar
                def _(a, pe2=pe2):
                    sec_evac(a, Usb, pe2, mode="grp")

                ac2 = s_ac.n - NGR

                @b.vector
                def _(v, ac2=ac2):
                    sec_tw(v, ac2, inverse=True)

                dv3 = s_dv.n
                pe3 = s_pe.n
                ac3 = s_ac.n

                @b.tensor
                def _(t, dv3=dv3, ac3=ac3):
                    sec_pivot(t, dv3, ac3)

                @b.scalar
                def _(a, pe3=pe3):
                    sec_pivev(a, pe3)

                ac4 = s_ac.n - 2 * NT2
                dv4 = s_dv.n
                pe4 = s_pe.n

                @b.tensor
                def _(t, ac4=ac4, dv4=dv4):
                    sec_is1(t, ac4, dv4)

                @b.vector
                def _(v, pe4=pe4, inG=inG, st=st):
                    sec_gate(v, pe4, inG, ynext if st < 2 else ynf32)

                dv5 = s_dv.n

                @b.gpsimd
                def _(g, st=st, chb=chb, dv5=dv5):
                    g.wait_ge(s_dv.h, dv5)
                    if st < 2:
                        g.dma_start(yh[(st + 1) % 2][:, chb * GW:(chb + 1) * GW],
                                    ynext[:]).then_inc(s_stg.h, 16)
                    else:
                        g.dma_start(ctx.dram["y3h"][:, chb * GW:(chb + 1) * GW],
                                    ynf32).then_inc(s_stg.h, 16)
                    s_stg.n += 16
                    g.wait_ge(s_stg.h, s_stg.n)

    xsrc = ctx.dram[src_name].rearrange("(c p) t -> p c t", p=128)
    ydst = ctx.dram[out_name].rearrange("(c p) t -> p c t", p=128)
    y3v = ctx.dram["y3h"].rearrange("b (cb p a) -> p cb b a", p=128, a=64)
    xb = ctx.ln_bufs()["xb"]
    y3b = Usb[:, 0:4096]
    s_r5 = ctx.sem("l1_r5")
    s_a5 = ctx.sem("l1_a5")
    s_o5 = ctx.sem("l1_o5")
    with nc.Block() as b:
        @b.sync
        def _(q):
            for k in range(NCK):
                if k >= 1:
                    q.wait_ge(s_o5.h, k * 16)
                q.dma_start(xb[:].rearrange("p (c f) -> p c f", c=NCH),
                            xsrc[:, :, k * 512:(k + 1) * 512]).then_inc(s_r5.h, 16)
                for cb in range(8):
                    q.dma_start(y3b[:, cb * 512:(cb + 1) * 512].rearrange(
                                    "p (b2 a) -> p b2 a", a=64),
                                y3v[:, cb, 8 * k:8 * k + 8, :]).then_inc(s_r5.h, 16)

        @b.vector
        def _(v):
            for k in range(NCK):
                v.wait_ge(s_r5.h, (k + 1) * 144)
                for c in range(NCH):
                    v.tensor_add(xb[:, c * 512:(c + 1) * 512], xb[:, c * 512:(c + 1) * 512],
                                 y3b[:, c * 512:(c + 1) * 512]).then_inc(s_a5.h, 1)

        @b.gpsimd
        def _(g):
            for k in range(NCK):
                g.wait_ge(s_a5.h, (k + 1) * 8)
                g.dma_start(ydst[:, :, k * 512:(k + 1) * 512],
                            xb[:].rearrange("p (c f) -> p c f", c=NCH)).then_inc(s_o5.h, 16)
            g.wait_ge(s_o5.h, NCK * 16)


def build_launch_l1attn():
    nc = bass.Bass("TRN2", target_bir_lowering=False, detect_race_conditions=False)
    ctx = Ctx(nc)
    ctx.din("xT", [D, T], F32)
    ctx.din("L1_attn_Wp", [1024, 4096])
    ctx.din("L1_attn_bp", [4096])
    ctx.din("L1_attn_Wf", [1024, 3072])
    ctx.din("L1_attn_bf", [3072])
    for k, v in BIGCONSTS.items():
        ctx.din(k, list(v.shape), F32)
    ctx.dout("yT", [D, T], F32)
    ctx.dint("lnh", [D, T], BF16)
    ctx.load_consts()
    RING = 2
    lnx = ctx.sbuf("lnring", [128, NCH * RING * CHUNK], BF16)
    lnx_at = lambda c, k: lnx[:, (c * RING + k % RING) * CHUNK:
                              (c * RING + k % RING + 1) * CHUNK]
    lnx_view = lambda k: lnx[:].rearrange("p (c s f) -> p c s f", c=NCH, s=RING)[:, :, k % RING, :]
    s_ln, s_lnst = emit_ln(ctx, "xT", lnx_at, "ln1a", bank0=0, ring=RING, to_hbm="lnh",
                           lnx_view=lnx_view)
    emit_l1attn(ctx, s_lnst, "lnh", "xT", "yT", lnx)
    return ctx



def _blk_l0attn(ctx, src, dst):
    ctx.dint("lnh_a0", [D, T], BF16)
    RING = 2
    lnx = ctx.sbuf("lnring", [128, NCH * RING * CHUNK], BF16)
    lnx_at = lambda c, k: lnx[:, (c * RING + k % RING) * CHUNK:
                              (c * RING + k % RING + 1) * CHUNK]
    lnx_view = lambda k: lnx[:].rearrange("p (c s f) -> p c s f", c=NCH, s=RING)[:, :, k % RING, :]
    s_ln, s_lnst = emit_ln(ctx, src, lnx_at, "ln0a", double_ln=True, bank0=0,
                           ring=RING, to_hbm="lnh_a0", lnx_view=lnx_view)
    emit_l0attn(ctx, s_lnst, "lnh_a0", src, dst, lnx)


def _blk_l0ffn(ctx, src, dst):
    lnx = ctx.sbuf("lnx0", [128, NCH * T], BF16)
    lnx_at = lambda c, k: lnx[:, c * T + k * CHUNK: c * T + (k + 1) * CHUNK]
    w = emit_l0ffn_weights(ctx)
    s_ln = emit_ln(ctx, src, lnx_at, "ln0", bank0=0)
    emit_l0ffn_main(ctx, w, lnx_at, s_ln, src, dst)


def _ln_ring_hbm(ctx, src, tag, lnh_name, double_ln=False, dtype=BF16):
    ctx.dint(lnh_name, [D, T], dtype)
    RING = 2
    lnx = ctx.sbuf("lnring", [128, NCH * RING * CHUNK], dtype)
    lnx_at = lambda c, k: lnx[:, (c * RING + k % RING) * CHUNK:
                              (c * RING + k % RING + 1) * CHUNK]
    lnx_view = lambda k: lnx[:].rearrange("p (c s f) -> p c s f", c=NCH, s=RING)[:, :, k % RING, :]
    s_ln, s_lnst = emit_ln(ctx, src, lnx_at, tag, double_ln=double_ln, bank0=0,
                           ring=RING, to_hbm=lnh_name, lnx_view=lnx_view)
    return lnx, s_lnst


def _blk_l1attn(ctx, src, dst):
    lnx, s_lnst = _ln_ring_hbm(ctx, src, "ln1a", "lnh_a1")
    emit_l1attn(ctx, s_lnst, "lnh_a1", src, dst, lnx)


def _blk_l1ffn(ctx, src, dst):
    ctx.dint("acch", [D, T], BF16)
    lnx, s_lnst = _ln_ring_hbm(ctx, src, "ln1f", "lnh_f1")
    emit_l1ffn(ctx, s_lnst, "lnh_f1", src, dst)


def _blk_l2attn_a(ctx, src, dst):
    ctx.dint("rn1h", [D, T], BF16)
    ctx.dint("rn2h", [D, T], BF16)
    ctx.dint("x1h", [D, T], F32)
    lnx, s_lnst = _ln_ring_hbm(ctx, src, "ln2a", "lnh_a2")
    emit_l2attn_a(ctx, s_lnst, "lnh_a2", dst)


def _blk_l2attn_b(ctx, src, x2name, dst):
    ctx.dint("rn3h", [D, T], BF16)
    ctx.dint("h1h", [4096, T], BF16)
    ctx.sems_f2m_pe = ctx.sem("f2m_pe")
    emit_l2attn_b(ctx, src, x2name, dst)


def _blk_l2ffn(ctx, src, dst):
    lnx, s_lnst = _ln_ring_hbm(ctx, src, "ln2f", "lnh_f2", dtype=F32)
    emit_l2ffn(ctx, s_lnst, "lnh_f2", src, dst)


def _blk_l3attn(ctx, src, dst):
    lnx, s_lnst = _ln_ring_hbm(ctx, src, "ln3a", "lnh_a3")
    emit_l3attn(ctx, s_lnst, "lnh_a3", src, dst)


def _blk_l3ffn(ctx, src, dst):
    lnx, s_lnst = _ln_ring_hbm(ctx, src, "ln3f", "lnh_f3")
    emit_l3ffn(ctx, s_lnst, "lnh_f3", src, dst)


FUSED_STEPS = [
    ("f0", ("alibit", "maskq", "q512", "blk64"), _blk_l0attn),
    ("f1", (), _blk_l0ffn),
    ("f2", (), _blk_l1attn),
    ("f3", ("sel8",), _blk_l1ffn),
    ("f4", (), _blk_l2attn_a),
    ("f5", (), _blk_l2attn_b),
    ("f6", (), _blk_l2ffn),
    ("f7", ("triu", "triuf"), _blk_l3attn),
    ("f8", (), _blk_l3ffn),
]

FUSED_WKEYS = (
    "L0_attn_gamma", "L0_attn_w_q", "L0_attn_w_kv", "L0_attn_null_kv",
    "L0_attn_q_scale", "L0_attn_k_scale", "L0_attn_w_out",
    "L0_ffn_W", "L0_ffn_b", "L0_ffn_lora_down", "L0_ffn_lora_up",
    "L0_ffn_down_aux", "L0_ffn_up_aux",
    "L1_attn_Wp", "L1_attn_bp", "L1_attn_Wf", "L1_attn_bf",
    "L1_ffn_gate_w", "L1_ffn_gate_b", "L1_ffn_sw1", "L1_ffn_sb1",
    "L1_ffn_sw2", "L1_ffn_sb2", "L1_ffn_rw1", "L1_ffn_rb1",
    "L1_ffn_rw2", "L1_ffn_rb2",
    "L2_attn_conv_dw_w", "L2_attn_conv_dw_b", "L2_attn_conv_pw_w",
    "L2_attn_conv_pw_b", "L2_attn_conv_norm_g", "L2_attn_gru_norm_g",
    "L2_attn_gru_w", "L2_attn_ff_norm_g", "L2_attn_ff_w1", "L2_attn_ff_b1",
    "L2_attn_ff_w2", "L2_attn_ff_b2",
    "L2_ffn_w_gate", "L2_ffn_w_up", "L2_ffn_w_down",
    "L3_attn_Wq", "L3_attn_Wk", "L3_attn_Wv", "L3_attn_Wo",
    "L3_ffn_W_base", "L3_ffn_b_base", "L3_ffn_W_enc", "L3_ffn_b_enc",
    "L3_ffn_W_dec", "L3_ffn_b_dec", "L3_ffn_W_down", "L3_ffn_b_down",
    "L3_ffn_W_up", "L3_ffn_b_up", "L3_ffn_connected_idx",
)

_WSHAPES = {
    "L0_attn_gamma": [1024], "L0_attn_w_q": [1024, 1024],
    "L0_attn_w_kv": [1024, 2048], "L0_attn_null_kv": [16, 4, 64],
    "L0_attn_q_scale": [64], "L0_attn_k_scale": [64], "L0_attn_w_out": [1024, 1024],
    "L0_ffn_W": [1024, 1024], "L0_ffn_b": [1024], "L0_ffn_lora_down": [4, 100],
    "L0_ffn_lora_up": [50, 4], "L0_ffn_down_aux": [100, 1024], "L0_ffn_up_aux": [1024, 50],
    "L1_attn_Wp": [1024, 4096], "L1_attn_bp": [4096],
    "L1_attn_Wf": [1024, 3072], "L1_attn_bf": [3072],
    "L1_ffn_gate_w": [1024, 8], "L1_ffn_gate_b": [8],
    "L1_ffn_sw1": [1, 1024, 2048], "L1_ffn_sb1": [1, 2048],
    "L1_ffn_sw2": [1, 1024, 1024], "L1_ffn_sb2": [1, 1024],
    "L1_ffn_rw1": [8, 1024, 2048], "L1_ffn_rb1": [8, 2048],
    "L1_ffn_rw2": [8, 1024, 1024], "L1_ffn_rb2": [8, 1024],
    "L2_attn_conv_dw_w": [1024, 1, 3], "L2_attn_conv_dw_b": [1024],
    "L2_attn_conv_pw_w": [1024, 1024], "L2_attn_conv_pw_b": [1024],
    "L2_attn_conv_norm_g": [1024], "L2_attn_gru_norm_g": [1024],
    "L2_attn_gru_w": [2048, 1024], "L2_attn_ff_norm_g": [1024],
    "L2_attn_ff_w1": [4096, 1024], "L2_attn_ff_b1": [4096],
    "L2_attn_ff_w2": [1024, 4096], "L2_attn_ff_b2": [1024],
    "L2_ffn_w_gate": [2816, 1024], "L2_ffn_w_up": [2816, 1024],
    "L2_ffn_w_down": [1024, 2816],
    "L3_attn_Wq": [256, 1024], "L3_attn_Wk": [256, 1024],
    "L3_attn_Wv": [256, 1024], "L3_attn_Wo": [1024, 256],
    "L3_ffn_W_base": [1024, 1024], "L3_ffn_b_base": [1024],
    "L3_ffn_W_enc": [8, 1024, 256], "L3_ffn_b_enc": [8, 256],
    "L3_ffn_W_dec": [8, 256, 1024], "L3_ffn_b_dec": [8, 1024],
    "L3_ffn_W_down": [8, 1024, 64], "L3_ffn_b_down": [8, 64],
    "L3_ffn_W_up": [8, 64, 1024], "L3_ffn_b_up": [8, 1024],
}


def build_launch_fused(nsteps=9):
    nc = bass.Bass("TRN2", target_bir_lowering=False, detect_race_conditions=False)
    ctx = Ctx(nc)
    I32 = mybir.dt.int32
    ctx.din("xT", [D, T], F32)
    for k in FUSED_WKEYS:
        ctx.din(k, _WSHAPES[k] if k != "L3_ffn_connected_idx" else [8],
                I32 if k == "L3_ffn_connected_idx" else F32)
    for k, v in BIGCONSTS.items():
        ctx.din(k, list(v.shape), F32)
    ctx.dout("yT", [D, T], F32)
    ctx.dout("t3dbg", [128, 64], F32)
    for i in range(1, 8):
        ctx.dint("hx%d" % i, [D, T], F32)
    ctx.dint("x2h", [D, T], F32)

    hand = ["xT", "hx1", "hx2", "hx3", "hx4", "hx5", "hx6", "hx7", "yT"]
    hi = 0
    for si in range(nsteps):
        tag, extras, fn = FUSED_STEPS[si]
        last = (si == nsteps - 1)
        with ctx.frame(tag, extras=extras):
            if fn is _blk_l2attn_a:
                fn(ctx, hand[hi], "x2h")
            elif fn is _blk_l2attn_b:
                fn(ctx, hand[hi], "x2h", "yT" if last else hand[hi + 1])
                hi += 1
            else:
                fn(ctx, hand[hi], "yT" if last else hand[hi + 1])
                hi += 1
    return ctx
